# version 02
# baseline (speedup 1.0000x reference)
import contextlib
import numpy as np
import ml_dtypes
import concourse.bass as bass
import concourse.mybir as mybir
from concourse.bass_utils import run_bass_kernel_spmd

F32 = mybir.dt.float32
BF16 = mybir.dt.bfloat16
U32 = mybir.dt.uint32
AF = mybir.ActivationFunctionType
ALU = mybir.AluOpType
AX = mybir.AxisListType

T = 4096
D = 1024
NBATCH = 2
S = 2048
NT = T // 128
NF = D // 128
E = 8
H_FFN = 4096
CAPACITY = 5121
NEGINF = -1.0e30
PI = float(np.pi)


class Sem:
    __slots__ = ("h", "n")

    def __init__(self, h):
        self.h = h
        self.n = 0

    def inc(self, ins, k=1):
        ins.then_inc(self.h, k)
        self.n += k
        return self.n

    def wait(self, eng, val=None):
        eng.wait_ge(self.h, self.n if val is None else val)


class K:

    def __init__(self, name):
        self.nc = bass.Bass("TRN2", target_bir_lowering=False, name=name)
        self.ctx = contextlib.ExitStack()
        self._sems = {}
        self.bar = self.sem("BAR")
        self.marks = {}

    def sem(self, name):
        if name not in self._sems:
            self._sems[name] = Sem(self.ctx.enter_context(self.nc.semaphore(name)))
        return self._sems[name]

    def sb(self, name, shape, dt):
        return self.ctx.enter_context(self.nc.sbuf_tensor(name, shape, dt))

    def ps(self, name, shape, dt=F32):
        return self.ctx.enter_context(self.nc.psum_tensor(name, shape, dt))

    def dram(self, name, shape, dt, kind="Internal"):
        return self.nc.dram_tensor(name, shape, dt, kind=kind)

    def close(self):
        self.ctx.close()


class Loc:

    def __init__(self, k):
        self.k = k
        self.ctx = contextlib.ExitStack()

    def sb(self, name, shape, dt):
        return self.ctx.enter_context(self.k.nc.sbuf_tensor(name, shape, dt))

    def ps(self, name, shape, dt=F32):
        return self.ctx.enter_context(self.k.nc.psum_tensor(name, shape, dt))

    def close(self):
        self.ctx.close()


def common_init(k, need_ident=True, need_consts=True):
    nc = k.nc
    k.identb = k.sb("identb", [128, 128], BF16)
    k.identf = k.sb("identf", [128, 128], F32)
    k.consts = k.sb("consts", [128, 8], F32)
    k.egrid = k.sb("egrid", [128, E], F32)
    k.ones_row = k.sb("ones_row", [1, 512], BF16)
    k.ones_rowf = k.sb("ones_rowf", [1, 512], F32)
    k.ones_colf8 = k.sb("ones_colf8", [E, 1], F32)
    init = k.sem("init")
    blk = k.nc.Block()
    blk.__enter__()

    @blk.gpsimd
    def _(g):
        for ident in (k.identb, k.identf):
            init.inc(g.memset(ident[:, :], 0.0), 1)
            g.wait_ge(init.h, init.n)
            g.affine_select(out=ident[:, :], in_=ident[:, :],
                            compare_op=ALU.not_equal, fill=1.0, base=0,
                            pattern=[[-1, 128]], channel_multiplier=1)
        g.memset(k.consts[:, 0:1], 1e-5)
        g.memset(k.consts[:, 1:2], 1e-6)
        g.memset(k.consts[:, 2:3], PI)
        g.memset(k.consts[:, 3:4], 1.0)
        g.memset(k.consts[:, 4:5], -PI)
        g.iota(k.egrid[:, :], [[1, E]], base=0, channel_multiplier=0,
               allow_small_or_imprecise_dtypes=True)
        g.memset(k.ones_row[:, :], 1.0)
        g.memset(k.ones_rowf[:, :], 1.0)
        init.inc(g.memset(k.ones_colf8[:, :], 1.0), 1)
    blk.__exit__(None, None, None)
    k.init_done = init.n
    k.init_sem = init


class Phase:

    def __init__(self, k):
        self.k = k
        self.entry = k.bar.n

    def begin(self, eng):
        eng.wait_ge(self.k.bar.h, self.entry)
        self.k.init_sem.wait(eng, self.k.init_done)

    def done(self, ins):
        self.k.bar.inc(ins, 1)

    def done_q(self, eng, dsem):
        eng.wait_ge(dsem.h, dsem.n)
        eng.sem_inc(self.k.bar.h, 1)
        self.k.bar.n += 1


def emit_ln(k, xres, t_tm, tag, eps_col_idx=0, n_tiles=NT):
    nc = k.nc
    ph = Phase(k)
    sdl = [k.sem("ln_d%d" % j) for j in range(4)]
    sv = k.sem("ln_v")
    sa = k.sem("ln_a")
    so = k.sem("ln_o")
    sf = k.sem("ln_f")
    NBUF = 4
    loc = Loc(k)
    xt = loc.sb("ln_x" + tag, [128, NBUF * D], F32)
    xc = loc.sb("ln_c" + tag, [128, NBUF * D], F32)
    sq = loc.sb("ln_q" + tag, [128, D], F32)
    tt = loc.sb("ln_t" + tag, [128, NBUF * D], BF16)
    cols = loc.sb("ln_s" + tag, [128, 8 * NBUF], F32)

    sdl0 = [x.n for x in sdl]
    sv0, sa0, so0, sf0 = sv.n, sa.n, so.n, sf.n
    blk = k.nc.Block()
    blk.__enter__()

    @blk.sync
    def _(sync):
        ph.begin(sync)
        for i in range(n_tiles):
            b = i % NBUF
            if i >= NBUF:
                sync.wait_ge(sf.h, sf0 + (i - NBUF + 1))
            sdl[b].inc(sync.dma_start(out=xt[:, b * D:(b + 1) * D],
                                      in_=xres[i * 128:(i + 1) * 128, :]), 16)
        for j in range(4):
            sync.wait_ge(sdl[j].h, sdl[j].n)
        sync.sem_inc(k.bar.h, 1)
    k.bar.n += 1

    @blk.vector
    def _(v):
        ph.begin(v)
        for i in range(n_tiles):
            b = i % NBUF
            c = 8 * b
            v.wait_ge(sdl[b].h, sdl0[b] + 16 * (i // NBUF + 1))
            v.reduce_sum(cols[:, c:c + 1], xt[:, b * D:(b + 1) * D], axis=AX.X).then_inc(sv.h, 1)
            v.wait_ge(sv.h, sv0 + 4 * i + 1)
            v.tensor_scalar_mul(cols[:, c + 1:c + 2], cols[:, c:c + 1], -1.0 / D).then_inc(sv.h, 1)
            v.wait_ge(sa.h, sa0 + 3 * (i + 1))
            v.reciprocal(cols[:, c + 3:c + 4], cols[:, c + 2:c + 3]).then_inc(sv.h, 1)
            v.wait_ge(sv.h, sv0 + 4 * i + 3)
            v.tensor_scalar_mul(tt[:, b * D:(b + 1) * D], xc[:, b * D:(b + 1) * D],
                                cols[:, c + 3:c + 4]).then_inc(sv.h, 1)
        sv.n = sv0 + 4 * n_tiles
        ph.done(_last_inc(v, k))

    @blk.scalar
    def _(s):
        ph.begin(s)
        for i in range(n_tiles):
            b = i % NBUF
            c = 8 * b
            s.wait_ge(sv.h, sv0 + 4 * i + 2)
            s.activation(xc[:, b * D:(b + 1) * D], xt[:, b * D:(b + 1) * D],
                         AF.Identity, bias=cols[:, c + 1:c + 2], scale=1.0).then_inc(sa.h, 1)
            s.wait_ge(sa.h, sa0 + 3 * i + 1)
            s.activation(sq[:, :], xc[:, b * D:(b + 1) * D], AF.Square,
                         accum_out=cols[:, c + 4:c + 5]).then_inc(sa.h, 1)
            s.wait_ge(sa.h, sa0 + 3 * i + 2)
            s.activation(cols[:, c + 2:c + 3], cols[:, c + 4:c + 5], AF.Sqrt,
                         bias=k.consts[:, eps_col_idx:eps_col_idx + 1],
                         scale=1.0 / D).then_inc(sa.h, 1)
        sa.n = sa0 + 3 * n_tiles
        ph.done(_last_inc_a(s, k))

    @blk.gpsimd
    def _(g):
        ph.begin(g)
        for i in range(n_tiles):
            b = i % NBUF
            if i >= 1:
                g.wait_ge(so.h, so0 + 16 * i)
                g.sem_inc(sf.h, 1)
                sf.n += 1
            g.wait_ge(sv.h, sv0 + 4 * (i + 1))
            so.inc(g.dma_start(out=t_tm[i * 128:(i + 1) * 128, :],
                               in_=tt[:, b * D:(b + 1) * D]), 16)
        ph.done_q(g, so)
    blk.__exit__(None, None, None)
    loc.close()


def _last_inc(eng, k):
    if hasattr(eng, "memset"):
        return eng.memset(k.consts[:, 5:6], 0.0)
    return eng.memzero(k.consts[:, 5:6])


def emit_tf(k, t_tm, t_fm, tag, rows=T, dcols=D):
    ph = Phase(k)
    sdl = [k.sem("tf_d%d" % j) for j in range(2)]
    so = k.sem("tf_o")
    sf = k.sem("tf_f")
    nchunk = dcols // 128
    loc = Loc(k)
    buf = loc.sb("tf_b" + tag, [128, 2 * rows], BF16)
    sdl0 = [x.n for x in sdl]
    so0, sf0 = so.n, sf.n
    blk = k.nc.Block()
    blk.__enter__()

    @blk.sync
    def _(sync):
        ph.begin(sync)
        for c in range(nchunk):
            b = c % 2
            if c >= 2:
                sync.wait_ge(sf.h, sf0 + (c - 1))
            sdl[b].inc(sync.dma_start(out=buf[:, b * rows:(b + 1) * rows],
                                      in_=t_tm[:, c * 128:(c + 1) * 128],
                                      transpose=True), 16)
        for j in range(2):
            sync.wait_ge(sdl[j].h, sdl[j].n)
        sync.sem_inc(k.bar.h, 1)
    k.bar.n += 1

    @blk.gpsimd
    def _(g):
        ph.begin(g)
        for c in range(nchunk):
            b = c % 2
            if c >= 1:
                g.wait_ge(so.h, so0 + 16 * c)
                g.sem_inc(sf.h, 1)
                sf.n += 1
            g.wait_ge(sdl[b].h, sdl0[b] + 16 * (c // 2 + 1))
            so.inc(g.dma_start(out=t_fm[c * 128:(c + 1) * 128, :],
                               in_=buf[:, b * rows:(b + 1) * rows]), 16)
        ph.done_q(g, so)
    blk.__exit__(None, None, None)
    loc.close()


def emit_routing_l0(k, t_fm, wg_f32, gates, tag):
    nc = k.nc
    cc_in = k.dram("rcc_in" + tag, [1, E], F32)
    cc_out = k.dram("rcc_out" + tag, [E, E], F32)

    loc = Loc(k)
    wg_sb = loc.sb("r_wg" + tag, [128, NF * E], BF16)
    tfm = loc.sb("r_t" + tag, [128, NF * 512], BF16)
    max8 = loc.sb("r_m8" + tag, [128, 8], F32)
    lgt = loc.sb("r_lg" + tag, [128, 2 * E], F32)
    idxu = loc.sb("r_iu" + tag, [128, 8], U32)
    cols = loc.sb("r_c" + tag, [128, 8], F32)
    expv = loc.sb("r_ex" + tag, [128, 8], F32)
    oh_all = loc.sb("r_oh" + tag, [128, NT * E], F32)
    cwh = loc.sb("r_cw" + tag, [128, NT], F32)
    ohT = loc.sb("r_ohT" + tag, [E, T], F32)
    zer8 = loc.sb("r_z8" + tag, [E, T], F32)
    scn = loc.sb("r_sc" + tag, [E, T], F32)
    sm8 = loc.sb("r_s8" + tag, [E, 16], F32)
    row1 = loc.sb("r_r1" + tag, [1, 16], F32)
    pidu = loc.sb("r_pd" + tag, [1, 2], U32)
    cnts = loc.sb("r_cn" + tag, [E, E], F32)
    k.ps_anchor = loc.ps("r_anc" + tag, [1, 8], F32)
    ps_lg = loc.ps("r_plg" + tag, [128, 1024], F32)
    ps_t = loc.ps("r_pt" + tag, [128, 128], F32)
    ps_s = loc.ps("r_ps" + tag, [E, 128], F32)

    sd = k.sem("r_d")
    st = k.sem("r_t")
    sv = k.sem("r_v")
    sa = k.sem("r_a")
    sg = k.sem("r_g")
    sgc = k.sem("r_gc")
    scc = k.sem("r_cc")

    import os as _os
    dbg = _os.environ.get("DEBUG_ROUTING")
    if dbg:
        lg_dump = k.dram("lgdump" + tag, [T, E], F32, kind="ExternalOutput")
        oh_dump = k.dram("ohdump" + tag, [128, NT * E], F32, kind="ExternalOutput")
        ohT_dump = k.dram("ohTdump" + tag, [E, T], F32, kind="ExternalOutput")
        oh4_dump = k.dram("oh4dump" + tag, [E, T], F32, kind="ExternalOutput")
        cw_dump = k.dram("cwdump" + tag, [128, NT], F32, kind="ExternalOutput")
        scn_dump = k.dram("scndump" + tag, [E, T], F32, kind="ExternalOutput")
    ph = Phase(k)
    sd0, st0, sv0, sa0 = sd.n, st.n, sv.n, sa.n
    blk = nc.Block()
    blk.__enter__()

    sdg = k.sem("r_dg")
    sdg0 = sdg.n

    @blk.gpsimd
    def _(g):
        ph.begin(g)
        sdg.inc(g.dma_start(
            out=wg_sb[:, :].rearrange("p (c e) -> p c e", c=NF),
            in_=wg_f32.ap().rearrange("(c p) e -> p c e", p=128)), 16)

    @blk.sync
    def _(sync):
        ph.begin(sync)
        for w in range(NT // 4):
            if w >= 1:
                sync.wait_ge(st.h, st0 + 4 * w)
            sd.inc(sync.dma_start(
                out=tfm[:, :].rearrange("p (c t) -> p c t", c=NF),
                in_=t_fm[:, w * 512:(w + 1) * 512].rearrange("(c p) t -> p c t", p=128)), 16)
        sync.wait_ge(sd.h, sd.n)
        sync.sem_inc(k.bar.h, 1)
    k.bar.n += 1

    @blk.tensor
    def _(t):
        ph.begin(t)
        t.wait_ge(sdg.h, sdg0 + 16)
        for i in range(NT):
            w, j = divmod(i, 4)
            b = i % 2
            t.wait_ge(sd.h, sd0 + 16 * (w + 1))
            import os as _os
            if _os.environ.get("R1_SERIAL"):
                if i >= 1:
                    t.wait_ge(sa.h, sa0 + i)
            elif i >= 2:
                t.wait_ge(sv.h, sv0 + 8 * (i - 2) + 1)
            ins = None
            for c in range(NF):
                ins = t.matmul(ps_lg[:, b * 512:b * 512 + E],
                               tfm[:, c * 512 + j * 128: c * 512 + (j + 1) * 128],
                               wg_sb[:, c * E:(c + 1) * E],
                               start=(c == 0), stop=(c == NF - 1))
            st.inc(ins, 1)
        ph.done(_last_inc_t(t, k))

    @blk.vector
    def _(v):
        ph.begin(v)
        for i in range(NT):
            b = i % 2
            v.wait_ge(st.h, st0 + i + 1)
            v.tensor_copy(lgt[:, b * E:(b + 1) * E], ps_lg[:, b * 512:b * 512 + E]).then_inc(sv.h, 1)
            v.wait_ge(sv.h, sv0 + 8 * i + 1)
            v.max(max8[:, :], lgt[:, b * E:(b + 1) * E]).then_inc(sv.h, 1)
            v.wait_ge(sv.h, sv0 + 8 * i + 2)
            v.max_index(idxu[:, :], max8[:, :], lgt[:, b * E:(b + 1) * E]).then_inc(sv.h, 1)
            v.wait_ge(sv.h, sv0 + 8 * i + 3)
            v.tensor_copy(cols[:, 0:1], idxu[:, 0:1]).then_inc(sv.h, 1)
            v.tensor_scalar_mul(cols[:, 1:2], max8[:, 0:1], -1.0).then_inc(sv.h, 1)
            v.wait_ge(sv.h, sv0 + 8 * i + 5)
            v.tensor_single_scalar(oh_all[:, i * E:(i + 1) * E], k.egrid[:, :],
                                   cols[:, 0:1], ALU.is_equal).then_inc(sv.h, 1)
            v.wait_ge(sa.h, sa0 + i + 1)
            v.reciprocal(cols[:, 3:4], cols[:, 2:3]).then_inc(sv.h, 1)
            if dbg:
                pass
            v.wait_ge(sv.h, sv0 + 8 * i + 7)
            v.tensor_scalar_mul(cwh[:, i:i + 1], cols[:, 3:4], 0.5).then_inc(sv.h, 1)
        sv.n = sv0 + 8 * NT
        ph.done(_last_inc(v, k))

    @blk.scalar
    def _(s):
        ph.begin(s)
        for i in range(NT):
            b = i % 2
            s.wait_ge(sv.h, sv0 + 8 * i + 5)
            s.activation(expv[:, :], lgt[:, b * E:(b + 1) * E], AF.Exp,
                         bias=cols[:, 1:2], accum_out=cols[:, 2:3]).then_inc(sa.h, 1)
        sa.n = sa0 + NT
        ph.done(_last_inc_a(s, k))

    if dbg:
        sdbg = k.sem("r_dbg")

        @blk.gpsimd
        def _(g):
            ph.begin(g)
            for i in range(NT):
                b = i % 2
                g.wait_ge(sa.h, sa0 + i + 1)
                sdbg.inc(g.dma_start(out=lg_dump[i * 128:(i + 1) * 128, :],
                                     in_=lgt[:, b * E:(b + 1) * E]), 16)
            g.wait_ge(sv.h, sv0 + 8 * NT)
            sdbg.inc(g.dma_start(out=oh_dump[:, :], in_=oh_all[:, :]), 16)
            g.wait_ge(sdbg.h, sdbg.n)
            g.sem_inc(k.bar.h, 1)
        k.bar.n += 1

    blk.__exit__(None, None, None)

    ph2 = Phase(k)
    st0, sv0 = st.n, sv.n
    blk = nc.Block()
    blk.__enter__()

    @blk.tensor
    def _(t):
        ph2.begin(t)
        for i in range(NT):
            if i >= 1:
                t.wait_ge(sv.h, sv0 + i)
            st.inc(t.transpose(ps_s[:, :], oh_all[:, i * E:(i + 1) * E],
                               k.identf[:, :]), 1)
        ph2.done(_last_inc_t(t, k))

    @blk.vector
    def _(v):
        ph2.begin(v)
        for i in range(NT):
            v.wait_ge(st.h, st0 + i + 1)
            v.tensor_copy(ohT[:, i * 128:(i + 1) * 128], ps_s[:, :]).then_inc(sv.h, 1)
        v.memset(zer8[:, :], 0.0).then_inc(sv.h, 1)
        v.wait_ge(sv.h, sv0 + NT + 1)
        v.tensor_tensor_scan(scn[:, :], ohT[:, :], zer8[:, :], 0.0,
                             ALU.add, ALU.add).then_inc(sv.h, 1)
        sv.n = sv0 + NT + 2
        ph2.done(_last_inc(v, k))

    if dbg:
        sdbg2 = k.sem("r_dbg2")

        @blk.gpsimd
        def _(g):
            ph2.begin(g)
            g.wait_ge(sv.h, sv.n)
            sdbg2.inc(g.dma_start(out=ohT_dump[:, :], in_=ohT[:, :]), 16)
            g.wait_ge(sdbg2.h, sdbg2.n)
            g.sem_inc(k.bar.h, 1)
        k.bar.n += 1
    blk.__exit__(None, None, None)

    ph3 = Phase(k)
    st0, sv0, sg0, scc0 = st.n, sv.n, sg.n, scc.n
    blk = nc.Block()
    blk.__enter__()

    @blk.tensor
    def _(t):
        ph3.begin(t)
        st.inc(t.transpose(ps_t[0:1, 0:E], scn[:, T - 1:T], k.identf[0:E, 0:E]), 1)

    @blk.vector
    def _(v):
        ph3.begin(v)
        v.wait_ge(st.h, st0 + 1)
        v.tensor_copy(row1[:, 0:E], ps_t[0:1, 0:E]).then_inc(sv.h, 1)
    sv.n += 1

    @blk.gpsimd
    def _(g):
        ph3.begin(g)
        g.wait_ge(sv.h, sv0 + 1)
        sg.inc(g.dma_start(out=cc_in[:, :], in_=row1[:, 0:E]), 16)
        g.wait_ge(sg.h, sg0 + 16)
        scc.inc(g.collective_compute(
            "AllGather", ALU.bypass, replica_groups=[list(range(8))],
            ins=[cc_in.ap().opt()], outs=[cc_out.ap().opt()]), 1)
        g.wait_ge(scc.h, scc0 + 1)
        sg.inc(g.dma_start(out=cnts[:, :], in_=cc_out[:, :]), 16)
        sg.inc(g.dma_start(out=pidu[:, 0:1], in_=k.nc.partition_id_tensor[:, :]), 16)
        g.iota(sm8[:, 0:1], [[0, 1]], base=0, channel_multiplier=1,
               allow_small_or_imprecise_dtypes=True)
        sgc.inc(g.memset(sm8[:, 1:2], 0.0), 1)

    gp_done = sg.n
    gpc_done = sgc.n

    @blk.vector
    def _(v):
        v.wait_ge(sg.h, gp_done)
        v.wait_ge(sgc.h, gpc_done)
        v.tensor_copy(row1[:, 8:9], pidu[:, 0:1]).then_inc(sv.h, 1)
        sv.n += 1
        ph3.done(_last_inc(v, k))
    blk.__exit__(None, None, None)

    ph4 = Phase(k)
    st0, sv0 = st.n, sv.n
    blk = nc.Block()
    blk.__enter__()

    @blk.tensor
    def _(t):
        ph4.begin(t)
        st.inc(t.matmul(ps_s[:, 0:1], k.ones_rowf[0:1, 0:E], row1[:, 8:9],
                        start=True, stop=True), 1)
        t.wait_ge(sv.h, sv0 + 2)
        st.inc(t.matmul(ps_s[:, 4:5], cnts[:, :], k.ones_colf8[:, 0:1],
                        start=True, stop=True), 1)
        ph4.done(_last_inc_t(t, k))

    @blk.vector
    def _(v):
        ph4.begin(v)
        v.wait_ge(st.h, st0 + 1)
        v.tensor_tensor(sm8[:, 2:3], sm8[:, 0:1], ps_s[:, 0:1], ALU.is_lt).then_inc(sv.h, 1)
        v.wait_ge(sv.h, sv0 + 1)
        v.tensor_scalar_mul(cnts[:, :], cnts[:, :], sm8[:, 2:3]).then_inc(sv.h, 1)
        v.wait_ge(st.h, st0 + 2)
        v.tensor_scalar_add(sm8[:, 3:4], ps_s[:, 4:5], -1.0).then_inc(sv.h, 1)
        v.wait_ge(sv.h, sv0 + 3)
        v.tensor_scalar_add(scn[:, :], scn[:, :], sm8[:, 3:4]).then_inc(sv.h, 1)
        v.wait_ge(sv.h, sv0 + 4)
        v.tensor_single_scalar(scn[:, :], scn[:, :], float(CAPACITY), ALU.is_lt).then_inc(sv.h, 1)
        v.wait_ge(sv.h, sv0 + 5)
        v.tensor_mul(ohT[:, :], ohT[:, :], scn[:, :]).then_inc(sv.h, 1)
        sv.n = sv0 + 6
        ph4.done(_last_inc(v, k))

    if dbg:
        sdbg4 = k.sem("r_dbg4")

        @blk.gpsimd
        def _(g):
            ph4.begin(g)
            g.wait_ge(sv.h, sv.n)
            sdbg4.inc(g.dma_start(out=oh4_dump[:, :], in_=ohT[:, :]), 16)
            sdbg4.inc(g.dma_start(out=scn_dump[:, :], in_=scn[:, :]), 16)
            sdbg4.inc(g.dma_start(out=cw_dump[:, :], in_=cwh[:, :]), 16)
            g.wait_ge(sdbg4.h, sdbg4.n)
            g.sem_inc(k.bar.h, 1)
        k.bar.n += 1
    blk.__exit__(None, None, None)

    ph5 = Phase(k)
    st0, sv0 = st.n, sv.n
    blk = nc.Block()
    blk.__enter__()

    @blk.tensor
    def _(t):
        ph5.begin(t)
        for i in range(NT):
            if i >= 1:
                t.wait_ge(sv.h, sv0 + i)
            st.inc(t.transpose(ps_t[:, 0:E], ohT[:, i * 128:(i + 1) * 128],
                               k.identf[0:E, 0:E]), 1)
        ph5.done(_last_inc_t(t, k))

    @blk.vector
    def _(v):
        ph5.begin(v)
        for i in range(NT):
            v.wait_ge(st.h, st0 + i + 1)
            v.tensor_scalar_mul(gates[:, i * E:(i + 1) * E], ps_t[:, 0:E],
                                cwh[:, i:i + 1]).then_inc(sv.h, 1)
        sv.n = sv0 + NT
        ph5.done(_last_inc(v, k))
    blk.__exit__(None, None, None)
    loc.close()


def _last_inc_t(t, k):
    return t.matmul(k.ps_anchor[:, 0:1], k.ones_rowf[0:1, 0:1],
                    k.ones_rowf[0:1, 0:1], start=True, stop=True)


def _last_inc_a(s, k):
    return s.memzero(k.consts[:, 6:7])


def emit_copy(k, dst, src, rows, cols, tag, nsplit=4):
    ph = Phase(k)
    sg = k.sem("cp_g")
    blk = k.nc.Block()
    blk.__enter__()

    @blk.gpsimd
    def _(g):
        ph.begin(g)
        step = rows // nsplit
        for i in range(nsplit):
            sg.inc(g.dma_start(out=dst[i * step:(i + 1) * step, :],
                               in_=src[i * step:(i + 1) * step, :]), 16)
        ph.done_q(g, sg)
    blk.__exit__(None, None, None)


def emit_ffn(k, t_fm, w1_bf, w2_bf, b1_f32, b2_f32, gates, xres, tag):
    nc = k.nc
    NHC = H_FFN // 128

    sd = k.sem("f_d")
    sll = [k.sem("f_l%d" % j) for j in range(2)]
    st = k.sem("f_t")
    sv = k.sem("f_v")
    sa = k.sem("f_a")
    sg = k.sem("f_g")
    sgf = k.sem("f_gf")

    loc0 = Loc(k)
    b1col_all = loc0.sb("f_b1c" + tag, [128, NHC * E], F32)
    b2sb = loc0.sb("f_b2s" + tag, [E, D], F32)
    php = Phase(k)
    locp = Loc(k)
    b1_tmp = locp.sb("f_b1t" + tag, [E, H_FFN], F32)
    k.ps_anchor = loc0.sb if False else locp.ps("f_anc" + tag, [1, 8], F32)
    ps_p = locp.ps("f_pp" + tag, [128, 8], F32)
    sd0, st0, sv0 = sd.n, st.n, sv.n
    blk = nc.Block()
    blk.__enter__()

    @blk.sync
    def _(sync):
        php.begin(sync)
        sd.inc(sync.dma_start(out=b1_tmp[:, :], in_=b1_f32[:, :]), 16)
        sd.inc(sync.dma_start(out=b2sb[:, :], in_=b2_f32[:, :]), 16)
        php.done_q(sync, sd)

    @blk.tensor
    def _(t):
        php.begin(t)
        t.wait_ge(sd.h, sd0 + 32)
        for c in range(NHC):
            if c >= 1:
                t.wait_ge(sv.h, sv0 + c)
            st.inc(t.transpose(ps_p[:, 0:E], b1_tmp[:, c * 128:(c + 1) * 128],
                               k.identf[0:E, 0:E]), 1)
        php.done(_last_inc_t(t, k))

    @blk.vector
    def _(v):
        php.begin(v)
        for c in range(NHC):
            v.wait_ge(st.h, st0 + c + 1)
            v.tensor_copy(b1col_all[:, c * E:(c + 1) * E], ps_p[:, 0:E]).then_inc(sv.h, 1)
        sv.n = sv0 + NHC
        php.done(_last_inc(v, k))
    blk.__exit__(None, None, None)
    locp.close()

    loc = Loc(k)
    w1sb = loc.sb("f_w1" + tag, [128, NF * H_FFN], BF16)
    w2sb = loc.sb("f_w2" + tag, [128, NHC * D], BF16)
    hsb = loc.sb("f_h" + tag, [128, NHC * 512], BF16)
    t4 = loc.sb("f_t4" + tag, [128, 2 * NF * 512], BF16)
    b2row = loc.sb("f_b2r" + tag, [1, D], F32)
    b2bc = loc.sb("f_b2b" + tag, [128, D], F32)
    yev = loc.sb("f_y" + tag, [128, 2 * 512], F32)
    k.ps_anchor = loc.ps("f_anc2" + tag, [1, 8], F32)
    ps_h = loc.ps("f_ph" + tag, [128, 2 * 512], F32)
    ps_y = loc.ps("f_py" + tag, [128, 2 * 512], F32)
    ps_b = loc.ps("f_pb" + tag, [128, D], F32)

    ph = Phase(k)
    sd0, st0, sv0, sa0, sg0 = sd.n, st.n, sv.n, sa.n, sg.n
    sll0 = [x.n for x in sll]
    sgf0 = sgf.n

    def st_mm(e, tw):
        return st0 + e * (2 + 8 * 40) + 2 + tw * 40

    def sa_h(e, tw):
        return sa0 + (e * 8 + tw) * 32

    def sv_y(e, tw):
        return sv0 + e * 129 + 1 + tw * 16

    blk = nc.Block()
    blk.__enter__()

    @blk.sync
    def _(sync):
        ph.begin(sync)
        for e in range(E):
            if e >= 1:
                sync.wait_ge(st.h, st_mm(e - 1, 8))
            sd.inc(sync.dma_start(
                out=w1sb[:, :].rearrange("p (c h) -> p c h", c=NF),
                in_=w1_bf[e, :, :].rearrange("(c p) h -> p c h", p=128)), 16)
            sd.inc(sync.dma_start(
                out=w2sb[:, :].rearrange("p (c d) -> p c d", c=NHC),
                in_=w2_bf[e, :, :].rearrange("(c p) d -> p c d", p=128)), 16)
        ph.done_q(sync, sd)

    @blk.gpsimd
    def _(g):
        ph.begin(g)
        for e in range(E):
            if e >= 1:
                g.wait_ge(st.h, st_mm(e - 1, 0))
            sg.inc(g.dma_start(out=b2row[:, :], in_=b2sb[e:e + 1, :]), 16)
            for tw in range(8):
                for yi in range(8):
                    tc, dh = divmod(yi, 2)
                    yb = yi % 2
                    g.wait_ge(sg.h, sg.n)
                    g.sem_inc(sgf.h, 1)
                    sgf.n += 1
                    g.wait_ge(sv.h, sv_y(e, tw) + yi * 2 + 2)
                    row0 = (tw * 4 + tc) * 128
                    sg.inc(g.dma_start(
                        out=xres[row0:row0 + 128, dh * 512:(dh + 1) * 512],
                        in_=yev[:, yb * 512:(yb + 1) * 512],
                        accum_op=ALU.add), 16)
        ph.done_q(g, sg)

    @blk.scalar
    def _(s):
        ph.begin(s)
        for e in range(E):
            for tw in range(8):
                b = tw % 2
                if e * 8 + tw >= 2:
                    ew, tww = divmod(e * 8 + tw - 2, 8)
                    s.wait_ge(st.h, st_mm(ew, tww) + 32)
                sll[b].inc(s.dma_start(
                    out=t4[:, b * NF * 512:(b + 1) * NF * 512]
                        .rearrange("p (c t) -> p c t", c=NF),
                    in_=t_fm[:, tw * 512:(tw + 1) * 512]
                        .rearrange("(c p) t -> p c t", p=128)), 16)
                for hc in range(NHC):
                    s.wait_ge(st.h, st_mm(e, tw) + hc + 1)
                    s.activation(hsb[:, hc * 512:(hc + 1) * 512],
                                 ps_h[:, (hc % 2) * 512:(hc % 2 + 1) * 512],
                                 AF.Gelu, bias=b1col_all[:, hc * E + e:hc * E + e + 1],
                                 scale=1.0).then_inc(sa.h, 1)
        sa.n = sa0 + E * 8 * NHC
        ph.done(_last_inc_a(s, k))

    @blk.tensor
    def _(t):
        ph.begin(t)
        for e in range(E):
            t.wait_ge(sg.h, sg0 + 16 * (e * 65 + 1))
            if e >= 1:
                t.wait_ge(sv.h, sv_y(e - 1, 0))
            t.wait_ge(sd.h, sd0 + 32 * (e + 1))
            st.inc(t.matmul(ps_b[:, 0:512], k.ones_rowf[0:1, 0:128],
                            b2row[:, 0:512], start=True, stop=True), 1)
            st.inc(t.matmul(ps_b[:, 512:1024], k.ones_rowf[0:1, 0:128],
                            b2row[:, 512:1024], start=True, stop=True), 1)
            for tw in range(8):
                b = tw % 2
                idx = e * 8 + tw
                t.wait_ge(sll[b].h, sll0[b] + 16 * (idx // 2 + 1))
                for hc in range(NHC):
                    hb = hc % 2
                    if hc >= 2:
                        t.wait_ge(sa.h, sa_h(e, tw) + hc - 1)
                    elif e * 8 + tw > 0:
                        pidx = e * 8 + tw - 1
                        t.wait_ge(sa.h, sa0 + pidx * 32 + 31 + hc)
                    ins = None
                    for c in range(NF):
                        ins = t.matmul(
                            ps_h[:, hb * 512:(hb + 1) * 512],
                            w1sb[:, c * H_FFN + hc * 128: c * H_FFN + (hc + 1) * 128],
                            t4[:, b * NF * 512 + c * 512: b * NF * 512 + (c + 1) * 512],
                            start=(c == 0), stop=(c == NF - 1))
                    st.inc(ins, 1)
                t.wait_ge(sa.h, sa_h(e, tw) + 32)
                for tc in range(4):
                    for dh in range(2):
                        yi = tc * 2 + dh
                        yb = yi % 2
                        gl = e * 64 + tw * 8 + yi - 2
                        if gl >= 0:
                            ge_, rem = divmod(gl, 64)
                            tw_, yi_ = divmod(rem, 8)
                            t.wait_ge(sv.h, sv_y(ge_, tw_) + yi_ * 2 + 1)
                        ins = None
                        for hc in range(NHC):
                            ins = t.matmul(
                                ps_y[:, yb * 512:(yb + 1) * 512],
                                hsb[:, hc * 512 + tc * 128: hc * 512 + tc * 128 + 128],
                                w2sb[:, hc * D + dh * 512: hc * D + (dh + 1) * 512],
                                start=(hc == 0), stop=(hc == NHC - 1))
                        st.inc(ins, 1)
        ph.done(_last_inc_t(t, k))

    @blk.vector
    def _(v):
        ph.begin(v)
        for e in range(E):
            v.wait_ge(st.h, st_mm(e, 0))
            v.tensor_copy(b2bc[:, :], ps_b[:, 0:D]).then_inc(sv.h, 1)
            for tw in range(8):
                for tc in range(4):
                    gcol = gates[:, (tw * 4 + tc) * E + e: (tw * 4 + tc) * E + e + 1]
                    for dh in range(2):
                        yi = tc * 2 + dh
                        yb = yi % 2
                        v.wait_ge(st.h, st_mm(e, tw) + 32 + yi + 1)
                        gl = e * 64 + tw * 8 + yi - 2
                        if gl >= 0:
                            v.wait_ge(sgf.h, sgf0 + gl + 2)
                        v.tensor_scalar_mul(yev[:, yb * 512:(yb + 1) * 512],
                                            ps_y[:, yb * 512:(yb + 1) * 512],
                                            gcol).then_inc(sv.h, 1)
                        v.wait_ge(sv.h, sv_y(e, tw) + yi * 2 + 1)
                        v.scalar_tensor_tensor(
                            yev[:, yb * 512:(yb + 1) * 512],
                            b2bc[:, dh * 512:(dh + 1) * 512], gcol,
                            yev[:, yb * 512:(yb + 1) * 512],
                            ALU.mult, ALU.add).then_inc(sv.h, 1)
        sv.n = sv0 + E * 129
        ph.done(_last_inc(v, k))

    blk.__exit__(None, None, None)
    loc.close()
    loc0.close()


def emit_rope_tables(k, loc_outer):
    nc = k.nc
    k._rope_n = getattr(k, "_rope_n", 0) + 1
    sfx = "%d" % k._rope_n
    k.cosT = loc_outer.sb("ropecos" + sfx, [128, T], F32)
    k.sinT = loc_outer.sb("ropesin" + sfx, [128, T], F32)
    loc = Loc(k)
    posr = loc.sb("rope_pr" + sfx, [1, T], F32)
    imodr = loc.sb("rope_im" + sfx, [1, 128], F32)
    icol = loc.sb("rope_ic" + sfx, [128, 2], F32)
    frq = loc.sb("rope_fq" + sfx, [128, T], F32)
    frq2 = loc.sb("rope_f2" + sfx, [128, T], F32)
    k.ps_anchor = loc.ps("rope_anc" + sfx, [1, 8], F32)
    ps = loc.ps("rope_ps" + sfx, [128, 512], F32)
    ps_i = loc.ps("rope_pi" + sfx, [128, 8], F32)
    ph = Phase(k)
    sg = k.sem("rope_g")
    st = k.sem("rope_t")
    sv = k.sem("rope_v")
    sa = k.sem("rope_a")
    sg0, st0, sv0, sa0 = sg.n, st.n, sv.n, sa.n
    blk = nc.Block()
    blk.__enter__()

    @blk.gpsimd
    def _(g):
        ph.begin(g)
        g.iota(posr[:, :], [[0, NBATCH], [1, S]], base=0, channel_multiplier=0,
               allow_small_or_imprecise_dtypes=True)
        sg.inc(g.iota(imodr[:, :], [[0, 4], [1, 32]], base=0,
                      channel_multiplier=0,
                      allow_small_or_imprecise_dtypes=True), 1)

    @blk.tensor
    def _(t):
        ph.begin(t)
        t.wait_ge(sa.h, sa0 + 1)
        st.inc(t.transpose(ps_i[:, 0:1], imodr[:, :], k.identf[0:1, 0:1]), 1)
        t.wait_ge(sg.h, sg0 + 1)
        for w in range(T // 512):
            if w >= 1:
                t.wait_ge(sv.h, sv0 + 1 + w)
            st.inc(t.matmul(ps[:, :], k.ones_rowf[0:1, 0:128],
                            posr[:, w * 512:(w + 1) * 512],
                            start=True, stop=True), 1)
        ph.done(_last_inc_t(t, k))

    @blk.vector
    def _(v):
        ph.begin(v)
        v.wait_ge(st.h, st0 + 1)
        v.tensor_copy(icol[:, 0:1], ps_i[:, 0:1]).then_inc(sv.h, 1)
        sv.n += 1
        v.wait_ge(sv.h, sv0 + 1)
        for w in range(T // 512):
            v.wait_ge(st.h, st0 + 1 + w + 1)
            v.tensor_scalar_mul(frq[:, w * 512:(w + 1) * 512], ps[:, :],
                                icol[:, 0:1]).then_inc(sv.h, 1)
        sv.n += T // 512
        MAGIC = 8388608.0
        inv2pi = 1.0 / (2.0 * PI)
        for tbl, shift in ((k.cosT, 0.25), (k.sinT, 0.0)):
            v.wait_ge(sv.h, sv.n)
            v.tensor_scalar(tbl[:, :], frq[:, :], inv2pi, shift,
                            ALU.mult, ALU.add).then_inc(sv.h, 1)
            sv.n += 1
            v.wait_ge(sv.h, sv.n)
            v.tensor_scalar(frq2[:, :], tbl[:, :], MAGIC, MAGIC,
                            ALU.add, ALU.subtract).then_inc(sv.h, 1)
            sv.n += 1
            v.wait_ge(sv.h, sv.n)
            v.tensor_sub(tbl[:, :], tbl[:, :], frq2[:, :]).then_inc(sv.h, 1)
            sv.n += 1
        ph.done(_last_inc(v, k))

    @blk.scalar
    def _(s):
        ph.begin(s)
        s.wait_ge(sg.h, sg0 + 1)
        s.activation(imodr[:, :], imodr[:, :], AF.Exp,
                     scale=-float(np.log(10000.0)) / 32.0).then_inc(sa.h, 1)
        sa.n += 1
        v_done = sv.n
        s.wait_ge(sv.h, v_done)
        s.activation(k.cosT[:, :], k.cosT[:, :], AF.Sin,
                     scale=2.0 * PI).then_inc(sa.h, 1)
        sa.n += 1
        s.wait_ge(sa.h, sa.n)
        s.activation(k.sinT[:, :], k.sinT[:, :], AF.Sin,
                     scale=2.0 * PI).then_inc(sa.h, 1)
        sa.n += 1
        ph.done(_last_inc_a(s, k))
    blk.__exit__(None, None, None)
    loc.close()


def emit_proj_fm(k, t_fm, w_bf, out_fm, OF, tag, rope=False, out_row0=0,
                 w_rot=None):
    nc = k.nc
    NOF = OF // 128
    loc = Loc(k)
    wsb = loc.sb("pf_w" + tag, [128, NF * OF], BF16)
    wrsb = loc.sb("pf_wr" + tag, [128, NF * OF], BF16) if rope else None
    tfm = loc.sb("pf_t" + tag, [128, 2 * NF * 512], BF16)
    ev = loc.sb("pf_e" + tag, [128, 2 * 512], BF16)
    ev32 = loc.sb("pf_e32" + tag, [128, 2 * 512], F32)
    k.ps_anchor = loc.ps("pf_anc" + tag, [1, 8], F32)
    ps = loc.ps("pf_ps" + tag, [128, 2 * 512], F32)
    ps2 = loc.ps("pf_ps2" + tag, [128, 2 * 512], F32) if rope else None
    sd = k.sem("pf_d")
    sll = [k.sem("pf_l%d" % j) for j in range(2)]
    st = k.sem("pf_t")
    sv = k.sem("pf_v")
    sa = k.sem("pf_a")
    sg = k.sem("pf_g")
    sgf = k.sem("pf_gf")
    ph = Phase(k)
    sll0 = [x.n for x in sll]
    sd0, st0, sv0, sa0, sg0 = sd.n, st.n, sv.n, sa.n, sg.n
    sgf0 = sgf.n
    NWIN = T // 512
    nev_ops = 4 if rope else 0

    blk = nc.Block()
    blk.__enter__()

    @blk.sync
    def _(sync):
        ph.begin(sync)
        sd.inc(sync.dma_start(
            out=wsb[:, :].rearrange("p (c f) -> p c f", c=NF),
            in_=w_bf.rearrange("(c p) f -> p c f", p=128)), 16)
        if rope:
            sd.inc(sync.dma_start(
                out=wrsb[:, :].rearrange("p (c f) -> p c f", c=NF),
                in_=w_rot.rearrange("(c p) f -> p c f", p=128)), 16)
        for tw in range(NWIN):
            if tw >= 2:
                sync.wait_ge(st.h, st0 + (tw - 1) * NOF * (2 if rope else 1))
            sll[tw % 2].inc(sync.dma_start(
                out=tfm[:, (tw % 2) * NF * 512:(tw % 2 + 1) * NF * 512]
                    .rearrange("p (c t) -> p c t", c=NF),
                in_=t_fm[:, tw * 512:(tw + 1) * 512]
                    .rearrange("(c p) t -> p c t", p=128)), 16)
        sync.wait_ge(sd.h, sd.n)
        for j in range(2):
            sync.wait_ge(sll[j].h, sll[j].n)
        sync.sem_inc(k.bar.h, 1)
    k.bar.n += 1

    @blk.tensor
    def _(t):
        ph.begin(t)
        t.wait_ge(sd.h, sd0 + (32 if rope else 16))
        for tw in range(NWIN):
            b = tw % 2
            t.wait_ge(sll[b].h, sll0[b] + 16 * (tw // 2 + 1))
            for ofc in range(NOF):
                i = tw * NOF + ofc
                pb = i % 2
                if i >= 2:
                    if rope:
                        t.wait_ge(sv.h, sv0 + (i - 1) * nev_ops)
                    else:
                        t.wait_ge(sa.h, sa0 + i - 1)
                ins = None
                for c in range(NF):
                    ins = t.matmul(
                        ps[:, pb * 512:(pb + 1) * 512],
                        wsb[:, c * OF + ofc * 128: c * OF + (ofc + 1) * 128],
                        tfm[:, b * NF * 512 + c * 512: b * NF * 512 + (c + 1) * 512],
                        start=(c == 0), stop=(c == NF - 1))
                st.inc(ins, 1)
                if rope:
                    for c in range(NF):
                        ins = t.matmul(
                            ps2[:, pb * 512:(pb + 1) * 512],
                            wrsb[:, c * OF + ofc * 128: c * OF + (ofc + 1) * 128],
                            tfm[:, b * NF * 512 + c * 512: b * NF * 512 + (c + 1) * 512],
                            start=(c == 0), stop=(c == NF - 1))
                    st.inc(ins, 1)
        ph.done(_last_inc_t(t, k))

    if not rope:
        @blk.scalar
        def _(s):
            ph.begin(s)
            for i in range(NWIN * NOF):
                tw, ofc = divmod(i, NOF)
                pb = i % 2
                eb = i % 2
                s.wait_ge(st.h, st0 + i + 1)
                if i >= 2:
                    s.wait_ge(sgf.h, sgf0 + i - 1)
                s.activation(ev[:, eb * 512:(eb + 1) * 512],
                             ps[:, pb * 512:(pb + 1) * 512],
                             AF.Copy, scale=1.0).then_inc(sa.h, 1)
            sa.n = sa0 + NWIN * NOF
            ph.done(_last_inc_a(s, k))

        @blk.gpsimd
        def _(g):
            ph.begin(g)
            for i in range(NWIN * NOF):
                tw, ofc = divmod(i, NOF)
                eb = i % 2
                if i >= 1:
                    g.wait_ge(sg.h, sg.n)
                    g.sem_inc(sgf.h, 1)
                    sgf.n += 1
                g.wait_ge(sa.h, sa0 + i + 1)
                sg.inc(g.dma_start(
                    out=out_fm[out_row0 + ofc * 128: out_row0 + (ofc + 1) * 128,
                               tw * 512:(tw + 1) * 512],
                    in_=ev[:, eb * 512:(eb + 1) * 512]), 16)
            ph.done_q(g, sg)
    else:
        @blk.vector
        def _(v):
            ph.begin(v)
            for i in range(NWIN * NOF):
                tw, ofc = divmod(i, NOF)
                pb = i % 2
                eb = i % 2
                tws = tw * 512
                base = sv0 + i * nev_ops
                v.wait_ge(st.h, st0 + 2 * i + 2)
                if i >= 1:
                    v.wait_ge(sv.h, base)
                if i >= 2:
                    v.wait_ge(sgf.h, sgf0 + i - 1)
                v.tensor_mul(ev32[:, eb * 512:(eb + 1) * 512],
                             ps[:, pb * 512:(pb + 1) * 512],
                             k.cosT[:, tws:tws + 512]).then_inc(sv.h, 1)
                v.tensor_mul(ev32[:, (1 - eb) * 512:(2 - eb) * 512],
                             ps2[:, pb * 512:(pb + 1) * 512],
                             k.sinT[:, tws:tws + 512]).then_inc(sv.h, 1)
                v.wait_ge(sv.h, base + 2)
                v.tensor_add(ev32[:, eb * 512:(eb + 1) * 512],
                             ev32[:, eb * 512:(eb + 1) * 512],
                             ev32[:, (1 - eb) * 512:(2 - eb) * 512]).then_inc(sv.h, 1)
                v.wait_ge(sv.h, base + 3)
                v.tensor_copy(ev[:, eb * 512:(eb + 1) * 512],
                              ev32[:, eb * 512:(eb + 1) * 512]).then_inc(sv.h, 1)
            sv.n = sv0 + NWIN * NOF * nev_ops
            ph.done(_last_inc(v, k))

        @blk.gpsimd
        def _(g):
            ph.begin(g)
            for i in range(NWIN * NOF):
                tw, ofc = divmod(i, NOF)
                eb = i % 2
                if i >= 1:
                    g.wait_ge(sg.h, sg.n)
                    g.sem_inc(sgf.h, 1)
                    sgf.n += 1
                g.wait_ge(sv.h, sv0 + (i + 1) * nev_ops)
                sg.inc(g.dma_start(
                    out=out_fm[out_row0 + ofc * 128: out_row0 + (ofc + 1) * 128,
                               tw * 512:(tw + 1) * 512],
                    in_=ev[:, eb * 512:(eb + 1) * 512]), 16)
            ph.done_q(g, sg)

    blk.__exit__(None, None, None)
    loc.close()


def emit_proj_tm(k, t_fm, w_bf, out_tm, OF, tag, out_col0=0):
    nc = k.nc
    NOH = OF // 512 if OF >= 512 else 1
    ofw = min(OF, 512)
    loc = Loc(k)
    wsb = loc.sb("pt_w" + tag, [128, NF * OF], BF16)
    tfm = loc.sb("pt_t" + tag, [128, 2 * NF * 512], BF16)
    ev = loc.sb("pt_e" + tag, [128, 2 * ofw], BF16)
    k.ps_anchor = loc.ps("pt_anc" + tag, [1, 8], F32)
    ps = loc.ps("pt_ps" + tag, [128, 2 * 512], F32)
    sd = k.sem("pt_d")
    sll = [k.sem("pt_l%d" % j) for j in range(2)]
    st = k.sem("pt_t")
    sa = k.sem("pt_a")
    sg = k.sem("pt_g")
    sgf = k.sem("pt_gf")
    ph = Phase(k)
    sll0 = [x.n for x in sll]
    sd0, st0, sa0, sg0 = sd.n, st.n, sa.n, sg.n
    sgf0 = sgf.n
    blk = nc.Block()
    blk.__enter__()

    @blk.sync
    def _(sync):
        ph.begin(sync)
        sd.inc(sync.dma_start(
            out=wsb[:, :].rearrange("p (c f) -> p c f", c=NF),
            in_=w_bf.rearrange("(c p) f -> p c f", p=128)), 16)
        for tw in range(T // 512):
            if tw >= 2:
                sync.wait_ge(st.h, st0 + (tw - 1) * 4 * NOH)
            sll[tw % 2].inc(sync.dma_start(
                out=tfm[:, (tw % 2) * NF * 512:(tw % 2 + 1) * NF * 512]
                    .rearrange("p (c t) -> p c t", c=NF),
                in_=t_fm[:, tw * 512:(tw + 1) * 512]
                    .rearrange("(c p) t -> p c t", p=128)), 16)
        sync.wait_ge(sd.h, sd.n)
        for j in range(2):
            sync.wait_ge(sll[j].h, sll[j].n)
        sync.sem_inc(k.bar.h, 1)
    k.bar.n += 1

    @blk.tensor
    def _(t):
        ph.begin(t)
        t.wait_ge(sd.h, sd0 + 16)
        for tw in range(T // 512):
            b = tw % 2
            t.wait_ge(sll[b].h, sll0[b] + 16 * (tw // 2 + 1))
            for tc in range(4):
                for oh in range(NOH):
                    i = (tw * 4 + tc) * NOH + oh
                    pb = i % 2
                    if i >= 2:
                        t.wait_ge(sa.h, sa0 + i - 1)
                    ins = None
                    for c in range(NF):
                        ins = t.matmul(
                            ps[:, pb * 512:pb * 512 + ofw],
                            tfm[:, b * NF * 512 + c * 512 + tc * 128:
                                b * NF * 512 + c * 512 + (tc + 1) * 128],
                            wsb[:, c * OF + oh * 512: c * OF + oh * 512 + ofw],
                            start=(c == 0), stop=(c == NF - 1))
                    st.inc(ins, 1)
        ph.done(_last_inc_t(t, k))

    @blk.scalar
    def _(s):
        ph.begin(s)
        NTOT = 32 * NOH
        for i in range(NTOT):
            pb = i % 2
            s.wait_ge(st.h, st0 + i + 1)
            if i >= 2:
                s.wait_ge(sgf.h, sgf0 + i - 1)
            s.activation(ev[:, pb * ofw:(pb + 1) * ofw],
                         ps[:, pb * 512:pb * 512 + ofw],
                         AF.Copy, scale=1.0).then_inc(sa.h, 1)
        sa.n = sa0 + NTOT
        ph.done(_last_inc_a(s, k))

    @blk.gpsimd
    def _(g):
        ph.begin(g)
        NTOT = 32 * NOH
        for i in range(NTOT):
            ti, oh = divmod(i, NOH)
            pb = i % 2
            if i >= 1:
                g.wait_ge(sg.h, sg.n)
                g.sem_inc(sgf.h, 1)
                sgf.n += 1
            g.wait_ge(sa.h, sa0 + i + 1)
            sg.inc(g.dma_start(
                out=out_tm[ti * 128:(ti + 1) * 128,
                           out_col0 + oh * 512: out_col0 + oh * 512 + ofw],
                in_=ev[:, pb * ofw:(pb + 1) * ofw]), 16)
        ph.done_q(g, sg)
    blk.__exit__(None, None, None)
    loc.close()


def emit_proj_res(k, a_fm, w_bf, xres, tag, scale=1.0, t_add=None, h_out=None):
    nc = k.nc
    loc = Loc(k)
    wsb = loc.sb("pr_w" + tag, [128, NF * D], BF16)
    afm = loc.sb("pr_t" + tag, [128, 2 * NF * 512], BF16)
    tadd = loc.sb("pr_a" + tag, [128, 1024], BF16) if t_add is not None else None
    ev = loc.sb("pr_e" + tag, [128, 2 * 512], F32)
    hv = loc.sb("pr_h" + tag, [128, 2 * 512], F32) if h_out is not None else None
    k.ps_anchor = loc.ps("pr_anc" + tag, [1, 8], F32)
    ps = loc.ps("pr_ps" + tag, [128, 2 * 512], F32)
    sd = k.sem("pr_d")
    sll = [k.sem("pr_l%d" % j) for j in range(2)]
    st = k.sem("pr_t")
    sv = k.sem("pr_v")
    sg = k.sem("pr_g")
    sgf = k.sem("pr_gf")
    sh = k.sem("pr_h")
    ph = Phase(k)
    sll0 = [x.n for x in sll]
    sd0, st0, sv0, sg0, sh0 = sd.n, st.n, sv.n, sg.n, sh.n
    sgf0 = sgf.n
    npt = 2 if h_out is not None else 1
    nd = 2 if h_out is not None else 1
    blk = nc.Block()
    blk.__enter__()

    @blk.sync
    def _(sync):
        ph.begin(sync)
        sd.inc(sync.dma_start(
            out=wsb[:, :].rearrange("p (c f) -> p c f", c=NF),
            in_=w_bf.rearrange("(c p) f -> p c f", p=128)), 16)
        for tw in range(T // 512):
            if tw >= 2:
                sync.wait_ge(st.h, st0 + (tw - 1) * 8)
            sll[tw % 2].inc(sync.dma_start(
                out=afm[:, (tw % 2) * NF * 512:(tw % 2 + 1) * NF * 512]
                    .rearrange("p (c t) -> p c t", c=NF),
                in_=a_fm[:, tw * 512:(tw + 1) * 512]
                    .rearrange("(c p) t -> p c t", p=128)), 16)
        sync.wait_ge(sd.h, sd.n)
        for j in range(2):
            sync.wait_ge(sll[j].h, sll[j].n)
        sync.sem_inc(k.bar.h, 1)
    k.bar.n += 1

    @blk.scalar
    def _(s):
        ph.begin(s)
        if t_add is not None:
            for ti in range(32):
                if ti >= 1:
                    s.wait_ge(sv.h, sv0 + ti * 2 * npt)
                sh.inc(s.dma_start(out=tadd[:, :],
                                   in_=t_add[ti * 128:(ti + 1) * 128, :]), 16)
        ph.done_q(s, sh)

    @blk.tensor
    def _(t):
        ph.begin(t)
        t.wait_ge(sd.h, sd0 + 16)
        for tw in range(T // 512):
            b = tw % 2
            t.wait_ge(sll[b].h, sll0[b] + 16 * (tw // 2 + 1))
            for tc in range(4):
                for dh in range(2):
                    i = (tw * 4 + tc) * 2 + dh
                    pb = i % 2
                    if i >= 2:
                        t.wait_ge(sv.h, sv0 + (i - 1) * npt)
                    ins = None
                    for c in range(NF):
                        ins = t.matmul(
                            ps[:, pb * 512:(pb + 1) * 512],
                            afm[:, b * NF * 512 + c * 512 + tc * 128:
                                b * NF * 512 + c * 512 + (tc + 1) * 128],
                            wsb[:, c * D + dh * 512: c * D + (dh + 1) * 512],
                            start=(c == 0), stop=(c == NF - 1))
                    st.inc(ins, 1)
        ph.done(_last_inc_t(t, k))

    @blk.vector
    def _(v):
        ph.begin(v)
        for i in range(64):
            ti, dh = divmod(i, 2)
            pb = i % 2
            v.wait_ge(st.h, st0 + i + 1)
            if i >= 2:
                v.wait_ge(sgf.h, sgf0 + i - 1)
            base = sv0 + i * npt
            if h_out is not None:
                v.wait_ge(sh.h, sh0 + 16 * (ti + 1))
                v.tensor_add(hv[:, pb * 512:(pb + 1) * 512],
                             ps[:, pb * 512:(pb + 1) * 512],
                             tadd[:, dh * 512:(dh + 1) * 512]).then_inc(sv.h, 1)
                v.wait_ge(sv.h, base + 1)
                v.tensor_scalar_mul(ev[:, pb * 512:(pb + 1) * 512],
                                    hv[:, pb * 512:(pb + 1) * 512],
                                    scale).then_inc(sv.h, 1)
            else:
                v.tensor_scalar_mul(ev[:, pb * 512:(pb + 1) * 512],
                                    ps[:, pb * 512:(pb + 1) * 512],
                                    scale).then_inc(sv.h, 1)
        sv.n = sv0 + 64 * npt
        ph.done(_last_inc(v, k))

    @blk.gpsimd
    def _(g):
        ph.begin(g)
        for i in range(64):
            ti, dh = divmod(i, 2)
            pb = i % 2
            if i >= 1:
                g.wait_ge(sg.h, sg.n)
                g.sem_inc(sgf.h, 1)
                sgf.n += 1
            g.wait_ge(sv.h, sv0 + (i + 1) * npt)
            sg.inc(g.dma_start(
                out=xres[ti * 128:(ti + 1) * 128, dh * 512:(dh + 1) * 512],
                in_=ev[:, pb * 512:(pb + 1) * 512],
                accum_op=ALU.add), 16)
            if h_out is not None:
                sg.inc(g.dma_start(
                    out=h_out[ti * 128:(ti + 1) * 128, dh * 512:(dh + 1) * 512],
                    in_=hv[:, pb * 512:(pb + 1) * 512]), 16)
        ph.done_q(g, sg)
    blk.__exit__(None, None, None)
    loc.close()


def emit_attn(k, q_fm, k_fm, v_tm, out_fm, tag, causal, alibi, kv_group, nbh=None):
    nc = k.nc
    NH = 16
    VKH = NH // kv_group
    if alibi:
        NAUG_A, NAUG_B, SROW = 68, 69, 68
        x_ = (2.0 ** 8) ** (1.0 / NH)
        s8 = [8.0 / x_ ** (h + 1) for h in range(NH)]
        s8_hi = [float(np.float32(ml_dtypes.bfloat16(v))) for v in s8]
        s8_lo = [float(np.float32(ml_dtypes.bfloat16(v - h_)))
                 for v, h_ in zip(s8, s8_hi)]
    else:
        NAUG_A, NAUG_B, SROW = 64, 65, 64

    if causal:
        nktw = [4 * w + 4 for w in range(4)]
        kwin = [qt // 4 + 1 for qt in range(16)]
    else:
        nktw = [16] * 4
        kwin = [4] * 16
    NKT = sum(nktw)
    A_sv = (2 if causal else 1) * 16
    svw = (4 if causal else 0) + 2
    st_bh = 17 + 2 * NKT + 4
    sv_bh = A_sv + 1 + 4 * svw
    sa_bh = NKT + 4
    stoff = [17]
    for w in range(3):
        stoff.append(stoff[-1] + 2 * nktw[w] + 1)
    svoff = [A_sv + 1]
    for w in range(3):
        svoff.append(svoff[-1] + svw)
    saoff = [0]
    for w in range(3):
        saoff.append(saoff[-1] + nktw[w] + 1)
    sld_bh = 48
    sgp_bh = (4 * 16 if alibi else 0) + 16
    sst0_holder = []

    loc = Loc(k)
    q_aug = loc.sb("at_q" + tag, [NAUG_B, S], BF16)
    k_aug = loc.sb("at_k" + tag, [NAUG_B, S], BF16)
    v_sb = loc.sb("at_v" + tag, [128, 16 * 65], BF16)
    pT = loc.sb("at_p" + tag, [128, 2 * 512], BF16)
    negm_all = loc.sb("at_nm" + tag, [128, 16], F32)
    stage = loc.sb("at_st" + tag, [16, 128], BF16)
    rl_row = loc.sb("at_rl" + tag, [1, 512], F32)
    rl_bc = loc.sb("at_rb" + tag, [64, 512], F32)
    out_sb = loc.sb("at_o" + tag, [64, 2 * 512], BF16)
    jrow_hi = loc.sb("at_jh" + tag, [1, S], BF16)
    jrow_lo = loc.sb("at_jl" + tag, [1, S], BF16)
    sh_row = loc.sb("at_sh" + tag, [1, S], BF16)
    sl_row = loc.sb("at_sl" + tag, [1, S], BF16)
    onesr = loc.sb("at_1r" + tag, [1, S], BF16)
    maskA = loc.sb("at_mA" + tag, [128, 4 * 512], F32) if causal else None
    maskB = loc.sb("at_mB" + tag, [128, 4 * 512], F32) if causal else None
    ps_sA = loc.ps("at_pA" + tag, [128, 4 * 512], F32)
    ps_sB = loc.ps("at_pB" + tag, [128, 2 * 512], F32)
    ps_pv = loc.ps("at_pv" + tag, [65, 512], F32)
    ps_ms = loc.ps("at_pm" + tag, [128, 512], F32)

    sld = k.sem("at_d")
    sgp = k.sem("at_g")
    sgc = k.sem("at_gc")
    sst = k.sem("at_s")
    sstf = k.sem("at_sf")
    st = k.sem("at_t")
    sv = k.sem("at_v")
    sa = k.sem("at_a")
    ph = Phase(k)
    sld0, sgp0, st0, sv0, sa0, sst0 = sld.n, sgp.n, st.n, sv.n, sa.n, sst.n
    sstf0 = sstf.n
    BH = NBATCH * NH if nbh is None else nbh

    def stb(bh):
        return st0 + bh * st_bh

    def svb(bh):
        return sv0 + bh * sv_bh

    def sab(bh):
        return sa0 + bh * sa_bh

    def sgpb(bh):
        return sgp0 + SETUP_GP + bh * sgp_bh

    SETUP_GP = 0

    blk = nc.Block()
    blk.__enter__()

    @blk.gpsimd
    def _(g):
        nonlocal_setup = []
        ph.begin(g)
        n = 0
        if alibi:
            g.iota(jrow_hi[:, :], [[256, 8], [0, 256]], base=0,
                   channel_multiplier=0, allow_small_or_imprecise_dtypes=True)
            g.iota(jrow_lo[:, :], [[0, 8], [1, 256]], base=0,
                   channel_multiplier=0, allow_small_or_imprecise_dtypes=True)
            sgc.inc(g.memset(onesr[:, :], 1.0), 1)
            g.wait_ge(sgc.h, sgc.n)
            sgp.inc(g.dma_start(out=k_aug[64:65, :], in_=jrow_hi[:, :]), 16)
            sgp.inc(g.dma_start(out=k_aug[65:66, :], in_=jrow_lo[:, :]), 16)
            sgp.inc(g.dma_start(out=k_aug[66:67, :], in_=jrow_hi[:, :]), 16)
            sgp.inc(g.dma_start(out=k_aug[67:68, :], in_=jrow_lo[:, :]), 16)
            n = 64
        else:
            sgc.inc(g.memset(onesr[:, :], 1.0), 1)
            g.wait_ge(sgc.h, sgc.n)
            n = 0
        sgp.inc(g.dma_start(out=k_aug[SROW:SROW + 1, :], in_=onesr[:, :]), 16)
        n += 16
        for kt in range(15):
            g.memset(v_sb[:, kt * 65 + 64:kt * 65 + 65], 1.0)
        if causal:
            sgc.inc(g.memset(v_sb[:, 15 * 65 + 64:15 * 65 + 65], 1.0), 1)
            for r in range(4):
                sgc.inc(g.memset(maskA[:, r * 512:(r + 1) * 512], 0.0), 1)
                g.wait_ge(sgc.h, sgc.n)
                g.affine_select(out=maskA[:, r * 512:(r + 1) * 512],
                                in_=maskA[:, r * 512:(r + 1) * 512],
                                compare_op=ALU.is_ge, fill=NEGINF,
                                base=r * 128, channel_multiplier=1,
                                pattern=[[-1, 512]])
                sgc.inc(g.memset(maskB[:, r * 512:(r + 1) * 512], 0.0), 1)
                g.wait_ge(sgc.h, sgc.n)
                sgc.inc(g.affine_select(out=maskB[:, r * 512:(r + 1) * 512],
                                in_=maskB[:, r * 512:(r + 1) * 512],
                                compare_op=ALU.is_ge, fill=NEGINF,
                                base=-r * 128, channel_multiplier=-1,
                                pattern=[[1, 512]]), 1)
        else:
            sgc.inc(g.memset(v_sb[:, 15 * 65 + 64:15 * 65 + 65], 1.0), 1)
        assert sgp.n - sgp0 == n
        k.marks["at_setup_gc" + tag] = sgc.n
        for bh in range(BH):
            b, h = divmod(bh, NH)
            g.wait_ge(st.h, stb(bh))
            if alibi:
                sgc.inc(g.memset(sh_row[:, :], s8_hi[h]), 1)
                sgc.inc(g.memset(sl_row[:, :], s8_lo[h]), 1)
                g.wait_ge(sgc.h, sgc.n)
                sgp.inc(g.dma_start(out=q_aug[64:65, :], in_=sh_row[:, :]), 16)
                sgp.inc(g.dma_start(out=q_aug[65:66, :], in_=sh_row[:, :]), 16)
                sgp.inc(g.dma_start(out=q_aug[66:67, :], in_=sl_row[:, :]), 16)
                sgp.inc(g.dma_start(out=q_aug[67:68, :], in_=sl_row[:, :]), 16)
            g.wait_ge(sv.h, svb(bh) + A_sv + 1)
            sgp.inc(g.dma_start(
                out=q_aug[SROW:SROW + 1, :].rearrange("o (p t) -> o p t", p=16),
                in_=stage[:, :]), 16)
            for w in range(4):
                gw = bh * 4 + w
                if gw >= 1:
                    g.wait_ge(sst.h, sst.n)
                    g.sem_inc(sstf.h, 1)
                    sstf.n += 1
                g.wait_ge(sv.h, svb(bh) + svoff[w] + svw)
                sst.inc(g.dma_start(
                    out=out_fm[h * 64:(h + 1) * 64,
                               b * S + w * 512: b * S + (w + 1) * 512],
                    in_=out_sb[:, (w % 2) * 512:(w % 2 + 1) * 512]), 16)
        g.wait_ge(sgp.h, sgp.n)
        ph.done_q(g, sst)

    SETUP_GP = (80 if alibi else 16)

    @blk.sync
    def _(sync):
        ph.begin(sync)
        for bh in range(BH):
            b, h = divmod(bh, NH)
            kvh = h // kv_group
            sync.wait_ge(st.h, stb(bh))
            sld.inc(sync.dma_start(
                out=q_aug[0:64, :],
                in_=q_fm[h * 64:(h + 1) * 64, b * S:(b + 1) * S]), 16)
            sld.inc(sync.dma_start(
                out=k_aug[0:64, :],
                in_=k_fm[kvh * 64:(kvh + 1) * 64, b * S:(b + 1) * S]), 16)
            sld.inc(sync.dma_start(
                out=v_sb[:, :].rearrange("p (kt x) -> p kt x", x=65)[:, :, 0:64],
                in_=v_tm[b * S:(b + 1) * S, kvh * 64:(kvh + 1) * 64]
                    .rearrange("(kt p) d -> p kt d", p=128)), 16)
        ph.done_q(sync, sld)

    SETUP_GC = sgc.n

    @blk.tensor
    def _(t):
        ph.begin(t)
        t.wait_ge(sgc.h, k.marks["at_setup_gc" + tag])
        for bh in range(BH):
            b, h = divmod(bh, NH)
            t.wait_ge(sld.h, sld0 + sld_bh * (bh + 1))
            if alibi:
                t.wait_ge(sgp.h, sgpb(bh) + 64)
            t.wait_ge(sa.h, sab(bh))
            for qt in range(16):
                if qt >= 1:
                    t.wait_ge(sv.h, svb(bh) + (2 * qt if causal else qt))
                elif bh >= 1:
                    t.wait_ge(sv.h, svb(bh - 1) + A_sv)
                ins = None
                for w2 in range(kwin[qt]):
                    ins = t.matmul(
                        ps_sA[:, w2 * 512:(w2 + 1) * 512],
                        q_aug[0:NAUG_A, qt * 128:(qt + 1) * 128],
                        k_aug[0:NAUG_A, w2 * 512:(w2 + 1) * 512],
                        start=True, stop=True)
                st.inc(ins, 1)
            t.wait_ge(sv.h, svb(bh) + A_sv)
            st.inc(t.transpose(ps_ms[0:16, 0:128], negm_all[:, :],
                               k.identf[:, :]), 1)
            t.wait_ge(sgp.h, sgpb(bh) + sgp_bh)
            for w in range(4):
                for j in range(nktw[w]):
                    kt = j
                    if j >= 2:
                        t.wait_ge(sa.h, sab(bh) + saoff[w] + j - 1)
                    elif w >= 1:
                        t.wait_ge(sa.h, sab(bh) + saoff[w - 1]
                                  + nktw[w - 1] - 1 + j)
                    ins = t.matmul(
                        ps_sB[:, (j % 2) * 512:(j % 2 + 1) * 512],
                        k_aug[0:NAUG_B, kt * 128:(kt + 1) * 128],
                        q_aug[0:NAUG_B, w * 512:(w + 1) * 512],
                        start=True, stop=True)
                    st.inc(ins, 1)
                    t.wait_ge(sa.h, sab(bh) + saoff[w] + j + 1)
                    if j == 0:
                        if w >= 1:
                            t.wait_ge(sv.h, svb(bh) + svoff[w - 1] + svw)
                        elif bh >= 1:
                            t.wait_ge(sv.h, svb(bh - 1) + sv_bh)
                    ins = t.matmul(ps_pv[:, :], v_sb[:, kt * 65:(kt + 1) * 65],
                                   pT[:, (j % 2) * 512:(j % 2 + 1) * 512],
                                   start=(j == 0), stop=(j == nktw[w] - 1))
                    st.inc(ins, 1)
                t.wait_ge(sv.h, svb(bh) + svoff[w] + svw - 1)
                st.inc(t.matmul(ps_ms[0:64, 0:512], k.ones_rowf[0:1, 0:64],
                                rl_row[:, :], start=True, stop=True), 1)
        t.wait_ge(sa.h, sa0 + BH * sa_bh)
        k.bar.inc(t.matmul(ps_ms[0:1, 0:1], k.ones_rowf[0:1, 0:1],
                           k.ones_rowf[0:1, 0:1], start=True, stop=True), 1)

    @blk.vector
    def _(v):
        ph.begin(v)
        if causal:
            v.wait_ge(sgc.h, k.marks["at_setup_gc" + tag])
        for bh in range(BH):
            for qt in range(16):
                v.wait_ge(st.h, stb(bh) + qt + 1)
                base = svb(bh)
                if causal:
                    dw = qt // 4
                    r = qt % 4
                    v.tensor_tensor(ps_sA[:, dw * 512:(dw + 1) * 512],
                                    ps_sA[:, dw * 512:(dw + 1) * 512],
                                    maskA[:, r * 512:(r + 1) * 512],
                                    ALU.add).then_inc(sv.h, 1)
                    v.wait_ge(sv.h, base + 2 * qt + 1)
                    v.reduce_max(negm_all[:, qt:qt + 1],
                                 ps_sA[:, 0:kwin[qt] * 512], axis=AX.X,
                                 negate=True).then_inc(sv.h, 1)
                else:
                    v.reduce_max(negm_all[:, qt:qt + 1],
                                 ps_sA[:, 0:kwin[qt] * 512], axis=AX.X,
                                 negate=True).then_inc(sv.h, 1)
            v.wait_ge(st.h, stb(bh) + 17)
            v.wait_ge(sv.h, svb(bh) + A_sv)
            v.tensor_copy(stage[:, :], ps_ms[0:16, 0:128]).then_inc(sv.h, 1)
            for w in range(4):
                base = svb(bh) + svoff[w]
                if causal:
                    for r2 in range(4):
                        kt = w * 4 + r2
                        j = kt
                        v.wait_ge(st.h, stb(bh) + stoff[w] + 2 * j + 1)
                        v.tensor_tensor(
                            ps_sB[:, (j % 2) * 512:(j % 2 + 1) * 512],
                            ps_sB[:, (j % 2) * 512:(j % 2 + 1) * 512],
                            maskB[:, r2 * 512:(r2 + 1) * 512],
                            ALU.add).then_inc(sv.h, 1)
                v.wait_ge(st.h, stb(bh) + stoff[w] + 2 * nktw[w])
                v.reciprocal(rl_row[:, :], ps_pv[64:65, :]).then_inc(sv.h, 1)
                v.wait_ge(sa.h, sab(bh) + saoff[w] + nktw[w] + 1)
                gl = bh * 4 + w - 2
                if gl >= 0:
                    v.wait_ge(sstf.h, sstf0 + gl + 2)
                v.tensor_mul(out_sb[:, (w % 2) * 512:(w % 2 + 1) * 512],
                             ps_pv[0:64, :], rl_bc[:, :]).then_inc(sv.h, 1)
        sv.n = sv0 + BH * sv_bh
        ph.done(_last_inc(v, k))

    @blk.scalar
    def _(s):
        ph.begin(s)
        for bh in range(BH):
            for w in range(4):
                for j in range(nktw[w]):
                    s.wait_ge(st.h, stb(bh) + stoff[w] + 2 * j + 1)
                    if causal and j >= w * 4:
                        r2 = j - w * 4
                        if 0 <= r2 < 4:
                            s.wait_ge(sv.h, svb(bh) + svoff[w] + r2 + 1)
                    if j >= 2:
                        s.wait_ge(st.h, stb(bh) + stoff[w] + 2 * (j - 1))
                    s.activation(pT[:, (j % 2) * 512:(j % 2 + 1) * 512],
                                 ps_sB[:, (j % 2) * 512:(j % 2 + 1) * 512],
                                 AF.Exp, scale=0.125).then_inc(sa.h, 1)
                s.wait_ge(st.h, stb(bh) + stoff[w] + 2 * nktw[w] + 1)
                if bh * 4 + w >= 1:
                    pw = bh * 4 + w - 1
                    s.wait_ge(sv.h, sv0 + (pw // 4) * sv_bh
                              + svoff[pw % 4] + svw)
                s.activation(rl_bc[:, :], ps_ms[0:64, 0:512], AF.Copy,
                             scale=1.0).then_inc(sa.h, 1)
        sa.n = sa0 + BH * sa_bh
        ph.done(_last_inc_a(s, k))

    blk.__exit__(None, None, None)
    loc.close()




def emit_rmsnorm(k, src, w_vec, out_tm, tag, src_dt=F32):
    nc = k.nc
    ph = Phase(k)
    sdl = [k.sem("rm_d%d" % j) for j in range(4)]
    sw = k.sem("rm_w")
    sv = k.sem("rm_v")
    sa = k.sem("rm_a")
    so = k.sem("rm_o")
    sf = k.sem("rm_f")
    st = k.sem("rm_t")
    NBUF = 4
    loc = Loc(k)
    k.ps_anchor = loc.ps("rm_anc" + tag, [1, 8], F32)
    xt = loc.sb("rm_x" + tag, [128, NBUF * D], src_dt)
    sq = loc.sb("rm_q" + tag, [128, D], F32)
    tt = loc.sb("rm_t" + tag, [128, NBUF * D], BF16)
    cols = loc.sb("rm_s" + tag, [128, 8 * NBUF], F32)
    wrow = loc.sb("rm_wr" + tag, [1, D], F32)
    wbc = loc.sb("rm_wb" + tag, [128, D], F32)
    ps_w = loc.ps("rm_pw" + tag, [128, D], F32)
    sdl0 = [x.n for x in sdl]
    sw0, sv0, sa0, so0, sf0, st0 = sw.n, sv.n, sa.n, so.n, sf.n, st.n
    blk = nc.Block()
    blk.__enter__()

    @blk.sync
    def _(sync):
        ph.begin(sync)
        sw.inc(sync.dma_start(out=wrow[:, :], in_=w_vec.rearrange("(o d) -> o d", o=1)), 16)
        for i in range(NT):
            b = i % NBUF
            if i >= NBUF:
                sync.wait_ge(sf.h, sf0 + (i - NBUF + 1))
            sdl[b].inc(sync.dma_start(out=xt[:, b * D:(b + 1) * D],
                                      in_=src[i * 128:(i + 1) * 128, :]), 16)
        for j in range(4):
            sync.wait_ge(sdl[j].h, sdl[j].n)
        sync.sem_inc(k.bar.h, 1)
    k.bar.n += 1

    @blk.tensor
    def _(t):
        ph.begin(t)
        t.wait_ge(sw.h, sw0 + 16)
        st.inc(t.matmul(ps_w[:, 0:512], k.ones_rowf[0:1, 0:128],
                        wrow[:, 0:512], start=True, stop=True), 1)
        st.inc(t.matmul(ps_w[:, 512:1024], k.ones_rowf[0:1, 0:128],
                        wrow[:, 512:1024], start=True, stop=True), 1)
        ph.done(_last_inc_t(t, k))

    @blk.scalar
    def _(s):
        ph.begin(s)
        for i in range(NT):
            b = i % NBUF
            c = 8 * b
            s.wait_ge(sdl[b].h, sdl0[b] + 16 * (i // NBUF + 1))
            s.activation(sq[:, :], xt[:, b * D:(b + 1) * D], AF.Square,
                         accum_out=cols[:, c:c + 1]).then_inc(sa.h, 1)
            s.wait_ge(sa.h, sa0 + 2 * i + 1)
            s.activation(cols[:, c + 1:c + 2], cols[:, c:c + 1], AF.Sqrt,
                         bias=k.consts[:, 1:2], scale=1.0 / D).then_inc(sa.h, 1)
        sa.n = sa0 + 2 * NT
        ph.done(_last_inc_a(s, k))

    @blk.vector
    def _(v):
        ph.begin(v)
        v.wait_ge(st.h, st0 + 2)
        v.tensor_copy(wbc[:, :], ps_w[:, :]).then_inc(sv.h, 1)
        for i in range(NT):
            b = i % NBUF
            c = 8 * b
            v.wait_ge(sa.h, sa0 + 2 * (i + 1))
            v.reciprocal(cols[:, c + 2:c + 3], cols[:, c + 1:c + 2]).then_inc(sv.h, 1)
            v.wait_ge(sv.h, sv0 + 2 * i + 2)
            v.scalar_tensor_tensor(tt[:, b * D:(b + 1) * D],
                                   xt[:, b * D:(b + 1) * D],
                                   cols[:, c + 2:c + 3], wbc[:, :],
                                   ALU.mult, ALU.mult).then_inc(sv.h, 1)
        sv.n = sv0 + 1 + 2 * NT
        ph.done(_last_inc(v, k))

    @blk.gpsimd
    def _(g):
        ph.begin(g)
        for i in range(NT):
            b = i % NBUF
            if i >= 1:
                g.wait_ge(so.h, so0 + 16 * i)
                g.sem_inc(sf.h, 1)
                sf.n += 1
            g.wait_ge(sv.h, sv0 + 2 * (i + 1) + 1)
            so.inc(g.dma_start(out=out_tm[i * 128:(i + 1) * 128, :],
                               in_=tt[:, b * D:(b + 1) * D]), 16)
        ph.done_q(g, so)
    blk.__exit__(None, None, None)
    loc.close()


def emit_routing_l1(k, t_fm, gate_w_f32, gates, tag):
    nc = k.nc
    loc = Loc(k)
    k.ps_anchor = loc.ps("q_anc" + tag, [1, 8], F32)
    gw_sb = loc.sb("q_gw" + tag, [E, D], BF16)
    gwT = loc.sb("q_gwT" + tag, [128, NF * E], BF16)
    tfm = loc.sb("q_t" + tag, [128, NF * 512], BF16)
    max8 = loc.sb("q_m8" + tag, [128, 8], F32)
    lgt = loc.sb("q_lg" + tag, [128, 2 * E], F32)
    idxu = loc.sb("q_iu" + tag, [128, 8], U32)
    cols = loc.sb("q_c" + tag, [128, 16], F32)
    oh0 = loc.sb("q_o0" + tag, [128, 8], F32)
    ps_lg = loc.ps("q_plg" + tag, [128, 1024], F32)
    ps_t = loc.ps("q_pt" + tag, [128, 128], BF16)
    sd = k.sem("q_d")
    st = k.sem("q_t")
    sv = k.sem("q_v")
    sa = k.sem("q_a")
    ph = Phase(k)
    sd0, st0, sv0, sa0 = sd.n, st.n, sv.n, sa.n
    NV = 13

    blk = nc.Block()
    blk.__enter__()

    sdg = k.sem("q_dg")
    sdg0 = sdg.n

    @blk.gpsimd
    def _(g):
        ph.begin(g)
        sdg.inc(g.dma_start(out=gw_sb[:, :], in_=gate_w_f32[:, :]), 16)

    @blk.sync
    def _(sync):
        ph.begin(sync)
        for w in range(NT // 4):
            if w >= 1:
                sync.wait_ge(st.h, st0 + NF + 4 * w)
            sd.inc(sync.dma_start(
                out=tfm[:, :].rearrange("p (c t) -> p c t", c=NF),
                in_=t_fm[:, w * 512:(w + 1) * 512].rearrange("(c p) t -> p c t", p=128)), 16)
        ph.done_q(sync, sd)

    @blk.tensor
    def _(t):
        ph.begin(t)
        t.wait_ge(sdg.h, sdg0 + 16)
        for c in range(NF):
            if c >= 1:
                t.wait_ge(sv.h, sv0 + c)
            st.inc(t.transpose(ps_t[:, 0:E], gw_sb[:, c * 128:(c + 1) * 128],
                               k.identb[0:E, 0:E]), 1)
        for i in range(NT):
            w, j = divmod(i, 4)
            b = i % 2
            t.wait_ge(sd.h, sd0 + 16 * (w + 1))
            t.wait_ge(sv.h, sv0 + NF)
            if i >= 2:
                t.wait_ge(sv.h, sv0 + NF + NV * (i - 2) + 1)
            ins = None
            for c in range(NF):
                ins = t.matmul(ps_lg[:, b * 512:b * 512 + E],
                               tfm[:, c * 512 + j * 128: c * 512 + (j + 1) * 128],
                               gwT[:, c * E:(c + 1) * E],
                               start=(c == 0), stop=(c == NF - 1))
            st.inc(ins, 1)
        ph.done(_last_inc_t(t, k))

    @blk.vector
    def _(v):
        ph.begin(v)
        for c in range(NF):
            v.wait_ge(st.h, st0 + c + 1)
            v.tensor_copy(gwT[:, c * E:(c + 1) * E], ps_t[:, 0:E]).then_inc(sv.h, 1)
        sv.n = sv0 + NF
        base0 = sv.n
        for i in range(NT):
            b = i % 2
            base = base0 + NV * i
            v.wait_ge(st.h, st0 + NF + i + 1)
            v.tensor_copy(lgt[:, b * E:(b + 1) * E], ps_lg[:, b * 512:b * 512 + E]).then_inc(sv.h, 1)
            v.wait_ge(sv.h, base + 1)
            v.max(max8[:, :], lgt[:, b * E:(b + 1) * E]).then_inc(sv.h, 1)
            v.wait_ge(sv.h, base + 2)
            v.max_index(idxu[:, :], max8[:, :], lgt[:, b * E:(b + 1) * E]).then_inc(sv.h, 1)
            v.wait_ge(sv.h, base + 3)
            v.tensor_copy(cols[:, 0:2], idxu[:, 0:2]).then_inc(sv.h, 1)
            v.tensor_tensor(cols[:, 2:3], max8[:, 1:2], max8[:, 0:1],
                            ALU.subtract).then_inc(sv.h, 1)
            v.wait_ge(sa.h, sa0 + i + 1)
            v.tensor_scalar_add(cols[:, 4:5], cols[:, 3:4], 1.0).then_inc(sv.h, 1)
            v.wait_ge(sv.h, base + 6)
            v.reciprocal(cols[:, 5:6], cols[:, 4:5]).then_inc(sv.h, 1)
            v.wait_ge(sv.h, base + 7)
            v.tensor_scalar(cols[:, 6:7], cols[:, 5:6], -0.5, 0.5,
                            ALU.mult, ALU.add).then_inc(sv.h, 1)
            v.tensor_scalar_mul(cols[:, 5:6], cols[:, 5:6], 0.5).then_inc(sv.h, 1)
            v.wait_ge(sv.h, base + 4)
            v.tensor_single_scalar(oh0[:, :], k.egrid[:, :], cols[:, 0:1],
                                   ALU.is_equal).then_inc(sv.h, 1)
            v.tensor_single_scalar(gates[:, i * E:(i + 1) * E], k.egrid[:, :],
                                   cols[:, 1:2], ALU.is_equal).then_inc(sv.h, 1)
            v.wait_ge(sv.h, base + 11)
            v.tensor_scalar_mul(gates[:, i * E:(i + 1) * E],
                                gates[:, i * E:(i + 1) * E],
                                cols[:, 6:7]).then_inc(sv.h, 1)
            v.wait_ge(sv.h, base + 12)
            v.scalar_tensor_tensor(gates[:, i * E:(i + 1) * E],
                                   oh0[:, :], cols[:, 5:6],
                                   gates[:, i * E:(i + 1) * E],
                                   ALU.mult, ALU.add).then_inc(sv.h, 1)
        sv.n = base0 + NV * NT
        ph.done(_last_inc(v, k))

    @blk.scalar
    def _(s):
        ph.begin(s)
        for i in range(NT):
            s.wait_ge(sv.h, sv0 + NF + NV * i + 5)
            s.activation(cols[:, 3:4], cols[:, 2:3], AF.Exp).then_inc(sa.h, 1)
        sa.n = sa0 + NT
        ph.done(_last_inc_a(s, k))
    blk.__exit__(None, None, None)
    loc.close()


def emit_l0attn(k, q_fm, k_fm, v_tm, a_fm, tag):
    nc = k.nc
    G = 204
    L = S - G
    NLC = (L + 127) // 128
    lastc = L - 128 * (NLC - 1)
    NH = 16
    BH = NBATCH * NH
    QSZ = [128, G - 128]

    loc = Loc(k)
    qg = loc.sb("la_qg" + tag, [64, 256], BF16)
    kg = loc.sb("la_kg" + tag, [64, 256], BF16)
    ql = loc.sb("la_ql" + tag, [64, L], BF16)
    kl = loc.sb("la_kl" + tag, [64, L], BF16)
    klsm = loc.sb("la_ks" + tag, [64, L], BF16)
    klsmT = loc.sb("la_kT" + tag, [128, NLC * 64], BF16)
    vg = loc.sb("la_vg" + tag, [128, 2 * 64], BF16)
    vl = loc.sb("la_vl" + tag, [128, NLC * 64], BF16)
    pg = loc.sb("la_pg" + tag, [128, 2 * 256], BF16)
    pgT = loc.sb("la_pT" + tag, [128, 2 * 204], BF16)
    kv_sb = loc.sb("la_kv" + tag, [64, 64], BF16)
    og = loc.sb("la_og" + tag, [64, 256], BF16)
    lo = loc.sb("la_lo" + tag, [64, L], BF16)
    cols = loc.sb("la_c" + tag, [128, 8], F32)
    k.ps_anchor = loc.ps("la_anc" + tag, [1, 8], F32)
    ps_g = loc.ps("la_psg" + tag, [128, 256], F32)
    ps_t = loc.ps("la_pst" + tag, [128, 128], BF16)
    ps_kv = loc.ps("la_psk" + tag, [64, 64], F32)
    ps_o = loc.ps("la_pso" + tag, [64, 512], F32)

    sld = k.sem("la_d")
    st = k.sem("la_t")
    sv = k.sem("la_v")
    sa = k.sem("la_a")
    sst = k.sem("la_s")
    sstf = k.sem("la_sf")
    ph = Phase(k)
    sld0, st0, sv0, sa0, sst0, sstf0 = sld.n, st.n, sv.n, sa.n, sst.n, sstf.n

    ST_BH = 2 + 4 + 1 + NLC + 1 + 4
    SV_BH = 8 + 4 + 3 + NLC + 1
    SA_BH = 2 + 1 + 1 + 4
    SLD_BH = 7 * 16
    SLD_BH = 8 * 16
    SST_BH = 2 * 16

    def stb(bh):
        return st0 + bh * ST_BH

    def svb(bh):
        return sv0 + bh * SV_BH

    def sab(bh):
        return sa0 + bh * SA_BH

    blk = nc.Block()
    blk.__enter__()

    @blk.sync
    def _(sync):
        ph.begin(sync)
        for bh in range(BH):
            b, h = divmod(bh, NH)
            r0, c0 = h * 64, b * S
            sync.wait_ge(st.h, stb(bh))
            sld.inc(sync.dma_start(out=qg[:, 0:G], in_=q_fm[r0:r0 + 64, c0:c0 + G]), 16)
            sld.inc(sync.dma_start(out=kg[:, 0:G], in_=k_fm[r0:r0 + 64, c0:c0 + G]), 16)
            sld.inc(sync.dma_start(out=ql[:, :], in_=q_fm[r0:r0 + 64, c0 + G:c0 + S]), 16)
            sld.inc(sync.dma_start(out=kl[:, :], in_=k_fm[r0:r0 + 64, c0 + G:c0 + S]), 16)
            sld.inc(sync.dma_start(out=vg[:, 0:64], in_=v_tm[c0:c0 + 128, r0:r0 + 64]), 16)
            sld.inc(sync.dma_start(out=vg[0:G - 128, 64:128],
                                   in_=v_tm[c0 + 128:c0 + G, r0:r0 + 64]), 16)
            sld.inc(sync.dma_start(
                out=vl[:, 0:(NLC - 1) * 64].rearrange("p (j d) -> p j d", d=64),
                in_=v_tm[c0 + G:c0 + G + (NLC - 1) * 128, r0:r0 + 64]
                    .rearrange("(j p) d -> p j d", p=128)), 16)
            sld.inc(sync.dma_start(out=vl[0:lastc, (NLC - 1) * 64:NLC * 64],
                                   in_=v_tm[c0 + G + (NLC - 1) * 128:c0 + S,
                                            r0:r0 + 64]), 16)
        ph.done_q(sync, sld)

    @blk.tensor
    def _(t):
        ph.begin(t)
        for bh in range(BH):
            t.wait_ge(sld.h, sld0 + SLD_BH * (bh + 1))
            if bh >= 1:
                t.wait_ge(sv.h, svb(bh))
                t.wait_ge(sa.h, sab(bh))
            for q2 in range(2):
                sz = QSZ[q2]
                st.inc(t.matmul(ps_g[0:sz, 0:G], qg[:, q2 * 128:q2 * 128 + sz],
                                kg[:, 0:G], start=True, stop=True), 1)
                t.wait_ge(sv.h, svb(bh) + 4 * q2 + 4)
            for q2 in range(2):
                sz = QSZ[q2]
                for kc in range(2):
                    cnt = 128 if kc == 0 else G - 128
                    t.wait_ge(sv.h, svb(bh) + 8 + 2 * q2 + kc)
                    st.inc(t.transpose(ps_t[0:cnt, 0:sz],
                                       pg[0:sz, q2 * 256 + kc * 128:
                                          q2 * 256 + kc * 128 + cnt],
                                       k.identb[0:sz, 0:sz]), 1)
            t.wait_ge(sv.h, svb(bh) + 12)
            t.matmul(ps_o[:, 0:G], vg[:, 0:64], pgT[:, 0:G],
                     start=True, stop=False)
            st.inc(t.matmul(ps_o[:, 0:G], vg[0:G - 128, 64:128],
                            pgT[0:G - 128, 204:204 + G], start=False,
                            stop=True), 1)
            for j in range(NLC):
                cnt = 128 if j < NLC - 1 else lastc
                t.wait_ge(sv.h, svb(bh) + 15 + j)
                st.inc(t.transpose(ps_t[0:cnt, 0:64],
                                   klsm[:, j * 128:j * 128 + cnt],
                                   k.identb[0:64, 0:64]), 1)
            t.wait_ge(sv.h, svb(bh) + 15 + NLC)
            ins = None
            for j in range(NLC):
                cnt = 128 if j < NLC - 1 else lastc
                ins = t.matmul(ps_kv[:, :], vl[0:cnt, j * 64:(j + 1) * 64],
                               klsmT[0:cnt, j * 64:(j + 1) * 64],
                               start=(j == 0), stop=(j == NLC - 1))
            st.inc(ins, 1)
            t.wait_ge(sv.h, svb(bh) + SV_BH)
            off = 0
            for w in range(4):
                ww = 512 if w < 3 else L - 1536
                if w >= 1:
                    t.wait_ge(sa.h, sab(bh) + 4 + w)
                else:
                    t.wait_ge(sa.h, sab(bh) + 3)
                st.inc(t.matmul(ps_o[:, 0:ww], kv_sb[:, :],
                                ql[:, off:off + ww], start=True, stop=True), 1)
                off += ww
        ph.done(_last_inc_t(t, k))

    @blk.vector
    def _(v):
        ph.begin(v)
        for bh in range(BH):
            base = svb(bh)
            for q2 in range(2):
                sz = QSZ[q2]
                v.wait_ge(st.h, stb(bh) + q2 + 1)
                v.reduce_max(cols[0:sz, 0:1], ps_g[0:sz, 0:G], axis=AX.X,
                             negate=True).then_inc(sv.h, 1)
                v.wait_ge(sv.h, base + 4 * q2 + 1)
                v.tensor_scalar_mul(cols[0:sz, 1:2], cols[0:sz, 0:1], 0.125).then_inc(sv.h, 1)
                v.wait_ge(sa.h, sab(bh) + q2 + 1)
                v.reciprocal(cols[0:sz, 3:4], cols[0:sz, 2:3]).then_inc(sv.h, 1)
                v.wait_ge(sv.h, base + 4 * q2 + 3)
                v.tensor_scalar_mul(pg[0:sz, q2 * 256:q2 * 256 + G],
                                    pg[0:sz, q2 * 256:q2 * 256 + G],
                                    cols[0:sz, 3:4]).then_inc(sv.h, 1)
            for q2 in range(2):
                sz = QSZ[q2]
                for kc in range(2):
                    cnt = 128 if kc == 0 else G - 128
                    v.wait_ge(st.h, stb(bh) + 2 + 2 * q2 + kc + 1)
                    v.tensor_copy(pgT[0:cnt, kc * 204 + q2 * 128:
                                      kc * 204 + q2 * 128 + sz],
                                  ps_t[0:cnt, 0:sz]).then_inc(sv.h, 1)
            v.wait_ge(sld.h, sld0 + SLD_BH * (bh + 1))
            v.reduce_max(cols[0:64, 4:5], kl[:, :], axis=AX.X, negate=True).then_inc(sv.h, 1)
            v.wait_ge(sa.h, sab(bh) + 4)
            v.reciprocal(cols[0:64, 6:7], cols[0:64, 5:6]).then_inc(sv.h, 1)
            v.wait_ge(sv.h, base + 14)
            v.tensor_scalar_mul(klsm[:, :], klsm[:, :], cols[0:64, 6:7]).then_inc(sv.h, 1)
            for j in range(NLC):
                cnt = 128 if j < NLC - 1 else lastc
                v.wait_ge(st.h, stb(bh) + 7 + j + 1)
                v.tensor_copy(klsmT[0:cnt, j * 64:(j + 1) * 64],
                              ps_t[0:cnt, 0:64]).then_inc(sv.h, 1)
            v.wait_ge(st.h, stb(bh) + 7 + NLC + 1)
            v.tensor_copy(kv_sb[:, :], ps_kv[:, :]).then_inc(sv.h, 1)
        sv.n = sv0 + BH * SV_BH
        ph.done(_last_inc(v, k))

    @blk.scalar
    def _(s):
        ph.begin(s)
        for bh in range(BH):
            base = sab(bh)
            for q2 in range(2):
                sz = QSZ[q2]
                s.wait_ge(sv.h, svb(bh) + 4 * q2 + 2)
                s.activation(pg[0:sz, q2 * 256:q2 * 256 + G], ps_g[0:sz, 0:G],
                             AF.Exp, bias=cols[0:sz, 1:2], scale=0.125,
                             accum_out=cols[0:sz, 2:3]).then_inc(sa.h, 1)
            s.wait_ge(st.h, stb(bh) + 7)
            if bh >= 1:
                s.wait_ge(sstf.h, sstf0 + 2 * bh - 1)
            s.activation(og[:, 0:G], ps_o[:, 0:G], AF.Copy,
                         scale=1.0).then_inc(sa.h, 1)
            s.wait_ge(sv.h, svb(bh) + 13)
            s.activation(klsm[:, :], kl[:, :], AF.Exp, bias=cols[0:64, 4:5],
                         accum_out=cols[0:64, 5:6]).then_inc(sa.h, 1)
            if bh >= 1:
                s.wait_ge(sstf.h, sstf0 + 2 * bh)
            off = 0
            for w in range(4):
                ww = 512 if w < 3 else L - 1536
                s.wait_ge(st.h, stb(bh) + 7 + NLC + 1 + w + 1)
                s.activation(lo[:, off:off + ww], ps_o[:, 0:ww], AF.Copy,
                             scale=1.0).then_inc(sa.h, 1)
                off += ww
        sa.n = sa0 + BH * SA_BH
        ph.done(_last_inc_a(s, k))

    @blk.gpsimd
    def _(g):
        ph.begin(g)
        for bh in range(BH):
            b, h = divmod(bh, NH)
            r0, c0 = h * 64, b * S
            for snum in range(2):
                gw = bh * 2 + snum
                if gw >= 1:
                    g.wait_ge(sst.h, sst.n)
                    g.sem_inc(sstf.h, 1)
                    sstf.n += 1
                if snum == 0:
                    g.wait_ge(sa.h, sab(bh) + 3)
                    sst.inc(g.dma_start(out=a_fm[r0:r0 + 64, c0:c0 + G],
                                        in_=og[:, 0:G]), 16)
                else:
                    g.wait_ge(sa.h, sab(bh) + SA_BH)
                    sst.inc(g.dma_start(out=a_fm[r0:r0 + 64, c0 + G:c0 + S],
                                        in_=lo[:, :]), 16)
        ph.done_q(g, sst)
    blk.__exit__(None, None, None)
    loc.close()


def emit_cast(k, dst, src, rows, tag, nsplit=4):
    ph = Phase(k)
    sg = k.sem("wc_g")
    blk = k.nc.Block()
    blk.__enter__()

    @blk.gpsimd
    def _(g):
        ph.begin(g)
        step = rows // nsplit
        for i in range(nsplit):
            sg.inc(g.dma_start(out=dst[i * step:(i + 1) * step, :],
                               in_=src[i * step:(i + 1) * step, :]), 16)
        ph.done_q(g, sg)
    blk.__exit__(None, None, None)


def emit_transpose_mat(k, dst, src, R, C, tag):
    ph = Phase(k)
    sdl = [k.sem("wt_d%d" % j) for j in range(2)]
    so = k.sem("wt_o")
    sf = k.sem("wt_f")
    loc = Loc(k)
    buf = loc.sb("wt_b" + tag, [128, 2 * R], BF16)
    sdl0 = [x.n for x in sdl]
    so0, sf0 = so.n, sf.n
    nch = C // 128
    blk = k.nc.Block()
    blk.__enter__()

    @blk.sync
    def _(sync):
        ph.begin(sync)
        for c in range(nch):
            b = c % 2
            if c >= 2:
                sync.wait_ge(sf.h, sf0 + (c - 1))
            sdl[b].inc(sync.dma_start(out=buf[:, b * R:(b + 1) * R],
                                      in_=src[:, c * 128:(c + 1) * 128],
                                      transpose=True), 16)
        for j in range(2):
            sync.wait_ge(sdl[j].h, sdl[j].n)
        sync.sem_inc(k.bar.h, 1)
    k.bar.n += 1

    @blk.gpsimd
    def _(g):
        ph.begin(g)
        for c in range(nch):
            b = c % 2
            if c >= 1:
                g.wait_ge(so.h, so0 + 16 * c)
                g.sem_inc(sf.h, 1)
                sf.n += 1
            g.wait_ge(sdl[b].h, sdl0[b] + 16 * (c // 2 + 1))
            so.inc(g.dma_start(out=dst[c * 128:(c + 1) * 128, :],
                               in_=buf[:, b * R:(b + 1) * R]), 16)
        ph.done_q(g, so)
    blk.__exit__(None, None, None)
    loc.close()


def emit_rotate(k, dst, src, R, tag):
    ph = Phase(k)
    sdl = [k.sem("wr_d%d" % j) for j in range(2)]
    sv = k.sem("wr_v")
    so = k.sem("wr_o")
    sf = k.sem("wr_f")
    loc = Loc(k)
    bin_ = loc.sb("wr_i" + tag, [128, 2 * D], BF16)
    bout = loc.sb("wr_o" + tag, [128, 2 * D], BF16)
    sdl0 = [x.n for x in sdl]
    sv0, so0, sf0 = sv.n, so.n, sf.n
    nch = R // 128
    blk = k.nc.Block()
    blk.__enter__()

    @blk.sync
    def _(sync):
        ph.begin(sync)
        for c in range(nch):
            b = c % 2
            if c >= 2:
                sync.wait_ge(sv.h, sv0 + 2 * (c - 1))
            sdl[b].inc(sync.dma_start(out=bin_[:, b * D:(b + 1) * D],
                                      in_=src[c * 128:(c + 1) * 128, :]), 16)
        for j in range(2):
            sync.wait_ge(sdl[j].h, sdl[j].n)
        sync.sem_inc(k.bar.h, 1)
    k.bar.n += 1

    @blk.vector
    def _(v):
        ph.begin(v)
        for c in range(nch):
            b = c % 2
            v.wait_ge(sdl[b].h, sdl0[b] + 16 * (c // 2 + 1))
            if c >= 2:
                v.wait_ge(sf.h, sf0 + c - 1)
            iv = bin_[:, b * D:(b + 1) * D].rearrange("p (x two c) -> p x two c", two=2, c=32)
            ov = bout[:, b * D:(b + 1) * D].rearrange("p (x two c) -> p x two c", two=2, c=32)
            v.tensor_scalar_mul(ov[:, :, 0, :], iv[:, :, 1, :], -1.0).then_inc(sv.h, 1)
            v.wait_ge(sv.h, sv0 + 2 * c + 1)
            v.tensor_scalar_mul(ov[:, :, 1, :], iv[:, :, 0, :], 1.0).then_inc(sv.h, 1)
        sv.n = sv0 + 2 * nch
        ph.done(_last_inc(v, k))

    @blk.gpsimd
    def _(g):
        ph.begin(g)
        for c in range(nch):
            b = c % 2
            if c >= 1:
                g.wait_ge(so.h, so0 + 16 * c)
                g.sem_inc(sf.h, 1)
                sf.n += 1
            g.wait_ge(sv.h, sv0 + 2 * (c + 1))
            so.inc(g.dma_start(out=dst[c * 128:(c + 1) * 128, :],
                               in_=bout[:, b * D:(b + 1) * D]), 16)
        ph.done_q(g, so)
    blk.__exit__(None, None, None)
    loc.close()


def build_prep():
    k = K("prep")
    nc = k.nc
    f = lambda n, s: k.dram(n, s, F32, kind="ExternalInput")
    o = lambda n, s: k.dram(n, s, BF16, kind="ExternalOutput")
    i_ = {}
    for n, shape in [("L0q", [D, D]), ("L0k", [D, D]), ("L0v", [D, D]),
                     ("L0o", [D, D]), ("A1q", [D, D]), ("A1kv", [D, 512]),
                     ("A1o", [D, D]),
                     ("F0w1", [E, D, H_FFN]), ("F0w2", [E, H_FFN, D]),
                     ("L1q", [D, D]), ("L1k", [D, D]), ("L1v", [D, D]),
                     ("L1o", [D, D]),
                     ("F1w1", [E, H_FFN, D]), ("F1w2", [E, D, H_FFN])]:
        i_[n] = f(n, shape)
    outs = {}
    for n, shape in [("L0q_b", [D, D]), ("L0qr_b", [D, D]), ("L0k_b", [D, D]),
                     ("L0kr_b", [D, D]), ("L0v_b", [D, D]), ("L0o_b", [D, D]),
                     ("A1q_b", [D, D]), ("A1kv_b", [D, 512]), ("A1o_b", [D, D]),
                     ("F0w1_b", [E, D, H_FFN]), ("F0w2_b", [E, H_FFN, D]),
                     ("L1qT_b", [D, D]), ("L1qTr_b", [D, D]),
                     ("L1kT_b", [D, D]), ("L1kTr_b", [D, D]),
                     ("L1vT_b", [D, D]), ("L1oT_b", [D, D]),
                     ("F1w1T_b", [E, D, H_FFN]), ("F1w2T_b", [E, H_FFN, D])]:
        outs[n] = o(n, shape)
    tmp = {n: k.dram("tmp_" + n, shape, BF16) for n, shape in
           [("L1q", [D, D]), ("L1k", [D, D]), ("L1v", [D, D]), ("L1o", [D, D]),
            ("F1w1", [E, H_FFN, D]), ("F1w2", [E, D, H_FFN])]}
    common_init(k)
    for n in ["L0q", "L0k", "L0v", "L0o", "A1q", "A1kv", "A1o"]:
        emit_cast(k, outs[n + "_b"], i_[n], D, "c" + n)
    for e in range(E):
        emit_cast(k, outs["F0w1_b"][e], i_["F0w1"][e], D, "cF0w1%d" % e, nsplit=2)
        emit_cast(k, outs["F0w2_b"][e], i_["F0w2"][e], H_FFN, "cF0w2%d" % e, nsplit=2)
    for n in ["L1q", "L1k", "L1v", "L1o"]:
        emit_cast(k, tmp[n], i_[n], D, "c" + n)
        emit_transpose_mat(k, outs[n + "T_b"], tmp[n], D, D, "t" + n)
    for e in range(E):
        emit_cast(k, tmp["F1w1"][e], i_["F1w1"][e], H_FFN, "cF1w1%d" % e, nsplit=2)
        emit_transpose_mat(k, outs["F1w1T_b"][e], tmp["F1w1"][e], H_FFN, D,
                           "tF1w1%d" % e)
        emit_cast(k, tmp["F1w2"][e], i_["F1w2"][e], D, "cF1w2%d" % e, nsplit=2)
        emit_transpose_mat(k, outs["F1w2T_b"][e], tmp["F1w2"][e], D, H_FFN,
                           "tF1w2%d" % e)
    emit_rotate(k, outs["L0qr_b"], outs["L0q_b"], D, "rL0q")
    emit_rotate(k, outs["L0kr_b"], outs["L0k_b"], D, "rL0k")
    emit_rotate(k, outs["L1qTr_b"], outs["L1qT_b"], D, "rL1q")
    emit_rotate(k, outs["L1kTr_b"], outs["L1kT_b"], D, "rL1k")
    return k


def build_s_l0ffn():
    k = K("s_l0ffn")
    xin = k.dram("xin", [T, D], F32, kind="ExternalInput")
    wg = k.dram("wg", [D, E], F32, kind="ExternalInput")
    w1b = k.dram("w1b", [E, D, H_FFN], BF16, kind="ExternalInput")
    w2b = k.dram("w2b", [E, H_FFN, D], BF16, kind="ExternalInput")
    b1 = k.dram("b1", [E, H_FFN], F32, kind="ExternalInput")
    b2 = k.dram("b2", [E, D], F32, kind="ExternalInput")
    xout = k.dram("xout", [T, D], F32, kind="ExternalOutput")
    t_tm = k.dram("t_tm", [T, D], BF16)
    t_fm = k.dram("t_fm", [D, T], BF16)
    common_init(k)
    gates = k.sb("gates", [128, NT * E], F32)
    emit_copy(k, xout, xin, T, D, "x")
    emit_ln(k, xin, t_tm, "0")
    emit_tf(k, t_tm, t_fm, "0")
    emit_routing_l0(k, t_fm, wg, gates, "0")
    emit_ffn(k, t_fm, w1b, w2b, b1, b2, gates, xout, "0")
    return k


def build_s_l0attn():
    k = K("s_l0attn")
    xin = k.dram("xin", [T, D], F32, kind="ExternalInput")
    wq = k.dram("wq", [D, D], BF16, kind="ExternalInput")
    wqr = k.dram("wqr", [D, D], BF16, kind="ExternalInput")
    wk = k.dram("wk", [D, D], BF16, kind="ExternalInput")
    wkr = k.dram("wkr", [D, D], BF16, kind="ExternalInput")
    wv = k.dram("wv", [D, D], BF16, kind="ExternalInput")
    wo = k.dram("wo", [D, D], BF16, kind="ExternalInput")
    xout = k.dram("xout", [T, D], F32, kind="ExternalOutput")
    t_tm = k.dram("t_tm", [T, D], BF16)
    t_fm = k.dram("t_fm", [D, T], BF16)
    q_fm = k.dram("q_fm", [D, T], BF16)
    k_fm = k.dram("k_fm", [D, T], BF16)
    v_tm = k.dram("v_tm", [T, D], BF16)
    a_fm = k.dram("a_fm", [D, T], BF16)
    common_init(k)
    rl = Loc(k)
    emit_rope_tables(k, rl)
    emit_copy(k, xout, xin, T, D, "x")
    emit_ln(k, xin, t_tm, "0")
    emit_tf(k, t_tm, t_fm, "0")
    emit_proj_fm(k, t_fm, wq.ap(), q_fm, D, "q", rope=True, w_rot=wqr.ap())
    emit_proj_fm(k, t_fm, wk.ap(), k_fm, D, "k", rope=True, w_rot=wkr.ap())
    emit_proj_tm(k, t_fm, wv.ap(), v_tm, D, "v")
    rl.close()
    emit_l0attn(k, q_fm, k_fm, v_tm, a_fm, "a")
    emit_proj_res(k, a_fm, wo.ap(), xout, "o", scale=1.0)
    return k


def build_s_l1ffn():
    k = K("s_l1ffn")
    xin = k.dram("xin", [T, D], F32, kind="ExternalInput")
    anw = k.dram("anw", [D], F32, kind="ExternalInput")
    wqT = k.dram("wqT", [D, D], BF16, kind="ExternalInput")
    wqTr = k.dram("wqTr", [D, D], BF16, kind="ExternalInput")
    wkT = k.dram("wkT", [D, D], BF16, kind="ExternalInput")
    wkTr = k.dram("wkTr", [D, D], BF16, kind="ExternalInput")
    wvT = k.dram("wvT", [D, D], BF16, kind="ExternalInput")
    woT = k.dram("woT", [D, D], BF16, kind="ExternalInput")
    mnw = k.dram("mnw", [D], F32, kind="ExternalInput")
    gw = k.dram("gw", [E, D], F32, kind="ExternalInput")
    w1T = k.dram("w1T", [E, D, H_FFN], BF16, kind="ExternalInput")
    w2T = k.dram("w2T", [E, H_FFN, D], BF16, kind="ExternalInput")
    b1 = k.dram("b1", [E, H_FFN], F32, kind="ExternalInput")
    b2 = k.dram("b2", [E, D], F32, kind="ExternalInput")
    xout = k.dram("xout", [T, D], F32, kind="ExternalOutput")
    t_tm = k.dram("t_tm", [T, D], BF16)
    t1_tm = k.dram("t1_tm", [T, D], BF16)
    t1_fm = k.dram("t1_fm", [D, T], BF16)
    q_fm = k.dram("q_fm", [D, T], BF16)
    k_fm = k.dram("k_fm", [D, T], BF16)
    v_tm = k.dram("v_tm", [T, D], BF16)
    a_fm = k.dram("a_fm", [D, T], BF16)
    h_int = k.dram("h_int", [T, D], F32)
    t2_tm = k.dram("t2_tm", [T, D], BF16)
    t2_fm = k.dram("t2_fm", [D, T], BF16)
    common_init(k)
    gates = k.sb("gates", [128, NT * E], F32)
    rl = Loc(k)
    emit_rope_tables(k, rl)
    emit_copy(k, xout, xin, T, D, "x")
    emit_ln(k, xin, t_tm, "0")
    emit_rmsnorm(k, t_tm, anw.ap(), t1_tm, "r1", src_dt=BF16)
    emit_tf(k, t1_tm, t1_fm, "1")
    emit_proj_fm(k, t1_fm, wqT.ap(), q_fm, D, "q", rope=True, w_rot=wqTr.ap())
    emit_proj_fm(k, t1_fm, wkT.ap(), k_fm, D, "k", rope=True, w_rot=wkTr.ap())
    emit_proj_tm(k, t1_fm, wvT.ap(), v_tm, D, "v")
    rl.close()
    emit_attn(k, q_fm, k_fm, v_tm, a_fm, "a", causal=True, alibi=False,
              kv_group=1)
    emit_proj_res(k, a_fm, woT.ap(), xout, "o", scale=0.5, t_add=t_tm,
                  h_out=h_int)
    emit_rmsnorm(k, h_int, mnw.ap(), t2_tm, "r2", src_dt=F32)
    emit_tf(k, t2_tm, t2_fm, "2")
    emit_routing_l1(k, t2_fm, gw, gates, "g")
    emit_ffn(k, t2_fm, w1T, w2T, b1, b2, gates, xout, "f")
    return k


def build_s_l1attn():
    k = K("s_l1attn")
    xin = k.dram("xin", [T, D], F32, kind="ExternalInput")
    wq = k.dram("wq", [D, D], BF16, kind="ExternalInput")
    wkv = k.dram("wkv", [D, 512], BF16, kind="ExternalInput")
    wo = k.dram("wo", [D, D], BF16, kind="ExternalInput")
    xout = k.dram("xout", [T, D], F32, kind="ExternalOutput")
    t_tm = k.dram("t_tm", [T, D], BF16)
    t_fm = k.dram("t_fm", [D, T], BF16)
    q_fm = k.dram("q_fm", [D, T], BF16)
    k_fm = k.dram("k_fm", [256, T], BF16)
    v_tm = k.dram("v_tm", [T, 256], BF16)
    a_fm = k.dram("a_fm", [D, T], BF16)
    common_init(k)
    emit_copy(k, xout, xin, T, D, "x")
    emit_ln(k, xin, t_tm, "0")
    emit_tf(k, t_tm, t_fm, "0")
    emit_proj_fm(k, t_fm, wq.ap(), q_fm, D, "q")
    emit_proj_fm(k, t_fm, wkv.ap()[:, 0:256], k_fm, 256, "k")
    emit_proj_tm(k, t_fm, wkv.ap()[:, 256:512], v_tm, 256, "v")
    emit_attn(k, q_fm, k_fm, v_tm, a_fm, "a", causal=False, alibi=True,
              kv_group=4)
    emit_proj_res(k, a_fm, wo.ap(), xout, "o", scale=1.0)
    return k


_PROGS = {}


def _prog(name, builder):
    if name not in _PROGS:
        _PROGS[name] = builder()
    return _PROGS[name]


def kernel(**inputs):
    return kernel_fused(**inputs)


def kernel_multilaunch(**inputs):
    x = np.ascontiguousarray(inputs["L0_attn_x"], dtype=np.float32)
    shards = [np.ascontiguousarray(s) for s in x.reshape(8, T, D)]
    cores = list(range(8))

    def run(k, per_core):
        res = run_bass_kernel_spmd(k.nc, per_core, core_ids=cores)
        return res.results

    kp = _prog("prep", build_prep)
    pin = dict(
        L0q=inputs["L0_attn_Wq"], L0k=inputs["L0_attn_Wk"],
        L0v=inputs["L0_attn_Wv"], L0o=inputs["L0_attn_Wout"],
        A1q=inputs["L1_attn_Wq"], A1kv=inputs["L1_attn_Wkv"],
        A1o=inputs["L1_attn_Wo"],
        F0w1=inputs["L0_ffn_W1"], F0w2=inputs["L0_ffn_W2"],
        L1q=inputs["L1_ffn_wq"], L1k=inputs["L1_ffn_wk"],
        L1v=inputs["L1_ffn_wv"], L1o=inputs["L1_ffn_wo"],
        F1w1=inputs["L1_ffn_w1"], F1w2=inputs["L1_ffn_w2"])
    pin = {k2: np.ascontiguousarray(v, dtype=np.float32) for k2, v in pin.items()}
    pres = run(kp, [pin] * 8)
    W = pres[0]

    kf0 = _prog("l0ffn", build_s_l0ffn)
    ka0 = _prog("l0attn", build_s_l0attn)
    kf1 = _prog("l1ffn", build_s_l1ffn)
    ka1 = _prog("l1attn", build_s_l1attn)

    f0_in = dict(wg=np.ascontiguousarray(inputs["L0_ffn_wg"], np.float32),
                 w1b=W["F0w1_b"], w2b=W["F0w2_b"],
                 b1=np.ascontiguousarray(inputs["L0_ffn_b1"], np.float32),
                 b2=np.ascontiguousarray(inputs["L0_ffn_b2"], np.float32))
    a0_in = dict(wq=W["L0q_b"], wqr=W["L0qr_b"], wk=W["L0k_b"],
                 wkr=W["L0kr_b"], wv=W["L0v_b"], wo=W["L0o_b"])
    f1_in = dict(anw=np.ascontiguousarray(inputs["L1_ffn_attn_norm_w"], np.float32),
                 wqT=W["L1qT_b"], wqTr=W["L1qTr_b"], wkT=W["L1kT_b"],
                 wkTr=W["L1kTr_b"], wvT=W["L1vT_b"], woT=W["L1oT_b"],
                 mnw=np.ascontiguousarray(inputs["L1_ffn_moe_norm_w"], np.float32),
                 gw=np.ascontiguousarray(inputs["L1_ffn_gate_w"], np.float32),
                 w1T=W["F1w1T_b"], w2T=W["F1w2T_b"],
                 b1=np.ascontiguousarray(inputs["L1_ffn_b1"], np.float32),
                 b2=np.ascontiguousarray(inputs["L1_ffn_b2"], np.float32))
    a1_in = dict(wq=W["A1q_b"], wkv=W["A1kv_b"], wo=W["A1o_b"])

    cur = shards
    for k2, extra in [(kf0, f0_in), (ka0, a0_in), (kf0, f0_in),
                      (kf1, f1_in), (ka1, a1_in), (kf1, f1_in)]:
        res = run(k2, [dict(xin=cur[c], **extra) for c in range(8)])
        cur = [res[c]["xout"] for c in range(8)]

    return np.concatenate([c[None] for c in cur], 0).reshape(16, S, D)


def emit_sub_l0ffn(k, xres, wg, w1b, w2b, b1, b2, gates, tag):
    t_tm = k.dram("t_tm" + tag, [T, D], BF16)
    t_fm = k.dram("t_fm" + tag, [D, T], BF16)
    emit_ln(k, xres, t_tm, tag)
    emit_tf(k, t_tm, t_fm, tag)
    emit_routing_l0(k, t_fm, wg, gates, tag)
    emit_ffn(k, t_fm, w1b, w2b, b1, b2, gates, xres, tag)


def emit_sub_l0attn(k, xres, wq, wqr, wk, wkr, wv, wo, tag):
    t_tm = k.dram("t_tm" + tag, [T, D], BF16)
    t_fm = k.dram("t_fm" + tag, [D, T], BF16)
    q_fm = k.dram("q_fm" + tag, [D, T], BF16)
    k_fm = k.dram("k_fm" + tag, [D, T], BF16)
    v_tm = k.dram("v_tm" + tag, [T, D], BF16)
    a_fm = k.dram("a_fm" + tag, [D, T], BF16)
    rl = Loc(k)
    emit_rope_tables(k, rl)
    emit_ln(k, xres, t_tm, tag)
    emit_tf(k, t_tm, t_fm, tag)
    emit_proj_fm(k, t_fm, wq.ap(), q_fm, D, tag + "q", rope=True, w_rot=wqr.ap())
    emit_proj_fm(k, t_fm, wk.ap(), k_fm, D, tag + "k", rope=True, w_rot=wkr.ap())
    emit_proj_tm(k, t_fm, wv.ap(), v_tm, D, tag + "v")
    rl.close()
    emit_l0attn(k, q_fm, k_fm, v_tm, a_fm, tag)
    emit_proj_res(k, a_fm, wo.ap(), xres, tag + "o", scale=1.0)


def emit_sub_l1ffn(k, xres, anw, wqT, wqTr, wkT, wkTr, wvT, woT, mnw, gw,
                   w1T, w2T, b1, b2, gates, tag):
    t_tm = k.dram("t_tm" + tag, [T, D], BF16)
    t1_tm = k.dram("t1_tm" + tag, [T, D], BF16)
    t1_fm = k.dram("t1_fm" + tag, [D, T], BF16)
    q_fm = k.dram("q_fm" + tag, [D, T], BF16)
    k_fm = k.dram("k_fm" + tag, [D, T], BF16)
    v_tm = k.dram("v_tm" + tag, [T, D], BF16)
    a_fm = k.dram("a_fm" + tag, [D, T], BF16)
    h_int = k.dram("h_int" + tag, [T, D], F32)
    t2_tm = k.dram("t2_tm" + tag, [T, D], BF16)
    t2_fm = k.dram("t2_fm" + tag, [D, T], BF16)
    rl = Loc(k)
    emit_rope_tables(k, rl)
    emit_ln(k, xres, t_tm, tag)
    emit_rmsnorm(k, t_tm, anw.ap(), t1_tm, tag + "r1", src_dt=BF16)
    emit_tf(k, t1_tm, t1_fm, tag + "1")
    emit_proj_fm(k, t1_fm, wqT.ap(), q_fm, D, tag + "q", rope=True,
                 w_rot=wqTr.ap())
    emit_proj_fm(k, t1_fm, wkT.ap(), k_fm, D, tag + "k", rope=True,
                 w_rot=wkTr.ap())
    emit_proj_tm(k, t1_fm, wvT.ap(), v_tm, D, tag + "v")
    rl.close()
    emit_attn(k, q_fm, k_fm, v_tm, a_fm, tag, causal=True, alibi=False,
              kv_group=1)
    emit_proj_res(k, a_fm, woT.ap(), xres, tag + "o", scale=0.5, t_add=t_tm,
                  h_out=h_int)
    emit_rmsnorm(k, h_int, mnw.ap(), t2_tm, tag + "r2", src_dt=F32)
    emit_tf(k, t2_tm, t2_fm, tag + "2")
    emit_routing_l1(k, t2_fm, gw, gates, tag)
    emit_ffn(k, t2_fm, w1T, w2T, b1, b2, gates, xres, tag + "f")


def emit_sub_l1attn(k, xres, wq, wkv, wo, tag):
    t_tm = k.dram("t_tm" + tag, [T, D], BF16)
    t_fm = k.dram("t_fm" + tag, [D, T], BF16)
    q_fm = k.dram("q_fm" + tag, [D, T], BF16)
    k_fm = k.dram("k_fm" + tag, [256, T], BF16)
    v_tm = k.dram("v_tm" + tag, [T, 256], BF16)
    a_fm = k.dram("a_fm" + tag, [D, T], BF16)
    emit_ln(k, xres, t_tm, tag)
    emit_tf(k, t_tm, t_fm, tag)
    emit_proj_fm(k, t_fm, wq.ap(), q_fm, D, tag + "q")
    emit_proj_fm(k, t_fm, wkv.ap()[:, 0:256], k_fm, 256, tag + "k")
    emit_proj_tm(k, t_fm, wkv.ap()[:, 256:512], v_tm, 256, tag + "v")
    emit_attn(k, q_fm, k_fm, v_tm, a_fm, tag, causal=False, alibi=True,
              kv_group=4)
    emit_proj_res(k, a_fm, wo.ap(), xres, tag + "o", scale=1.0)


def build_fused():
    k = K("fused")
    f = lambda n, shp: k.dram(n, shp, F32, kind="ExternalInput")
    xin = f("L0_attn_x", [T, D])
    i_ = {n: f(n, shp) for n, shp in [
        ("L0_attn_Wq", [D, D]), ("L0_attn_Wk", [D, D]), ("L0_attn_Wv", [D, D]),
        ("L0_attn_Wout", [D, D]), ("L0_ffn_wg", [D, E]),
        ("L0_ffn_W1", [E, D, H_FFN]), ("L0_ffn_b1", [E, H_FFN]),
        ("L0_ffn_W2", [E, H_FFN, D]), ("L0_ffn_b2", [E, D]),
        ("L1_attn_Wq", [D, D]), ("L1_attn_Wkv", [D, 512]),
        ("L1_attn_Wo", [D, D]), ("L1_ffn_attn_norm_w", [D]),
        ("L1_ffn_wq", [D, D]), ("L1_ffn_wk", [D, D]), ("L1_ffn_wv", [D, D]),
        ("L1_ffn_wo", [D, D]), ("L1_ffn_moe_norm_w", [D]),
        ("L1_ffn_gate_w", [E, D]), ("L1_ffn_w1", [E, H_FFN, D]),
        ("L1_ffn_b1", [E, H_FFN]), ("L1_ffn_w2", [E, D, H_FFN]),
        ("L1_ffn_b2", [E, D])]}
    xout = k.dram("xout", [T, D], F32, kind="ExternalOutput")
    b = lambda n, shp: k.dram(n, shp, BF16, kind="Internal")
    W = {n: b(n, shp) for n, shp in [
        ("L0q_b", [D, D]), ("L0qr_b", [D, D]), ("L0k_b", [D, D]),
        ("L0kr_b", [D, D]), ("L0v_b", [D, D]), ("L0o_b", [D, D]),
        ("A1q_b", [D, D]), ("A1kv_b", [D, 512]), ("A1o_b", [D, D]),
        ("F0w1_b", [E, D, H_FFN]), ("F0w2_b", [E, H_FFN, D]),
        ("L1qT_b", [D, D]), ("L1qTr_b", [D, D]), ("L1kT_b", [D, D]),
        ("L1kTr_b", [D, D]), ("L1vT_b", [D, D]), ("L1oT_b", [D, D]),
        ("F1w1T_b", [E, D, H_FFN]), ("F1w2T_b", [E, H_FFN, D]),
        ("tmp_L1q", [D, D]), ("tmp_L1k", [D, D]), ("tmp_L1v", [D, D]),
        ("tmp_L1o", [D, D]), ("tmp_F1w1", [E, H_FFN, D]),
        ("tmp_F1w2", [E, D, H_FFN])]}
    common_init(k)
    gates = k.sb("gates", [128, NT * E], F32)
    for src, dst in [("L0_attn_Wq", "L0q_b"), ("L0_attn_Wk", "L0k_b"),
                     ("L0_attn_Wv", "L0v_b"), ("L0_attn_Wout", "L0o_b"),
                     ("L1_attn_Wq", "A1q_b"), ("L1_attn_Wkv", "A1kv_b"),
                     ("L1_attn_Wo", "A1o_b")]:
        emit_cast(k, W[dst], i_[src], D, "c" + dst)
    for e in range(E):
        emit_cast(k, W["F0w1_b"][e], i_["L0_ffn_W1"][e], D, "cf1%d" % e, nsplit=2)
        emit_cast(k, W["F0w2_b"][e], i_["L0_ffn_W2"][e], H_FFN, "cf2%d" % e, nsplit=2)
    for src, tm, dst in [("L1_ffn_wq", "tmp_L1q", "L1qT_b"),
                         ("L1_ffn_wk", "tmp_L1k", "L1kT_b"),
                         ("L1_ffn_wv", "tmp_L1v", "L1vT_b"),
                         ("L1_ffn_wo", "tmp_L1o", "L1oT_b")]:
        emit_cast(k, W[tm], i_[src], D, "c" + tm)
        emit_transpose_mat(k, W[dst], W[tm], D, D, "t" + dst)
    for e in range(E):
        emit_cast(k, W["tmp_F1w1"][e], i_["L1_ffn_w1"][e], H_FFN, "cg1%d" % e, nsplit=2)
        emit_transpose_mat(k, W["F1w1T_b"][e], W["tmp_F1w1"][e], H_FFN, D, "tg1%d" % e)
        emit_cast(k, W["tmp_F1w2"][e], i_["L1_ffn_w2"][e], D, "cg2%d" % e, nsplit=2)
        emit_transpose_mat(k, W["F1w2T_b"][e], W["tmp_F1w2"][e], D, H_FFN, "tg2%d" % e)
    emit_rotate(k, W["L0qr_b"], W["L0q_b"], D, "r0q")
    emit_rotate(k, W["L0kr_b"], W["L0k_b"], D, "r0k")
    emit_rotate(k, W["L1qTr_b"], W["L1qT_b"], D, "r1q")
    emit_rotate(k, W["L1kTr_b"], W["L1kT_b"], D, "r1k")
    emit_copy(k, xout, xin, T, D, "x")
    emit_sub_l0ffn(k, xout, i_["L0_ffn_wg"], W["F0w1_b"], W["F0w2_b"],
                   i_["L0_ffn_b1"], i_["L0_ffn_b2"], gates, "A")
    emit_sub_l0attn(k, xout, W["L0q_b"], W["L0qr_b"], W["L0k_b"], W["L0kr_b"],
                    W["L0v_b"], W["L0o_b"], "B")
    emit_sub_l0ffn(k, xout, i_["L0_ffn_wg"], W["F0w1_b"], W["F0w2_b"],
                   i_["L0_ffn_b1"], i_["L0_ffn_b2"], gates, "C")
    emit_sub_l1ffn(k, xout, i_["L1_ffn_attn_norm_w"], W["L1qT_b"],
                   W["L1qTr_b"], W["L1kT_b"], W["L1kTr_b"], W["L1vT_b"],
                   W["L1oT_b"], i_["L1_ffn_moe_norm_w"], i_["L1_ffn_gate_w"],
                   W["F1w1T_b"], W["F1w2T_b"], i_["L1_ffn_b1"],
                   i_["L1_ffn_b2"], gates, "D")
    emit_sub_l1attn(k, xout, W["A1q_b"], W["A1kv_b"], W["A1o_b"], "E")
    emit_sub_l1ffn(k, xout, i_["L1_ffn_attn_norm_w"], W["L1qT_b"],
                   W["L1qTr_b"], W["L1kT_b"], W["L1kTr_b"], W["L1vT_b"],
                   W["L1oT_b"], i_["L1_ffn_moe_norm_w"], i_["L1_ffn_gate_w"],
                   W["F1w1T_b"], W["F1w2T_b"], i_["L1_ffn_b1"],
                   i_["L1_ffn_b2"], gates, "F")
    return k


def kernel_fused(**inputs):
    x = np.ascontiguousarray(inputs["L0_attn_x"], dtype=np.float32)
    shards = [np.ascontiguousarray(s) for s in x.reshape(8, T, D)]
    k = _prog("fused", build_fused)
    base = {n: np.ascontiguousarray(inputs[n], dtype=np.float32)
            for n in inputs if n != "L0_attn_x"}
    in_maps = [dict(L0_attn_x=shards[c], **base) for c in range(8)]
    res = run_bass_kernel_spmd(k.nc, in_maps, core_ids=list(range(8)))
    cur = [res.results[c]["xout"] for c in range(8)]
    return np.concatenate([c[None] for c in cur], 0).reshape(16, S, D)


# version 03 (final)
# speedup vs baseline: 1.4736x; 1.4736x over previous
import contextlib
import numpy as np
import ml_dtypes
import concourse.bass as bass
import concourse.mybir as mybir
import concourse.library_config as libcfg
from concourse.bass_utils import run_bass_kernel_spmd

F32 = mybir.dt.float32
BF16 = mybir.dt.bfloat16
U32 = mybir.dt.uint32
I16 = mybir.dt.int16
AF = mybir.ActivationFunctionType
ALU = mybir.AluOpType
AX = mybir.AxisListType

T = 4096
D = 1024
NBATCH = 2
S = 2048
NT = T // 128
NF = D // 128
E = 8
H_FFN = 4096
CAPACITY = 5121
NEGINF = -1.0e30
PI = float(np.pi)


class Sem:
    __slots__ = ("h", "n")

    def __init__(self, h):
        self.h = h
        self.n = 0

    def inc(self, ins, k=1):
        ins.then_inc(self.h, k)
        self.n += k
        return self.n

    def wait(self, eng, val=None):
        eng.wait_ge(self.h, self.n if val is None else val)


class K:

    def __init__(self, name):
        self.nc = bass.Bass("TRN2", target_bir_lowering=False, name=name)
        self.ctx = contextlib.ExitStack()
        self._sems = {}
        self.bar = self.sem("BAR")
        self.marks = {}

    def sem(self, name):
        if name not in self._sems:
            self._sems[name] = Sem(self.ctx.enter_context(self.nc.semaphore(name)))
        return self._sems[name]

    def sb(self, name, shape, dt):
        return self.ctx.enter_context(self.nc.sbuf_tensor(name, shape, dt))

    def ps(self, name, shape, dt=F32):
        return self.ctx.enter_context(self.nc.psum_tensor(name, shape, dt))

    def dram(self, name, shape, dt, kind="Internal"):
        return self.nc.dram_tensor(name, shape, dt, kind=kind)

    def close(self):
        self.ctx.close()


class Loc:

    def __init__(self, k):
        self.k = k
        self.ctx = contextlib.ExitStack()

    def sb(self, name, shape, dt):
        return self.ctx.enter_context(self.k.nc.sbuf_tensor(name, shape, dt))

    def ps(self, name, shape, dt=F32):
        return self.ctx.enter_context(self.k.nc.psum_tensor(name, shape, dt))

    def close(self):
        self.ctx.close()


def common_init(k, need_ident=True, need_consts=True):
    nc = k.nc
    k.identb = k.sb("identb", [128, 128], BF16)
    k.identf = k.sb("identf", [128, 128], F32)
    k.consts = k.sb("consts", [128, 8], F32)
    k.egrid = k.sb("egrid", [128, E], F32)
    k.ones_row = k.sb("ones_row", [1, 512], BF16)
    k.ones_rowf = k.sb("ones_rowf", [1, 512], F32)
    k.ones_colf8 = k.sb("ones_colf8", [E, 1], F32)
    k.piota8 = k.sb("piota8", [E, 1], F32)
    k.tokid16 = k.sb("tokid16", [128, NT], I16)
    k.tokid32 = k.sb("tokid32", [128, NT], U32)
    init = k.sem("init")
    blk = k.nc.Block()
    blk.__enter__()

    @blk.gpsimd
    def _(g):
        for ident in (k.identb, k.identf):
            init.inc(g.memset(ident[:, :], 0.0), 1)
            g.wait_ge(init.h, init.n)
            g.affine_select(out=ident[:, :], in_=ident[:, :],
                            compare_op=ALU.not_equal, fill=1.0, base=0,
                            pattern=[[-1, 128]], channel_multiplier=1)
        g.memset(k.consts[:, 0:1], 1e-5)
        g.memset(k.consts[:, 1:2], 1e-6)
        g.memset(k.consts[:, 2:3], PI)
        g.memset(k.consts[:, 3:4], 1.0)
        g.memset(k.consts[:, 4:5], -PI)
        g.iota(k.egrid[:, :], [[1, E]], base=0, channel_multiplier=0,
               allow_small_or_imprecise_dtypes=True)
        g.memset(k.ones_row[:, :], 1.0)
        g.memset(k.ones_rowf[:, :], 1.0)
        g.iota(k.piota8[:, :], [[0, 1]], base=0, channel_multiplier=1,
               allow_small_or_imprecise_dtypes=True)
        g.iota(k.tokid16[:, :], [[128, NT]], base=0, channel_multiplier=1,
               allow_small_or_imprecise_dtypes=True)
        g.iota(k.tokid32[:, :], [[128, NT]], base=0, channel_multiplier=1,
               allow_small_or_imprecise_dtypes=True)
        init.inc(g.memset(k.ones_colf8[:, :], 1.0), 1)
    blk.__exit__(None, None, None)
    k.init_done = init.n
    k.init_sem = init


class Phase:

    def __init__(self, k):
        self.k = k
        self.entry = k.bar.n

    def begin(self, eng):
        eng.wait_ge(self.k.bar.h, self.entry)
        self.k.init_sem.wait(eng, self.k.init_done)

    def done(self, ins):
        self.k.bar.inc(ins, 1)

    def done_q(self, eng, dsem):
        eng.wait_ge(dsem.h, dsem.n)
        eng.sem_inc(self.k.bar.h, 1)
        self.k.bar.n += 1


def emit_ln(k, xres, t_tm, tag, eps_col_idx=0, n_tiles=NT):
    nc = k.nc
    ph = Phase(k)
    sdl = [k.sem("ln_d%d" % j) for j in range(4)]
    sv = k.sem("ln_v")
    sa = k.sem("ln_a")
    so = k.sem("ln_o")
    sf = k.sem("ln_f")
    NBUF = 4
    loc = Loc(k)
    xt = loc.sb("ln_x" + tag, [128, NBUF * D], F32)
    xc = loc.sb("ln_c" + tag, [128, NBUF * D], F32)
    sq = loc.sb("ln_q" + tag, [128, D], F32)
    tt = loc.sb("ln_t" + tag, [128, NBUF * D], BF16)
    cols = loc.sb("ln_s" + tag, [128, 8 * NBUF], F32)

    sdl0 = [x.n for x in sdl]
    sv0, sa0, so0, sf0 = sv.n, sa.n, so.n, sf.n
    blk = k.nc.Block()
    blk.__enter__()

    @blk.sync
    def _(sync):
        ph.begin(sync)
        for i in range(n_tiles):
            b = i % NBUF
            if i >= NBUF:
                sync.wait_ge(sf.h, sf0 + (i - NBUF + 1))
            sdl[b].inc(sync.dma_start(out=xt[:, b * D:(b + 1) * D],
                                      in_=xres[i * 128:(i + 1) * 128, :]), 16)
        for j in range(4):
            sync.wait_ge(sdl[j].h, sdl[j].n)
        sync.sem_inc(k.bar.h, 1)
    k.bar.n += 1

    @blk.vector
    def _(v):
        ph.begin(v)
        for i in range(n_tiles):
            b = i % NBUF
            c = 8 * b
            v.wait_ge(sdl[b].h, sdl0[b] + 16 * (i // NBUF + 1))
            v.reduce_sum(cols[:, c:c + 1], xt[:, b * D:(b + 1) * D], axis=AX.X).then_inc(sv.h, 1)
            v.wait_ge(sv.h, sv0 + 4 * i + 1)
            v.tensor_scalar_mul(cols[:, c + 1:c + 2], cols[:, c:c + 1], -1.0 / D).then_inc(sv.h, 1)
            v.wait_ge(sa.h, sa0 + 3 * (i + 1))
            v.reciprocal(cols[:, c + 3:c + 4], cols[:, c + 2:c + 3]).then_inc(sv.h, 1)
            v.wait_ge(sv.h, sv0 + 4 * i + 3)
            v.tensor_scalar_mul(tt[:, b * D:(b + 1) * D], xc[:, b * D:(b + 1) * D],
                                cols[:, c + 3:c + 4]).then_inc(sv.h, 1)
        sv.n = sv0 + 4 * n_tiles
        ph.done(_last_inc(v, k))

    @blk.scalar
    def _(s):
        ph.begin(s)
        for i in range(n_tiles):
            b = i % NBUF
            c = 8 * b
            s.wait_ge(sv.h, sv0 + 4 * i + 2)
            s.activation(xc[:, b * D:(b + 1) * D], xt[:, b * D:(b + 1) * D],
                         AF.Identity, bias=cols[:, c + 1:c + 2], scale=1.0).then_inc(sa.h, 1)
            s.wait_ge(sa.h, sa0 + 3 * i + 1)
            s.activation(sq[:, :], xc[:, b * D:(b + 1) * D], AF.Square,
                         accum_out=cols[:, c + 4:c + 5]).then_inc(sa.h, 1)
            s.wait_ge(sa.h, sa0 + 3 * i + 2)
            s.activation(cols[:, c + 2:c + 3], cols[:, c + 4:c + 5], AF.Sqrt,
                         bias=k.consts[:, eps_col_idx:eps_col_idx + 1],
                         scale=1.0 / D).then_inc(sa.h, 1)
        sa.n = sa0 + 3 * n_tiles
        ph.done(_last_inc_a(s, k))

    @blk.gpsimd
    def _(g):
        ph.begin(g)
        for i in range(n_tiles):
            b = i % NBUF
            if i >= 1:
                g.wait_ge(so.h, so0 + 16 * i)
                g.sem_inc(sf.h, 1)
                sf.n += 1
            g.wait_ge(sv.h, sv0 + 4 * (i + 1))
            so.inc(g.dma_start(out=t_tm[i * 128:(i + 1) * 128, :],
                               in_=tt[:, b * D:(b + 1) * D]), 16)
        ph.done_q(g, so)
    blk.__exit__(None, None, None)
    loc.close()


def _last_inc(eng, k):
    if hasattr(eng, "memset"):
        return eng.memset(k.consts[:, 5:6], 0.0)
    return eng.memzero(k.consts[:, 5:6])


def emit_tf(k, t_tm, t_fm, tag, rows=T, dcols=D):
    ph = Phase(k)
    sdl = [k.sem("tf_d%d" % j) for j in range(2)]
    so = k.sem("tf_o")
    sf = k.sem("tf_f")
    nchunk = dcols // 128
    loc = Loc(k)
    buf = loc.sb("tf_b" + tag, [128, 2 * rows], BF16)
    sdl0 = [x.n for x in sdl]
    so0, sf0 = so.n, sf.n
    blk = k.nc.Block()
    blk.__enter__()

    @blk.sync
    def _(sync):
        ph.begin(sync)
        for c in range(nchunk):
            b = c % 2
            if c >= 2:
                sync.wait_ge(sf.h, sf0 + (c - 1))
            sdl[b].inc(sync.dma_start(out=buf[:, b * rows:(b + 1) * rows],
                                      in_=t_tm[:, c * 128:(c + 1) * 128],
                                      transpose=True), 16)
        for j in range(2):
            sync.wait_ge(sdl[j].h, sdl[j].n)
        sync.sem_inc(k.bar.h, 1)
    k.bar.n += 1

    @blk.gpsimd
    def _(g):
        ph.begin(g)
        for c in range(nchunk):
            b = c % 2
            if c >= 1:
                g.wait_ge(so.h, so0 + 16 * c)
                g.sem_inc(sf.h, 1)
                sf.n += 1
            g.wait_ge(sdl[b].h, sdl0[b] + 16 * (c // 2 + 1))
            so.inc(g.dma_start(out=t_fm[c * 128:(c + 1) * 128, :],
                               in_=buf[:, b * rows:(b + 1) * rows]), 16)
        ph.done_q(g, so)
    blk.__exit__(None, None, None)
    loc.close()


def emit_routing_l0(k, t_fm, wg_f32, gates, tag):
    nc = k.nc
    cc_in = k.dram("rcc_in" + tag, [1, E], F32)
    cc_out = k.dram("rcc_out" + tag, [E, E], F32)

    loc = Loc(k)
    wg_sb = loc.sb("r_wg" + tag, [128, NF * E], BF16)
    tfm = loc.sb("r_t" + tag, [128, NF * 512], BF16)
    max8 = loc.sb("r_m8" + tag, [128, 8], F32)
    lgt = loc.sb("r_lg" + tag, [128, 2 * E], F32)
    idxu = loc.sb("r_iu" + tag, [128, 8], U32)
    cols = loc.sb("r_c" + tag, [128, 8], F32)
    expv = loc.sb("r_ex" + tag, [128, 8], F32)
    oh_all = loc.sb("r_oh" + tag, [128, NT * E], F32)
    cwh = loc.sb("r_cw" + tag, [128, NT], F32)
    ohT = loc.sb("r_ohT" + tag, [E, T], F32)
    zer8 = loc.sb("r_z8" + tag, [E, T], F32)
    scn = loc.sb("r_sc" + tag, [E, T], F32)
    sm8 = loc.sb("r_s8" + tag, [E, 16], F32)
    row1 = loc.sb("r_r1" + tag, [1, 16], F32)
    pidu = loc.sb("r_pd" + tag, [1, 2], U32)
    cnts = loc.sb("r_cn" + tag, [E, E], F32)
    k.ps_anchor = loc.ps("r_anc" + tag, [1, 8], F32)
    ps_lg = loc.ps("r_plg" + tag, [128, 1024], F32)
    ps_t = loc.ps("r_pt" + tag, [128, 128], F32)
    ps_s = loc.ps("r_ps" + tag, [E, 128], F32)

    sd = k.sem("r_d")
    st = k.sem("r_t")
    sv = k.sem("r_v")
    sa = k.sem("r_a")
    sg = k.sem("r_g")
    scc = k.sem("r_cc")

    import os as _os
    dbg = _os.environ.get("DEBUG_ROUTING")
    if dbg:
        lg_dump = k.dram("lgdump" + tag, [T, E], F32, kind="ExternalOutput")
        oh_dump = k.dram("ohdump" + tag, [128, NT * E], F32, kind="ExternalOutput")
        ohT_dump = k.dram("ohTdump" + tag, [E, T], F32, kind="ExternalOutput")
        oh4_dump = k.dram("oh4dump" + tag, [E, T], F32, kind="ExternalOutput")
        cw_dump = k.dram("cwdump" + tag, [128, NT], F32, kind="ExternalOutput")
        scn_dump = k.dram("scndump" + tag, [E, T], F32, kind="ExternalOutput")
    ph = Phase(k)
    sd0, st0, sv0, sa0 = sd.n, st.n, sv.n, sa.n
    blk = nc.Block()
    blk.__enter__()

    sdg = k.sem("r_dg")
    sdg0 = sdg.n

    @blk.gpsimd
    def _(g):
        ph.begin(g)
        sdg.inc(g.dma_start(
            out=wg_sb[:, :].rearrange("p (c e) -> p c e", c=NF),
            in_=wg_f32.ap().rearrange("(c p) e -> p c e", p=128)), 16)

    @blk.sync
    def _(sync):
        ph.begin(sync)
        for w in range(NT // 4):
            if w >= 1:
                sync.wait_ge(st.h, st0 + 4 * w)
            sd.inc(sync.dma_start(
                out=tfm[:, :].rearrange("p (c t) -> p c t", c=NF),
                in_=t_fm[:, w * 512:(w + 1) * 512].rearrange("(c p) t -> p c t", p=128)), 16)
        sync.wait_ge(sd.h, sd.n)
        sync.sem_inc(k.bar.h, 1)
    k.bar.n += 1

    @blk.tensor
    def _(t):
        ph.begin(t)
        t.wait_ge(sdg.h, sdg0 + 16)
        for i in range(NT):
            w, j = divmod(i, 4)
            b = i % 2
            t.wait_ge(sd.h, sd0 + 16 * (w + 1))
            import os as _os
            if _os.environ.get("R1_SERIAL"):
                if i >= 1:
                    t.wait_ge(sa.h, sa0 + i)
            elif i >= 2:
                t.wait_ge(sv.h, sv0 + 8 * (i - 2) + 1)
            ins = None
            for c in range(NF):
                ins = t.matmul(ps_lg[:, b * 512:b * 512 + E],
                               tfm[:, c * 512 + j * 128: c * 512 + (j + 1) * 128],
                               wg_sb[:, c * E:(c + 1) * E],
                               start=(c == 0), stop=(c == NF - 1))
            st.inc(ins, 1)
        ph.done(_last_inc_t(t, k))

    @blk.vector
    def _(v):
        ph.begin(v)
        for i in range(NT):
            b = i % 2
            v.wait_ge(st.h, st0 + i + 1)
            v.tensor_copy(lgt[:, b * E:(b + 1) * E], ps_lg[:, b * 512:b * 512 + E]).then_inc(sv.h, 1)
            v.wait_ge(sv.h, sv0 + 8 * i + 1)
            v.max(max8[:, :], lgt[:, b * E:(b + 1) * E]).then_inc(sv.h, 1)
            v.wait_ge(sv.h, sv0 + 8 * i + 2)
            v.max_index(idxu[:, :], max8[:, :], lgt[:, b * E:(b + 1) * E]).then_inc(sv.h, 1)
            v.wait_ge(sv.h, sv0 + 8 * i + 3)
            v.tensor_copy(cols[:, 0:1], idxu[:, 0:1]).then_inc(sv.h, 1)
            v.tensor_scalar_mul(cols[:, 1:2], max8[:, 0:1], -1.0).then_inc(sv.h, 1)
            v.wait_ge(sv.h, sv0 + 8 * i + 5)
            v.tensor_single_scalar(oh_all[:, i * E:(i + 1) * E], k.egrid[:, :],
                                   cols[:, 0:1], ALU.is_equal).then_inc(sv.h, 1)
            v.wait_ge(sa.h, sa0 + i + 1)
            v.reciprocal(cols[:, 3:4], cols[:, 2:3]).then_inc(sv.h, 1)
            if dbg:
                pass
            v.wait_ge(sv.h, sv0 + 8 * i + 7)
            v.tensor_scalar_mul(cwh[:, i:i + 1], cols[:, 3:4], 0.5).then_inc(sv.h, 1)
        sv.n = sv0 + 8 * NT
        ph.done(_last_inc(v, k))

    @blk.scalar
    def _(s):
        ph.begin(s)
        for i in range(NT):
            b = i % 2
            s.wait_ge(sv.h, sv0 + 8 * i + 5)
            s.activation(expv[:, :], lgt[:, b * E:(b + 1) * E], AF.Exp,
                         bias=cols[:, 1:2], accum_out=cols[:, 2:3]).then_inc(sa.h, 1)
        sa.n = sa0 + NT
        ph.done(_last_inc_a(s, k))

    if dbg:
        sdbg = k.sem("r_dbg")

        @blk.gpsimd
        def _(g):
            ph.begin(g)
            for i in range(NT):
                b = i % 2
                g.wait_ge(sa.h, sa0 + i + 1)
                sdbg.inc(g.dma_start(out=lg_dump[i * 128:(i + 1) * 128, :],
                                     in_=lgt[:, b * E:(b + 1) * E]), 16)
            g.wait_ge(sv.h, sv0 + 8 * NT)
            sdbg.inc(g.dma_start(out=oh_dump[:, :], in_=oh_all[:, :]), 16)
            g.wait_ge(sdbg.h, sdbg.n)
            g.sem_inc(k.bar.h, 1)
        k.bar.n += 1

    blk.__exit__(None, None, None)

    ph2 = Phase(k)
    st0, sv0 = st.n, sv.n
    blk = nc.Block()
    blk.__enter__()

    @blk.tensor
    def _(t):
        ph2.begin(t)
        for i in range(NT):
            if i >= 1:
                t.wait_ge(sv.h, sv0 + i)
            st.inc(t.transpose(ps_s[:, :], oh_all[:, i * E:(i + 1) * E],
                               k.identf[:, :]), 1)
        ph2.done(_last_inc_t(t, k))

    @blk.vector
    def _(v):
        ph2.begin(v)
        for i in range(NT):
            v.wait_ge(st.h, st0 + i + 1)
            v.tensor_copy(ohT[:, i * 128:(i + 1) * 128], ps_s[:, :]).then_inc(sv.h, 1)
        v.memset(zer8[:, :], 0.0).then_inc(sv.h, 1)
        v.wait_ge(sv.h, sv0 + NT + 1)
        v.tensor_tensor_scan(scn[:, :], ohT[:, :], zer8[:, :], 0.0,
                             ALU.add, ALU.add).then_inc(sv.h, 1)
        sv.n = sv0 + NT + 2
        ph2.done(_last_inc(v, k))

    if dbg:
        sdbg2 = k.sem("r_dbg2")

        @blk.gpsimd
        def _(g):
            ph2.begin(g)
            g.wait_ge(sv.h, sv.n)
            sdbg2.inc(g.dma_start(out=ohT_dump[:, :], in_=ohT[:, :]), 16)
            g.wait_ge(sdbg2.h, sdbg2.n)
            g.sem_inc(k.bar.h, 1)
        k.bar.n += 1
    blk.__exit__(None, None, None)

    ph3 = Phase(k)
    st0, sv0, sg0, scc0 = st.n, sv.n, sg.n, scc.n
    blk = nc.Block()
    blk.__enter__()

    @blk.tensor
    def _(t):
        ph3.begin(t)
        st.inc(t.transpose(ps_t[0:1, 0:E], scn[:, T - 1:T], k.identf[0:E, 0:E]), 1)

    @blk.vector
    def _(v):
        ph3.begin(v)
        v.wait_ge(st.h, st0 + 1)
        v.tensor_copy(row1[:, 0:E], ps_t[0:1, 0:E]).then_inc(sv.h, 1)
    sv.n += 1

    @blk.gpsimd
    def _(g):
        ph3.begin(g)
        g.wait_ge(sv.h, sv0 + 1)
        sg.inc(g.dma_start(out=cc_in[:, :], in_=row1[:, 0:E]), 16)
        g.wait_ge(sg.h, sg0 + 16)
        scc.inc(g.collective_compute(
            "AllGather", ALU.bypass, replica_groups=[list(range(8))],
            ins=[cc_in.ap().opt()], outs=[cc_out.ap().opt()]), 1)
        g.wait_ge(scc.h, scc0 + 1)
        sg.inc(g.dma_start(out=cnts[:, :], in_=cc_out[:, :]), 16)
        sg.inc(g.dma_start(out=pidu[:, 0:1], in_=k.nc.partition_id_tensor[:, :]), 16)

    gp_done = sg.n

    @blk.vector
    def _(v):
        v.wait_ge(sg.h, gp_done)
        v.tensor_copy(row1[:, 8:9], pidu[:, 0:1]).then_inc(sv.h, 1)
        sv.n += 1
        ph3.done(_last_inc(v, k))
    blk.__exit__(None, None, None)

    ph4 = Phase(k)
    st0, sv0 = st.n, sv.n
    blk = nc.Block()
    blk.__enter__()

    @blk.tensor
    def _(t):
        ph4.begin(t)
        st.inc(t.matmul(ps_s[:, 0:1], k.ones_rowf[0:1, 0:E], row1[:, 8:9],
                        start=True, stop=True), 1)
        t.wait_ge(sv.h, sv0 + 2)
        st.inc(t.matmul(ps_s[:, 4:5], cnts[:, :], k.ones_colf8[:, 0:1],
                        start=True, stop=True), 1)
        ph4.done(_last_inc_t(t, k))

    @blk.vector
    def _(v):
        ph4.begin(v)
        v.wait_ge(st.h, st0 + 1)
        v.tensor_tensor(sm8[:, 2:3], k.piota8[:, 0:1], ps_s[:, 0:1], ALU.is_lt).then_inc(sv.h, 1)
        v.wait_ge(sv.h, sv0 + 1)
        v.tensor_scalar_mul(cnts[:, :], cnts[:, :], sm8[:, 2:3]).then_inc(sv.h, 1)
        v.wait_ge(st.h, st0 + 2)
        v.tensor_scalar_add(sm8[:, 3:4], ps_s[:, 4:5], -1.0).then_inc(sv.h, 1)
        v.wait_ge(sv.h, sv0 + 3)
        v.tensor_scalar_add(scn[:, :], scn[:, :], sm8[:, 3:4]).then_inc(sv.h, 1)
        v.wait_ge(sv.h, sv0 + 4)
        v.tensor_single_scalar(scn[:, :], scn[:, :], float(CAPACITY), ALU.is_lt).then_inc(sv.h, 1)
        v.wait_ge(sv.h, sv0 + 5)
        v.tensor_mul(ohT[:, :], ohT[:, :], scn[:, :]).then_inc(sv.h, 1)
        sv.n = sv0 + 6
        ph4.done(_last_inc(v, k))

    if dbg:
        sdbg4 = k.sem("r_dbg4")

        @blk.gpsimd
        def _(g):
            ph4.begin(g)
            g.wait_ge(sv.h, sv.n)
            sdbg4.inc(g.dma_start(out=oh4_dump[:, :], in_=ohT[:, :]), 16)
            sdbg4.inc(g.dma_start(out=scn_dump[:, :], in_=scn[:, :]), 16)
            sdbg4.inc(g.dma_start(out=cw_dump[:, :], in_=cwh[:, :]), 16)
            g.wait_ge(sdbg4.h, sdbg4.n)
            g.sem_inc(k.bar.h, 1)
        k.bar.n += 1
    blk.__exit__(None, None, None)

    ph5 = Phase(k)
    st0, sv0 = st.n, sv.n
    blk = nc.Block()
    blk.__enter__()

    @blk.tensor
    def _(t):
        ph5.begin(t)
        for i in range(NT):
            if i >= 1:
                t.wait_ge(sv.h, sv0 + i)
            st.inc(t.transpose(ps_t[:, 0:E], ohT[:, i * 128:(i + 1) * 128],
                               k.identf[0:E, 0:E]), 1)
        ph5.done(_last_inc_t(t, k))

    @blk.vector
    def _(v):
        ph5.begin(v)
        for i in range(NT):
            v.wait_ge(st.h, st0 + i + 1)
            v.tensor_scalar_mul(gates[:, i * E:(i + 1) * E], ps_t[:, 0:E],
                                cwh[:, i:i + 1]).then_inc(sv.h, 1)
        sv.n = sv0 + NT
        ph5.done(_last_inc(v, k))
    blk.__exit__(None, None, None)
    loc.close()


def _last_inc_t(t, k):
    return t.matmul(k.ps_anchor[:, 0:1], k.ones_rowf[0:1, 0:1],
                    k.ones_rowf[0:1, 0:1], start=True, stop=True)


def _last_inc_a(s, k):
    return s.memzero(k.consts[:, 6:7])


def emit_copy(k, dst, src, rows, cols, tag, nsplit=4):
    ph = Phase(k)
    sg = k.sem("cp_g")
    blk = k.nc.Block()
    blk.__enter__()

    @blk.gpsimd
    def _(g):
        ph.begin(g)
        step = rows // nsplit
        for i in range(nsplit):
            sg.inc(g.dma_start(out=dst[i * step:(i + 1) * step, :],
                               in_=src[i * step:(i + 1) * step, :]), 16)
        ph.done_q(g, sg)
    blk.__exit__(None, None, None)


def emit_ffn(k, t_fm, w1_bf, w2_bf, b1_f32, b2_f32, gates, xres, tag):
    nc = k.nc
    NHC = H_FFN // 128

    sd = k.sem("f_d")
    sll = [k.sem("f_l%d" % j) for j in range(2)]
    st = k.sem("f_t")
    sv = k.sem("f_v")
    sa = k.sem("f_a")
    sg = k.sem("f_g")
    sgf = k.sem("f_gf")

    loc0 = Loc(k)
    b1col_all = loc0.sb("f_b1c" + tag, [128, NHC * E], F32)
    b2sb = loc0.sb("f_b2s" + tag, [E, D], F32)
    php = Phase(k)
    locp = Loc(k)
    b1_tmp = locp.sb("f_b1t" + tag, [E, H_FFN], F32)
    k.ps_anchor = loc0.sb if False else locp.ps("f_anc" + tag, [1, 8], F32)
    ps_p = locp.ps("f_pp" + tag, [128, 8], F32)
    sd0, st0, sv0 = sd.n, st.n, sv.n
    blk = nc.Block()
    blk.__enter__()

    @blk.sync
    def _(sync):
        php.begin(sync)
        sd.inc(sync.dma_start(out=b1_tmp[:, :], in_=b1_f32[:, :]), 16)
        sd.inc(sync.dma_start(out=b2sb[:, :], in_=b2_f32[:, :]), 16)
        php.done_q(sync, sd)

    @blk.tensor
    def _(t):
        php.begin(t)
        t.wait_ge(sd.h, sd0 + 32)
        for c in range(NHC):
            if c >= 1:
                t.wait_ge(sv.h, sv0 + c)
            st.inc(t.transpose(ps_p[:, 0:E], b1_tmp[:, c * 128:(c + 1) * 128],
                               k.identf[0:E, 0:E]), 1)
        php.done(_last_inc_t(t, k))

    @blk.vector
    def _(v):
        php.begin(v)
        for c in range(NHC):
            v.wait_ge(st.h, st0 + c + 1)
            v.tensor_copy(b1col_all[:, c * E:(c + 1) * E], ps_p[:, 0:E]).then_inc(sv.h, 1)
        sv.n = sv0 + NHC
        php.done(_last_inc(v, k))
    blk.__exit__(None, None, None)
    locp.close()

    loc = Loc(k)
    w1sb = loc.sb("f_w1" + tag, [128, NF * H_FFN], BF16)
    w2sb = loc.sb("f_w2" + tag, [128, NHC * D], BF16)
    hsb = loc.sb("f_h" + tag, [128, NHC * 512], BF16)
    t4 = loc.sb("f_t4" + tag, [128, 2 * NF * 512], BF16)
    b2row = loc.sb("f_b2r" + tag, [1, D], F32)
    b2bc = loc.sb("f_b2b" + tag, [128, D], F32)
    yev = loc.sb("f_y" + tag, [128, 2 * 512], F32)
    k.ps_anchor = loc.ps("f_anc2" + tag, [1, 8], F32)
    ps_h = loc.ps("f_ph" + tag, [128, 2 * 512], F32)
    ps_y = loc.ps("f_py" + tag, [128, 2 * 512], F32)
    ps_b = loc.ps("f_pb" + tag, [128, D], F32)

    ph = Phase(k)
    sd0, st0, sv0, sa0, sg0 = sd.n, st.n, sv.n, sa.n, sg.n
    sll0 = [x.n for x in sll]
    sgf0 = sgf.n

    def st_mm(e, tw):
        return st0 + e * (2 + 8 * 40) + 2 + tw * 40

    def sa_h(e, tw):
        return sa0 + (e * 8 + tw) * 32

    def sv_y(e, tw):
        return sv0 + e * 129 + 1 + tw * 16

    blk = nc.Block()
    blk.__enter__()

    @blk.sync
    def _(sync):
        ph.begin(sync)
        for e in range(E):
            if e >= 1:
                sync.wait_ge(st.h, st_mm(e - 1, 8))
            sd.inc(sync.dma_start(
                out=w1sb[:, :].rearrange("p (c h) -> p c h", c=NF),
                in_=w1_bf[e, :, :].rearrange("(c p) h -> p c h", p=128)), 16)
            sd.inc(sync.dma_start(
                out=w2sb[:, :].rearrange("p (c d) -> p c d", c=NHC),
                in_=w2_bf[e, :, :].rearrange("(c p) d -> p c d", p=128)), 16)
        ph.done_q(sync, sd)

    @blk.gpsimd
    def _(g):
        ph.begin(g)
        for e in range(E):
            if e >= 1:
                g.wait_ge(st.h, st_mm(e - 1, 0))
            sg.inc(g.dma_start(out=b2row[:, :], in_=b2sb[e:e + 1, :]), 16)
            for tw in range(8):
                for yi in range(8):
                    tc, dh = divmod(yi, 2)
                    yb = yi % 2
                    g.wait_ge(sg.h, sg.n)
                    g.sem_inc(sgf.h, 1)
                    sgf.n += 1
                    g.wait_ge(sv.h, sv_y(e, tw) + yi * 2 + 2)
                    row0 = (tw * 4 + tc) * 128
                    sg.inc(g.dma_start(
                        out=xres[row0:row0 + 128, dh * 512:(dh + 1) * 512],
                        in_=yev[:, yb * 512:(yb + 1) * 512],
                        accum_op=ALU.add), 16)
        ph.done_q(g, sg)

    @blk.scalar
    def _(s):
        ph.begin(s)
        for e in range(E):
            for tw in range(8):
                b = tw % 2
                if e * 8 + tw >= 2:
                    ew, tww = divmod(e * 8 + tw - 2, 8)
                    s.wait_ge(st.h, st_mm(ew, tww) + 32)
                sll[b].inc(s.dma_start(
                    out=t4[:, b * NF * 512:(b + 1) * NF * 512]
                        .rearrange("p (c t) -> p c t", c=NF),
                    in_=t_fm[:, tw * 512:(tw + 1) * 512]
                        .rearrange("(c p) t -> p c t", p=128)), 16)
                for hc in range(NHC):
                    s.wait_ge(st.h, st_mm(e, tw) + hc + 1)
                    s.activation(hsb[:, hc * 512:(hc + 1) * 512],
                                 ps_h[:, (hc % 2) * 512:(hc % 2 + 1) * 512],
                                 AF.Gelu, bias=b1col_all[:, hc * E + e:hc * E + e + 1],
                                 scale=1.0).then_inc(sa.h, 1)
        sa.n = sa0 + E * 8 * NHC
        ph.done(_last_inc_a(s, k))

    @blk.tensor
    def _(t):
        ph.begin(t)
        for e in range(E):
            t.wait_ge(sg.h, sg0 + 16 * (e * 65 + 1))
            if e >= 1:
                t.wait_ge(sv.h, sv_y(e - 1, 0))
            t.wait_ge(sd.h, sd0 + 32 * (e + 1))
            st.inc(t.matmul(ps_b[:, 0:512], k.ones_rowf[0:1, 0:128],
                            b2row[:, 0:512], start=True, stop=True), 1)
            st.inc(t.matmul(ps_b[:, 512:1024], k.ones_rowf[0:1, 0:128],
                            b2row[:, 512:1024], start=True, stop=True), 1)
            for tw in range(8):
                b = tw % 2
                idx = e * 8 + tw
                t.wait_ge(sll[b].h, sll0[b] + 16 * (idx // 2 + 1))
                for hc in range(NHC):
                    hb = hc % 2
                    if hc >= 2:
                        t.wait_ge(sa.h, sa_h(e, tw) + hc - 1)
                    elif e * 8 + tw > 0:
                        pidx = e * 8 + tw - 1
                        t.wait_ge(sa.h, sa0 + pidx * 32 + 31 + hc)
                    ins = None
                    for c in range(NF):
                        ins = t.matmul(
                            ps_h[:, hb * 512:(hb + 1) * 512],
                            w1sb[:, c * H_FFN + hc * 128: c * H_FFN + (hc + 1) * 128],
                            t4[:, b * NF * 512 + c * 512: b * NF * 512 + (c + 1) * 512],
                            start=(c == 0), stop=(c == NF - 1))
                    st.inc(ins, 1)
                t.wait_ge(sa.h, sa_h(e, tw) + 32)
                for tc in range(4):
                    for dh in range(2):
                        yi = tc * 2 + dh
                        yb = yi % 2
                        gl = e * 64 + tw * 8 + yi - 2
                        if gl >= 0:
                            ge_, rem = divmod(gl, 64)
                            tw_, yi_ = divmod(rem, 8)
                            t.wait_ge(sv.h, sv_y(ge_, tw_) + yi_ * 2 + 1)
                        ins = None
                        for hc in range(NHC):
                            ins = t.matmul(
                                ps_y[:, yb * 512:(yb + 1) * 512],
                                hsb[:, hc * 512 + tc * 128: hc * 512 + tc * 128 + 128],
                                w2sb[:, hc * D + dh * 512: hc * D + (dh + 1) * 512],
                                start=(hc == 0), stop=(hc == NHC - 1))
                        st.inc(ins, 1)
        ph.done(_last_inc_t(t, k))

    @blk.vector
    def _(v):
        ph.begin(v)
        for e in range(E):
            v.wait_ge(st.h, st_mm(e, 0))
            v.tensor_copy(b2bc[:, :], ps_b[:, 0:D]).then_inc(sv.h, 1)
            for tw in range(8):
                for tc in range(4):
                    gcol = gates[:, (tw * 4 + tc) * E + e: (tw * 4 + tc) * E + e + 1]
                    for dh in range(2):
                        yi = tc * 2 + dh
                        yb = yi % 2
                        v.wait_ge(st.h, st_mm(e, tw) + 32 + yi + 1)
                        gl = e * 64 + tw * 8 + yi - 2
                        if gl >= 0:
                            v.wait_ge(sgf.h, sgf0 + gl + 2)
                        v.tensor_scalar_mul(yev[:, yb * 512:(yb + 1) * 512],
                                            ps_y[:, yb * 512:(yb + 1) * 512],
                                            gcol).then_inc(sv.h, 1)
                        v.wait_ge(sv.h, sv_y(e, tw) + yi * 2 + 1)
                        v.scalar_tensor_tensor(
                            yev[:, yb * 512:(yb + 1) * 512],
                            b2bc[:, dh * 512:(dh + 1) * 512], gcol,
                            yev[:, yb * 512:(yb + 1) * 512],
                            ALU.mult, ALU.add).then_inc(sv.h, 1)
        sv.n = sv0 + E * 129
        ph.done(_last_inc(v, k))

    blk.__exit__(None, None, None)
    loc.close()
    loc0.close()


def emit_listbuild(k, gates, ig_gate, ig_tok32, t_tm, tag, capc, ncho):
    nc = k.nc
    loc = Loc(k)
    m8 = loc.sb("lb_m8" + tag, [128, 8], F32)
    iu = loc.sb("lb_iu" + tag, [128, 8], U32)
    ec = loc.sb("lb_ec" + tag, [128, NT * ncho], F32)
    gv = loc.sb("lb_gv" + tag, [128, NT * ncho], F32)
    nz = loc.sb("lb_nz" + tag, [128, NT * E], F32)
    ohs = loc.sb("lb_oh" + tag, [128, 8], F32)
    prod = loc.sb("lb_pr" + tag, [128, 8], F32)
    rk = loc.sb("lb_rk" + tag, [128, 2], F32)
    slotf = loc.sb("lb_sf" + tag, [128, 2], F32)
    bnd = loc.sb("lb_bn" + tag, [128, 2], F32)
    rtm = loc.sb("lb_rt" + tag, [128, 8], F32)
    slotu = loc.sb("lb_su" + tag, [128, NT * ncho], U32)
    nzT = loc.sb("lb_nT" + tag, [E, T], F32)
    scnT = loc.sb("lb_sT" + tag, [E, T], F32)
    zer8 = loc.sb("lb_z8" + tag, [E, T], F32)
    zbf = loc.sb("lb_zb" + tag, [128, D], BF16)
    zgat = loc.sb("lb_zg" + tag, [128, capc * E // 128], F32)
    pf32 = loc.sb("lb_pf" + tag, [128, capc * E // 128], U32)
    k.ps_anchor = loc.ps("lb_anc" + tag, [1, 8], F32)
    ps_a = loc.ps("lb_pa" + tag, [E, 128], F32)
    ps_b = loc.ps("lb_pb" + tag, [128, 8], F32)

    sv = k.sem("q_v")
    st = k.sem("q_t")
    NVL = 7 if ncho == 2 else 5

    ph = Phase(k)
    sv0, st0 = sv.n, st.n
    svB0 = sv0 + NT * NVL
    blk = nc.Block()
    blk.__enter__()

    @blk.vector
    def _(v):
        ph.begin(v)
        for i in range(NT):
            base = sv0 + i * NVL
            gt = gates[:, i * E:(i + 1) * E]
            v.max(m8[:, :], gt).then_inc(sv.h, 1)
            v.wait_ge(sv.h, base + 1)
            v.max_index(iu[:, :], m8[:, :], gt).then_inc(sv.h, 1)
            v.wait_ge(sv.h, base + 2)
            v.tensor_copy(ec[:, i * ncho:(i + 1) * ncho],
                          iu[:, 0:ncho]).then_inc(sv.h, 1)
            v.tensor_copy(gv[:, i * ncho:(i + 1) * ncho],
                          m8[:, 0:ncho]).then_inc(sv.h, 1)
            v.wait_ge(sv.h, base + 3)
            v.tensor_single_scalar(nz[:, i * E:(i + 1) * E], k.egrid[:, :],
                                   ec[:, i * ncho:i * ncho + 1],
                                   ALU.is_equal).then_inc(sv.h, 1)
            if ncho == 2:
                v.tensor_single_scalar(ohs[:, :], k.egrid[:, :],
                                       ec[:, i * ncho + 1:i * ncho + 2],
                                       ALU.is_equal).then_inc(sv.h, 1)
                v.wait_ge(sv.h, base + 6)
                v.tensor_tensor(nz[:, i * E:(i + 1) * E],
                                nz[:, i * E:(i + 1) * E], ohs[:, :],
                                ALU.add).then_inc(sv.h, 1)
        sv.n = sv0 + NT * NVL
        for i in range(NT):
            v.wait_ge(st.h, st0 + i + 1)
            v.tensor_copy(nzT[:, i * 128:(i + 1) * 128],
                          ps_a[:, :]).then_inc(sv.h, 1)
        v.memset(zer8[:, :], 0.0).then_inc(sv.h, 1)
        sv.n = svB0 + NT + 1
        v.wait_ge(sv.h, sv.n)
        v.tensor_tensor_scan(scnT[:, :], nzT[:, :], zer8[:, :], 0.0,
                             ALU.add, ALU.add).then_inc(sv.h, 1)
        sv.n += 1
        ph.done(_last_inc(v, k))

    @blk.tensor
    def _(t):
        ph.begin(t)
        for i in range(NT):
            t.wait_ge(sv.h, sv0 + i * NVL + NVL)
            if i >= 1:
                t.wait_ge(sv.h, svB0 + i)
            st.inc(t.transpose(ps_a[:, :], nz[:, i * E:(i + 1) * E],
                               k.identf[:, :]), 1)
        ph.done(_last_inc_t(t, k))
    blk.__exit__(None, None, None)

    ph2 = Phase(k)
    sv0, st0 = sv.n, st.n
    sg = k.sem("r_g")
    sgi = k.sem("r_dg")
    sg0, sgi0 = sg.n, sgi.n
    MAGIC = 12582912.0
    CW128 = capc // 128
    NV2 = 1 + 11 * ncho
    base_i = lambda i: sv0 + 2 + i * NV2
    blk = nc.Block()
    blk.__enter__()

    @blk.vector
    def _(v):
        ph2.begin(v)
        v.memset(zbf[:, :], 0.0).then_inc(sv.h, 1)
        v.memset(zgat[:, :], 0.0).then_inc(sv.h, 1)
        for i in range(NT):
            b = base_i(i)
            v.wait_ge(st.h, st0 + i + 1)
            v.tensor_copy(rtm[:, :], ps_b[:, :]).then_inc(sv.h, 1)
            for kk in range(ncho):
                c0 = i * ncho + kk
                bb = b + 1 + 11 * kk
                ecc = ec[:, c0:c0 + 1]
                jj = slotf[:, 0:1]
                jdiv = slotf[:, 1:2]
                jmod = bnd[:, 0:1]
                t2 = bnd[:, 1:2]
                sf = rk[:, 1:2]
                v.wait_ge(sv.h, bb)
                v.tensor_single_scalar(ohs[:, :], k.egrid[:, :], ecc,
                                       ALU.is_equal).then_inc(sv.h, 1)
                v.wait_ge(sv.h, bb + 1)
                v.tensor_tensor(prod[:, :], rtm[:, :], ohs[:, :],
                                ALU.mult).then_inc(sv.h, 1)
                v.wait_ge(sv.h, bb + 2)
                v.max(m8[:, :], prod[:, :]).then_inc(sv.h, 1)
                v.wait_ge(sv.h, bb + 3)
                v.tensor_scalar_add(jj, m8[:, 0:1], -1.0).then_inc(sv.h, 1)
                v.wait_ge(sv.h, bb + 4)
                v.tensor_scalar_min(jj, jj, float(capc - 1)).then_inc(sv.h, 1)
                v.wait_ge(sv.h, bb + 5)
                v.tensor_scalar(jdiv, jj, 1.0 / 128, -63.5 / 128, ALU.mult,
                                ALU.add).then_inc(sv.h, 1)
                v.wait_ge(sv.h, bb + 6)
                v.tensor_scalar(jdiv, jdiv, MAGIC, -MAGIC, ALU.add,
                                ALU.add).then_inc(sv.h, 1)
                v.wait_ge(sv.h, bb + 7)
                v.scalar_tensor_tensor(jmod, jdiv, -128.0, jj,
                                       ALU.mult, ALU.add).then_inc(sv.h, 1)
                v.wait_ge(sv.h, bb + 8)
                v.scalar_tensor_tensor(t2, ecc, float(CW128), jdiv,
                                       ALU.mult, ALU.add).then_inc(sv.h, 1)
                v.wait_ge(sv.h, bb + 9)
                v.scalar_tensor_tensor(sf, jmod, float(E * CW128), t2,
                                       ALU.mult, ALU.add).then_inc(sv.h, 1)
                v.wait_ge(sv.h, bb + 10)
                v.tensor_copy(slotu[:, c0:c0 + 1], sf).then_inc(sv.h, 1)
                v.wait_ge(sv.h, bb + 11)
        sv.n = sv0 + 2 + NT * NV2
        ph2.done(_last_inc(v, k))

    @blk.tensor
    def _(t):
        ph2.begin(t)
        for i in range(NT):
            if i >= 1:
                t.wait_ge(sv.h, base_i(i - 1) + 1)
            st.inc(t.transpose(ps_b[:, 0:E], scnT[:, i * 128:(i + 1) * 128],
                               k.identf[0:E, 0:E]), 1)
        ph2.done(_last_inc_t(t, k))

    @blk.gpsimd
    def _(g):
        ph2.begin(g)
        sgc = k.sem("r_gc")
        sgc.inc(g.iota(pf32[:, :], [[0, capc * E // 128]], base=T,
                       channel_multiplier=1,
                       allow_small_or_imprecise_dtypes=True), 1)
        g.wait_ge(sgc.h, sgc.n)
        g.wait_ge(sv.h, sv0 + 2)
        sg.inc(g.dma_start(out=t_tm[T:T + 128, :], in_=zbf[:, :]), 16)
        sg.inc(g.dma_start(
            out=ig_gate.ap().rearrange("(p c) -> p c", p=128),
            in_=zgat[:, :]), 16)
        sg.inc(g.dma_start(
            out=ig_tok32.ap().rearrange("(p c) -> p c", p=128),
            in_=pf32[:, :]), 16)
        g.wait_ge(sg.h, sg0 + 48)
        for i in range(NT):
            for kk in range(ncho):
                c0 = i * ncho + kk
                g.wait_ge(sv.h, base_i(i) + 1 + 11 * kk + 11)
                sgi.inc(g.indirect_dma_start(
                    out=ig_gate.ap().rearrange("(a b) -> a b", b=1),
                    out_offset=bass.IndirectOffsetOnAxis(
                        slotu[:, c0:c0 + 1], 0),
                    in_=gv[:, c0:c0 + 1], in_offset=None), 16)
                sgi.inc(g.indirect_dma_start(
                    out=ig_tok32.ap().rearrange("(a b) -> a b", b=1),
                    out_offset=bass.IndirectOffsetOnAxis(
                        slotu[:, c0:c0 + 1], 0),
                    in_=k.tokid32[:, i:i + 1], in_offset=None), 16)
        ph2.done_q(g, sgi)
    blk.__exit__(None, None, None)
    loc.close()


def emit_ffn_routed(k, t_tm, w1_bf, w2_bf, b1_f32, b2_f32, ig_gate,
                    ig_tok32, xres, tag, capc):
    nc = k.nc
    NHC = H_FFN // 128
    NW = capc // 512
    CW128 = capc // 128

    sd = k.sem("f_d")
    sq = k.sem("ln_d0")
    st = k.sem("f_t")
    sv = k.sem("f_v")
    sa = k.sem("f_a")
    sg = k.sem("f_g")
    sgts = [k.sem("ln_d%d" % (s + 1)) for s in range(3)]
    ssc = k.sem("ln_o")

    loc0 = Loc(k)
    b1col_all = loc0.sb("f_b1c" + tag, [128, NHC * E], F32)
    php = Phase(k)
    locp = Loc(k)
    b1_tmp = locp.sb("f_b1t" + tag, [E, H_FFN], F32)
    k.ps_anchor = locp.ps("f_anc" + tag, [1, 8], F32)
    ps_p = locp.ps("f_pp" + tag, [128, 8], F32)
    sd0, st0, sv0 = sd.n, st.n, sv.n
    blk = nc.Block()
    blk.__enter__()

    @blk.sync
    def _(sync):
        php.begin(sync)
        sd.inc(sync.dma_start(out=b1_tmp[:, :], in_=b1_f32[:, :]), 16)
        php.done_q(sync, sd)

    @blk.tensor
    def _(t):
        php.begin(t)
        t.wait_ge(sd.h, sd0 + 16)
        for c in range(NHC):
            if c >= 1:
                t.wait_ge(sv.h, sv0 + c)
            st.inc(t.transpose(ps_p[:, 0:E], b1_tmp[:, c * 128:(c + 1) * 128],
                               k.identf[0:E, 0:E]), 1)
        php.done(_last_inc_t(t, k))

    @blk.vector
    def _(v):
        php.begin(v)
        for c in range(NHC):
            v.wait_ge(st.h, st0 + c + 1)
            v.tensor_copy(b1col_all[:, c * E:(c + 1) * E], ps_p[:, 0:E]).then_inc(sv.h, 1)
        sv.n = sv0 + NHC
        php.done(_last_inc(v, k))
    blk.__exit__(None, None, None)
    locp.close()

    loc = Loc(k)
    CPE = capc // 128
    w1sb = loc.sb("f_w1" + tag, [128, NF * H_FFN], BF16)
    w2sb = loc.sb("f_w2" + tag, [128, NHC * D], BF16)
    hsb = loc.sb("f_h" + tag, [128, NHC * 512], BF16)
    t4g = loc.sb("fr_t4" + tag, [128, NF * capc], BF16)
    xg = loc.sb("fr_xg" + tag, [128, 3 * D], BF16)
    gvw = loc.sb("fr_gv" + tag, [128, E * CW128], F32)
    itok = loc.sb("fr_it" + tag, [128, E * CW128], U32)
    b2row = loc.sb("f_b2r" + tag, [1, D], BF16)
    b2bc = loc.sb("f_b2b" + tag, [128, D], F32)
    yg = loc.sb("fr_y" + tag, [128, D], F32)
    k.ps_anchor = loc.ps("f_anc2" + tag, [1, 8], F32)
    ps_h = loc.ps("f_ph" + tag, [128, 2 * 512], F32)
    ps_y = loc.ps("f_py" + tag, [128, 2 * 512], F32)
    ps_b = loc.ps("f_pb" + tag, [128, 512], F32)
    ps_tr = loc.ps("f_ptr" + tag, [128, 2 * D], BF16)

    ph = Phase(k)
    sd0, sq0, st0, sv0, sa0, sg0 = sd.n, sq.n, st.n, sv.n, sa.n, sg.n
    sgts0 = [s.n for s in sgts]
    ssc0 = ssc.n
    stt = k.sem("r_t")
    stv = k.sem("r_v")
    stt0, stv0 = stt.n, stv.n
    SVE = 2 + 16 * NW

    def st_b2(e):
        return st0 + e * (2 + 40 * NW) + 2

    def st_mm(e, w):
        return st_b2(e) + w * 40

    def sa_h(e, w):
        return sa0 + (e * NW + w) * 32

    def sv_yy(e, w):
        return sv0 + e * SVE + 2 + w * 16

    def emit_combine(g, e, w, tc):
        g.wait_ge(sv.h, sv_yy(e, w) + 4 * (tc + 1))
        ssc.inc(g.indirect_dma_start(
            out=xres[:, :],
            out_offset=bass.IndirectOffsetOnAxis(
                itok[:, e * CW128 + w * 4 + tc: e * CW128 + w * 4 + tc + 1], 0),
            in_=yg[:, :], in_offset=None, compute_op=ALU.add), 16)

    blk = nc.Block()
    blk.__enter__()

    @blk.sync
    def _(sync):
        ph.begin(sync)
        sq.inc(sync.dma_start(
            out=gvw[:, :],
            in_=ig_gate.ap().rearrange("(p c) -> p c", p=128)), 16)
        sq.inc(sync.dma_start(
            out=itok[:, :],
            in_=ig_tok32.ap().rearrange("(p c) -> p c", p=128)), 16)
        for e in range(E):
            if e >= 1:
                sync.wait_ge(st.h, st_mm(e - 1, NW))
            sd.inc(sync.dma_start(
                out=w1sb[:, :].rearrange("p (c h) -> p c h", c=NF),
                in_=w1_bf[e, :, :].rearrange("(c p) h -> p c h", p=128)), 16)
            sd.inc(sync.dma_start(
                out=w2sb[:, :].rearrange("p (c d) -> p c d", c=NHC),
                in_=w2_bf[e, :, :].rearrange("(c p) d -> p c d", p=128)), 16)
        sync.wait_ge(sq.h, sq.n)
        ph.done_q(sync, sd)

    @blk.gpsimd
    def _(g):
        ph.begin(g)
        g.wait_ge(sq.h, sq0 + 16 * 2)
        for e in range(E):
            if e >= 1:
                g.wait_ge(st.h, st_b2(e - 1))
            sg.inc(g.dma_start(out=b2row[:, :], in_=b2_f32[e:e + 1, :]), 16)
            if e >= 1:
                for w in range(NW - 1):
                    for tc in range(4):
                        emit_combine(g, e - 1, w, tc)

            def gath(j):
                ci = e * CPE + j
                if ci >= 3:
                    g.wait_ge(stt.h, stt0 + ci - 2)
                sgts[ci % 3].inc(g.indirect_dma_start(
                    out=xg[:, (ci % 3) * D:(ci % 3 + 1) * D],
                    out_offset=None,
                    in_=t_tm[:, :],
                    in_offset=bass.IndirectOffsetOnAxis(
                        itok[:, e * CW128 + j: e * CW128 + j + 1], 0)), 16)

            for j in range(3):
                gath(j)
            if e >= 1:
                for tc in range(4):
                    emit_combine(g, e - 1, NW - 1, tc)
            for j in range(3, CPE):
                gath(j)
        for w in range(NW):
            for tc in range(4):
                emit_combine(g, E - 1, w, tc)
        for s in sgts:
            g.wait_ge(s.h, s.n)
        g.wait_ge(ssc.h, ssc.n)
        g.sem_inc(k.bar.h, 1)
    k.bar.n += 1

    @blk.scalar
    def _(s):
        ph.begin(s)
        for e in range(E):
            for w in range(NW):
                for hc in range(NHC):
                    s.wait_ge(st.h, st_mm(e, w) + hc + 1)
                    s.activation(hsb[:, hc * 512:(hc + 1) * 512],
                                 ps_h[:, (hc % 2) * 512:(hc % 2 + 1) * 512],
                                 AF.Gelu,
                                 bias=b1col_all[:, hc * E + e:hc * E + e + 1],
                                 scale=1.0).then_inc(sa.h, 1)
        sa.n = sa0 + E * NW * NHC
        ph.done(_last_inc_a(s, k))

    @blk.tensor
    def _(t):
        ph.begin(t)
        for e in range(E):
            t.wait_ge(sg.h, sg0 + 16 * (e + 1))
            if e >= 1:
                t.wait_ge(sv.h, sv0 + (e - 1) * SVE + 2)
            t.wait_ge(sd.h, sd0 + 32 * (e + 1))
            st.inc(t.matmul(ps_b[:, 0:512], k.ones_row[0:1, 0:128],
                            b2row[:, 0:512], start=True, stop=True), 1)
            t.wait_ge(sv.h, sv0 + e * SVE + 1)
            st.inc(t.matmul(ps_b[:, 0:512], k.ones_row[0:1, 0:128],
                            b2row[:, 512:1024], start=True, stop=True), 1)
            for j in range(CPE):
                ci = e * CPE + j
                t.wait_ge(sgts[ci % 3].h, sgts0[ci % 3] + 16 * (ci // 3 + 1))
                if ci >= 2:
                    t.wait_ge(stv.h, stv0 + ci - 1)
                ins = None
                for f in range(NF):
                    ins = t.transpose(
                        ps_tr[:, (ci % 2) * D + f * 128:
                              (ci % 2) * D + (f + 1) * 128],
                        xg[:, (ci % 3) * D + f * 128:
                           (ci % 3) * D + (f + 1) * 128],
                        k.identb[:, :])
                stt.inc(ins, 1)
            t.wait_ge(stv.h, stv0 + (e + 1) * CPE)
            for w in range(NW):
                gw = e * NW + w
                for hc in range(NHC):
                    hb = hc % 2
                    if hc >= 2:
                        t.wait_ge(sa.h, sa_h(e, w) + hc - 1)
                    elif gw > 0:
                        t.wait_ge(sa.h, sa0 + (gw - 1) * 32 + 31 + hc)
                    ins = None
                    for c in range(NF):
                        ins = t.matmul(
                            ps_h[:, hb * 512:(hb + 1) * 512],
                            w1sb[:, c * H_FFN + hc * 128: c * H_FFN + (hc + 1) * 128],
                            t4g[:, c * capc + w * 512: c * capc + (w + 1) * 512],
                            start=(c == 0), stop=(c == NF - 1))
                    st.inc(ins, 1)
                t.wait_ge(sa.h, sa_h(e, w) + 32)
                for tc in range(4):
                    for dh in range(2):
                        yi = tc * 2 + dh
                        yb = yi % 2
                        gl = gw * 8 + yi - 2
                        if gl >= 0:
                            gwp, yip = divmod(gl, 8)
                            ep, wp = divmod(gwp, NW)
                            t.wait_ge(sv.h, sv_yy(ep, wp) + yip * 2 + 1)
                        ins = None
                        for hc in range(NHC):
                            ins = t.matmul(
                                ps_y[:, yb * 512:(yb + 1) * 512],
                                hsb[:, hc * 512 + tc * 128: hc * 512 + tc * 128 + 128],
                                w2sb[:, hc * D + dh * 512: hc * D + (dh + 1) * 512],
                                start=(hc == 0), stop=(hc == NHC - 1))
                        st.inc(ins, 1)
        ph.done(_last_inc_t(t, k))

    @blk.vector
    def _(v):
        ph.begin(v)
        for e in range(E):
            v.wait_ge(st.h, st_b2(e) - 1)
            if e >= 1:
                v.wait_ge(sv.h, sv0 + e * SVE)
            v.tensor_copy(b2bc[:, 0:512], ps_b[:, 0:512]).then_inc(sv.h, 1)
            v.wait_ge(st.h, st_b2(e))
            v.tensor_copy(b2bc[:, 512:1024], ps_b[:, 0:512]).then_inc(sv.h, 1)
            for j in range(CPE):
                ci = e * CPE + j
                v.wait_ge(stt.h, stt0 + ci + 1)
                ins = None
                for f in range(NF):
                    ins = v.tensor_copy(
                        t4g[:, f * capc + j * 128: f * capc + (j + 1) * 128],
                        ps_tr[:, (ci % 2) * D + f * 128:
                              (ci % 2) * D + (f + 1) * 128])
                ins.then_inc(stv.h, 1)
                stv.n += 1
            for w in range(NW):
                gw = e * NW + w
                for tc in range(4):
                    gcol = gvw[:, e * CW128 + w * 4 + tc:
                               e * CW128 + w * 4 + tc + 1]
                    ci = gw * 4 + tc
                    for dh in range(2):
                        yi = tc * 2 + dh
                        yb = yi % 2
                        ys = yg[:, dh * 512:(dh + 1) * 512]
                        v.wait_ge(st.h, st_mm(e, w) + 32 + yi + 1)
                        if ci >= 1 and dh == 0:
                            v.wait_ge(ssc.h, ssc0 + 16 * ci)
                        v.tensor_scalar_mul(ys, ps_y[:, yb * 512:(yb + 1) * 512],
                                            gcol).then_inc(sv.h, 1)
                        v.wait_ge(sv.h, sv_yy(e, w) + yi * 2 + 1)
                        v.scalar_tensor_tensor(
                            ys, b2bc[:, dh * 512:(dh + 1) * 512], gcol, ys,
                            ALU.mult, ALU.add).then_inc(sv.h, 1)
        sv.n = sv0 + E * SVE
        ph.done(_last_inc(v, k))

    blk.__exit__(None, None, None)
    loc.close()
    loc0.close()


def emit_rope_tables(k, loc_outer):
    nc = k.nc
    k._rope_n = getattr(k, "_rope_n", 0) + 1
    sfx = "%d" % k._rope_n
    k.cosT = loc_outer.sb("ropecos" + sfx, [128, T], F32)
    k.sinT = loc_outer.sb("ropesin" + sfx, [128, T], F32)
    loc = Loc(k)
    posr = loc.sb("rope_pr" + sfx, [1, T], F32)
    imodr = loc.sb("rope_im" + sfx, [1, 128], F32)
    icol = loc.sb("rope_ic" + sfx, [128, 2], F32)
    frq = loc.sb("rope_fq" + sfx, [128, T], F32)
    frq2 = loc.sb("rope_f2" + sfx, [128, T], F32)
    k.ps_anchor = loc.ps("rope_anc" + sfx, [1, 8], F32)
    ps = loc.ps("rope_ps" + sfx, [128, 512], F32)
    ps_i = loc.ps("rope_pi" + sfx, [128, 8], F32)
    ph = Phase(k)
    sg = k.sem("rope_g")
    st = k.sem("rope_t")
    sv = k.sem("rope_v")
    sa = k.sem("rope_a")
    sg0, st0, sv0, sa0 = sg.n, st.n, sv.n, sa.n
    blk = nc.Block()
    blk.__enter__()

    @blk.gpsimd
    def _(g):
        ph.begin(g)
        g.iota(posr[:, :], [[0, NBATCH], [1, S]], base=0, channel_multiplier=0,
               allow_small_or_imprecise_dtypes=True)
        sg.inc(g.iota(imodr[:, :], [[0, 4], [1, 32]], base=0,
                      channel_multiplier=0,
                      allow_small_or_imprecise_dtypes=True), 1)

    @blk.tensor
    def _(t):
        ph.begin(t)
        t.wait_ge(sa.h, sa0 + 1)
        st.inc(t.transpose(ps_i[:, 0:1], imodr[:, :], k.identf[0:1, 0:1]), 1)
        t.wait_ge(sg.h, sg0 + 1)
        for w in range(T // 512):
            if w >= 1:
                t.wait_ge(sv.h, sv0 + 1 + w)
            st.inc(t.matmul(ps[:, :], k.ones_rowf[0:1, 0:128],
                            posr[:, w * 512:(w + 1) * 512],
                            start=True, stop=True), 1)
        ph.done(_last_inc_t(t, k))

    @blk.vector
    def _(v):
        ph.begin(v)
        v.wait_ge(st.h, st0 + 1)
        v.tensor_copy(icol[:, 0:1], ps_i[:, 0:1]).then_inc(sv.h, 1)
        sv.n += 1
        v.wait_ge(sv.h, sv0 + 1)
        for w in range(T // 512):
            v.wait_ge(st.h, st0 + 1 + w + 1)
            v.tensor_scalar_mul(frq[:, w * 512:(w + 1) * 512], ps[:, :],
                                icol[:, 0:1]).then_inc(sv.h, 1)
        sv.n += T // 512
        MAGIC = 8388608.0
        inv2pi = 1.0 / (2.0 * PI)
        for tbl, shift in ((k.cosT, 0.25), (k.sinT, 0.0)):
            v.wait_ge(sv.h, sv.n)
            v.tensor_scalar(tbl[:, :], frq[:, :], inv2pi, shift,
                            ALU.mult, ALU.add).then_inc(sv.h, 1)
            sv.n += 1
            v.wait_ge(sv.h, sv.n)
            v.tensor_scalar(frq2[:, :], tbl[:, :], MAGIC, MAGIC,
                            ALU.add, ALU.subtract).then_inc(sv.h, 1)
            sv.n += 1
            v.wait_ge(sv.h, sv.n)
            v.tensor_sub(tbl[:, :], tbl[:, :], frq2[:, :]).then_inc(sv.h, 1)
            sv.n += 1
        ph.done(_last_inc(v, k))

    @blk.scalar
    def _(s):
        ph.begin(s)
        s.wait_ge(sg.h, sg0 + 1)
        s.activation(imodr[:, :], imodr[:, :], AF.Exp,
                     scale=-float(np.log(10000.0)) / 32.0).then_inc(sa.h, 1)
        sa.n += 1
        v_done = sv.n
        s.wait_ge(sv.h, v_done)
        s.activation(k.cosT[:, :], k.cosT[:, :], AF.Sin,
                     scale=2.0 * PI).then_inc(sa.h, 1)
        sa.n += 1
        s.wait_ge(sa.h, sa.n)
        s.activation(k.sinT[:, :], k.sinT[:, :], AF.Sin,
                     scale=2.0 * PI).then_inc(sa.h, 1)
        sa.n += 1
        ph.done(_last_inc_a(s, k))
    blk.__exit__(None, None, None)
    loc.close()


def emit_proj_fm(k, t_fm, w_bf, out_fm, OF, tag, rope=False, out_row0=0,
                 w_rot=None):
    nc = k.nc
    NOF = OF // 128
    loc = Loc(k)
    wsb = loc.sb("pf_w" + tag, [128, NF * OF], BF16)
    wrsb = loc.sb("pf_wr" + tag, [128, NF * OF], BF16) if rope else None
    tfm = loc.sb("pf_t" + tag, [128, 2 * NF * 512], BF16)
    ev = loc.sb("pf_e" + tag, [128, 2 * 512], BF16)
    ev32 = loc.sb("pf_e32" + tag, [128, 2 * 512], F32)
    k.ps_anchor = loc.ps("pf_anc" + tag, [1, 8], F32)
    ps = loc.ps("pf_ps" + tag, [128, 2 * 512], F32)
    ps2 = loc.ps("pf_ps2" + tag, [128, 2 * 512], F32) if rope else None
    sd = k.sem("pf_d")
    sll = [k.sem("pf_l%d" % j) for j in range(2)]
    st = k.sem("pf_t")
    sv = k.sem("pf_v")
    sa = k.sem("pf_a")
    sg = k.sem("pf_g")
    sgf = k.sem("pf_gf")
    ph = Phase(k)
    sll0 = [x.n for x in sll]
    sd0, st0, sv0, sa0, sg0 = sd.n, st.n, sv.n, sa.n, sg.n
    sgf0 = sgf.n
    NWIN = T // 512
    nev_ops = 4 if rope else 0

    blk = nc.Block()
    blk.__enter__()

    @blk.sync
    def _(sync):
        ph.begin(sync)
        sd.inc(sync.dma_start(
            out=wsb[:, :].rearrange("p (c f) -> p c f", c=NF),
            in_=w_bf.rearrange("(c p) f -> p c f", p=128)), 16)
        if rope:
            sd.inc(sync.dma_start(
                out=wrsb[:, :].rearrange("p (c f) -> p c f", c=NF),
                in_=w_rot.rearrange("(c p) f -> p c f", p=128)), 16)
        for tw in range(NWIN):
            if tw >= 2:
                sync.wait_ge(st.h, st0 + (tw - 1) * NOF * (2 if rope else 1))
            sll[tw % 2].inc(sync.dma_start(
                out=tfm[:, (tw % 2) * NF * 512:(tw % 2 + 1) * NF * 512]
                    .rearrange("p (c t) -> p c t", c=NF),
                in_=t_fm[:, tw * 512:(tw + 1) * 512]
                    .rearrange("(c p) t -> p c t", p=128)), 16)
        sync.wait_ge(sd.h, sd.n)
        for j in range(2):
            sync.wait_ge(sll[j].h, sll[j].n)
        sync.sem_inc(k.bar.h, 1)
    k.bar.n += 1

    @blk.tensor
    def _(t):
        ph.begin(t)
        t.wait_ge(sd.h, sd0 + (32 if rope else 16))
        for tw in range(NWIN):
            b = tw % 2
            t.wait_ge(sll[b].h, sll0[b] + 16 * (tw // 2 + 1))
            for ofc in range(NOF):
                i = tw * NOF + ofc
                pb = i % 2
                if i >= 2:
                    if rope:
                        t.wait_ge(sv.h, sv0 + (i - 1) * nev_ops)
                    else:
                        t.wait_ge(sa.h, sa0 + i - 1)
                ins = None
                for c in range(NF):
                    ins = t.matmul(
                        ps[:, pb * 512:(pb + 1) * 512],
                        wsb[:, c * OF + ofc * 128: c * OF + (ofc + 1) * 128],
                        tfm[:, b * NF * 512 + c * 512: b * NF * 512 + (c + 1) * 512],
                        start=(c == 0), stop=(c == NF - 1))
                st.inc(ins, 1)
                if rope:
                    for c in range(NF):
                        ins = t.matmul(
                            ps2[:, pb * 512:(pb + 1) * 512],
                            wrsb[:, c * OF + ofc * 128: c * OF + (ofc + 1) * 128],
                            tfm[:, b * NF * 512 + c * 512: b * NF * 512 + (c + 1) * 512],
                            start=(c == 0), stop=(c == NF - 1))
                    st.inc(ins, 1)
        ph.done(_last_inc_t(t, k))

    if not rope:
        @blk.scalar
        def _(s):
            ph.begin(s)
            for i in range(NWIN * NOF):
                tw, ofc = divmod(i, NOF)
                pb = i % 2
                eb = i % 2
                s.wait_ge(st.h, st0 + i + 1)
                if i >= 2:
                    s.wait_ge(sgf.h, sgf0 + i - 1)
                s.activation(ev[:, eb * 512:(eb + 1) * 512],
                             ps[:, pb * 512:(pb + 1) * 512],
                             AF.Copy, scale=1.0).then_inc(sa.h, 1)
            sa.n = sa0 + NWIN * NOF
            ph.done(_last_inc_a(s, k))

        @blk.gpsimd
        def _(g):
            ph.begin(g)
            for i in range(NWIN * NOF):
                tw, ofc = divmod(i, NOF)
                eb = i % 2
                if i >= 1:
                    g.wait_ge(sg.h, sg.n)
                    g.sem_inc(sgf.h, 1)
                    sgf.n += 1
                g.wait_ge(sa.h, sa0 + i + 1)
                sg.inc(g.dma_start(
                    out=out_fm[out_row0 + ofc * 128: out_row0 + (ofc + 1) * 128,
                               tw * 512:(tw + 1) * 512],
                    in_=ev[:, eb * 512:(eb + 1) * 512]), 16)
            ph.done_q(g, sg)
    else:
        @blk.vector
        def _(v):
            ph.begin(v)
            for i in range(NWIN * NOF):
                tw, ofc = divmod(i, NOF)
                pb = i % 2
                eb = i % 2
                tws = tw * 512
                base = sv0 + i * nev_ops
                v.wait_ge(st.h, st0 + 2 * i + 2)
                if i >= 1:
                    v.wait_ge(sv.h, base)
                if i >= 2:
                    v.wait_ge(sgf.h, sgf0 + i - 1)
                v.tensor_mul(ev32[:, eb * 512:(eb + 1) * 512],
                             ps[:, pb * 512:(pb + 1) * 512],
                             k.cosT[:, tws:tws + 512]).then_inc(sv.h, 1)
                v.tensor_mul(ev32[:, (1 - eb) * 512:(2 - eb) * 512],
                             ps2[:, pb * 512:(pb + 1) * 512],
                             k.sinT[:, tws:tws + 512]).then_inc(sv.h, 1)
                v.wait_ge(sv.h, base + 2)
                v.tensor_add(ev32[:, eb * 512:(eb + 1) * 512],
                             ev32[:, eb * 512:(eb + 1) * 512],
                             ev32[:, (1 - eb) * 512:(2 - eb) * 512]).then_inc(sv.h, 1)
                v.wait_ge(sv.h, base + 3)
                v.tensor_copy(ev[:, eb * 512:(eb + 1) * 512],
                              ev32[:, eb * 512:(eb + 1) * 512]).then_inc(sv.h, 1)
            sv.n = sv0 + NWIN * NOF * nev_ops
            ph.done(_last_inc(v, k))

        @blk.gpsimd
        def _(g):
            ph.begin(g)
            for i in range(NWIN * NOF):
                tw, ofc = divmod(i, NOF)
                eb = i % 2
                if i >= 1:
                    g.wait_ge(sg.h, sg.n)
                    g.sem_inc(sgf.h, 1)
                    sgf.n += 1
                g.wait_ge(sv.h, sv0 + (i + 1) * nev_ops)
                sg.inc(g.dma_start(
                    out=out_fm[out_row0 + ofc * 128: out_row0 + (ofc + 1) * 128,
                               tw * 512:(tw + 1) * 512],
                    in_=ev[:, eb * 512:(eb + 1) * 512]), 16)
            ph.done_q(g, sg)

    blk.__exit__(None, None, None)
    loc.close()


def emit_proj_tm(k, t_fm, w_bf, out_tm, OF, tag, out_col0=0):
    nc = k.nc
    NOH = OF // 512 if OF >= 512 else 1
    ofw = min(OF, 512)
    loc = Loc(k)
    wsb = loc.sb("pt_w" + tag, [128, NF * OF], BF16)
    tfm = loc.sb("pt_t" + tag, [128, 2 * NF * 512], BF16)
    ev = loc.sb("pt_e" + tag, [128, 2 * ofw], BF16)
    k.ps_anchor = loc.ps("pt_anc" + tag, [1, 8], F32)
    ps = loc.ps("pt_ps" + tag, [128, 2 * 512], F32)
    sd = k.sem("pt_d")
    sll = [k.sem("pt_l%d" % j) for j in range(2)]
    st = k.sem("pt_t")
    sa = k.sem("pt_a")
    sg = k.sem("pt_g")
    sgf = k.sem("pt_gf")
    ph = Phase(k)
    sll0 = [x.n for x in sll]
    sd0, st0, sa0, sg0 = sd.n, st.n, sa.n, sg.n
    sgf0 = sgf.n
    blk = nc.Block()
    blk.__enter__()

    @blk.sync
    def _(sync):
        ph.begin(sync)
        sd.inc(sync.dma_start(
            out=wsb[:, :].rearrange("p (c f) -> p c f", c=NF),
            in_=w_bf.rearrange("(c p) f -> p c f", p=128)), 16)
        for tw in range(T // 512):
            if tw >= 2:
                sync.wait_ge(st.h, st0 + (tw - 1) * 4 * NOH)
            sll[tw % 2].inc(sync.dma_start(
                out=tfm[:, (tw % 2) * NF * 512:(tw % 2 + 1) * NF * 512]
                    .rearrange("p (c t) -> p c t", c=NF),
                in_=t_fm[:, tw * 512:(tw + 1) * 512]
                    .rearrange("(c p) t -> p c t", p=128)), 16)
        sync.wait_ge(sd.h, sd.n)
        for j in range(2):
            sync.wait_ge(sll[j].h, sll[j].n)
        sync.sem_inc(k.bar.h, 1)
    k.bar.n += 1

    @blk.tensor
    def _(t):
        ph.begin(t)
        t.wait_ge(sd.h, sd0 + 16)
        for tw in range(T // 512):
            b = tw % 2
            t.wait_ge(sll[b].h, sll0[b] + 16 * (tw // 2 + 1))
            for tc in range(4):
                for oh in range(NOH):
                    i = (tw * 4 + tc) * NOH + oh
                    pb = i % 2
                    if i >= 2:
                        t.wait_ge(sa.h, sa0 + i - 1)
                    ins = None
                    for c in range(NF):
                        ins = t.matmul(
                            ps[:, pb * 512:pb * 512 + ofw],
                            tfm[:, b * NF * 512 + c * 512 + tc * 128:
                                b * NF * 512 + c * 512 + (tc + 1) * 128],
                            wsb[:, c * OF + oh * 512: c * OF + oh * 512 + ofw],
                            start=(c == 0), stop=(c == NF - 1))
                    st.inc(ins, 1)
        ph.done(_last_inc_t(t, k))

    @blk.scalar
    def _(s):
        ph.begin(s)
        NTOT = 32 * NOH
        for i in range(NTOT):
            pb = i % 2
            s.wait_ge(st.h, st0 + i + 1)
            if i >= 2:
                s.wait_ge(sgf.h, sgf0 + i - 1)
            s.activation(ev[:, pb * ofw:(pb + 1) * ofw],
                         ps[:, pb * 512:pb * 512 + ofw],
                         AF.Copy, scale=1.0).then_inc(sa.h, 1)
        sa.n = sa0 + NTOT
        ph.done(_last_inc_a(s, k))

    @blk.gpsimd
    def _(g):
        ph.begin(g)
        NTOT = 32 * NOH
        for i in range(NTOT):
            ti, oh = divmod(i, NOH)
            pb = i % 2
            if i >= 1:
                g.wait_ge(sg.h, sg.n)
                g.sem_inc(sgf.h, 1)
                sgf.n += 1
            g.wait_ge(sa.h, sa0 + i + 1)
            sg.inc(g.dma_start(
                out=out_tm[ti * 128:(ti + 1) * 128,
                           out_col0 + oh * 512: out_col0 + oh * 512 + ofw],
                in_=ev[:, pb * ofw:(pb + 1) * ofw]), 16)
        ph.done_q(g, sg)
    blk.__exit__(None, None, None)
    loc.close()


def emit_proj_res(k, a_fm, w_bf, xres, tag, scale=1.0, t_add=None, h_out=None):
    nc = k.nc
    loc = Loc(k)
    wsb = loc.sb("pr_w" + tag, [128, NF * D], BF16)
    afm = loc.sb("pr_t" + tag, [128, 2 * NF * 512], BF16)
    tadd = loc.sb("pr_a" + tag, [128, 1024], BF16) if t_add is not None else None
    ev = loc.sb("pr_e" + tag, [128, 2 * 512], F32)
    hv = loc.sb("pr_h" + tag, [128, 2 * 512], F32) if h_out is not None else None
    k.ps_anchor = loc.ps("pr_anc" + tag, [1, 8], F32)
    ps = loc.ps("pr_ps" + tag, [128, 2 * 512], F32)
    sd = k.sem("pr_d")
    sll = [k.sem("pr_l%d" % j) for j in range(2)]
    st = k.sem("pr_t")
    sv = k.sem("pr_v")
    sg = k.sem("pr_g")
    sgf = k.sem("pr_gf")
    sh = k.sem("pr_h")
    ph = Phase(k)
    sll0 = [x.n for x in sll]
    sd0, st0, sv0, sg0, sh0 = sd.n, st.n, sv.n, sg.n, sh.n
    sgf0 = sgf.n
    npt = 2 if h_out is not None else 1
    nd = 2 if h_out is not None else 1
    blk = nc.Block()
    blk.__enter__()

    @blk.sync
    def _(sync):
        ph.begin(sync)
        sd.inc(sync.dma_start(
            out=wsb[:, :].rearrange("p (c f) -> p c f", c=NF),
            in_=w_bf.rearrange("(c p) f -> p c f", p=128)), 16)
        for tw in range(T // 512):
            if tw >= 2:
                sync.wait_ge(st.h, st0 + (tw - 1) * 8)
            sll[tw % 2].inc(sync.dma_start(
                out=afm[:, (tw % 2) * NF * 512:(tw % 2 + 1) * NF * 512]
                    .rearrange("p (c t) -> p c t", c=NF),
                in_=a_fm[:, tw * 512:(tw + 1) * 512]
                    .rearrange("(c p) t -> p c t", p=128)), 16)
        sync.wait_ge(sd.h, sd.n)
        for j in range(2):
            sync.wait_ge(sll[j].h, sll[j].n)
        sync.sem_inc(k.bar.h, 1)
    k.bar.n += 1

    @blk.scalar
    def _(s):
        ph.begin(s)
        if t_add is not None:
            for ti in range(32):
                if ti >= 1:
                    s.wait_ge(sv.h, sv0 + ti * 2 * npt)
                sh.inc(s.dma_start(out=tadd[:, :],
                                   in_=t_add[ti * 128:(ti + 1) * 128, :]), 16)
        ph.done_q(s, sh)

    @blk.tensor
    def _(t):
        ph.begin(t)
        t.wait_ge(sd.h, sd0 + 16)
        for tw in range(T // 512):
            b = tw % 2
            t.wait_ge(sll[b].h, sll0[b] + 16 * (tw // 2 + 1))
            for tc in range(4):
                for dh in range(2):
                    i = (tw * 4 + tc) * 2 + dh
                    pb = i % 2
                    if i >= 2:
                        t.wait_ge(sv.h, sv0 + (i - 1) * npt)
                    ins = None
                    for c in range(NF):
                        ins = t.matmul(
                            ps[:, pb * 512:(pb + 1) * 512],
                            afm[:, b * NF * 512 + c * 512 + tc * 128:
                                b * NF * 512 + c * 512 + (tc + 1) * 128],
                            wsb[:, c * D + dh * 512: c * D + (dh + 1) * 512],
                            start=(c == 0), stop=(c == NF - 1))
                    st.inc(ins, 1)
        ph.done(_last_inc_t(t, k))

    @blk.vector
    def _(v):
        ph.begin(v)
        for i in range(64):
            ti, dh = divmod(i, 2)
            pb = i % 2
            v.wait_ge(st.h, st0 + i + 1)
            if i >= 2:
                v.wait_ge(sgf.h, sgf0 + i - 1)
            base = sv0 + i * npt
            if h_out is not None:
                v.wait_ge(sh.h, sh0 + 16 * (ti + 1))
                v.tensor_add(hv[:, pb * 512:(pb + 1) * 512],
                             ps[:, pb * 512:(pb + 1) * 512],
                             tadd[:, dh * 512:(dh + 1) * 512]).then_inc(sv.h, 1)
                v.wait_ge(sv.h, base + 1)
                v.tensor_scalar_mul(ev[:, pb * 512:(pb + 1) * 512],
                                    hv[:, pb * 512:(pb + 1) * 512],
                                    scale).then_inc(sv.h, 1)
            else:
                v.tensor_scalar_mul(ev[:, pb * 512:(pb + 1) * 512],
                                    ps[:, pb * 512:(pb + 1) * 512],
                                    scale).then_inc(sv.h, 1)
        sv.n = sv0 + 64 * npt
        ph.done(_last_inc(v, k))

    @blk.gpsimd
    def _(g):
        ph.begin(g)
        for i in range(64):
            ti, dh = divmod(i, 2)
            pb = i % 2
            if i >= 1:
                g.wait_ge(sg.h, sg.n)
                g.sem_inc(sgf.h, 1)
                sgf.n += 1
            g.wait_ge(sv.h, sv0 + (i + 1) * npt)
            sg.inc(g.dma_start(
                out=xres[ti * 128:(ti + 1) * 128, dh * 512:(dh + 1) * 512],
                in_=ev[:, pb * 512:(pb + 1) * 512],
                accum_op=ALU.add), 16)
            if h_out is not None:
                sg.inc(g.dma_start(
                    out=h_out[ti * 128:(ti + 1) * 128, dh * 512:(dh + 1) * 512],
                    in_=hv[:, pb * 512:(pb + 1) * 512]), 16)
        ph.done_q(g, sg)
    blk.__exit__(None, None, None)
    loc.close()


def emit_attn(k, q_fm, k_fm, v_tm, out_fm, tag, causal, alibi, kv_group, nbh=None):
    nc = k.nc
    NH = 16
    VKH = NH // kv_group
    if alibi:
        NAUG_A, NAUG_B, SROW = 68, 69, 68
        x_ = (2.0 ** 8) ** (1.0 / NH)
        s8 = [8.0 / x_ ** (h + 1) for h in range(NH)]
        s8_hi = [float(np.float32(ml_dtypes.bfloat16(v))) for v in s8]
        s8_lo = [float(np.float32(ml_dtypes.bfloat16(v - h_)))
                 for v, h_ in zip(s8, s8_hi)]
    else:
        NAUG_A, NAUG_B, SROW = 64, 65, 64

    if causal:
        nktw = [4 * w + 4 for w in range(4)]
        kwin = [qt // 4 + 1 for qt in range(16)]
    else:
        nktw = [16] * 4
        kwin = [4] * 16
    NKT = sum(nktw)
    A_sv = (2 if causal else 1) * 16
    svw = (4 if causal else 0) + 2
    st_bh = 17 + 2 * NKT + 4
    sv_bh = A_sv + 1 + 4 * svw
    sa_bh = NKT + 4
    stoff = [17]
    for w in range(3):
        stoff.append(stoff[-1] + 2 * nktw[w] + 1)
    svoff = [A_sv + 1]
    for w in range(3):
        svoff.append(svoff[-1] + svw)
    saoff = [0]
    for w in range(3):
        saoff.append(saoff[-1] + nktw[w] + 1)
    sld_bh = 48
    sgp_bh = (4 * 16 if alibi else 0) + 16
    sst0_holder = []

    loc = Loc(k)
    q_aug = loc.sb("at_q" + tag, [NAUG_B, S], BF16)
    k_aug = loc.sb("at_k" + tag, [NAUG_B, S], BF16)
    v_sb = loc.sb("at_v" + tag, [128, 16 * 65], BF16)
    pT = loc.sb("at_p" + tag, [128, 2 * 512], BF16)
    negm_all = loc.sb("at_nm" + tag, [128, 16], F32)
    stage = loc.sb("at_st" + tag, [16, 128], BF16)
    rl_row = loc.sb("at_rl" + tag, [1, 512], F32)
    rl_bc = loc.sb("at_rb" + tag, [64, 512], F32)
    out_sb = loc.sb("at_o" + tag, [64, 2 * 512], BF16)
    jrow_hi = loc.sb("at_jh" + tag, [1, S], BF16)
    jrow_lo = loc.sb("at_jl" + tag, [1, S], BF16)
    sh_row = loc.sb("at_sh" + tag, [1, S], BF16)
    sl_row = loc.sb("at_sl" + tag, [1, S], BF16)
    onesr = loc.sb("at_1r" + tag, [1, S], BF16)
    maskA = loc.sb("at_mA" + tag, [128, 4 * 512], F32) if causal else None
    maskB = loc.sb("at_mB" + tag, [128, 4 * 512], F32) if causal else None
    ps_sA = loc.ps("at_pA" + tag, [128, 4 * 512], F32)
    ps_sB = loc.ps("at_pB" + tag, [128, 2 * 512], F32)
    ps_pv = loc.ps("at_pv" + tag, [65, 512], F32)
    ps_ms = loc.ps("at_pm" + tag, [128, 512], F32)

    sld = k.sem("at_d")
    sgp = k.sem("at_g")
    sgc = k.sem("at_gc")
    sst = k.sem("at_s")
    sstf = k.sem("at_sf")
    st = k.sem("at_t")
    sv = k.sem("at_v")
    sa = k.sem("at_a")
    ph = Phase(k)
    sld0, sgp0, st0, sv0, sa0, sst0 = sld.n, sgp.n, st.n, sv.n, sa.n, sst.n
    sstf0 = sstf.n
    BH = NBATCH * NH if nbh is None else nbh

    def stb(bh):
        return st0 + bh * st_bh

    def svb(bh):
        return sv0 + bh * sv_bh

    def sab(bh):
        return sa0 + bh * sa_bh

    def sgpb(bh):
        return sgp0 + SETUP_GP + bh * sgp_bh

    SETUP_GP = 0

    blk = nc.Block()
    blk.__enter__()

    @blk.gpsimd
    def _(g):
        nonlocal_setup = []
        ph.begin(g)
        n = 0
        if alibi:
            g.iota(jrow_hi[:, :], [[256, 8], [0, 256]], base=0,
                   channel_multiplier=0, allow_small_or_imprecise_dtypes=True)
            g.iota(jrow_lo[:, :], [[0, 8], [1, 256]], base=0,
                   channel_multiplier=0, allow_small_or_imprecise_dtypes=True)
            sgc.inc(g.memset(onesr[:, :], 1.0), 1)
            g.wait_ge(sgc.h, sgc.n)
            sgp.inc(g.dma_start(out=k_aug[64:65, :], in_=jrow_hi[:, :]), 16)
            sgp.inc(g.dma_start(out=k_aug[65:66, :], in_=jrow_lo[:, :]), 16)
            sgp.inc(g.dma_start(out=k_aug[66:67, :], in_=jrow_hi[:, :]), 16)
            sgp.inc(g.dma_start(out=k_aug[67:68, :], in_=jrow_lo[:, :]), 16)
            n = 64
        else:
            sgc.inc(g.memset(onesr[:, :], 1.0), 1)
            g.wait_ge(sgc.h, sgc.n)
            n = 0
        sgp.inc(g.dma_start(out=k_aug[SROW:SROW + 1, :], in_=onesr[:, :]), 16)
        n += 16
        for kt in range(15):
            g.memset(v_sb[:, kt * 65 + 64:kt * 65 + 65], 1.0)
        if causal:
            sgc.inc(g.memset(v_sb[:, 15 * 65 + 64:15 * 65 + 65], 1.0), 1)
            for r in range(4):
                sgc.inc(g.memset(maskA[:, r * 512:(r + 1) * 512], 0.0), 1)
                g.wait_ge(sgc.h, sgc.n)
                g.affine_select(out=maskA[:, r * 512:(r + 1) * 512],
                                in_=maskA[:, r * 512:(r + 1) * 512],
                                compare_op=ALU.is_ge, fill=NEGINF,
                                base=r * 128, channel_multiplier=1,
                                pattern=[[-1, 512]])
                sgc.inc(g.memset(maskB[:, r * 512:(r + 1) * 512], 0.0), 1)
                g.wait_ge(sgc.h, sgc.n)
                sgc.inc(g.affine_select(out=maskB[:, r * 512:(r + 1) * 512],
                                in_=maskB[:, r * 512:(r + 1) * 512],
                                compare_op=ALU.is_ge, fill=NEGINF,
                                base=-r * 128, channel_multiplier=-1,
                                pattern=[[1, 512]]), 1)
        else:
            sgc.inc(g.memset(v_sb[:, 15 * 65 + 64:15 * 65 + 65], 1.0), 1)
        assert sgp.n - sgp0 == n
        k.marks["at_setup_gc" + tag] = sgc.n
        for bh in range(BH):
            b, h = divmod(bh, NH)
            g.wait_ge(st.h, stb(bh))
            if alibi:
                sgc.inc(g.memset(sh_row[:, :], s8_hi[h]), 1)
                sgc.inc(g.memset(sl_row[:, :], s8_lo[h]), 1)
                g.wait_ge(sgc.h, sgc.n)
                sgp.inc(g.dma_start(out=q_aug[64:65, :], in_=sh_row[:, :]), 16)
                sgp.inc(g.dma_start(out=q_aug[65:66, :], in_=sh_row[:, :]), 16)
                sgp.inc(g.dma_start(out=q_aug[66:67, :], in_=sl_row[:, :]), 16)
                sgp.inc(g.dma_start(out=q_aug[67:68, :], in_=sl_row[:, :]), 16)
            g.wait_ge(sv.h, svb(bh) + A_sv + 1)
            sgp.inc(g.dma_start(
                out=q_aug[SROW:SROW + 1, :].rearrange("o (p t) -> o p t", p=16),
                in_=stage[:, :]), 16)
            for w in range(4):
                gw = bh * 4 + w
                if gw >= 1:
                    g.wait_ge(sst.h, sst.n)
                    g.sem_inc(sstf.h, 1)
                    sstf.n += 1
                g.wait_ge(sv.h, svb(bh) + svoff[w] + svw)
                sst.inc(g.dma_start(
                    out=out_fm[h * 64:(h + 1) * 64,
                               b * S + w * 512: b * S + (w + 1) * 512],
                    in_=out_sb[:, (w % 2) * 512:(w % 2 + 1) * 512]), 16)
        g.wait_ge(sgp.h, sgp.n)
        ph.done_q(g, sst)

    SETUP_GP = (80 if alibi else 16)

    @blk.sync
    def _(sync):
        ph.begin(sync)
        for bh in range(BH):
            b, h = divmod(bh, NH)
            kvh = h // kv_group
            sync.wait_ge(st.h, stb(bh))
            sld.inc(sync.dma_start(
                out=q_aug[0:64, :],
                in_=q_fm[h * 64:(h + 1) * 64, b * S:(b + 1) * S]), 16)
            sld.inc(sync.dma_start(
                out=k_aug[0:64, :],
                in_=k_fm[kvh * 64:(kvh + 1) * 64, b * S:(b + 1) * S]), 16)
            sld.inc(sync.dma_start(
                out=v_sb[:, :].rearrange("p (kt x) -> p kt x", x=65)[:, :, 0:64],
                in_=v_tm[b * S:(b + 1) * S, kvh * 64:(kvh + 1) * 64]
                    .rearrange("(kt p) d -> p kt d", p=128)), 16)
        ph.done_q(sync, sld)

    SETUP_GC = sgc.n

    @blk.tensor
    def _(t):
        ph.begin(t)
        t.wait_ge(sgc.h, k.marks["at_setup_gc" + tag])
        for bh in range(BH):
            b, h = divmod(bh, NH)
            t.wait_ge(sld.h, sld0 + sld_bh * (bh + 1))
            if alibi:
                t.wait_ge(sgp.h, sgpb(bh) + 64)
            t.wait_ge(sa.h, sab(bh))
            for qt in range(16):
                if qt >= 1:
                    t.wait_ge(sv.h, svb(bh) + (2 * qt if causal else qt))
                elif bh >= 1:
                    t.wait_ge(sv.h, svb(bh - 1) + A_sv)
                ins = None
                for w2 in range(kwin[qt]):
                    ins = t.matmul(
                        ps_sA[:, w2 * 512:(w2 + 1) * 512],
                        q_aug[0:NAUG_A, qt * 128:(qt + 1) * 128],
                        k_aug[0:NAUG_A, w2 * 512:(w2 + 1) * 512],
                        start=True, stop=True)
                st.inc(ins, 1)
            t.wait_ge(sv.h, svb(bh) + A_sv)
            st.inc(t.transpose(ps_ms[0:16, 0:128], negm_all[:, :],
                               k.identf[:, :]), 1)
            t.wait_ge(sgp.h, sgpb(bh) + sgp_bh)
            for w in range(4):
                for j in range(nktw[w]):
                    kt = j
                    if j >= 2:
                        t.wait_ge(sa.h, sab(bh) + saoff[w] + j - 1)
                    elif w >= 1:
                        t.wait_ge(sa.h, sab(bh) + saoff[w - 1]
                                  + nktw[w - 1] - 1 + j)
                    ins = t.matmul(
                        ps_sB[:, (j % 2) * 512:(j % 2 + 1) * 512],
                        k_aug[0:NAUG_B, kt * 128:(kt + 1) * 128],
                        q_aug[0:NAUG_B, w * 512:(w + 1) * 512],
                        start=True, stop=True)
                    st.inc(ins, 1)
                    t.wait_ge(sa.h, sab(bh) + saoff[w] + j + 1)
                    if j == 0:
                        if w >= 1:
                            t.wait_ge(sv.h, svb(bh) + svoff[w - 1] + svw)
                        elif bh >= 1:
                            t.wait_ge(sv.h, svb(bh - 1) + sv_bh)
                    ins = t.matmul(ps_pv[:, :], v_sb[:, kt * 65:(kt + 1) * 65],
                                   pT[:, (j % 2) * 512:(j % 2 + 1) * 512],
                                   start=(j == 0), stop=(j == nktw[w] - 1))
                    st.inc(ins, 1)
                t.wait_ge(sv.h, svb(bh) + svoff[w] + svw - 1)
                st.inc(t.matmul(ps_ms[0:64, 0:512], k.ones_rowf[0:1, 0:64],
                                rl_row[:, :], start=True, stop=True), 1)
        t.wait_ge(sa.h, sa0 + BH * sa_bh)
        k.bar.inc(t.matmul(ps_ms[0:1, 0:1], k.ones_rowf[0:1, 0:1],
                           k.ones_rowf[0:1, 0:1], start=True, stop=True), 1)

    @blk.vector
    def _(v):
        ph.begin(v)
        if causal:
            v.wait_ge(sgc.h, k.marks["at_setup_gc" + tag])
        for bh in range(BH):
            for qt in range(16):
                v.wait_ge(st.h, stb(bh) + qt + 1)
                base = svb(bh)
                if causal:
                    dw = qt // 4
                    r = qt % 4
                    v.tensor_tensor(ps_sA[:, dw * 512:(dw + 1) * 512],
                                    ps_sA[:, dw * 512:(dw + 1) * 512],
                                    maskA[:, r * 512:(r + 1) * 512],
                                    ALU.add).then_inc(sv.h, 1)
                    v.wait_ge(sv.h, base + 2 * qt + 1)
                    v.reduce_max(negm_all[:, qt:qt + 1],
                                 ps_sA[:, 0:kwin[qt] * 512], axis=AX.X,
                                 negate=True).then_inc(sv.h, 1)
                else:
                    v.reduce_max(negm_all[:, qt:qt + 1],
                                 ps_sA[:, 0:kwin[qt] * 512], axis=AX.X,
                                 negate=True).then_inc(sv.h, 1)
            v.wait_ge(st.h, stb(bh) + 17)
            v.wait_ge(sv.h, svb(bh) + A_sv)
            v.tensor_copy(stage[:, :], ps_ms[0:16, 0:128]).then_inc(sv.h, 1)
            for w in range(4):
                base = svb(bh) + svoff[w]
                if causal:
                    for r2 in range(4):
                        kt = w * 4 + r2
                        j = kt
                        v.wait_ge(st.h, stb(bh) + stoff[w] + 2 * j + 1)
                        v.tensor_tensor(
                            ps_sB[:, (j % 2) * 512:(j % 2 + 1) * 512],
                            ps_sB[:, (j % 2) * 512:(j % 2 + 1) * 512],
                            maskB[:, r2 * 512:(r2 + 1) * 512],
                            ALU.add).then_inc(sv.h, 1)
                v.wait_ge(st.h, stb(bh) + stoff[w] + 2 * nktw[w])
                v.reciprocal(rl_row[:, :], ps_pv[64:65, :]).then_inc(sv.h, 1)
                v.wait_ge(sa.h, sab(bh) + saoff[w] + nktw[w] + 1)
                gl = bh * 4 + w - 2
                if gl >= 0:
                    v.wait_ge(sstf.h, sstf0 + gl + 2)
                v.tensor_mul(out_sb[:, (w % 2) * 512:(w % 2 + 1) * 512],
                             ps_pv[0:64, :], rl_bc[:, :]).then_inc(sv.h, 1)
        sv.n = sv0 + BH * sv_bh
        ph.done(_last_inc(v, k))

    @blk.scalar
    def _(s):
        ph.begin(s)
        for bh in range(BH):
            for w in range(4):
                for j in range(nktw[w]):
                    s.wait_ge(st.h, stb(bh) + stoff[w] + 2 * j + 1)
                    if causal and j >= w * 4:
                        r2 = j - w * 4
                        if 0 <= r2 < 4:
                            s.wait_ge(sv.h, svb(bh) + svoff[w] + r2 + 1)
                    if j >= 2:
                        s.wait_ge(st.h, stb(bh) + stoff[w] + 2 * (j - 1))
                    s.activation(pT[:, (j % 2) * 512:(j % 2 + 1) * 512],
                                 ps_sB[:, (j % 2) * 512:(j % 2 + 1) * 512],
                                 AF.Exp, scale=0.125).then_inc(sa.h, 1)
                s.wait_ge(st.h, stb(bh) + stoff[w] + 2 * nktw[w] + 1)
                if bh * 4 + w >= 1:
                    pw = bh * 4 + w - 1
                    s.wait_ge(sv.h, sv0 + (pw // 4) * sv_bh
                              + svoff[pw % 4] + svw)
                s.activation(rl_bc[:, :], ps_ms[0:64, 0:512], AF.Copy,
                             scale=1.0).then_inc(sa.h, 1)
        sa.n = sa0 + BH * sa_bh
        ph.done(_last_inc_a(s, k))

    blk.__exit__(None, None, None)
    loc.close()




def emit_rmsnorm(k, src, w_vec, out_tm, tag, src_dt=F32):
    nc = k.nc
    ph = Phase(k)
    sdl = [k.sem("rm_d%d" % j) for j in range(4)]
    sw = k.sem("rm_w")
    sv = k.sem("rm_v")
    sa = k.sem("rm_a")
    so = k.sem("rm_o")
    sf = k.sem("rm_f")
    st = k.sem("rm_t")
    NBUF = 4
    loc = Loc(k)
    k.ps_anchor = loc.ps("rm_anc" + tag, [1, 8], F32)
    xt = loc.sb("rm_x" + tag, [128, NBUF * D], src_dt)
    sq = loc.sb("rm_q" + tag, [128, D], F32)
    tt = loc.sb("rm_t" + tag, [128, NBUF * D], BF16)
    cols = loc.sb("rm_s" + tag, [128, 8 * NBUF], F32)
    wrow = loc.sb("rm_wr" + tag, [1, D], F32)
    wbc = loc.sb("rm_wb" + tag, [128, D], F32)
    ps_w = loc.ps("rm_pw" + tag, [128, D], F32)
    sdl0 = [x.n for x in sdl]
    sw0, sv0, sa0, so0, sf0, st0 = sw.n, sv.n, sa.n, so.n, sf.n, st.n
    blk = nc.Block()
    blk.__enter__()

    @blk.sync
    def _(sync):
        ph.begin(sync)
        sw.inc(sync.dma_start(out=wrow[:, :], in_=w_vec.rearrange("(o d) -> o d", o=1)), 16)
        for i in range(NT):
            b = i % NBUF
            if i >= NBUF:
                sync.wait_ge(sf.h, sf0 + (i - NBUF + 1))
            sdl[b].inc(sync.dma_start(out=xt[:, b * D:(b + 1) * D],
                                      in_=src[i * 128:(i + 1) * 128, :]), 16)
        for j in range(4):
            sync.wait_ge(sdl[j].h, sdl[j].n)
        sync.sem_inc(k.bar.h, 1)
    k.bar.n += 1

    @blk.tensor
    def _(t):
        ph.begin(t)
        t.wait_ge(sw.h, sw0 + 16)
        st.inc(t.matmul(ps_w[:, 0:512], k.ones_rowf[0:1, 0:128],
                        wrow[:, 0:512], start=True, stop=True), 1)
        st.inc(t.matmul(ps_w[:, 512:1024], k.ones_rowf[0:1, 0:128],
                        wrow[:, 512:1024], start=True, stop=True), 1)
        ph.done(_last_inc_t(t, k))

    @blk.scalar
    def _(s):
        ph.begin(s)
        for i in range(NT):
            b = i % NBUF
            c = 8 * b
            s.wait_ge(sdl[b].h, sdl0[b] + 16 * (i // NBUF + 1))
            s.activation(sq[:, :], xt[:, b * D:(b + 1) * D], AF.Square,
                         accum_out=cols[:, c:c + 1]).then_inc(sa.h, 1)
            s.wait_ge(sa.h, sa0 + 2 * i + 1)
            s.activation(cols[:, c + 1:c + 2], cols[:, c:c + 1], AF.Sqrt,
                         bias=k.consts[:, 1:2], scale=1.0 / D).then_inc(sa.h, 1)
        sa.n = sa0 + 2 * NT
        ph.done(_last_inc_a(s, k))

    @blk.vector
    def _(v):
        ph.begin(v)
        v.wait_ge(st.h, st0 + 2)
        v.tensor_copy(wbc[:, :], ps_w[:, :]).then_inc(sv.h, 1)
        for i in range(NT):
            b = i % NBUF
            c = 8 * b
            v.wait_ge(sa.h, sa0 + 2 * (i + 1))
            v.reciprocal(cols[:, c + 2:c + 3], cols[:, c + 1:c + 2]).then_inc(sv.h, 1)
            v.wait_ge(sv.h, sv0 + 2 * i + 2)
            v.scalar_tensor_tensor(tt[:, b * D:(b + 1) * D],
                                   xt[:, b * D:(b + 1) * D],
                                   cols[:, c + 2:c + 3], wbc[:, :],
                                   ALU.mult, ALU.mult).then_inc(sv.h, 1)
        sv.n = sv0 + 1 + 2 * NT
        ph.done(_last_inc(v, k))

    @blk.gpsimd
    def _(g):
        ph.begin(g)
        for i in range(NT):
            b = i % NBUF
            if i >= 1:
                g.wait_ge(so.h, so0 + 16 * i)
                g.sem_inc(sf.h, 1)
                sf.n += 1
            g.wait_ge(sv.h, sv0 + 2 * (i + 1) + 1)
            so.inc(g.dma_start(out=out_tm[i * 128:(i + 1) * 128, :],
                               in_=tt[:, b * D:(b + 1) * D]), 16)
        ph.done_q(g, so)
    blk.__exit__(None, None, None)
    loc.close()


def emit_routing_l1(k, t_fm, gate_w_f32, gates, tag):
    nc = k.nc
    loc = Loc(k)
    k.ps_anchor = loc.ps("q_anc" + tag, [1, 8], F32)
    gw_sb = loc.sb("q_gw" + tag, [E, D], BF16)
    gwT = loc.sb("q_gwT" + tag, [128, NF * E], BF16)
    tfm = loc.sb("q_t" + tag, [128, NF * 512], BF16)
    max8 = loc.sb("q_m8" + tag, [128, 8], F32)
    lgt = loc.sb("q_lg" + tag, [128, 2 * E], F32)
    idxu = loc.sb("q_iu" + tag, [128, 8], U32)
    cols = loc.sb("q_c" + tag, [128, 16], F32)
    oh0 = loc.sb("q_o0" + tag, [128, 8], F32)
    ps_lg = loc.ps("q_plg" + tag, [128, 1024], F32)
    ps_t = loc.ps("q_pt" + tag, [128, 128], BF16)
    sd = k.sem("q_d")
    st = k.sem("q_t")
    sv = k.sem("q_v")
    sa = k.sem("q_a")
    ph = Phase(k)
    sd0, st0, sv0, sa0 = sd.n, st.n, sv.n, sa.n
    NV = 13

    blk = nc.Block()
    blk.__enter__()

    sdg = k.sem("q_dg")
    sdg0 = sdg.n

    @blk.gpsimd
    def _(g):
        ph.begin(g)
        sdg.inc(g.dma_start(out=gw_sb[:, :], in_=gate_w_f32[:, :]), 16)

    @blk.sync
    def _(sync):
        ph.begin(sync)
        for w in range(NT // 4):
            if w >= 1:
                sync.wait_ge(st.h, st0 + NF + 4 * w)
            sd.inc(sync.dma_start(
                out=tfm[:, :].rearrange("p (c t) -> p c t", c=NF),
                in_=t_fm[:, w * 512:(w + 1) * 512].rearrange("(c p) t -> p c t", p=128)), 16)
        ph.done_q(sync, sd)

    @blk.tensor
    def _(t):
        ph.begin(t)
        t.wait_ge(sdg.h, sdg0 + 16)
        for c in range(NF):
            if c >= 1:
                t.wait_ge(sv.h, sv0 + c)
            st.inc(t.transpose(ps_t[:, 0:E], gw_sb[:, c * 128:(c + 1) * 128],
                               k.identb[0:E, 0:E]), 1)
        for i in range(NT):
            w, j = divmod(i, 4)
            b = i % 2
            t.wait_ge(sd.h, sd0 + 16 * (w + 1))
            t.wait_ge(sv.h, sv0 + NF)
            if i >= 2:
                t.wait_ge(sv.h, sv0 + NF + NV * (i - 2) + 1)
            ins = None
            for c in range(NF):
                ins = t.matmul(ps_lg[:, b * 512:b * 512 + E],
                               tfm[:, c * 512 + j * 128: c * 512 + (j + 1) * 128],
                               gwT[:, c * E:(c + 1) * E],
                               start=(c == 0), stop=(c == NF - 1))
            st.inc(ins, 1)
        ph.done(_last_inc_t(t, k))

    @blk.vector
    def _(v):
        ph.begin(v)
        for c in range(NF):
            v.wait_ge(st.h, st0 + c + 1)
            v.tensor_copy(gwT[:, c * E:(c + 1) * E], ps_t[:, 0:E]).then_inc(sv.h, 1)
        sv.n = sv0 + NF
        base0 = sv.n
        for i in range(NT):
            b = i % 2
            base = base0 + NV * i
            v.wait_ge(st.h, st0 + NF + i + 1)
            v.tensor_copy(lgt[:, b * E:(b + 1) * E], ps_lg[:, b * 512:b * 512 + E]).then_inc(sv.h, 1)
            v.wait_ge(sv.h, base + 1)
            v.max(max8[:, :], lgt[:, b * E:(b + 1) * E]).then_inc(sv.h, 1)
            v.wait_ge(sv.h, base + 2)
            v.max_index(idxu[:, :], max8[:, :], lgt[:, b * E:(b + 1) * E]).then_inc(sv.h, 1)
            v.wait_ge(sv.h, base + 3)
            v.tensor_copy(cols[:, 0:2], idxu[:, 0:2]).then_inc(sv.h, 1)
            v.tensor_tensor(cols[:, 2:3], max8[:, 1:2], max8[:, 0:1],
                            ALU.subtract).then_inc(sv.h, 1)
            v.wait_ge(sa.h, sa0 + i + 1)
            v.tensor_scalar_add(cols[:, 4:5], cols[:, 3:4], 1.0).then_inc(sv.h, 1)
            v.wait_ge(sv.h, base + 6)
            v.reciprocal(cols[:, 5:6], cols[:, 4:5]).then_inc(sv.h, 1)
            v.wait_ge(sv.h, base + 7)
            v.tensor_scalar(cols[:, 6:7], cols[:, 5:6], -0.5, 0.5,
                            ALU.mult, ALU.add).then_inc(sv.h, 1)
            v.tensor_scalar_mul(cols[:, 5:6], cols[:, 5:6], 0.5).then_inc(sv.h, 1)
            v.wait_ge(sv.h, base + 4)
            v.tensor_single_scalar(oh0[:, :], k.egrid[:, :], cols[:, 0:1],
                                   ALU.is_equal).then_inc(sv.h, 1)
            v.tensor_single_scalar(gates[:, i * E:(i + 1) * E], k.egrid[:, :],
                                   cols[:, 1:2], ALU.is_equal).then_inc(sv.h, 1)
            v.wait_ge(sv.h, base + 11)
            v.tensor_scalar_mul(gates[:, i * E:(i + 1) * E],
                                gates[:, i * E:(i + 1) * E],
                                cols[:, 6:7]).then_inc(sv.h, 1)
            v.wait_ge(sv.h, base + 12)
            v.scalar_tensor_tensor(gates[:, i * E:(i + 1) * E],
                                   oh0[:, :], cols[:, 5:6],
                                   gates[:, i * E:(i + 1) * E],
                                   ALU.mult, ALU.add).then_inc(sv.h, 1)
        sv.n = base0 + NV * NT
        ph.done(_last_inc(v, k))

    @blk.scalar
    def _(s):
        ph.begin(s)
        for i in range(NT):
            s.wait_ge(sv.h, sv0 + NF + NV * i + 5)
            s.activation(cols[:, 3:4], cols[:, 2:3], AF.Exp).then_inc(sa.h, 1)
        sa.n = sa0 + NT
        ph.done(_last_inc_a(s, k))
    blk.__exit__(None, None, None)
    loc.close()


def emit_l0attn(k, q_fm, k_fm, v_tm, a_fm, tag):
    nc = k.nc
    G = 204
    L = S - G
    NLC = (L + 127) // 128
    lastc = L - 128 * (NLC - 1)
    NH = 16
    BH = NBATCH * NH
    QSZ = [128, G - 128]

    loc = Loc(k)
    qg = loc.sb("la_qg" + tag, [64, 256], BF16)
    kg = loc.sb("la_kg" + tag, [64, 256], BF16)
    ql = loc.sb("la_ql" + tag, [64, L], BF16)
    kl = loc.sb("la_kl" + tag, [64, L], BF16)
    klsm = loc.sb("la_ks" + tag, [64, L], BF16)
    klsmT = loc.sb("la_kT" + tag, [128, NLC * 64], BF16)
    vg = loc.sb("la_vg" + tag, [128, 2 * 64], BF16)
    vl = loc.sb("la_vl" + tag, [128, NLC * 64], BF16)
    pg = loc.sb("la_pg" + tag, [128, 2 * 256], BF16)
    pgT = loc.sb("la_pT" + tag, [128, 2 * 204], BF16)
    kv_sb = loc.sb("la_kv" + tag, [64, 64], BF16)
    og = loc.sb("la_og" + tag, [64, 256], BF16)
    lo = loc.sb("la_lo" + tag, [64, L], BF16)
    cols = loc.sb("la_c" + tag, [128, 8], F32)
    k.ps_anchor = loc.ps("la_anc" + tag, [1, 8], F32)
    ps_g = loc.ps("la_psg" + tag, [128, 256], F32)
    ps_t = loc.ps("la_pst" + tag, [128, 128], BF16)
    ps_kv = loc.ps("la_psk" + tag, [64, 64], F32)
    ps_o = loc.ps("la_pso" + tag, [64, 512], F32)

    sld = k.sem("la_d")
    st = k.sem("la_t")
    sv = k.sem("la_v")
    sa = k.sem("la_a")
    sst = k.sem("la_s")
    sstf = k.sem("la_sf")
    ph = Phase(k)
    sld0, st0, sv0, sa0, sst0, sstf0 = sld.n, st.n, sv.n, sa.n, sst.n, sstf.n

    ST_BH = 2 + 4 + 1 + NLC + 1 + 4
    SV_BH = 8 + 4 + 3 + NLC + 1
    SA_BH = 2 + 1 + 1 + 4
    SLD_BH = 7 * 16
    SLD_BH = 8 * 16
    SST_BH = 2 * 16

    def stb(bh):
        return st0 + bh * ST_BH

    def svb(bh):
        return sv0 + bh * SV_BH

    def sab(bh):
        return sa0 + bh * SA_BH

    blk = nc.Block()
    blk.__enter__()

    @blk.sync
    def _(sync):
        ph.begin(sync)
        for bh in range(BH):
            b, h = divmod(bh, NH)
            r0, c0 = h * 64, b * S
            sync.wait_ge(st.h, stb(bh))
            sld.inc(sync.dma_start(out=qg[:, 0:G], in_=q_fm[r0:r0 + 64, c0:c0 + G]), 16)
            sld.inc(sync.dma_start(out=kg[:, 0:G], in_=k_fm[r0:r0 + 64, c0:c0 + G]), 16)
            sld.inc(sync.dma_start(out=ql[:, :], in_=q_fm[r0:r0 + 64, c0 + G:c0 + S]), 16)
            sld.inc(sync.dma_start(out=kl[:, :], in_=k_fm[r0:r0 + 64, c0 + G:c0 + S]), 16)
            sld.inc(sync.dma_start(out=vg[:, 0:64], in_=v_tm[c0:c0 + 128, r0:r0 + 64]), 16)
            sld.inc(sync.dma_start(out=vg[0:G - 128, 64:128],
                                   in_=v_tm[c0 + 128:c0 + G, r0:r0 + 64]), 16)
            sld.inc(sync.dma_start(
                out=vl[:, 0:(NLC - 1) * 64].rearrange("p (j d) -> p j d", d=64),
                in_=v_tm[c0 + G:c0 + G + (NLC - 1) * 128, r0:r0 + 64]
                    .rearrange("(j p) d -> p j d", p=128)), 16)
            sld.inc(sync.dma_start(out=vl[0:lastc, (NLC - 1) * 64:NLC * 64],
                                   in_=v_tm[c0 + G + (NLC - 1) * 128:c0 + S,
                                            r0:r0 + 64]), 16)
        ph.done_q(sync, sld)

    @blk.tensor
    def _(t):
        ph.begin(t)
        for bh in range(BH):
            t.wait_ge(sld.h, sld0 + SLD_BH * (bh + 1))
            if bh >= 1:
                t.wait_ge(sv.h, svb(bh))
                t.wait_ge(sa.h, sab(bh))
            for q2 in range(2):
                sz = QSZ[q2]
                st.inc(t.matmul(ps_g[0:sz, 0:G], qg[:, q2 * 128:q2 * 128 + sz],
                                kg[:, 0:G], start=True, stop=True), 1)
                t.wait_ge(sv.h, svb(bh) + 4 * q2 + 4)
            for q2 in range(2):
                sz = QSZ[q2]
                for kc in range(2):
                    cnt = 128 if kc == 0 else G - 128
                    t.wait_ge(sv.h, svb(bh) + 8 + 2 * q2 + kc)
                    st.inc(t.transpose(ps_t[0:cnt, 0:sz],
                                       pg[0:sz, q2 * 256 + kc * 128:
                                          q2 * 256 + kc * 128 + cnt],
                                       k.identb[0:sz, 0:sz]), 1)
            t.wait_ge(sv.h, svb(bh) + 12)
            t.matmul(ps_o[:, 0:G], vg[:, 0:64], pgT[:, 0:G],
                     start=True, stop=False)
            st.inc(t.matmul(ps_o[:, 0:G], vg[0:G - 128, 64:128],
                            pgT[0:G - 128, 204:204 + G], start=False,
                            stop=True), 1)
            for j in range(NLC):
                cnt = 128 if j < NLC - 1 else lastc
                t.wait_ge(sv.h, svb(bh) + 15 + j)
                st.inc(t.transpose(ps_t[0:cnt, 0:64],
                                   klsm[:, j * 128:j * 128 + cnt],
                                   k.identb[0:64, 0:64]), 1)
            t.wait_ge(sv.h, svb(bh) + 15 + NLC)
            ins = None
            for j in range(NLC):
                cnt = 128 if j < NLC - 1 else lastc
                ins = t.matmul(ps_kv[:, :], vl[0:cnt, j * 64:(j + 1) * 64],
                               klsmT[0:cnt, j * 64:(j + 1) * 64],
                               start=(j == 0), stop=(j == NLC - 1))
            st.inc(ins, 1)
            t.wait_ge(sv.h, svb(bh) + SV_BH)
            off = 0
            for w in range(4):
                ww = 512 if w < 3 else L - 1536
                if w >= 1:
                    t.wait_ge(sa.h, sab(bh) + 4 + w)
                else:
                    t.wait_ge(sa.h, sab(bh) + 3)
                st.inc(t.matmul(ps_o[:, 0:ww], kv_sb[:, :],
                                ql[:, off:off + ww], start=True, stop=True), 1)
                off += ww
        ph.done(_last_inc_t(t, k))

    @blk.vector
    def _(v):
        ph.begin(v)
        for bh in range(BH):
            base = svb(bh)
            for q2 in range(2):
                sz = QSZ[q2]
                v.wait_ge(st.h, stb(bh) + q2 + 1)
                v.reduce_max(cols[0:sz, 0:1], ps_g[0:sz, 0:G], axis=AX.X,
                             negate=True).then_inc(sv.h, 1)
                v.wait_ge(sv.h, base + 4 * q2 + 1)
                v.tensor_scalar_mul(cols[0:sz, 1:2], cols[0:sz, 0:1], 0.125).then_inc(sv.h, 1)
                v.wait_ge(sa.h, sab(bh) + q2 + 1)
                v.reciprocal(cols[0:sz, 3:4], cols[0:sz, 2:3]).then_inc(sv.h, 1)
                v.wait_ge(sv.h, base + 4 * q2 + 3)
                v.tensor_scalar_mul(pg[0:sz, q2 * 256:q2 * 256 + G],
                                    pg[0:sz, q2 * 256:q2 * 256 + G],
                                    cols[0:sz, 3:4]).then_inc(sv.h, 1)
            for q2 in range(2):
                sz = QSZ[q2]
                for kc in range(2):
                    cnt = 128 if kc == 0 else G - 128
                    v.wait_ge(st.h, stb(bh) + 2 + 2 * q2 + kc + 1)
                    v.tensor_copy(pgT[0:cnt, kc * 204 + q2 * 128:
                                      kc * 204 + q2 * 128 + sz],
                                  ps_t[0:cnt, 0:sz]).then_inc(sv.h, 1)
            v.wait_ge(sld.h, sld0 + SLD_BH * (bh + 1))
            v.reduce_max(cols[0:64, 4:5], kl[:, :], axis=AX.X, negate=True).then_inc(sv.h, 1)
            v.wait_ge(sa.h, sab(bh) + 4)
            v.reciprocal(cols[0:64, 6:7], cols[0:64, 5:6]).then_inc(sv.h, 1)
            v.wait_ge(sv.h, base + 14)
            v.tensor_scalar_mul(klsm[:, :], klsm[:, :], cols[0:64, 6:7]).then_inc(sv.h, 1)
            for j in range(NLC):
                cnt = 128 if j < NLC - 1 else lastc
                v.wait_ge(st.h, stb(bh) + 7 + j + 1)
                v.tensor_copy(klsmT[0:cnt, j * 64:(j + 1) * 64],
                              ps_t[0:cnt, 0:64]).then_inc(sv.h, 1)
            v.wait_ge(st.h, stb(bh) + 7 + NLC + 1)
            v.tensor_copy(kv_sb[:, :], ps_kv[:, :]).then_inc(sv.h, 1)
        sv.n = sv0 + BH * SV_BH
        ph.done(_last_inc(v, k))

    @blk.scalar
    def _(s):
        ph.begin(s)
        for bh in range(BH):
            base = sab(bh)
            for q2 in range(2):
                sz = QSZ[q2]
                s.wait_ge(sv.h, svb(bh) + 4 * q2 + 2)
                s.activation(pg[0:sz, q2 * 256:q2 * 256 + G], ps_g[0:sz, 0:G],
                             AF.Exp, bias=cols[0:sz, 1:2], scale=0.125,
                             accum_out=cols[0:sz, 2:3]).then_inc(sa.h, 1)
            s.wait_ge(st.h, stb(bh) + 7)
            if bh >= 1:
                s.wait_ge(sstf.h, sstf0 + 2 * bh - 1)
            s.activation(og[:, 0:G], ps_o[:, 0:G], AF.Copy,
                         scale=1.0).then_inc(sa.h, 1)
            s.wait_ge(sv.h, svb(bh) + 13)
            s.activation(klsm[:, :], kl[:, :], AF.Exp, bias=cols[0:64, 4:5],
                         accum_out=cols[0:64, 5:6]).then_inc(sa.h, 1)
            if bh >= 1:
                s.wait_ge(sstf.h, sstf0 + 2 * bh)
            off = 0
            for w in range(4):
                ww = 512 if w < 3 else L - 1536
                s.wait_ge(st.h, stb(bh) + 7 + NLC + 1 + w + 1)
                s.activation(lo[:, off:off + ww], ps_o[:, 0:ww], AF.Copy,
                             scale=1.0).then_inc(sa.h, 1)
                off += ww
        sa.n = sa0 + BH * SA_BH
        ph.done(_last_inc_a(s, k))

    @blk.gpsimd
    def _(g):
        ph.begin(g)
        for bh in range(BH):
            b, h = divmod(bh, NH)
            r0, c0 = h * 64, b * S
            for snum in range(2):
                gw = bh * 2 + snum
                if gw >= 1:
                    g.wait_ge(sst.h, sst.n)
                    g.sem_inc(sstf.h, 1)
                    sstf.n += 1
                if snum == 0:
                    g.wait_ge(sa.h, sab(bh) + 3)
                    sst.inc(g.dma_start(out=a_fm[r0:r0 + 64, c0:c0 + G],
                                        in_=og[:, 0:G]), 16)
                else:
                    g.wait_ge(sa.h, sab(bh) + SA_BH)
                    sst.inc(g.dma_start(out=a_fm[r0:r0 + 64, c0 + G:c0 + S],
                                        in_=lo[:, :]), 16)
        ph.done_q(g, sst)
    blk.__exit__(None, None, None)
    loc.close()


def emit_cast(k, dst, src, rows, tag, nsplit=4):
    ph = Phase(k)
    sg = k.sem("wc_g")
    blk = k.nc.Block()
    blk.__enter__()

    @blk.gpsimd
    def _(g):
        ph.begin(g)
        step = rows // nsplit
        for i in range(nsplit):
            sg.inc(g.dma_start(out=dst[i * step:(i + 1) * step, :],
                               in_=src[i * step:(i + 1) * step, :]), 16)
        ph.done_q(g, sg)
    blk.__exit__(None, None, None)


def emit_transpose_mat(k, dst, src, R, C, tag):
    ph = Phase(k)
    sdl = [k.sem("wt_d%d" % j) for j in range(2)]
    so = k.sem("wt_o")
    sf = k.sem("wt_f")
    loc = Loc(k)
    buf = loc.sb("wt_b" + tag, [128, 2 * R], BF16)
    sdl0 = [x.n for x in sdl]
    so0, sf0 = so.n, sf.n
    nch = C // 128
    blk = k.nc.Block()
    blk.__enter__()

    @blk.sync
    def _(sync):
        ph.begin(sync)
        for c in range(nch):
            b = c % 2
            if c >= 2:
                sync.wait_ge(sf.h, sf0 + (c - 1))
            sdl[b].inc(sync.dma_start(out=buf[:, b * R:(b + 1) * R],
                                      in_=src[:, c * 128:(c + 1) * 128],
                                      transpose=True), 16)
        for j in range(2):
            sync.wait_ge(sdl[j].h, sdl[j].n)
        sync.sem_inc(k.bar.h, 1)
    k.bar.n += 1

    @blk.gpsimd
    def _(g):
        ph.begin(g)
        for c in range(nch):
            b = c % 2
            if c >= 1:
                g.wait_ge(so.h, so0 + 16 * c)
                g.sem_inc(sf.h, 1)
                sf.n += 1
            g.wait_ge(sdl[b].h, sdl0[b] + 16 * (c // 2 + 1))
            so.inc(g.dma_start(out=dst[c * 128:(c + 1) * 128, :],
                               in_=buf[:, b * R:(b + 1) * R]), 16)
        ph.done_q(g, so)
    blk.__exit__(None, None, None)
    loc.close()


def emit_rotate(k, dst, src, R, tag):
    ph = Phase(k)
    sdl = [k.sem("wr_d%d" % j) for j in range(2)]
    sv = k.sem("wr_v")
    so = k.sem("wr_o")
    sf = k.sem("wr_f")
    loc = Loc(k)
    bin_ = loc.sb("wr_i" + tag, [128, 2 * D], BF16)
    bout = loc.sb("wr_o" + tag, [128, 2 * D], BF16)
    sdl0 = [x.n for x in sdl]
    sv0, so0, sf0 = sv.n, so.n, sf.n
    nch = R // 128
    blk = k.nc.Block()
    blk.__enter__()

    @blk.sync
    def _(sync):
        ph.begin(sync)
        for c in range(nch):
            b = c % 2
            if c >= 2:
                sync.wait_ge(sv.h, sv0 + 2 * (c - 1))
            sdl[b].inc(sync.dma_start(out=bin_[:, b * D:(b + 1) * D],
                                      in_=src[c * 128:(c + 1) * 128, :]), 16)
        for j in range(2):
            sync.wait_ge(sdl[j].h, sdl[j].n)
        sync.sem_inc(k.bar.h, 1)
    k.bar.n += 1

    @blk.vector
    def _(v):
        ph.begin(v)
        for c in range(nch):
            b = c % 2
            v.wait_ge(sdl[b].h, sdl0[b] + 16 * (c // 2 + 1))
            if c >= 2:
                v.wait_ge(sf.h, sf0 + c - 1)
            iv = bin_[:, b * D:(b + 1) * D].rearrange("p (x two c) -> p x two c", two=2, c=32)
            ov = bout[:, b * D:(b + 1) * D].rearrange("p (x two c) -> p x two c", two=2, c=32)
            v.tensor_scalar_mul(ov[:, :, 0, :], iv[:, :, 1, :], -1.0).then_inc(sv.h, 1)
            v.wait_ge(sv.h, sv0 + 2 * c + 1)
            v.tensor_scalar_mul(ov[:, :, 1, :], iv[:, :, 0, :], 1.0).then_inc(sv.h, 1)
        sv.n = sv0 + 2 * nch
        ph.done(_last_inc(v, k))

    @blk.gpsimd
    def _(g):
        ph.begin(g)
        for c in range(nch):
            b = c % 2
            if c >= 1:
                g.wait_ge(so.h, so0 + 16 * c)
                g.sem_inc(sf.h, 1)
                sf.n += 1
            g.wait_ge(sv.h, sv0 + 2 * (c + 1))
            so.inc(g.dma_start(out=dst[c * 128:(c + 1) * 128, :],
                               in_=bout[:, b * D:(b + 1) * D]), 16)
        ph.done_q(g, so)
    blk.__exit__(None, None, None)
    loc.close()


def build_prep():
    k = K("prep")
    nc = k.nc
    f = lambda n, s: k.dram(n, s, F32, kind="ExternalInput")
    o = lambda n, s: k.dram(n, s, BF16, kind="ExternalOutput")
    i_ = {}
    for n, shape in [("L0q", [D, D]), ("L0k", [D, D]), ("L0v", [D, D]),
                     ("L0o", [D, D]), ("A1q", [D, D]), ("A1kv", [D, 512]),
                     ("A1o", [D, D]),
                     ("F0w1", [E, D, H_FFN]), ("F0w2", [E, H_FFN, D]),
                     ("L1q", [D, D]), ("L1k", [D, D]), ("L1v", [D, D]),
                     ("L1o", [D, D]),
                     ("F1w1", [E, H_FFN, D]), ("F1w2", [E, D, H_FFN])]:
        i_[n] = f(n, shape)
    outs = {}
    for n, shape in [("L0q_b", [D, D]), ("L0qr_b", [D, D]), ("L0k_b", [D, D]),
                     ("L0kr_b", [D, D]), ("L0v_b", [D, D]), ("L0o_b", [D, D]),
                     ("A1q_b", [D, D]), ("A1kv_b", [D, 512]), ("A1o_b", [D, D]),
                     ("F0w1_b", [E, D, H_FFN]), ("F0w2_b", [E, H_FFN, D]),
                     ("L1qT_b", [D, D]), ("L1qTr_b", [D, D]),
                     ("L1kT_b", [D, D]), ("L1kTr_b", [D, D]),
                     ("L1vT_b", [D, D]), ("L1oT_b", [D, D]),
                     ("F1w1T_b", [E, D, H_FFN]), ("F1w2T_b", [E, H_FFN, D])]:
        outs[n] = o(n, shape)
    tmp = {n: k.dram("tmp_" + n, shape, BF16) for n, shape in
           [("L1q", [D, D]), ("L1k", [D, D]), ("L1v", [D, D]), ("L1o", [D, D]),
            ("F1w1", [E, H_FFN, D]), ("F1w2", [E, D, H_FFN])]}
    common_init(k)
    for n in ["L0q", "L0k", "L0v", "L0o", "A1q", "A1kv", "A1o"]:
        emit_cast(k, outs[n + "_b"], i_[n], D, "c" + n)
    for e in range(E):
        emit_cast(k, outs["F0w1_b"][e], i_["F0w1"][e], D, "cF0w1%d" % e, nsplit=2)
        emit_cast(k, outs["F0w2_b"][e], i_["F0w2"][e], H_FFN, "cF0w2%d" % e, nsplit=2)
    for n in ["L1q", "L1k", "L1v", "L1o"]:
        emit_cast(k, tmp[n], i_[n], D, "c" + n)
        emit_transpose_mat(k, outs[n + "T_b"], tmp[n], D, D, "t" + n)
    for e in range(E):
        emit_cast(k, tmp["F1w1"][e], i_["F1w1"][e], H_FFN, "cF1w1%d" % e, nsplit=2)
        emit_transpose_mat(k, outs["F1w1T_b"][e], tmp["F1w1"][e], H_FFN, D,
                           "tF1w1%d" % e)
        emit_cast(k, tmp["F1w2"][e], i_["F1w2"][e], D, "cF1w2%d" % e, nsplit=2)
        emit_transpose_mat(k, outs["F1w2T_b"][e], tmp["F1w2"][e], D, H_FFN,
                           "tF1w2%d" % e)
    emit_rotate(k, outs["L0qr_b"], outs["L0q_b"], D, "rL0q")
    emit_rotate(k, outs["L0kr_b"], outs["L0k_b"], D, "rL0k")
    emit_rotate(k, outs["L1qTr_b"], outs["L1qT_b"], D, "rL1q")
    emit_rotate(k, outs["L1kTr_b"], outs["L1kT_b"], D, "rL1k")
    return k


def build_s_l0ffn():
    k = K("s_l0ffn")
    xin = k.dram("xin", [T, D], F32, kind="ExternalInput")
    wg = k.dram("wg", [D, E], F32, kind="ExternalInput")
    w1b = k.dram("w1b", [E, D, H_FFN], BF16, kind="ExternalInput")
    w2b = k.dram("w2b", [E, H_FFN, D], BF16, kind="ExternalInput")
    b1 = k.dram("b1", [E, H_FFN], F32, kind="ExternalInput")
    b2 = k.dram("b2", [E, D], F32, kind="ExternalInput")
    xout = k.dram("xout", [T, D], F32, kind="ExternalOutput")
    t_tm = k.dram("t_tm", [T, D], BF16)
    t_fm = k.dram("t_fm", [D, T], BF16)
    common_init(k)
    gates = k.sb("gates", [128, NT * E], F32)
    emit_copy(k, xout, xin, T, D, "x")
    emit_ln(k, xin, t_tm, "0")
    emit_tf(k, t_tm, t_fm, "0")
    emit_routing_l0(k, t_fm, wg, gates, "0")
    emit_ffn(k, t_fm, w1b, w2b, b1, b2, gates, xout, "0")
    return k


def build_s_l0attn():
    k = K("s_l0attn")
    xin = k.dram("xin", [T, D], F32, kind="ExternalInput")
    wq = k.dram("wq", [D, D], BF16, kind="ExternalInput")
    wqr = k.dram("wqr", [D, D], BF16, kind="ExternalInput")
    wk = k.dram("wk", [D, D], BF16, kind="ExternalInput")
    wkr = k.dram("wkr", [D, D], BF16, kind="ExternalInput")
    wv = k.dram("wv", [D, D], BF16, kind="ExternalInput")
    wo = k.dram("wo", [D, D], BF16, kind="ExternalInput")
    xout = k.dram("xout", [T, D], F32, kind="ExternalOutput")
    t_tm = k.dram("t_tm", [T, D], BF16)
    t_fm = k.dram("t_fm", [D, T], BF16)
    q_fm = k.dram("q_fm", [D, T], BF16)
    k_fm = k.dram("k_fm", [D, T], BF16)
    v_tm = k.dram("v_tm", [T, D], BF16)
    a_fm = k.dram("a_fm", [D, T], BF16)
    common_init(k)
    rl = Loc(k)
    emit_rope_tables(k, rl)
    emit_copy(k, xout, xin, T, D, "x")
    emit_ln(k, xin, t_tm, "0")
    emit_tf(k, t_tm, t_fm, "0")
    emit_proj_fm(k, t_fm, wq.ap(), q_fm, D, "q", rope=True, w_rot=wqr.ap())
    emit_proj_fm(k, t_fm, wk.ap(), k_fm, D, "k", rope=True, w_rot=wkr.ap())
    emit_proj_tm(k, t_fm, wv.ap(), v_tm, D, "v")
    rl.close()
    emit_l0attn(k, q_fm, k_fm, v_tm, a_fm, "a")
    emit_proj_res(k, a_fm, wo.ap(), xout, "o", scale=1.0)
    return k


def build_s_l1ffn():
    k = K("s_l1ffn")
    xin = k.dram("xin", [T, D], F32, kind="ExternalInput")
    anw = k.dram("anw", [D], F32, kind="ExternalInput")
    wqT = k.dram("wqT", [D, D], BF16, kind="ExternalInput")
    wqTr = k.dram("wqTr", [D, D], BF16, kind="ExternalInput")
    wkT = k.dram("wkT", [D, D], BF16, kind="ExternalInput")
    wkTr = k.dram("wkTr", [D, D], BF16, kind="ExternalInput")
    wvT = k.dram("wvT", [D, D], BF16, kind="ExternalInput")
    woT = k.dram("woT", [D, D], BF16, kind="ExternalInput")
    mnw = k.dram("mnw", [D], F32, kind="ExternalInput")
    gw = k.dram("gw", [E, D], F32, kind="ExternalInput")
    w1T = k.dram("w1T", [E, D, H_FFN], BF16, kind="ExternalInput")
    w2T = k.dram("w2T", [E, H_FFN, D], BF16, kind="ExternalInput")
    b1 = k.dram("b1", [E, H_FFN], F32, kind="ExternalInput")
    b2 = k.dram("b2", [E, D], F32, kind="ExternalInput")
    xout = k.dram("xout", [T, D], F32, kind="ExternalOutput")
    t_tm = k.dram("t_tm", [T, D], BF16)
    t1_tm = k.dram("t1_tm", [T, D], BF16)
    t1_fm = k.dram("t1_fm", [D, T], BF16)
    q_fm = k.dram("q_fm", [D, T], BF16)
    k_fm = k.dram("k_fm", [D, T], BF16)
    v_tm = k.dram("v_tm", [T, D], BF16)
    a_fm = k.dram("a_fm", [D, T], BF16)
    h_int = k.dram("h_int", [T, D], F32)
    t2_tm = k.dram("t2_tm", [T, D], BF16)
    t2_fm = k.dram("t2_fm", [D, T], BF16)
    common_init(k)
    gates = k.sb("gates", [128, NT * E], F32)
    rl = Loc(k)
    emit_rope_tables(k, rl)
    emit_copy(k, xout, xin, T, D, "x")
    emit_ln(k, xin, t_tm, "0")
    emit_rmsnorm(k, t_tm, anw.ap(), t1_tm, "r1", src_dt=BF16)
    emit_tf(k, t1_tm, t1_fm, "1")
    emit_proj_fm(k, t1_fm, wqT.ap(), q_fm, D, "q", rope=True, w_rot=wqTr.ap())
    emit_proj_fm(k, t1_fm, wkT.ap(), k_fm, D, "k", rope=True, w_rot=wkTr.ap())
    emit_proj_tm(k, t1_fm, wvT.ap(), v_tm, D, "v")
    rl.close()
    emit_attn(k, q_fm, k_fm, v_tm, a_fm, "a", causal=True, alibi=False,
              kv_group=1)
    emit_proj_res(k, a_fm, woT.ap(), xout, "o", scale=0.5, t_add=t_tm,
                  h_out=h_int)
    emit_rmsnorm(k, h_int, mnw.ap(), t2_tm, "r2", src_dt=F32)
    emit_tf(k, t2_tm, t2_fm, "2")
    emit_routing_l1(k, t2_fm, gw, gates, "g")
    emit_ffn(k, t2_fm, w1T, w2T, b1, b2, gates, xout, "f")
    return k


def build_s_l1attn():
    k = K("s_l1attn")
    xin = k.dram("xin", [T, D], F32, kind="ExternalInput")
    wq = k.dram("wq", [D, D], BF16, kind="ExternalInput")
    wkv = k.dram("wkv", [D, 512], BF16, kind="ExternalInput")
    wo = k.dram("wo", [D, D], BF16, kind="ExternalInput")
    xout = k.dram("xout", [T, D], F32, kind="ExternalOutput")
    t_tm = k.dram("t_tm", [T, D], BF16)
    t_fm = k.dram("t_fm", [D, T], BF16)
    q_fm = k.dram("q_fm", [D, T], BF16)
    k_fm = k.dram("k_fm", [256, T], BF16)
    v_tm = k.dram("v_tm", [T, 256], BF16)
    a_fm = k.dram("a_fm", [D, T], BF16)
    common_init(k)
    emit_copy(k, xout, xin, T, D, "x")
    emit_ln(k, xin, t_tm, "0")
    emit_tf(k, t_tm, t_fm, "0")
    emit_proj_fm(k, t_fm, wq.ap(), q_fm, D, "q")
    emit_proj_fm(k, t_fm, wkv.ap()[:, 0:256], k_fm, 256, "k")
    emit_proj_tm(k, t_fm, wkv.ap()[:, 256:512], v_tm, 256, "v")
    emit_attn(k, q_fm, k_fm, v_tm, a_fm, "a", causal=False, alibi=True,
              kv_group=4)
    emit_proj_res(k, a_fm, wo.ap(), xout, "o", scale=1.0)
    return k


_PROGS = {}


def _prog(name, builder):
    if name not in _PROGS:
        _PROGS[name] = builder()
    return _PROGS[name]


def kernel(**inputs):
    return kernel_fused(**inputs)


def kernel_multilaunch(**inputs):
    x = np.ascontiguousarray(inputs["L0_attn_x"], dtype=np.float32)
    shards = [np.ascontiguousarray(s) for s in x.reshape(8, T, D)]
    cores = list(range(8))

    def run(k, per_core):
        res = run_bass_kernel_spmd(k.nc, per_core, core_ids=cores)
        return res.results

    kp = _prog("prep", build_prep)
    pin = dict(
        L0q=inputs["L0_attn_Wq"], L0k=inputs["L0_attn_Wk"],
        L0v=inputs["L0_attn_Wv"], L0o=inputs["L0_attn_Wout"],
        A1q=inputs["L1_attn_Wq"], A1kv=inputs["L1_attn_Wkv"],
        A1o=inputs["L1_attn_Wo"],
        F0w1=inputs["L0_ffn_W1"], F0w2=inputs["L0_ffn_W2"],
        L1q=inputs["L1_ffn_wq"], L1k=inputs["L1_ffn_wk"],
        L1v=inputs["L1_ffn_wv"], L1o=inputs["L1_ffn_wo"],
        F1w1=inputs["L1_ffn_w1"], F1w2=inputs["L1_ffn_w2"])
    pin = {k2: np.ascontiguousarray(v, dtype=np.float32) for k2, v in pin.items()}
    pres = run(kp, [pin] * 8)
    W = pres[0]

    kf0 = _prog("l0ffn", build_s_l0ffn)
    ka0 = _prog("l0attn", build_s_l0attn)
    kf1 = _prog("l1ffn", build_s_l1ffn)
    ka1 = _prog("l1attn", build_s_l1attn)

    f0_in = dict(wg=np.ascontiguousarray(inputs["L0_ffn_wg"], np.float32),
                 w1b=W["F0w1_b"], w2b=W["F0w2_b"],
                 b1=np.ascontiguousarray(inputs["L0_ffn_b1"], np.float32),
                 b2=np.ascontiguousarray(inputs["L0_ffn_b2"], np.float32))
    a0_in = dict(wq=W["L0q_b"], wqr=W["L0qr_b"], wk=W["L0k_b"],
                 wkr=W["L0kr_b"], wv=W["L0v_b"], wo=W["L0o_b"])
    f1_in = dict(anw=np.ascontiguousarray(inputs["L1_ffn_attn_norm_w"], np.float32),
                 wqT=W["L1qT_b"], wqTr=W["L1qTr_b"], wkT=W["L1kT_b"],
                 wkTr=W["L1kTr_b"], wvT=W["L1vT_b"], woT=W["L1oT_b"],
                 mnw=np.ascontiguousarray(inputs["L1_ffn_moe_norm_w"], np.float32),
                 gw=np.ascontiguousarray(inputs["L1_ffn_gate_w"], np.float32),
                 w1T=W["F1w1T_b"], w2T=W["F1w2T_b"],
                 b1=np.ascontiguousarray(inputs["L1_ffn_b1"], np.float32),
                 b2=np.ascontiguousarray(inputs["L1_ffn_b2"], np.float32))
    a1_in = dict(wq=W["A1q_b"], wkv=W["A1kv_b"], wo=W["A1o_b"])

    cur = shards
    for k2, extra in [(kf0, f0_in), (ka0, a0_in), (kf0, f0_in),
                      (kf1, f1_in), (ka1, a1_in), (kf1, f1_in)]:
        res = run(k2, [dict(xin=cur[c], **extra) for c in range(8)])
        cur = [res[c]["xout"] for c in range(8)]

    return np.concatenate([c[None] for c in cur], 0).reshape(16, S, D)


CAPC_L0 = 1024
CAPC_L1 = 1536


def emit_sub_l0ffn(k, xres, wg, w1b, w2b, b1, b2, gates, tag):
    t_tm = k.dram("t_tm" + tag, [T + 128, D], BF16)
    t_fm = k.dram("t_fm" + tag, [D, T], BF16)
    ig_gate = k.dram("ig_gate" + tag, [E * CAPC_L0], F32)
    ig_tok32 = k.dram("ig_tok32" + tag, [E * CAPC_L0], U32)
    emit_ln(k, xres, t_tm, tag)
    emit_tf(k, t_tm, t_fm, tag)
    emit_routing_l0(k, t_fm, wg, gates, tag)
    emit_listbuild(k, gates, ig_gate, ig_tok32, t_tm, tag, CAPC_L0, 1)
    emit_ffn_routed(k, t_tm, w1b, w2b, b1, b2, ig_gate, ig_tok32,
                    xres, tag, CAPC_L0)


def emit_sub_l0attn(k, xres, wq, wqr, wk, wkr, wv, wo, tag):
    t_tm = k.dram("t_tm" + tag, [T, D], BF16)
    t_fm = k.dram("t_fm" + tag, [D, T], BF16)
    q_fm = k.dram("q_fm" + tag, [D, T], BF16)
    k_fm = k.dram("k_fm" + tag, [D, T], BF16)
    v_tm = k.dram("v_tm" + tag, [T, D], BF16)
    a_fm = k.dram("a_fm" + tag, [D, T], BF16)
    rl = Loc(k)
    emit_rope_tables(k, rl)
    emit_ln(k, xres, t_tm, tag)
    emit_tf(k, t_tm, t_fm, tag)
    emit_proj_fm(k, t_fm, wq.ap(), q_fm, D, tag + "q", rope=True, w_rot=wqr.ap())
    emit_proj_fm(k, t_fm, wk.ap(), k_fm, D, tag + "k", rope=True, w_rot=wkr.ap())
    emit_proj_tm(k, t_fm, wv.ap(), v_tm, D, tag + "v")
    rl.close()
    emit_l0attn(k, q_fm, k_fm, v_tm, a_fm, tag)
    emit_proj_res(k, a_fm, wo.ap(), xres, tag + "o", scale=1.0)


def emit_sub_l1ffn(k, xres, anw, wqT, wqTr, wkT, wkTr, wvT, woT, mnw, gw,
                   w1T, w2T, b1, b2, gates, tag):
    t_tm = k.dram("t_tm" + tag, [T, D], BF16)
    t1_tm = k.dram("t1_tm" + tag, [T, D], BF16)
    t1_fm = k.dram("t1_fm" + tag, [D, T], BF16)
    q_fm = k.dram("q_fm" + tag, [D, T], BF16)
    k_fm = k.dram("k_fm" + tag, [D, T], BF16)
    v_tm = k.dram("v_tm" + tag, [T, D], BF16)
    a_fm = k.dram("a_fm" + tag, [D, T], BF16)
    h_int = k.dram("h_int" + tag, [T, D], F32)
    t2_tm = k.dram("t2_tm" + tag, [T + 128, D], BF16)
    t2_fm = k.dram("t2_fm" + tag, [D, T], BF16)
    rl = Loc(k)
    emit_rope_tables(k, rl)
    emit_ln(k, xres, t_tm, tag)
    emit_rmsnorm(k, t_tm, anw.ap(), t1_tm, tag + "r1", src_dt=BF16)
    emit_tf(k, t1_tm, t1_fm, tag + "1")
    emit_proj_fm(k, t1_fm, wqT.ap(), q_fm, D, tag + "q", rope=True,
                 w_rot=wqTr.ap())
    emit_proj_fm(k, t1_fm, wkT.ap(), k_fm, D, tag + "k", rope=True,
                 w_rot=wkTr.ap())
    emit_proj_tm(k, t1_fm, wvT.ap(), v_tm, D, tag + "v")
    rl.close()
    emit_attn(k, q_fm, k_fm, v_tm, a_fm, tag, causal=True, alibi=False,
              kv_group=1)
    emit_proj_res(k, a_fm, woT.ap(), xres, tag + "o", scale=0.5, t_add=t_tm,
                  h_out=h_int)
    emit_rmsnorm(k, h_int, mnw.ap(), t2_tm, tag + "r2", src_dt=F32)
    emit_tf(k, t2_tm, t2_fm, tag + "2")
    emit_routing_l1(k, t2_fm, gw, gates, tag)
    ig_gate = k.dram("ig_gate" + tag, [E * CAPC_L1], F32)
    ig_tok32 = k.dram("ig_tok32" + tag, [E * CAPC_L1], U32)
    emit_listbuild(k, gates, ig_gate, ig_tok32, t2_tm, tag, CAPC_L1, 2)
    emit_ffn_routed(k, t2_tm, w1T, w2T, b1, b2, ig_gate, ig_tok32,
                    xres, tag + "f", CAPC_L1)


def emit_sub_l1attn(k, xres, wq, wkv, wo, tag):
    t_tm = k.dram("t_tm" + tag, [T, D], BF16)
    t_fm = k.dram("t_fm" + tag, [D, T], BF16)
    q_fm = k.dram("q_fm" + tag, [D, T], BF16)
    k_fm = k.dram("k_fm" + tag, [256, T], BF16)
    v_tm = k.dram("v_tm" + tag, [T, 256], BF16)
    a_fm = k.dram("a_fm" + tag, [D, T], BF16)
    emit_ln(k, xres, t_tm, tag)
    emit_tf(k, t_tm, t_fm, tag)
    emit_proj_fm(k, t_fm, wq.ap(), q_fm, D, tag + "q")
    emit_proj_fm(k, t_fm, wkv.ap()[:, 0:256], k_fm, 256, tag + "k")
    emit_proj_tm(k, t_fm, wkv.ap()[:, 256:512], v_tm, 256, tag + "v")
    emit_attn(k, q_fm, k_fm, v_tm, a_fm, tag, causal=False, alibi=True,
              kv_group=4)
    emit_proj_res(k, a_fm, wo.ap(), xres, tag + "o", scale=1.0)


def build_fused():
    k = K("fused")
    f = lambda n, shp: k.dram(n, shp, F32, kind="ExternalInput")
    xin = f("L0_attn_x", [T, D])
    i_ = {n: f(n, shp) for n, shp in [
        ("L0_attn_Wq", [D, D]), ("L0_attn_Wk", [D, D]), ("L0_attn_Wv", [D, D]),
        ("L0_attn_Wout", [D, D]), ("L0_ffn_wg", [D, E]),
        ("L0_ffn_W1", [E, D, H_FFN]), ("L0_ffn_b1", [E, H_FFN]),
        ("L0_ffn_W2", [E, H_FFN, D]), ("L0_ffn_b2", [E, D]),
        ("L1_attn_Wq", [D, D]), ("L1_attn_Wkv", [D, 512]),
        ("L1_attn_Wo", [D, D]), ("L1_ffn_attn_norm_w", [D]),
        ("L1_ffn_wq", [D, D]), ("L1_ffn_wk", [D, D]), ("L1_ffn_wv", [D, D]),
        ("L1_ffn_wo", [D, D]), ("L1_ffn_moe_norm_w", [D]),
        ("L1_ffn_gate_w", [E, D]), ("L1_ffn_w1", [E, H_FFN, D]),
        ("L1_ffn_b1", [E, H_FFN]), ("L1_ffn_w2", [E, D, H_FFN]),
        ("L1_ffn_b2", [E, D])]}
    xout = k.dram("xout", [T, D], F32, kind="ExternalOutput")
    b = lambda n, shp: k.dram(n, shp, BF16, kind="Internal")
    W = {n: b(n, shp) for n, shp in [
        ("L0q_b", [D, D]), ("L0qr_b", [D, D]), ("L0k_b", [D, D]),
        ("L0kr_b", [D, D]), ("L0v_b", [D, D]), ("L0o_b", [D, D]),
        ("A1q_b", [D, D]), ("A1kv_b", [D, 512]), ("A1o_b", [D, D]),
        ("F0w1_b", [E, D, H_FFN]), ("F0w2_b", [E, H_FFN, D]),
        ("L1qT_b", [D, D]), ("L1qTr_b", [D, D]), ("L1kT_b", [D, D]),
        ("L1kTr_b", [D, D]), ("L1vT_b", [D, D]), ("L1oT_b", [D, D]),
        ("F1w1T_b", [E, D, H_FFN]), ("F1w2T_b", [E, H_FFN, D]),
        ("tmp_L1q", [D, D]), ("tmp_L1k", [D, D]), ("tmp_L1v", [D, D]),
        ("tmp_L1o", [D, D]), ("tmp_F1w1", [E, H_FFN, D]),
        ("tmp_F1w2", [E, D, H_FFN])]}
    common_init(k)
    gates = k.sb("gates", [128, NT * E], F32)
    for src, dst in [("L0_attn_Wq", "L0q_b"), ("L0_attn_Wk", "L0k_b"),
                     ("L0_attn_Wv", "L0v_b"), ("L0_attn_Wout", "L0o_b"),
                     ("L1_attn_Wq", "A1q_b"), ("L1_attn_Wkv", "A1kv_b"),
                     ("L1_attn_Wo", "A1o_b")]:
        emit_cast(k, W[dst], i_[src], D, "c" + dst)
    for e in range(E):
        emit_cast(k, W["F0w1_b"][e], i_["L0_ffn_W1"][e], D, "cf1%d" % e, nsplit=2)
        emit_cast(k, W["F0w2_b"][e], i_["L0_ffn_W2"][e], H_FFN, "cf2%d" % e, nsplit=2)
    for src, tm, dst in [("L1_ffn_wq", "tmp_L1q", "L1qT_b"),
                         ("L1_ffn_wk", "tmp_L1k", "L1kT_b"),
                         ("L1_ffn_wv", "tmp_L1v", "L1vT_b"),
                         ("L1_ffn_wo", "tmp_L1o", "L1oT_b")]:
        emit_cast(k, W[tm], i_[src], D, "c" + tm)
        emit_transpose_mat(k, W[dst], W[tm], D, D, "t" + dst)
    for e in range(E):
        emit_cast(k, W["tmp_F1w1"][e], i_["L1_ffn_w1"][e], H_FFN, "cg1%d" % e, nsplit=2)
        emit_transpose_mat(k, W["F1w1T_b"][e], W["tmp_F1w1"][e], H_FFN, D, "tg1%d" % e)
        emit_cast(k, W["tmp_F1w2"][e], i_["L1_ffn_w2"][e], D, "cg2%d" % e, nsplit=2)
        emit_transpose_mat(k, W["F1w2T_b"][e], W["tmp_F1w2"][e], D, H_FFN, "tg2%d" % e)
    emit_rotate(k, W["L0qr_b"], W["L0q_b"], D, "r0q")
    emit_rotate(k, W["L0kr_b"], W["L0k_b"], D, "r0k")
    emit_rotate(k, W["L1qTr_b"], W["L1qT_b"], D, "r1q")
    emit_rotate(k, W["L1kTr_b"], W["L1kT_b"], D, "r1k")
    xres = k.dram("xres_big", [T + 128, D], F32)
    emit_copy(k, xres, xin, T, D, "x")
    emit_sub_l0ffn(k, xres, i_["L0_ffn_wg"], W["F0w1_b"], W["F0w2_b"],
                   i_["L0_ffn_b1"], i_["L0_ffn_b2"], gates, "A")
    emit_sub_l0attn(k, xres, W["L0q_b"], W["L0qr_b"], W["L0k_b"], W["L0kr_b"],
                    W["L0v_b"], W["L0o_b"], "B")
    emit_sub_l0ffn(k, xres, i_["L0_ffn_wg"], W["F0w1_b"], W["F0w2_b"],
                   i_["L0_ffn_b1"], i_["L0_ffn_b2"], gates, "C")
    emit_sub_l1ffn(k, xres, i_["L1_ffn_attn_norm_w"], W["L1qT_b"],
                   W["L1qTr_b"], W["L1kT_b"], W["L1kTr_b"], W["L1vT_b"],
                   W["L1oT_b"], i_["L1_ffn_moe_norm_w"], i_["L1_ffn_gate_w"],
                   W["F1w1T_b"], W["F1w2T_b"], i_["L1_ffn_b1"],
                   i_["L1_ffn_b2"], gates, "D")
    emit_sub_l1attn(k, xres, W["A1q_b"], W["A1kv_b"], W["A1o_b"], "E")
    emit_sub_l1ffn(k, xres, i_["L1_ffn_attn_norm_w"], W["L1qT_b"],
                   W["L1qTr_b"], W["L1kT_b"], W["L1kTr_b"], W["L1vT_b"],
                   W["L1oT_b"], i_["L1_ffn_moe_norm_w"], i_["L1_ffn_gate_w"],
                   W["F1w1T_b"], W["F1w2T_b"], i_["L1_ffn_b1"],
                   i_["L1_ffn_b2"], gates, "F")
    emit_copy(k, xout, xres, T, D, "fin")
    return k


def kernel_fused(**inputs):
    x = np.ascontiguousarray(inputs["L0_attn_x"], dtype=np.float32)
    shards = [np.ascontiguousarray(s) for s in x.reshape(8, T, D)]
    k = _prog("fused", build_fused)
    base = {n: np.ascontiguousarray(inputs[n], dtype=np.float32)
            for n in inputs if n != "L0_attn_x"}
    in_maps = [dict(L0_attn_x=shards[c], **base) for c in range(8)]
    res = run_bass_kernel_spmd(k.nc, in_maps, core_ids=list(range(8)))
    cur = [res.results[c]["xout"] for c in range(8)]
    return np.concatenate([c[None] for c in cur], 0).reshape(16, S, D)


# version 04
# speedup vs baseline: 1.5039x; 1.0206x over previous
import contextlib
import numpy as np
import ml_dtypes
import concourse.bass as bass
import concourse.mybir as mybir
import concourse.library_config as libcfg
from concourse.bass_utils import run_bass_kernel_spmd

F32 = mybir.dt.float32
BF16 = mybir.dt.bfloat16
U32 = mybir.dt.uint32
I16 = mybir.dt.int16
AF = mybir.ActivationFunctionType
ALU = mybir.AluOpType
AX = mybir.AxisListType

T = 4096
D = 1024
NBATCH = 2
S = 2048
NT = T // 128
NF = D // 128
E = 8
H_FFN = 4096
CAPACITY = 5121
NEGINF = -1.0e30
PI = float(np.pi)


class Sem:
    __slots__ = ("h", "n")

    def __init__(self, h):
        self.h = h
        self.n = 0

    def inc(self, ins, k=1):
        ins.then_inc(self.h, k)
        self.n += k
        return self.n

    def wait(self, eng, val=None):
        eng.wait_ge(self.h, self.n if val is None else val)


class K:

    def __init__(self, name):
        self.nc = bass.Bass("TRN2", target_bir_lowering=False, name=name)
        self.ctx = contextlib.ExitStack()
        self._sems = {}
        self.bar = self.sem("BAR")
        self.marks = {}

    def sem(self, name):
        if name not in self._sems:
            self._sems[name] = Sem(self.ctx.enter_context(self.nc.semaphore(name)))
        return self._sems[name]

    def sb(self, name, shape, dt):
        return self.ctx.enter_context(self.nc.sbuf_tensor(name, shape, dt))

    def ps(self, name, shape, dt=F32):
        return self.ctx.enter_context(self.nc.psum_tensor(name, shape, dt))

    def dram(self, name, shape, dt, kind="Internal"):
        return self.nc.dram_tensor(name, shape, dt, kind=kind)

    def close(self):
        self.ctx.close()


class Loc:

    def __init__(self, k):
        self.k = k
        self.ctx = contextlib.ExitStack()

    def sb(self, name, shape, dt):
        return self.ctx.enter_context(self.k.nc.sbuf_tensor(name, shape, dt))

    def ps(self, name, shape, dt=F32):
        return self.ctx.enter_context(self.k.nc.psum_tensor(name, shape, dt))

    def close(self):
        self.ctx.close()


def common_init(k, need_ident=True, need_consts=True):
    nc = k.nc
    k.identb = k.sb("identb", [128, 128], BF16)
    k.identf = k.sb("identf", [128, 128], F32)
    k.consts = k.sb("consts", [128, 8], F32)
    k.egrid = k.sb("egrid", [128, E], F32)
    k.ones_row = k.sb("ones_row", [1, 512], BF16)
    k.ones_rowf = k.sb("ones_rowf", [1, 512], F32)
    k.ones_colf8 = k.sb("ones_colf8", [E, 1], F32)
    k.piota8 = k.sb("piota8", [E, 1], F32)
    k.tokid16 = k.sb("tokid16", [128, NT], I16)
    k.tokid32 = k.sb("tokid32", [128, NT], U32)
    init = k.sem("init")
    blk = k.nc.Block()
    blk.__enter__()

    @blk.gpsimd
    def _(g):
        for ident in (k.identb, k.identf):
            init.inc(g.memset(ident[:, :], 0.0), 1)
            g.wait_ge(init.h, init.n)
            g.affine_select(out=ident[:, :], in_=ident[:, :],
                            compare_op=ALU.not_equal, fill=1.0, base=0,
                            pattern=[[-1, 128]], channel_multiplier=1)
        g.memset(k.consts[:, 0:1], 1e-5)
        g.memset(k.consts[:, 1:2], 1e-6)
        g.memset(k.consts[:, 2:3], PI)
        g.memset(k.consts[:, 3:4], 1.0)
        g.memset(k.consts[:, 4:5], -PI)
        g.iota(k.egrid[:, :], [[1, E]], base=0, channel_multiplier=0,
               allow_small_or_imprecise_dtypes=True)
        g.memset(k.ones_row[:, :], 1.0)
        g.memset(k.ones_rowf[:, :], 1.0)
        g.iota(k.piota8[:, :], [[0, 1]], base=0, channel_multiplier=1,
               allow_small_or_imprecise_dtypes=True)
        g.iota(k.tokid16[:, :], [[128, NT]], base=0, channel_multiplier=1,
               allow_small_or_imprecise_dtypes=True)
        g.iota(k.tokid32[:, :], [[128, NT]], base=0, channel_multiplier=1,
               allow_small_or_imprecise_dtypes=True)
        init.inc(g.memset(k.ones_colf8[:, :], 1.0), 1)
    blk.__exit__(None, None, None)
    k.init_done = init.n
    k.init_sem = init


class Phase:

    def __init__(self, k):
        self.k = k
        self.entry = k.bar.n

    def begin(self, eng):
        eng.wait_ge(self.k.bar.h, self.entry)
        self.k.init_sem.wait(eng, self.k.init_done)

    def done(self, ins):
        self.k.bar.inc(ins, 1)

    def done_q(self, eng, dsem):
        eng.wait_ge(dsem.h, dsem.n)
        eng.sem_inc(self.k.bar.h, 1)
        self.k.bar.n += 1


def emit_ln(k, xres, t_tm, tag, eps_col_idx=0, n_tiles=NT):
    nc = k.nc
    ph = Phase(k)
    sdl = [k.sem("ln_d%d" % j) for j in range(4)]
    sv = k.sem("ln_v")
    sa = k.sem("ln_a")
    so = k.sem("ln_o")
    sf = k.sem("ln_f")
    NBUF = 4
    loc = Loc(k)
    xt = loc.sb("ln_x" + tag, [128, NBUF * D], F32)
    xc = loc.sb("ln_c" + tag, [128, NBUF * D], F32)
    sq = loc.sb("ln_q" + tag, [128, D], F32)
    tt = loc.sb("ln_t" + tag, [128, NBUF * D], BF16)
    cols = loc.sb("ln_s" + tag, [128, 8 * NBUF], F32)

    sdl0 = [x.n for x in sdl]
    sv0, sa0, so0, sf0 = sv.n, sa.n, so.n, sf.n
    blk = k.nc.Block()
    blk.__enter__()

    @blk.sync
    def _(sync):
        ph.begin(sync)
        for i in range(n_tiles):
            b = i % NBUF
            if i >= NBUF:
                sync.wait_ge(sf.h, sf0 + (i - NBUF + 1))
            sdl[b].inc(sync.dma_start(out=xt[:, b * D:(b + 1) * D],
                                      in_=xres[i * 128:(i + 1) * 128, :]), 16)
        for j in range(4):
            sync.wait_ge(sdl[j].h, sdl[j].n)
        sync.sem_inc(k.bar.h, 1)
    k.bar.n += 1

    @blk.vector
    def _(v):
        ph.begin(v)
        for i in range(n_tiles):
            b = i % NBUF
            c = 8 * b
            v.wait_ge(sdl[b].h, sdl0[b] + 16 * (i // NBUF + 1))
            v.reduce_sum(cols[:, c:c + 1], xt[:, b * D:(b + 1) * D], axis=AX.X).then_inc(sv.h, 1)
            v.wait_ge(sv.h, sv0 + 4 * i + 1)
            v.tensor_scalar_mul(cols[:, c + 1:c + 2], cols[:, c:c + 1], -1.0 / D).then_inc(sv.h, 1)
            v.wait_ge(sa.h, sa0 + 3 * (i + 1))
            v.reciprocal(cols[:, c + 3:c + 4], cols[:, c + 2:c + 3]).then_inc(sv.h, 1)
            v.wait_ge(sv.h, sv0 + 4 * i + 3)
            v.tensor_scalar_mul(tt[:, b * D:(b + 1) * D], xc[:, b * D:(b + 1) * D],
                                cols[:, c + 3:c + 4]).then_inc(sv.h, 1)
        sv.n = sv0 + 4 * n_tiles
        ph.done(_last_inc(v, k))

    @blk.scalar
    def _(s):
        ph.begin(s)
        for i in range(n_tiles):
            b = i % NBUF
            c = 8 * b
            s.wait_ge(sv.h, sv0 + 4 * i + 2)
            s.activation(xc[:, b * D:(b + 1) * D], xt[:, b * D:(b + 1) * D],
                         AF.Identity, bias=cols[:, c + 1:c + 2], scale=1.0).then_inc(sa.h, 1)
            s.wait_ge(sa.h, sa0 + 3 * i + 1)
            s.activation(sq[:, :], xc[:, b * D:(b + 1) * D], AF.Square,
                         accum_out=cols[:, c + 4:c + 5]).then_inc(sa.h, 1)
            s.wait_ge(sa.h, sa0 + 3 * i + 2)
            s.activation(cols[:, c + 2:c + 3], cols[:, c + 4:c + 5], AF.Sqrt,
                         bias=k.consts[:, eps_col_idx:eps_col_idx + 1],
                         scale=1.0 / D).then_inc(sa.h, 1)
        sa.n = sa0 + 3 * n_tiles
        ph.done(_last_inc_a(s, k))

    @blk.gpsimd
    def _(g):
        ph.begin(g)
        for i in range(n_tiles):
            b = i % NBUF
            if i >= 1:
                g.wait_ge(so.h, so0 + 16 * i)
                g.sem_inc(sf.h, 1)
                sf.n += 1
            g.wait_ge(sv.h, sv0 + 4 * (i + 1))
            so.inc(g.dma_start(out=t_tm[i * 128:(i + 1) * 128, :],
                               in_=tt[:, b * D:(b + 1) * D]), 16)
        ph.done_q(g, so)
    blk.__exit__(None, None, None)
    loc.close()


def _last_inc(eng, k):
    if hasattr(eng, "memset"):
        return eng.memset(k.consts[:, 5:6], 0.0)
    return eng.memzero(k.consts[:, 5:6])


def emit_tf(k, t_tm, t_fm, tag, rows=T, dcols=D):
    ph = Phase(k)
    sdl = [k.sem("tf_d%d" % j) for j in range(2)]
    so = k.sem("tf_o")
    sf = k.sem("tf_f")
    nchunk = dcols // 128
    loc = Loc(k)
    buf = loc.sb("tf_b" + tag, [128, 2 * rows], BF16)
    sdl0 = [x.n for x in sdl]
    so0, sf0 = so.n, sf.n
    blk = k.nc.Block()
    blk.__enter__()

    @blk.sync
    def _(sync):
        ph.begin(sync)
        for c in range(nchunk):
            b = c % 2
            if c >= 2:
                sync.wait_ge(sf.h, sf0 + (c - 1))
            sdl[b].inc(sync.dma_start(out=buf[:, b * rows:(b + 1) * rows],
                                      in_=t_tm[:, c * 128:(c + 1) * 128],
                                      transpose=True), 16)
        for j in range(2):
            sync.wait_ge(sdl[j].h, sdl[j].n)
        sync.sem_inc(k.bar.h, 1)
    k.bar.n += 1

    @blk.gpsimd
    def _(g):
        ph.begin(g)
        for c in range(nchunk):
            b = c % 2
            if c >= 1:
                g.wait_ge(so.h, so0 + 16 * c)
                g.sem_inc(sf.h, 1)
                sf.n += 1
            g.wait_ge(sdl[b].h, sdl0[b] + 16 * (c // 2 + 1))
            so.inc(g.dma_start(out=t_fm[c * 128:(c + 1) * 128, :],
                               in_=buf[:, b * rows:(b + 1) * rows]), 16)
        ph.done_q(g, so)
    blk.__exit__(None, None, None)
    loc.close()


def emit_routing_l0(k, t_fm, wg_f32, gates, tag):
    nc = k.nc
    cc_in = k.dram("rcc_in" + tag, [1, E], F32)
    cc_out = k.dram("rcc_out" + tag, [E, E], F32)

    loc = Loc(k)
    wg_sb = loc.sb("r_wg" + tag, [128, NF * E], BF16)
    tfm = loc.sb("r_t" + tag, [128, NF * 512], BF16)
    max8 = loc.sb("r_m8" + tag, [128, 8], F32)
    lgt = loc.sb("r_lg" + tag, [128, 2 * E], F32)
    idxu = loc.sb("r_iu" + tag, [128, 8], U32)
    cols = loc.sb("r_c" + tag, [128, 8], F32)
    expv = loc.sb("r_ex" + tag, [128, 8], F32)
    oh_all = loc.sb("r_oh" + tag, [128, NT * E], F32)
    cwh = loc.sb("r_cw" + tag, [128, NT], F32)
    ohT = loc.sb("r_ohT" + tag, [E, T], F32)
    zer8 = loc.sb("r_z8" + tag, [E, T], F32)
    scn = loc.sb("r_sc" + tag, [E, T], F32)
    sm8 = loc.sb("r_s8" + tag, [E, 16], F32)
    row1 = loc.sb("r_r1" + tag, [1, 16], F32)
    pidu = loc.sb("r_pd" + tag, [1, 2], U32)
    cnts = loc.sb("r_cn" + tag, [E, E], F32)
    k.ps_anchor = loc.ps("r_anc" + tag, [1, 8], F32)
    ps_lg = loc.ps("r_plg" + tag, [128, 1024], F32)
    ps_t = loc.ps("r_pt" + tag, [128, 128], F32)
    ps_s = loc.ps("r_ps" + tag, [E, 128], F32)

    sd = k.sem("r_d")
    st = k.sem("r_t")
    sv = k.sem("r_v")
    sa = k.sem("r_a")
    sg = k.sem("r_g")
    scc = k.sem("r_cc")

    import os as _os
    dbg = _os.environ.get("DEBUG_ROUTING")
    if dbg:
        lg_dump = k.dram("lgdump" + tag, [T, E], F32, kind="ExternalOutput")
        oh_dump = k.dram("ohdump" + tag, [128, NT * E], F32, kind="ExternalOutput")
        ohT_dump = k.dram("ohTdump" + tag, [E, T], F32, kind="ExternalOutput")
        oh4_dump = k.dram("oh4dump" + tag, [E, T], F32, kind="ExternalOutput")
        cw_dump = k.dram("cwdump" + tag, [128, NT], F32, kind="ExternalOutput")
        scn_dump = k.dram("scndump" + tag, [E, T], F32, kind="ExternalOutput")
    ph = Phase(k)
    sd0, st0, sv0, sa0 = sd.n, st.n, sv.n, sa.n
    blk = nc.Block()
    blk.__enter__()

    sdg = k.sem("r_dg")
    sdg0 = sdg.n

    @blk.gpsimd
    def _(g):
        ph.begin(g)
        sdg.inc(g.dma_start(
            out=wg_sb[:, :].rearrange("p (c e) -> p c e", c=NF),
            in_=wg_f32.ap().rearrange("(c p) e -> p c e", p=128)), 16)

    @blk.sync
    def _(sync):
        ph.begin(sync)
        for w in range(NT // 4):
            if w >= 1:
                sync.wait_ge(st.h, st0 + 4 * w)
            sd.inc(sync.dma_start(
                out=tfm[:, :].rearrange("p (c t) -> p c t", c=NF),
                in_=t_fm[:, w * 512:(w + 1) * 512].rearrange("(c p) t -> p c t", p=128)), 16)
        sync.wait_ge(sd.h, sd.n)
        sync.sem_inc(k.bar.h, 1)
    k.bar.n += 1

    @blk.tensor
    def _(t):
        ph.begin(t)
        t.wait_ge(sdg.h, sdg0 + 16)
        for i in range(NT):
            w, j = divmod(i, 4)
            b = i % 2
            t.wait_ge(sd.h, sd0 + 16 * (w + 1))
            import os as _os
            if _os.environ.get("R1_SERIAL"):
                if i >= 1:
                    t.wait_ge(sa.h, sa0 + i)
            elif i >= 2:
                t.wait_ge(sv.h, sv0 + 8 * (i - 2) + 1)
            ins = None
            for c in range(NF):
                ins = t.matmul(ps_lg[:, b * 512:b * 512 + E],
                               tfm[:, c * 512 + j * 128: c * 512 + (j + 1) * 128],
                               wg_sb[:, c * E:(c + 1) * E],
                               start=(c == 0), stop=(c == NF - 1))
            st.inc(ins, 1)
        ph.done(_last_inc_t(t, k))

    @blk.vector
    def _(v):
        ph.begin(v)
        for i in range(NT):
            b = i % 2
            v.wait_ge(st.h, st0 + i + 1)
            v.tensor_copy(lgt[:, b * E:(b + 1) * E], ps_lg[:, b * 512:b * 512 + E]).then_inc(sv.h, 1)
            v.wait_ge(sv.h, sv0 + 8 * i + 1)
            v.max(max8[:, :], lgt[:, b * E:(b + 1) * E]).then_inc(sv.h, 1)
            v.wait_ge(sv.h, sv0 + 8 * i + 2)
            v.max_index(idxu[:, :], max8[:, :], lgt[:, b * E:(b + 1) * E]).then_inc(sv.h, 1)
            v.wait_ge(sv.h, sv0 + 8 * i + 3)
            v.tensor_copy(cols[:, 0:1], idxu[:, 0:1]).then_inc(sv.h, 1)
            v.tensor_scalar_mul(cols[:, 1:2], max8[:, 0:1], -1.0).then_inc(sv.h, 1)
            v.wait_ge(sv.h, sv0 + 8 * i + 5)
            v.tensor_single_scalar(oh_all[:, i * E:(i + 1) * E], k.egrid[:, :],
                                   cols[:, 0:1], ALU.is_equal).then_inc(sv.h, 1)
            v.wait_ge(sa.h, sa0 + i + 1)
            v.reciprocal(cols[:, 3:4], cols[:, 2:3]).then_inc(sv.h, 1)
            if dbg:
                pass
            v.wait_ge(sv.h, sv0 + 8 * i + 7)
            v.tensor_scalar_mul(cwh[:, i:i + 1], cols[:, 3:4], 0.5).then_inc(sv.h, 1)
        sv.n = sv0 + 8 * NT
        ph.done(_last_inc(v, k))

    @blk.scalar
    def _(s):
        ph.begin(s)
        for i in range(NT):
            b = i % 2
            s.wait_ge(sv.h, sv0 + 8 * i + 5)
            s.activation(expv[:, :], lgt[:, b * E:(b + 1) * E], AF.Exp,
                         bias=cols[:, 1:2], accum_out=cols[:, 2:3]).then_inc(sa.h, 1)
        sa.n = sa0 + NT
        ph.done(_last_inc_a(s, k))

    if dbg:
        sdbg = k.sem("r_dbg")

        @blk.gpsimd
        def _(g):
            ph.begin(g)
            for i in range(NT):
                b = i % 2
                g.wait_ge(sa.h, sa0 + i + 1)
                sdbg.inc(g.dma_start(out=lg_dump[i * 128:(i + 1) * 128, :],
                                     in_=lgt[:, b * E:(b + 1) * E]), 16)
            g.wait_ge(sv.h, sv0 + 8 * NT)
            sdbg.inc(g.dma_start(out=oh_dump[:, :], in_=oh_all[:, :]), 16)
            g.wait_ge(sdbg.h, sdbg.n)
            g.sem_inc(k.bar.h, 1)
        k.bar.n += 1

    blk.__exit__(None, None, None)

    ph2 = Phase(k)
    st0, sv0 = st.n, sv.n
    blk = nc.Block()
    blk.__enter__()

    @blk.tensor
    def _(t):
        ph2.begin(t)
        for i in range(NT):
            if i >= 1:
                t.wait_ge(sv.h, sv0 + i)
            st.inc(t.transpose(ps_s[:, :], oh_all[:, i * E:(i + 1) * E],
                               k.identf[:, :]), 1)
        ph2.done(_last_inc_t(t, k))

    @blk.vector
    def _(v):
        ph2.begin(v)
        for i in range(NT):
            v.wait_ge(st.h, st0 + i + 1)
            v.tensor_copy(ohT[:, i * 128:(i + 1) * 128], ps_s[:, :]).then_inc(sv.h, 1)
        v.memset(zer8[:, :], 0.0).then_inc(sv.h, 1)
        v.wait_ge(sv.h, sv0 + NT + 1)
        v.tensor_tensor_scan(scn[:, :], ohT[:, :], zer8[:, :], 0.0,
                             ALU.add, ALU.add).then_inc(sv.h, 1)
        sv.n = sv0 + NT + 2
        ph2.done(_last_inc(v, k))

    if dbg:
        sdbg2 = k.sem("r_dbg2")

        @blk.gpsimd
        def _(g):
            ph2.begin(g)
            g.wait_ge(sv.h, sv.n)
            sdbg2.inc(g.dma_start(out=ohT_dump[:, :], in_=ohT[:, :]), 16)
            g.wait_ge(sdbg2.h, sdbg2.n)
            g.sem_inc(k.bar.h, 1)
        k.bar.n += 1
    blk.__exit__(None, None, None)

    ph3 = Phase(k)
    st0, sv0, sg0, scc0 = st.n, sv.n, sg.n, scc.n
    blk = nc.Block()
    blk.__enter__()

    @blk.tensor
    def _(t):
        ph3.begin(t)
        st.inc(t.transpose(ps_t[0:1, 0:E], scn[:, T - 1:T], k.identf[0:E, 0:E]), 1)

    @blk.vector
    def _(v):
        ph3.begin(v)
        v.wait_ge(st.h, st0 + 1)
        v.tensor_copy(row1[:, 0:E], ps_t[0:1, 0:E]).then_inc(sv.h, 1)
    sv.n += 1

    @blk.gpsimd
    def _(g):
        ph3.begin(g)
        g.wait_ge(sv.h, sv0 + 1)
        sg.inc(g.dma_start(out=cc_in[:, :], in_=row1[:, 0:E]), 16)
        g.wait_ge(sg.h, sg0 + 16)
        scc.inc(g.collective_compute(
            "AllGather", ALU.bypass, replica_groups=[list(range(8))],
            ins=[cc_in.ap().opt()], outs=[cc_out.ap().opt()]), 1)
        g.wait_ge(scc.h, scc0 + 1)
        sg.inc(g.dma_start(out=cnts[:, :], in_=cc_out[:, :]), 16)
        sg.inc(g.dma_start(out=pidu[:, 0:1], in_=k.nc.partition_id_tensor[:, :]), 16)

    gp_done = sg.n

    @blk.vector
    def _(v):
        v.wait_ge(sg.h, gp_done)
        v.tensor_copy(row1[:, 8:9], pidu[:, 0:1]).then_inc(sv.h, 1)
        sv.n += 1
        ph3.done(_last_inc(v, k))
    blk.__exit__(None, None, None)

    ph4 = Phase(k)
    st0, sv0 = st.n, sv.n
    blk = nc.Block()
    blk.__enter__()

    @blk.tensor
    def _(t):
        ph4.begin(t)
        st.inc(t.matmul(ps_s[:, 0:1], k.ones_rowf[0:1, 0:E], row1[:, 8:9],
                        start=True, stop=True), 1)
        t.wait_ge(sv.h, sv0 + 2)
        st.inc(t.matmul(ps_s[:, 4:5], cnts[:, :], k.ones_colf8[:, 0:1],
                        start=True, stop=True), 1)
        ph4.done(_last_inc_t(t, k))

    @blk.vector
    def _(v):
        ph4.begin(v)
        v.wait_ge(st.h, st0 + 1)
        v.tensor_tensor(sm8[:, 2:3], k.piota8[:, 0:1], ps_s[:, 0:1], ALU.is_lt).then_inc(sv.h, 1)
        v.wait_ge(sv.h, sv0 + 1)
        v.tensor_scalar_mul(cnts[:, :], cnts[:, :], sm8[:, 2:3]).then_inc(sv.h, 1)
        v.wait_ge(st.h, st0 + 2)
        v.tensor_scalar_add(sm8[:, 3:4], ps_s[:, 4:5], -1.0).then_inc(sv.h, 1)
        v.wait_ge(sv.h, sv0 + 3)
        v.tensor_scalar_add(scn[:, :], scn[:, :], sm8[:, 3:4]).then_inc(sv.h, 1)
        v.wait_ge(sv.h, sv0 + 4)
        v.tensor_single_scalar(scn[:, :], scn[:, :], float(CAPACITY), ALU.is_lt).then_inc(sv.h, 1)
        v.wait_ge(sv.h, sv0 + 5)
        v.tensor_mul(ohT[:, :], ohT[:, :], scn[:, :]).then_inc(sv.h, 1)
        sv.n = sv0 + 6
        ph4.done(_last_inc(v, k))

    if dbg:
        sdbg4 = k.sem("r_dbg4")

        @blk.gpsimd
        def _(g):
            ph4.begin(g)
            g.wait_ge(sv.h, sv.n)
            sdbg4.inc(g.dma_start(out=oh4_dump[:, :], in_=ohT[:, :]), 16)
            sdbg4.inc(g.dma_start(out=scn_dump[:, :], in_=scn[:, :]), 16)
            sdbg4.inc(g.dma_start(out=cw_dump[:, :], in_=cwh[:, :]), 16)
            g.wait_ge(sdbg4.h, sdbg4.n)
            g.sem_inc(k.bar.h, 1)
        k.bar.n += 1
    blk.__exit__(None, None, None)

    ph5 = Phase(k)
    st0, sv0 = st.n, sv.n
    blk = nc.Block()
    blk.__enter__()

    @blk.tensor
    def _(t):
        ph5.begin(t)
        for i in range(NT):
            if i >= 1:
                t.wait_ge(sv.h, sv0 + i)
            st.inc(t.transpose(ps_t[:, 0:E], ohT[:, i * 128:(i + 1) * 128],
                               k.identf[0:E, 0:E]), 1)
        ph5.done(_last_inc_t(t, k))

    @blk.vector
    def _(v):
        ph5.begin(v)
        for i in range(NT):
            v.wait_ge(st.h, st0 + i + 1)
            v.tensor_scalar_mul(gates[:, i * E:(i + 1) * E], ps_t[:, 0:E],
                                cwh[:, i:i + 1]).then_inc(sv.h, 1)
        sv.n = sv0 + NT
        ph5.done(_last_inc(v, k))
    blk.__exit__(None, None, None)
    loc.close()


def _last_inc_t(t, k):
    return t.matmul(k.ps_anchor[:, 0:1], k.ones_rowf[0:1, 0:1],
                    k.ones_rowf[0:1, 0:1], start=True, stop=True)


def _last_inc_a(s, k):
    return s.memzero(k.consts[:, 6:7])


def emit_copy(k, dst, src, rows, cols, tag, nsplit=4):
    ph = Phase(k)
    sg = k.sem("cp_g")
    blk = k.nc.Block()
    blk.__enter__()

    @blk.gpsimd
    def _(g):
        ph.begin(g)
        step = rows // nsplit
        for i in range(nsplit):
            sg.inc(g.dma_start(out=dst[i * step:(i + 1) * step, :],
                               in_=src[i * step:(i + 1) * step, :]), 16)
        ph.done_q(g, sg)
    blk.__exit__(None, None, None)


def emit_ffn(k, t_fm, w1_bf, w2_bf, b1_f32, b2_f32, gates, xres, tag):
    nc = k.nc
    NHC = H_FFN // 128

    sd = k.sem("f_d")
    sll = [k.sem("f_l%d" % j) for j in range(2)]
    st = k.sem("f_t")
    sv = k.sem("f_v")
    sa = k.sem("f_a")
    sg = k.sem("f_g")
    sgf = k.sem("f_gf")

    loc0 = Loc(k)
    b1col_all = loc0.sb("f_b1c" + tag, [128, NHC * E], F32)
    b2sb = loc0.sb("f_b2s" + tag, [E, D], F32)
    php = Phase(k)
    locp = Loc(k)
    b1_tmp = locp.sb("f_b1t" + tag, [E, H_FFN], F32)
    k.ps_anchor = loc0.sb if False else locp.ps("f_anc" + tag, [1, 8], F32)
    ps_p = locp.ps("f_pp" + tag, [128, 8], F32)
    sd0, st0, sv0 = sd.n, st.n, sv.n
    blk = nc.Block()
    blk.__enter__()

    @blk.sync
    def _(sync):
        php.begin(sync)
        sd.inc(sync.dma_start(out=b1_tmp[:, :], in_=b1_f32[:, :]), 16)
        sd.inc(sync.dma_start(out=b2sb[:, :], in_=b2_f32[:, :]), 16)
        php.done_q(sync, sd)

    @blk.tensor
    def _(t):
        php.begin(t)
        t.wait_ge(sd.h, sd0 + 32)
        for c in range(NHC):
            if c >= 1:
                t.wait_ge(sv.h, sv0 + c)
            st.inc(t.transpose(ps_p[:, 0:E], b1_tmp[:, c * 128:(c + 1) * 128],
                               k.identf[0:E, 0:E]), 1)
        php.done(_last_inc_t(t, k))

    @blk.vector
    def _(v):
        php.begin(v)
        for c in range(NHC):
            v.wait_ge(st.h, st0 + c + 1)
            v.tensor_copy(b1col_all[:, c * E:(c + 1) * E], ps_p[:, 0:E]).then_inc(sv.h, 1)
        sv.n = sv0 + NHC
        php.done(_last_inc(v, k))
    blk.__exit__(None, None, None)
    locp.close()

    loc = Loc(k)
    w1sb = loc.sb("f_w1" + tag, [128, NF * H_FFN], BF16)
    w2sb = loc.sb("f_w2" + tag, [128, NHC * D], BF16)
    hsb = loc.sb("f_h" + tag, [128, NHC * 512], BF16)
    t4 = loc.sb("f_t4" + tag, [128, 2 * NF * 512], BF16)
    b2row = loc.sb("f_b2r" + tag, [1, D], F32)
    b2bc = loc.sb("f_b2b" + tag, [128, D], F32)
    yev = loc.sb("f_y" + tag, [128, 2 * 512], F32)
    k.ps_anchor = loc.ps("f_anc2" + tag, [1, 8], F32)
    ps_h = loc.ps("f_ph" + tag, [128, 2 * 512], F32)
    ps_y = loc.ps("f_py" + tag, [128, 2 * 512], F32)
    ps_b = loc.ps("f_pb" + tag, [128, D], F32)

    ph = Phase(k)
    sd0, st0, sv0, sa0, sg0 = sd.n, st.n, sv.n, sa.n, sg.n
    sll0 = [x.n for x in sll]
    sgf0 = sgf.n

    def st_mm(e, tw):
        return st0 + e * (2 + 8 * 40) + 2 + tw * 40

    def sa_h(e, tw):
        return sa0 + (e * 8 + tw) * 32

    def sv_y(e, tw):
        return sv0 + e * 129 + 1 + tw * 16

    blk = nc.Block()
    blk.__enter__()

    @blk.sync
    def _(sync):
        ph.begin(sync)
        for e in range(E):
            if e >= 1:
                sync.wait_ge(st.h, st_mm(e - 1, 8))
            sd.inc(sync.dma_start(
                out=w1sb[:, :].rearrange("p (c h) -> p c h", c=NF),
                in_=w1_bf[e, :, :].rearrange("(c p) h -> p c h", p=128)), 16)
            sd.inc(sync.dma_start(
                out=w2sb[:, :].rearrange("p (c d) -> p c d", c=NHC),
                in_=w2_bf[e, :, :].rearrange("(c p) d -> p c d", p=128)), 16)
        ph.done_q(sync, sd)

    @blk.gpsimd
    def _(g):
        ph.begin(g)
        for e in range(E):
            if e >= 1:
                g.wait_ge(st.h, st_mm(e - 1, 0))
            sg.inc(g.dma_start(out=b2row[:, :], in_=b2sb[e:e + 1, :]), 16)
            for tw in range(8):
                for yi in range(8):
                    tc, dh = divmod(yi, 2)
                    yb = yi % 2
                    g.wait_ge(sg.h, sg.n)
                    g.sem_inc(sgf.h, 1)
                    sgf.n += 1
                    g.wait_ge(sv.h, sv_y(e, tw) + yi * 2 + 2)
                    row0 = (tw * 4 + tc) * 128
                    sg.inc(g.dma_start(
                        out=xres[row0:row0 + 128, dh * 512:(dh + 1) * 512],
                        in_=yev[:, yb * 512:(yb + 1) * 512],
                        accum_op=ALU.add), 16)
        ph.done_q(g, sg)

    @blk.scalar
    def _(s):
        ph.begin(s)
        for e in range(E):
            for tw in range(8):
                b = tw % 2
                if e * 8 + tw >= 2:
                    ew, tww = divmod(e * 8 + tw - 2, 8)
                    s.wait_ge(st.h, st_mm(ew, tww) + 32)
                sll[b].inc(s.dma_start(
                    out=t4[:, b * NF * 512:(b + 1) * NF * 512]
                        .rearrange("p (c t) -> p c t", c=NF),
                    in_=t_fm[:, tw * 512:(tw + 1) * 512]
                        .rearrange("(c p) t -> p c t", p=128)), 16)
                for hc in range(NHC):
                    s.wait_ge(st.h, st_mm(e, tw) + hc + 1)
                    s.activation(hsb[:, hc * 512:(hc + 1) * 512],
                                 ps_h[:, (hc % 2) * 512:(hc % 2 + 1) * 512],
                                 AF.Gelu, bias=b1col_all[:, hc * E + e:hc * E + e + 1],
                                 scale=1.0).then_inc(sa.h, 1)
        sa.n = sa0 + E * 8 * NHC
        ph.done(_last_inc_a(s, k))

    @blk.tensor
    def _(t):
        ph.begin(t)
        for e in range(E):
            t.wait_ge(sg.h, sg0 + 16 * (e * 65 + 1))
            if e >= 1:
                t.wait_ge(sv.h, sv_y(e - 1, 0))
            t.wait_ge(sd.h, sd0 + 32 * (e + 1))
            st.inc(t.matmul(ps_b[:, 0:512], k.ones_rowf[0:1, 0:128],
                            b2row[:, 0:512], start=True, stop=True), 1)
            st.inc(t.matmul(ps_b[:, 512:1024], k.ones_rowf[0:1, 0:128],
                            b2row[:, 512:1024], start=True, stop=True), 1)
            for tw in range(8):
                b = tw % 2
                idx = e * 8 + tw
                t.wait_ge(sll[b].h, sll0[b] + 16 * (idx // 2 + 1))
                for hc in range(NHC):
                    hb = hc % 2
                    if hc >= 2:
                        t.wait_ge(sa.h, sa_h(e, tw) + hc - 1)
                    elif e * 8 + tw > 0:
                        pidx = e * 8 + tw - 1
                        t.wait_ge(sa.h, sa0 + pidx * 32 + 31 + hc)
                    ins = None
                    for c in range(NF):
                        ins = t.matmul(
                            ps_h[:, hb * 512:(hb + 1) * 512],
                            w1sb[:, c * H_FFN + hc * 128: c * H_FFN + (hc + 1) * 128],
                            t4[:, b * NF * 512 + c * 512: b * NF * 512 + (c + 1) * 512],
                            start=(c == 0), stop=(c == NF - 1))
                    st.inc(ins, 1)
                t.wait_ge(sa.h, sa_h(e, tw) + 32)
                for tc in range(4):
                    for dh in range(2):
                        yi = tc * 2 + dh
                        yb = yi % 2
                        gl = e * 64 + tw * 8 + yi - 2
                        if gl >= 0:
                            ge_, rem = divmod(gl, 64)
                            tw_, yi_ = divmod(rem, 8)
                            t.wait_ge(sv.h, sv_y(ge_, tw_) + yi_ * 2 + 1)
                        ins = None
                        for hc in range(NHC):
                            ins = t.matmul(
                                ps_y[:, yb * 512:(yb + 1) * 512],
                                hsb[:, hc * 512 + tc * 128: hc * 512 + tc * 128 + 128],
                                w2sb[:, hc * D + dh * 512: hc * D + (dh + 1) * 512],
                                start=(hc == 0), stop=(hc == NHC - 1))
                        st.inc(ins, 1)
        ph.done(_last_inc_t(t, k))

    @blk.vector
    def _(v):
        ph.begin(v)
        for e in range(E):
            v.wait_ge(st.h, st_mm(e, 0))
            v.tensor_copy(b2bc[:, :], ps_b[:, 0:D]).then_inc(sv.h, 1)
            for tw in range(8):
                for tc in range(4):
                    gcol = gates[:, (tw * 4 + tc) * E + e: (tw * 4 + tc) * E + e + 1]
                    for dh in range(2):
                        yi = tc * 2 + dh
                        yb = yi % 2
                        v.wait_ge(st.h, st_mm(e, tw) + 32 + yi + 1)
                        gl = e * 64 + tw * 8 + yi - 2
                        if gl >= 0:
                            v.wait_ge(sgf.h, sgf0 + gl + 2)
                        v.tensor_scalar_mul(yev[:, yb * 512:(yb + 1) * 512],
                                            ps_y[:, yb * 512:(yb + 1) * 512],
                                            gcol).then_inc(sv.h, 1)
                        v.wait_ge(sv.h, sv_y(e, tw) + yi * 2 + 1)
                        v.scalar_tensor_tensor(
                            yev[:, yb * 512:(yb + 1) * 512],
                            b2bc[:, dh * 512:(dh + 1) * 512], gcol,
                            yev[:, yb * 512:(yb + 1) * 512],
                            ALU.mult, ALU.add).then_inc(sv.h, 1)
        sv.n = sv0 + E * 129
        ph.done(_last_inc(v, k))

    blk.__exit__(None, None, None)
    loc.close()
    loc0.close()


def emit_listbuild(k, gates, ig_gate, ig_tok32, t_tm, tag, capc, ncho,
                   prefetch=None):
    nc = k.nc
    loc = Loc(k)
    m8 = loc.sb("lb_m8" + tag, [128, 8], F32)
    iu = loc.sb("lb_iu" + tag, [128, 8], U32)
    ec = loc.sb("lb_ec" + tag, [128, NT * ncho], F32)
    gv = loc.sb("lb_gv" + tag, [128, NT * ncho], F32)
    nz = loc.sb("lb_nz" + tag, [128, NT * E], F32)
    ohs = loc.sb("lb_oh" + tag, [128, 8], F32)
    prod = loc.sb("lb_pr" + tag, [128, 8], F32)
    rk = loc.sb("lb_rk" + tag, [128, 2], F32)
    slotf = loc.sb("lb_sf" + tag, [128, 2], F32)
    bnd = loc.sb("lb_bn" + tag, [128, 2], F32)
    rtm = loc.sb("lb_rt" + tag, [128, 8], F32)
    slotu = loc.sb("lb_su" + tag, [128, NT * ncho], U32)
    nzT = loc.sb("lb_nT" + tag, [E, T], F32)
    scnT = loc.sb("lb_sT" + tag, [E, T], F32)
    zer8 = loc.sb("lb_z8" + tag, [E, T], F32)
    zbf = loc.sb("lb_zb" + tag, [128, D], BF16)
    zgat = loc.sb("lb_zg" + tag, [128, capc * E // 128], F32)
    pf32 = loc.sb("lb_pf" + tag, [128, capc * E // 128], U32)
    k.ps_anchor = loc.ps("lb_anc" + tag, [1, 8], F32)
    ps_a = loc.ps("lb_pa" + tag, [E, 128], F32)
    ps_b = loc.ps("lb_pb" + tag, [128, 8], F32)

    sv = k.sem("q_v")
    st = k.sem("q_t")
    NVL = 7 if ncho == 2 else 5

    ph = Phase(k)
    sv0, st0 = sv.n, st.n
    svB0 = sv0 + NT * NVL
    blk = nc.Block()
    blk.__enter__()

    @blk.vector
    def _(v):
        ph.begin(v)
        for i in range(NT):
            base = sv0 + i * NVL
            gt = gates[:, i * E:(i + 1) * E]
            v.max(m8[:, :], gt).then_inc(sv.h, 1)
            v.wait_ge(sv.h, base + 1)
            v.max_index(iu[:, :], m8[:, :], gt).then_inc(sv.h, 1)
            v.wait_ge(sv.h, base + 2)
            v.tensor_copy(ec[:, i * ncho:(i + 1) * ncho],
                          iu[:, 0:ncho]).then_inc(sv.h, 1)
            v.tensor_copy(gv[:, i * ncho:(i + 1) * ncho],
                          m8[:, 0:ncho]).then_inc(sv.h, 1)
            v.wait_ge(sv.h, base + 3)
            v.tensor_single_scalar(nz[:, i * E:(i + 1) * E], k.egrid[:, :],
                                   ec[:, i * ncho:i * ncho + 1],
                                   ALU.is_equal).then_inc(sv.h, 1)
            if ncho == 2:
                v.tensor_single_scalar(ohs[:, :], k.egrid[:, :],
                                       ec[:, i * ncho + 1:i * ncho + 2],
                                       ALU.is_equal).then_inc(sv.h, 1)
                v.wait_ge(sv.h, base + 6)
                v.tensor_tensor(nz[:, i * E:(i + 1) * E],
                                nz[:, i * E:(i + 1) * E], ohs[:, :],
                                ALU.add).then_inc(sv.h, 1)
        sv.n = sv0 + NT * NVL
        for i in range(NT):
            v.wait_ge(st.h, st0 + i + 1)
            v.tensor_copy(nzT[:, i * 128:(i + 1) * 128],
                          ps_a[:, :]).then_inc(sv.h, 1)
        v.memset(zer8[:, :], 0.0).then_inc(sv.h, 1)
        sv.n = svB0 + NT + 1
        v.wait_ge(sv.h, sv.n)
        v.tensor_tensor_scan(scnT[:, :], nzT[:, :], zer8[:, :], 0.0,
                             ALU.add, ALU.add).then_inc(sv.h, 1)
        sv.n += 1
        ph.done(_last_inc(v, k))

    @blk.tensor
    def _(t):
        ph.begin(t)
        for i in range(NT):
            t.wait_ge(sv.h, sv0 + i * NVL + NVL)
            if i >= 1:
                t.wait_ge(sv.h, svB0 + i)
            st.inc(t.transpose(ps_a[:, :], nz[:, i * E:(i + 1) * E],
                               k.identf[:, :]), 1)
        ph.done(_last_inc_t(t, k))

    if prefetch is not None:
        w1sb, w2sb, w1_bf, w2_bf = prefetch
        sd = k.sem("f_d")
        sw2 = k.sem("tf_d0")

        @blk.sync
        def _(sync):
            ph.begin(sync)
            sd.inc(sync.dma_start(
                out=w1sb[:, :].rearrange("p (c h) -> p c h", c=NF),
                in_=w1_bf[0, :, :].rearrange("(c p) h -> p c h", p=128)), 16)
            sw2.inc(sync.dma_start(
                out=w2sb[:, :].rearrange("p (c d) -> p c d", c=H_FFN // 128),
                in_=w2_bf[0, :, :].rearrange("(c p) d -> p c d", p=128)), 16)
            sync.wait_ge(sd.h, sd.n)
            sync.wait_ge(sw2.h, sw2.n)
            sync.sem_inc(k.bar.h, 1)
        k.bar.n += 1
    blk.__exit__(None, None, None)

    ph2 = Phase(k)
    sv0, st0 = sv.n, st.n
    sg = k.sem("r_g")
    sgi = k.sem("r_dg")
    sg0, sgi0 = sg.n, sgi.n
    MAGIC = 12582912.0
    CW128 = capc // 128
    NV2 = 1 + 11 * ncho
    base_i = lambda i: sv0 + 2 + i * NV2
    blk = nc.Block()
    blk.__enter__()

    @blk.vector
    def _(v):
        ph2.begin(v)
        v.memset(zbf[:, :], 0.0).then_inc(sv.h, 1)
        v.memset(zgat[:, :], 0.0).then_inc(sv.h, 1)
        for i in range(NT):
            b = base_i(i)
            v.wait_ge(st.h, st0 + i + 1)
            v.tensor_copy(rtm[:, :], ps_b[:, :]).then_inc(sv.h, 1)
            for kk in range(ncho):
                c0 = i * ncho + kk
                bb = b + 1 + 11 * kk
                ecc = ec[:, c0:c0 + 1]
                jj = slotf[:, 0:1]
                jdiv = slotf[:, 1:2]
                jmod = bnd[:, 0:1]
                t2 = bnd[:, 1:2]
                sf = rk[:, 1:2]
                v.wait_ge(sv.h, bb)
                v.tensor_single_scalar(ohs[:, :], k.egrid[:, :], ecc,
                                       ALU.is_equal).then_inc(sv.h, 1)
                v.wait_ge(sv.h, bb + 1)
                v.tensor_tensor(prod[:, :], rtm[:, :], ohs[:, :],
                                ALU.mult).then_inc(sv.h, 1)
                v.wait_ge(sv.h, bb + 2)
                v.max(m8[:, :], prod[:, :]).then_inc(sv.h, 1)
                v.wait_ge(sv.h, bb + 3)
                v.tensor_scalar_add(jj, m8[:, 0:1], -1.0).then_inc(sv.h, 1)
                v.wait_ge(sv.h, bb + 4)
                v.tensor_scalar_min(jj, jj, float(capc - 1)).then_inc(sv.h, 1)
                v.wait_ge(sv.h, bb + 5)
                v.tensor_scalar(jdiv, jj, 1.0 / 128, -63.5 / 128, ALU.mult,
                                ALU.add).then_inc(sv.h, 1)
                v.wait_ge(sv.h, bb + 6)
                v.tensor_scalar(jdiv, jdiv, MAGIC, -MAGIC, ALU.add,
                                ALU.add).then_inc(sv.h, 1)
                v.wait_ge(sv.h, bb + 7)
                v.scalar_tensor_tensor(jmod, jdiv, -128.0, jj,
                                       ALU.mult, ALU.add).then_inc(sv.h, 1)
                v.wait_ge(sv.h, bb + 8)
                v.scalar_tensor_tensor(t2, ecc, float(CW128), jdiv,
                                       ALU.mult, ALU.add).then_inc(sv.h, 1)
                v.wait_ge(sv.h, bb + 9)
                v.scalar_tensor_tensor(sf, jmod, float(E * CW128), t2,
                                       ALU.mult, ALU.add).then_inc(sv.h, 1)
                v.wait_ge(sv.h, bb + 10)
                v.tensor_copy(slotu[:, c0:c0 + 1], sf).then_inc(sv.h, 1)
                v.wait_ge(sv.h, bb + 11)
        sv.n = sv0 + 2 + NT * NV2
        ph2.done(_last_inc(v, k))

    @blk.tensor
    def _(t):
        ph2.begin(t)
        for i in range(NT):
            if i >= 1:
                t.wait_ge(sv.h, base_i(i - 1) + 1)
            st.inc(t.transpose(ps_b[:, 0:E], scnT[:, i * 128:(i + 1) * 128],
                               k.identf[0:E, 0:E]), 1)
        ph2.done(_last_inc_t(t, k))

    @blk.gpsimd
    def _(g):
        ph2.begin(g)
        sgc = k.sem("r_gc")
        sgc.inc(g.iota(pf32[:, :], [[0, capc * E // 128]], base=T,
                       channel_multiplier=1,
                       allow_small_or_imprecise_dtypes=True), 1)
        g.wait_ge(sgc.h, sgc.n)
        g.wait_ge(sv.h, sv0 + 2)
        sg.inc(g.dma_start(out=t_tm[T:T + 128, :], in_=zbf[:, :]), 16)
        sg.inc(g.dma_start(
            out=ig_gate.ap().rearrange("(p c) -> p c", p=128),
            in_=zgat[:, :]), 16)
        sg.inc(g.dma_start(
            out=ig_tok32.ap().rearrange("(p c) -> p c", p=128),
            in_=pf32[:, :]), 16)
        g.wait_ge(sg.h, sg0 + 48)
        for i in range(NT):
            for kk in range(ncho):
                c0 = i * ncho + kk
                g.wait_ge(sv.h, base_i(i) + 1 + 11 * kk + 11)
                sgi.inc(g.indirect_dma_start(
                    out=ig_gate.ap().rearrange("(a b) -> a b", b=1),
                    out_offset=bass.IndirectOffsetOnAxis(
                        slotu[:, c0:c0 + 1], 0),
                    in_=gv[:, c0:c0 + 1], in_offset=None), 16)
                sgi.inc(g.indirect_dma_start(
                    out=ig_tok32.ap().rearrange("(a b) -> a b", b=1),
                    out_offset=bass.IndirectOffsetOnAxis(
                        slotu[:, c0:c0 + 1], 0),
                    in_=k.tokid32[:, i:i + 1], in_offset=None), 16)
        ph2.done_q(g, sgi)
    blk.__exit__(None, None, None)
    loc.close()


def emit_ffn_routed(k, t_tm, w1_bf, w2_bf, b1_f32, b2_f32, ig_gate,
                    ig_tok32, xres, tag, capc, w1sb, w2sb):
    nc = k.nc
    NHC = H_FFN // 128
    NW = capc // 512
    CW128 = capc // 128

    sd = k.sem("f_d")
    sw2 = k.sem("tf_d0")
    sq = k.sem("ln_d0")
    st = k.sem("f_t")
    sv = k.sem("f_v")
    sa = k.sem("f_a")
    sg = k.sem("f_g")
    sgts = [k.sem("ln_d%d" % (s + 1)) for s in range(3)]
    ssc = k.sem("ln_o")

    loc0 = Loc(k)
    b1col_all = loc0.sb("f_b1c" + tag, [128, NHC * E], F32)
    php = Phase(k)
    locp = Loc(k)
    b1_tmp = locp.sb("f_b1t" + tag, [E, H_FFN], F32)
    k.ps_anchor = locp.ps("f_anc" + tag, [1, 8], F32)
    ps_p = locp.ps("f_pp" + tag, [128, 8], F32)
    sd0, st0, sv0 = sd.n, st.n, sv.n
    blk = nc.Block()
    blk.__enter__()

    @blk.sync
    def _(sync):
        php.begin(sync)
        sd.inc(sync.dma_start(out=b1_tmp[:, :], in_=b1_f32[:, :]), 16)
        php.done_q(sync, sd)

    @blk.tensor
    def _(t):
        php.begin(t)
        t.wait_ge(sd.h, sd0 + 16)
        for c in range(NHC):
            if c >= 1:
                t.wait_ge(sv.h, sv0 + c)
            st.inc(t.transpose(ps_p[:, 0:E], b1_tmp[:, c * 128:(c + 1) * 128],
                               k.identf[0:E, 0:E]), 1)
        php.done(_last_inc_t(t, k))

    @blk.vector
    def _(v):
        php.begin(v)
        for c in range(NHC):
            v.wait_ge(st.h, st0 + c + 1)
            v.tensor_copy(b1col_all[:, c * E:(c + 1) * E], ps_p[:, 0:E]).then_inc(sv.h, 1)
        sv.n = sv0 + NHC
        php.done(_last_inc(v, k))
    blk.__exit__(None, None, None)
    locp.close()

    loc = Loc(k)
    CPE = capc // 128
    hsb = loc.sb("f_h" + tag, [128, NHC * 512], BF16)
    t4g = loc.sb("fr_t4" + tag, [128, NF * capc], BF16)
    xg = loc.sb("fr_xg" + tag, [128, 3 * D], BF16)
    gvw = loc.sb("fr_gv" + tag, [128, E * CW128], F32)
    itok = loc.sb("fr_it" + tag, [128, E * CW128], U32)
    b2row = loc.sb("f_b2r" + tag, [1, D], BF16)
    b2bc = loc.sb("f_b2b" + tag, [128, D], F32)
    yg = loc.sb("fr_y" + tag, [128, D], F32)
    k.ps_anchor = loc.ps("f_anc2" + tag, [1, 8], F32)
    ps_h = loc.ps("f_ph" + tag, [128, 2 * 512], F32)
    ps_y = loc.ps("f_py" + tag, [128, 2 * 512], F32)
    ps_b = loc.ps("f_pb" + tag, [128, 512], F32)
    ps_tr = loc.ps("f_ptr" + tag, [128, 2 * D], BF16)

    ph = Phase(k)
    sd0, sq0, st0, sv0, sa0, sg0 = sd.n, sq.n, st.n, sv.n, sa.n, sg.n
    sw20 = sw2.n
    sgts0 = [s.n for s in sgts]
    ssc0 = ssc.n
    stt = k.sem("r_t")
    stv = k.sem("r_v")
    stt0, stv0 = stt.n, stv.n
    SVE = 2 + 16 * NW

    def st_b2(e):
        return st0 + e * (2 + 40 * NW) + 2

    def st_mm(e, w):
        return st_b2(e) + w * 40

    def sa_h(e, w):
        return sa0 + (e * NW + w) * 32

    def sv_yy(e, w):
        return sv0 + e * SVE + 2 + w * 16

    def emit_combine(g, e, w, tc):
        g.wait_ge(sv.h, sv_yy(e, w) + 4 * (tc + 1))
        ssc.inc(g.indirect_dma_start(
            out=xres[:, :],
            out_offset=bass.IndirectOffsetOnAxis(
                itok[:, e * CW128 + w * 4 + tc: e * CW128 + w * 4 + tc + 1], 0),
            in_=yg[:, :], in_offset=None, compute_op=ALU.add), 16)

    blk = nc.Block()
    blk.__enter__()

    @blk.sync
    def _(sync):
        ph.begin(sync)
        sq.inc(sync.dma_start(
            out=gvw[:, :],
            in_=ig_gate.ap().rearrange("(p c) -> p c", p=128)), 16)
        sq.inc(sync.dma_start(
            out=itok[:, :],
            in_=ig_tok32.ap().rearrange("(p c) -> p c", p=128)), 16)
        for e in range(1, E):
            sync.wait_ge(st.h, st_mm(e - 1, NW - 1) + 32)
            sd.inc(sync.dma_start(
                out=w1sb[:, :].rearrange("p (c h) -> p c h", c=NF),
                in_=w1_bf[e, :, :].rearrange("(c p) h -> p c h", p=128)), 16)
            sync.wait_ge(st.h, st_mm(e - 1, NW))
            sw2.inc(sync.dma_start(
                out=w2sb[:, :].rearrange("p (c d) -> p c d", c=NHC),
                in_=w2_bf[e, :, :].rearrange("(c p) d -> p c d", p=128)), 16)
        sync.wait_ge(sq.h, sq.n)
        sync.wait_ge(sw2.h, sw2.n)
        ph.done_q(sync, sd)

    @blk.gpsimd
    def _(g):
        ph.begin(g)
        g.wait_ge(sq.h, sq0 + 16 * 2)
        for e in range(E):
            if e >= 1:
                g.wait_ge(st.h, st_b2(e - 1))
            sg.inc(g.dma_start(out=b2row[:, :], in_=b2_f32[e:e + 1, :]), 16)
            if e >= 1:
                for w in range(NW - 1):
                    for tc in range(4):
                        emit_combine(g, e - 1, w, tc)

            def gath(j):
                ci = e * CPE + j
                if ci >= 3:
                    g.wait_ge(stt.h, stt0 + ci - 2)
                sgts[ci % 3].inc(g.indirect_dma_start(
                    out=xg[:, (ci % 3) * D:(ci % 3 + 1) * D],
                    out_offset=None,
                    in_=t_tm[:, :],
                    in_offset=bass.IndirectOffsetOnAxis(
                        itok[:, e * CW128 + j: e * CW128 + j + 1], 0)), 16)

            for j in range(3):
                gath(j)
            if e >= 1:
                for tc in range(4):
                    emit_combine(g, e - 1, NW - 1, tc)
            for j in range(3, CPE):
                gath(j)
        for w in range(NW):
            for tc in range(4):
                emit_combine(g, E - 1, w, tc)
        for s in sgts:
            g.wait_ge(s.h, s.n)
        g.wait_ge(ssc.h, ssc.n)
        g.sem_inc(k.bar.h, 1)
    k.bar.n += 1

    @blk.scalar
    def _(s):
        ph.begin(s)
        for e in range(E):
            for w in range(NW):
                for hc in range(NHC):
                    s.wait_ge(st.h, st_mm(e, w) + hc + 1)
                    s.activation(hsb[:, hc * 512:(hc + 1) * 512],
                                 ps_h[:, (hc % 2) * 512:(hc % 2 + 1) * 512],
                                 AF.Gelu,
                                 bias=b1col_all[:, hc * E + e:hc * E + e + 1],
                                 scale=1.0).then_inc(sa.h, 1)
        sa.n = sa0 + E * NW * NHC
        ph.done(_last_inc_a(s, k))

    @blk.tensor
    def _(t):
        ph.begin(t)
        for e in range(E):
            t.wait_ge(sg.h, sg0 + 16 * (e + 1))
            if e >= 1:
                t.wait_ge(sv.h, sv0 + (e - 1) * SVE + 2)
            st.inc(t.matmul(ps_b[:, 0:512], k.ones_row[0:1, 0:128],
                            b2row[:, 0:512], start=True, stop=True), 1)
            t.wait_ge(sv.h, sv0 + e * SVE + 1)
            st.inc(t.matmul(ps_b[:, 0:512], k.ones_row[0:1, 0:128],
                            b2row[:, 512:1024], start=True, stop=True), 1)
            for j in range(CPE):
                ci = e * CPE + j
                t.wait_ge(sgts[ci % 3].h, sgts0[ci % 3] + 16 * (ci // 3 + 1))
                if ci >= 2:
                    t.wait_ge(stv.h, stv0 + ci - 1)
                ins = None
                for f in range(NF):
                    ins = t.transpose(
                        ps_tr[:, (ci % 2) * D + f * 128:
                              (ci % 2) * D + (f + 1) * 128],
                        xg[:, (ci % 3) * D + f * 128:
                           (ci % 3) * D + (f + 1) * 128],
                        k.identb[:, :])
                stt.inc(ins, 1)
            t.wait_ge(stv.h, stv0 + (e + 1) * CPE)
            if e >= 1:
                t.wait_ge(sd.h, sd0 + 16 * e)
            for w in range(NW):
                gw = e * NW + w
                for hc in range(NHC):
                    hb = hc % 2
                    if hc >= 2:
                        t.wait_ge(sa.h, sa_h(e, w) + hc - 1)
                    elif gw > 0:
                        t.wait_ge(sa.h, sa0 + (gw - 1) * 32 + 31 + hc)
                    ins = None
                    for c in range(NF):
                        ins = t.matmul(
                            ps_h[:, hb * 512:(hb + 1) * 512],
                            w1sb[:, c * H_FFN + hc * 128: c * H_FFN + (hc + 1) * 128],
                            t4g[:, c * capc + w * 512: c * capc + (w + 1) * 512],
                            start=(c == 0), stop=(c == NF - 1))
                    st.inc(ins, 1)
                t.wait_ge(sa.h, sa_h(e, w) + 32)
                if e >= 1 and w == 0:
                    t.wait_ge(sw2.h, sw20 + 16 * e)
                for tc in range(4):
                    for dh in range(2):
                        yi = tc * 2 + dh
                        yb = yi % 2
                        gl = gw * 8 + yi - 2
                        if gl >= 0:
                            gwp, yip = divmod(gl, 8)
                            ep, wp = divmod(gwp, NW)
                            t.wait_ge(sv.h, sv_yy(ep, wp) + yip * 2 + 1)
                        ins = None
                        for hc in range(NHC):
                            ins = t.matmul(
                                ps_y[:, yb * 512:(yb + 1) * 512],
                                hsb[:, hc * 512 + tc * 128: hc * 512 + tc * 128 + 128],
                                w2sb[:, hc * D + dh * 512: hc * D + (dh + 1) * 512],
                                start=(hc == 0), stop=(hc == NHC - 1))
                        st.inc(ins, 1)
        ph.done(_last_inc_t(t, k))

    @blk.vector
    def _(v):
        ph.begin(v)
        for e in range(E):
            v.wait_ge(st.h, st_b2(e) - 1)
            if e >= 1:
                v.wait_ge(sv.h, sv0 + e * SVE)
            v.tensor_copy(b2bc[:, 0:512], ps_b[:, 0:512]).then_inc(sv.h, 1)
            v.wait_ge(st.h, st_b2(e))
            v.tensor_copy(b2bc[:, 512:1024], ps_b[:, 0:512]).then_inc(sv.h, 1)
            for j in range(CPE):
                ci = e * CPE + j
                v.wait_ge(stt.h, stt0 + ci + 1)
                ins = None
                for f in range(NF):
                    ins = v.tensor_copy(
                        t4g[:, f * capc + j * 128: f * capc + (j + 1) * 128],
                        ps_tr[:, (ci % 2) * D + f * 128:
                              (ci % 2) * D + (f + 1) * 128])
                ins.then_inc(stv.h, 1)
                stv.n += 1
            for w in range(NW):
                gw = e * NW + w
                for tc in range(4):
                    gcol = gvw[:, e * CW128 + w * 4 + tc:
                               e * CW128 + w * 4 + tc + 1]
                    ci = gw * 4 + tc
                    for dh in range(2):
                        yi = tc * 2 + dh
                        yb = yi % 2
                        ys = yg[:, dh * 512:(dh + 1) * 512]
                        v.wait_ge(st.h, st_mm(e, w) + 32 + yi + 1)
                        if ci >= 1 and dh == 0:
                            v.wait_ge(ssc.h, ssc0 + 16 * ci)
                        v.tensor_scalar_mul(ys, ps_y[:, yb * 512:(yb + 1) * 512],
                                            gcol).then_inc(sv.h, 1)
                        v.wait_ge(sv.h, sv_yy(e, w) + yi * 2 + 1)
                        v.scalar_tensor_tensor(
                            ys, b2bc[:, dh * 512:(dh + 1) * 512], gcol, ys,
                            ALU.mult, ALU.add).then_inc(sv.h, 1)
        sv.n = sv0 + E * SVE
        ph.done(_last_inc(v, k))

    blk.__exit__(None, None, None)
    loc.close()
    loc0.close()


def emit_rope_tables(k, loc_outer):
    nc = k.nc
    k._rope_n = getattr(k, "_rope_n", 0) + 1
    sfx = "%d" % k._rope_n
    k.cosT = loc_outer.sb("ropecos" + sfx, [128, T], F32)
    k.sinT = loc_outer.sb("ropesin" + sfx, [128, T], F32)
    loc = Loc(k)
    posr = loc.sb("rope_pr" + sfx, [1, T], F32)
    imodr = loc.sb("rope_im" + sfx, [1, 128], F32)
    icol = loc.sb("rope_ic" + sfx, [128, 2], F32)
    frq = loc.sb("rope_fq" + sfx, [128, T], F32)
    frq2 = loc.sb("rope_f2" + sfx, [128, T], F32)
    k.ps_anchor = loc.ps("rope_anc" + sfx, [1, 8], F32)
    ps = loc.ps("rope_ps" + sfx, [128, 512], F32)
    ps_i = loc.ps("rope_pi" + sfx, [128, 8], F32)
    ph = Phase(k)
    sg = k.sem("rope_g")
    st = k.sem("rope_t")
    sv = k.sem("rope_v")
    sa = k.sem("rope_a")
    sg0, st0, sv0, sa0 = sg.n, st.n, sv.n, sa.n
    blk = nc.Block()
    blk.__enter__()

    @blk.gpsimd
    def _(g):
        ph.begin(g)
        g.iota(posr[:, :], [[0, NBATCH], [1, S]], base=0, channel_multiplier=0,
               allow_small_or_imprecise_dtypes=True)
        sg.inc(g.iota(imodr[:, :], [[0, 4], [1, 32]], base=0,
                      channel_multiplier=0,
                      allow_small_or_imprecise_dtypes=True), 1)

    @blk.tensor
    def _(t):
        ph.begin(t)
        t.wait_ge(sa.h, sa0 + 1)
        st.inc(t.transpose(ps_i[:, 0:1], imodr[:, :], k.identf[0:1, 0:1]), 1)
        t.wait_ge(sg.h, sg0 + 1)
        for w in range(T // 512):
            if w >= 1:
                t.wait_ge(sv.h, sv0 + 1 + w)
            st.inc(t.matmul(ps[:, :], k.ones_rowf[0:1, 0:128],
                            posr[:, w * 512:(w + 1) * 512],
                            start=True, stop=True), 1)
        ph.done(_last_inc_t(t, k))

    @blk.vector
    def _(v):
        ph.begin(v)
        v.wait_ge(st.h, st0 + 1)
        v.tensor_copy(icol[:, 0:1], ps_i[:, 0:1]).then_inc(sv.h, 1)
        sv.n += 1
        v.wait_ge(sv.h, sv0 + 1)
        for w in range(T // 512):
            v.wait_ge(st.h, st0 + 1 + w + 1)
            v.tensor_scalar_mul(frq[:, w * 512:(w + 1) * 512], ps[:, :],
                                icol[:, 0:1]).then_inc(sv.h, 1)
        sv.n += T // 512
        MAGIC = 8388608.0
        inv2pi = 1.0 / (2.0 * PI)
        for tbl, shift in ((k.cosT, 0.25), (k.sinT, 0.0)):
            v.wait_ge(sv.h, sv.n)
            v.tensor_scalar(tbl[:, :], frq[:, :], inv2pi, shift,
                            ALU.mult, ALU.add).then_inc(sv.h, 1)
            sv.n += 1
            v.wait_ge(sv.h, sv.n)
            v.tensor_scalar(frq2[:, :], tbl[:, :], MAGIC, MAGIC,
                            ALU.add, ALU.subtract).then_inc(sv.h, 1)
            sv.n += 1
            v.wait_ge(sv.h, sv.n)
            v.tensor_sub(tbl[:, :], tbl[:, :], frq2[:, :]).then_inc(sv.h, 1)
            sv.n += 1
        ph.done(_last_inc(v, k))

    @blk.scalar
    def _(s):
        ph.begin(s)
        s.wait_ge(sg.h, sg0 + 1)
        s.activation(imodr[:, :], imodr[:, :], AF.Exp,
                     scale=-float(np.log(10000.0)) / 32.0).then_inc(sa.h, 1)
        sa.n += 1
        v_done = sv.n
        s.wait_ge(sv.h, v_done)
        s.activation(k.cosT[:, :], k.cosT[:, :], AF.Sin,
                     scale=2.0 * PI).then_inc(sa.h, 1)
        sa.n += 1
        s.wait_ge(sa.h, sa.n)
        s.activation(k.sinT[:, :], k.sinT[:, :], AF.Sin,
                     scale=2.0 * PI).then_inc(sa.h, 1)
        sa.n += 1
        ph.done(_last_inc_a(s, k))
    blk.__exit__(None, None, None)
    loc.close()


def emit_proj_fm(k, t_fm, w_bf, out_fm, OF, tag, rope=False, out_row0=0,
                 w_rot=None):
    nc = k.nc
    NOF = OF // 128
    loc = Loc(k)
    wsb = loc.sb("pf_w" + tag, [128, NF * OF], BF16)
    wrsb = loc.sb("pf_wr" + tag, [128, NF * OF], BF16) if rope else None
    tfm = loc.sb("pf_t" + tag, [128, 2 * NF * 512], BF16)
    ev = loc.sb("pf_e" + tag, [128, 2 * 512], BF16)
    ev32 = loc.sb("pf_e32" + tag, [128, 2 * 512], F32)
    k.ps_anchor = loc.ps("pf_anc" + tag, [1, 8], F32)
    ps = loc.ps("pf_ps" + tag, [128, 2 * 512], F32)
    ps2 = loc.ps("pf_ps2" + tag, [128, 2 * 512], F32) if rope else None
    sd = k.sem("pf_d")
    sll = [k.sem("pf_l%d" % j) for j in range(2)]
    st = k.sem("pf_t")
    sv = k.sem("pf_v")
    sa = k.sem("pf_a")
    sg = k.sem("pf_g")
    sgf = k.sem("pf_gf")
    ph = Phase(k)
    sll0 = [x.n for x in sll]
    sd0, st0, sv0, sa0, sg0 = sd.n, st.n, sv.n, sa.n, sg.n
    sgf0 = sgf.n
    NWIN = T // 512
    nev_ops = 4 if rope else 0

    blk = nc.Block()
    blk.__enter__()

    @blk.sync
    def _(sync):
        ph.begin(sync)
        sd.inc(sync.dma_start(
            out=wsb[:, :].rearrange("p (c f) -> p c f", c=NF),
            in_=w_bf.rearrange("(c p) f -> p c f", p=128)), 16)
        if rope:
            sd.inc(sync.dma_start(
                out=wrsb[:, :].rearrange("p (c f) -> p c f", c=NF),
                in_=w_rot.rearrange("(c p) f -> p c f", p=128)), 16)
        for tw in range(NWIN):
            if tw >= 2:
                sync.wait_ge(st.h, st0 + (tw - 1) * NOF * (2 if rope else 1))
            sll[tw % 2].inc(sync.dma_start(
                out=tfm[:, (tw % 2) * NF * 512:(tw % 2 + 1) * NF * 512]
                    .rearrange("p (c t) -> p c t", c=NF),
                in_=t_fm[:, tw * 512:(tw + 1) * 512]
                    .rearrange("(c p) t -> p c t", p=128)), 16)
        sync.wait_ge(sd.h, sd.n)
        for j in range(2):
            sync.wait_ge(sll[j].h, sll[j].n)
        sync.sem_inc(k.bar.h, 1)
    k.bar.n += 1

    @blk.tensor
    def _(t):
        ph.begin(t)
        t.wait_ge(sd.h, sd0 + (32 if rope else 16))
        for tw in range(NWIN):
            b = tw % 2
            t.wait_ge(sll[b].h, sll0[b] + 16 * (tw // 2 + 1))
            for ofc in range(NOF):
                i = tw * NOF + ofc
                pb = i % 2
                if i >= 2:
                    if rope:
                        t.wait_ge(sv.h, sv0 + (i - 1) * nev_ops)
                    else:
                        t.wait_ge(sa.h, sa0 + i - 1)
                ins = None
                for c in range(NF):
                    ins = t.matmul(
                        ps[:, pb * 512:(pb + 1) * 512],
                        wsb[:, c * OF + ofc * 128: c * OF + (ofc + 1) * 128],
                        tfm[:, b * NF * 512 + c * 512: b * NF * 512 + (c + 1) * 512],
                        start=(c == 0), stop=(c == NF - 1))
                st.inc(ins, 1)
                if rope:
                    for c in range(NF):
                        ins = t.matmul(
                            ps2[:, pb * 512:(pb + 1) * 512],
                            wrsb[:, c * OF + ofc * 128: c * OF + (ofc + 1) * 128],
                            tfm[:, b * NF * 512 + c * 512: b * NF * 512 + (c + 1) * 512],
                            start=(c == 0), stop=(c == NF - 1))
                    st.inc(ins, 1)
        ph.done(_last_inc_t(t, k))

    if not rope:
        @blk.scalar
        def _(s):
            ph.begin(s)
            for i in range(NWIN * NOF):
                tw, ofc = divmod(i, NOF)
                pb = i % 2
                eb = i % 2
                s.wait_ge(st.h, st0 + i + 1)
                if i >= 2:
                    s.wait_ge(sgf.h, sgf0 + i - 1)
                s.activation(ev[:, eb * 512:(eb + 1) * 512],
                             ps[:, pb * 512:(pb + 1) * 512],
                             AF.Copy, scale=1.0).then_inc(sa.h, 1)
            sa.n = sa0 + NWIN * NOF
            ph.done(_last_inc_a(s, k))

        @blk.gpsimd
        def _(g):
            ph.begin(g)
            for i in range(NWIN * NOF):
                tw, ofc = divmod(i, NOF)
                eb = i % 2
                if i >= 1:
                    g.wait_ge(sg.h, sg.n)
                    g.sem_inc(sgf.h, 1)
                    sgf.n += 1
                g.wait_ge(sa.h, sa0 + i + 1)
                sg.inc(g.dma_start(
                    out=out_fm[out_row0 + ofc * 128: out_row0 + (ofc + 1) * 128,
                               tw * 512:(tw + 1) * 512],
                    in_=ev[:, eb * 512:(eb + 1) * 512]), 16)
            ph.done_q(g, sg)
    else:
        @blk.vector
        def _(v):
            ph.begin(v)
            for i in range(NWIN * NOF):
                tw, ofc = divmod(i, NOF)
                pb = i % 2
                eb = i % 2
                tws = tw * 512
                base = sv0 + i * nev_ops
                v.wait_ge(st.h, st0 + 2 * i + 2)
                if i >= 1:
                    v.wait_ge(sv.h, base)
                if i >= 2:
                    v.wait_ge(sgf.h, sgf0 + i - 1)
                v.tensor_mul(ev32[:, eb * 512:(eb + 1) * 512],
                             ps[:, pb * 512:(pb + 1) * 512],
                             k.cosT[:, tws:tws + 512]).then_inc(sv.h, 1)
                v.tensor_mul(ev32[:, (1 - eb) * 512:(2 - eb) * 512],
                             ps2[:, pb * 512:(pb + 1) * 512],
                             k.sinT[:, tws:tws + 512]).then_inc(sv.h, 1)
                v.wait_ge(sv.h, base + 2)
                v.tensor_add(ev32[:, eb * 512:(eb + 1) * 512],
                             ev32[:, eb * 512:(eb + 1) * 512],
                             ev32[:, (1 - eb) * 512:(2 - eb) * 512]).then_inc(sv.h, 1)
                v.wait_ge(sv.h, base + 3)
                v.tensor_copy(ev[:, eb * 512:(eb + 1) * 512],
                              ev32[:, eb * 512:(eb + 1) * 512]).then_inc(sv.h, 1)
            sv.n = sv0 + NWIN * NOF * nev_ops
            ph.done(_last_inc(v, k))

        @blk.gpsimd
        def _(g):
            ph.begin(g)
            for i in range(NWIN * NOF):
                tw, ofc = divmod(i, NOF)
                eb = i % 2
                if i >= 1:
                    g.wait_ge(sg.h, sg.n)
                    g.sem_inc(sgf.h, 1)
                    sgf.n += 1
                g.wait_ge(sv.h, sv0 + (i + 1) * nev_ops)
                sg.inc(g.dma_start(
                    out=out_fm[out_row0 + ofc * 128: out_row0 + (ofc + 1) * 128,
                               tw * 512:(tw + 1) * 512],
                    in_=ev[:, eb * 512:(eb + 1) * 512]), 16)
            ph.done_q(g, sg)

    blk.__exit__(None, None, None)
    loc.close()


def emit_proj_tm(k, t_fm, w_bf, out_tm, OF, tag, out_col0=0):
    nc = k.nc
    NOH = OF // 512 if OF >= 512 else 1
    ofw = min(OF, 512)
    loc = Loc(k)
    wsb = loc.sb("pt_w" + tag, [128, NF * OF], BF16)
    tfm = loc.sb("pt_t" + tag, [128, 2 * NF * 512], BF16)
    ev = loc.sb("pt_e" + tag, [128, 2 * ofw], BF16)
    k.ps_anchor = loc.ps("pt_anc" + tag, [1, 8], F32)
    ps = loc.ps("pt_ps" + tag, [128, 2 * 512], F32)
    sd = k.sem("pt_d")
    sll = [k.sem("pt_l%d" % j) for j in range(2)]
    st = k.sem("pt_t")
    sa = k.sem("pt_a")
    sg = k.sem("pt_g")
    sgf = k.sem("pt_gf")
    ph = Phase(k)
    sll0 = [x.n for x in sll]
    sd0, st0, sa0, sg0 = sd.n, st.n, sa.n, sg.n
    sgf0 = sgf.n
    blk = nc.Block()
    blk.__enter__()

    @blk.sync
    def _(sync):
        ph.begin(sync)
        sd.inc(sync.dma_start(
            out=wsb[:, :].rearrange("p (c f) -> p c f", c=NF),
            in_=w_bf.rearrange("(c p) f -> p c f", p=128)), 16)
        for tw in range(T // 512):
            if tw >= 2:
                sync.wait_ge(st.h, st0 + (tw - 1) * 4 * NOH)
            sll[tw % 2].inc(sync.dma_start(
                out=tfm[:, (tw % 2) * NF * 512:(tw % 2 + 1) * NF * 512]
                    .rearrange("p (c t) -> p c t", c=NF),
                in_=t_fm[:, tw * 512:(tw + 1) * 512]
                    .rearrange("(c p) t -> p c t", p=128)), 16)
        sync.wait_ge(sd.h, sd.n)
        for j in range(2):
            sync.wait_ge(sll[j].h, sll[j].n)
        sync.sem_inc(k.bar.h, 1)
    k.bar.n += 1

    @blk.tensor
    def _(t):
        ph.begin(t)
        t.wait_ge(sd.h, sd0 + 16)
        for tw in range(T // 512):
            b = tw % 2
            t.wait_ge(sll[b].h, sll0[b] + 16 * (tw // 2 + 1))
            for tc in range(4):
                for oh in range(NOH):
                    i = (tw * 4 + tc) * NOH + oh
                    pb = i % 2
                    if i >= 2:
                        t.wait_ge(sa.h, sa0 + i - 1)
                    ins = None
                    for c in range(NF):
                        ins = t.matmul(
                            ps[:, pb * 512:pb * 512 + ofw],
                            tfm[:, b * NF * 512 + c * 512 + tc * 128:
                                b * NF * 512 + c * 512 + (tc + 1) * 128],
                            wsb[:, c * OF + oh * 512: c * OF + oh * 512 + ofw],
                            start=(c == 0), stop=(c == NF - 1))
                    st.inc(ins, 1)
        ph.done(_last_inc_t(t, k))

    @blk.scalar
    def _(s):
        ph.begin(s)
        NTOT = 32 * NOH
        for i in range(NTOT):
            pb = i % 2
            s.wait_ge(st.h, st0 + i + 1)
            if i >= 2:
                s.wait_ge(sgf.h, sgf0 + i - 1)
            s.activation(ev[:, pb * ofw:(pb + 1) * ofw],
                         ps[:, pb * 512:pb * 512 + ofw],
                         AF.Copy, scale=1.0).then_inc(sa.h, 1)
        sa.n = sa0 + NTOT
        ph.done(_last_inc_a(s, k))

    @blk.gpsimd
    def _(g):
        ph.begin(g)
        NTOT = 32 * NOH
        for i in range(NTOT):
            ti, oh = divmod(i, NOH)
            pb = i % 2
            if i >= 1:
                g.wait_ge(sg.h, sg.n)
                g.sem_inc(sgf.h, 1)
                sgf.n += 1
            g.wait_ge(sa.h, sa0 + i + 1)
            sg.inc(g.dma_start(
                out=out_tm[ti * 128:(ti + 1) * 128,
                           out_col0 + oh * 512: out_col0 + oh * 512 + ofw],
                in_=ev[:, pb * ofw:(pb + 1) * ofw]), 16)
        ph.done_q(g, sg)
    blk.__exit__(None, None, None)
    loc.close()


def emit_proj_res(k, a_fm, w_bf, xres, tag, scale=1.0, t_add=None, h_out=None):
    nc = k.nc
    loc = Loc(k)
    wsb = loc.sb("pr_w" + tag, [128, NF * D], BF16)
    afm = loc.sb("pr_t" + tag, [128, 2 * NF * 512], BF16)
    tadd = loc.sb("pr_a" + tag, [128, 1024], BF16) if t_add is not None else None
    ev = loc.sb("pr_e" + tag, [128, 2 * 512], F32)
    hv = loc.sb("pr_h" + tag, [128, 2 * 512], F32) if h_out is not None else None
    k.ps_anchor = loc.ps("pr_anc" + tag, [1, 8], F32)
    ps = loc.ps("pr_ps" + tag, [128, 2 * 512], F32)
    sd = k.sem("pr_d")
    sll = [k.sem("pr_l%d" % j) for j in range(2)]
    st = k.sem("pr_t")
    sv = k.sem("pr_v")
    sg = k.sem("pr_g")
    sgf = k.sem("pr_gf")
    sh = k.sem("pr_h")
    ph = Phase(k)
    sll0 = [x.n for x in sll]
    sd0, st0, sv0, sg0, sh0 = sd.n, st.n, sv.n, sg.n, sh.n
    sgf0 = sgf.n
    npt = 2 if h_out is not None else 1
    nd = 2 if h_out is not None else 1
    blk = nc.Block()
    blk.__enter__()

    @blk.sync
    def _(sync):
        ph.begin(sync)
        sd.inc(sync.dma_start(
            out=wsb[:, :].rearrange("p (c f) -> p c f", c=NF),
            in_=w_bf.rearrange("(c p) f -> p c f", p=128)), 16)
        for tw in range(T // 512):
            if tw >= 2:
                sync.wait_ge(st.h, st0 + (tw - 1) * 8)
            sll[tw % 2].inc(sync.dma_start(
                out=afm[:, (tw % 2) * NF * 512:(tw % 2 + 1) * NF * 512]
                    .rearrange("p (c t) -> p c t", c=NF),
                in_=a_fm[:, tw * 512:(tw + 1) * 512]
                    .rearrange("(c p) t -> p c t", p=128)), 16)
        sync.wait_ge(sd.h, sd.n)
        for j in range(2):
            sync.wait_ge(sll[j].h, sll[j].n)
        sync.sem_inc(k.bar.h, 1)
    k.bar.n += 1

    @blk.scalar
    def _(s):
        ph.begin(s)
        if t_add is not None:
            for ti in range(32):
                if ti >= 1:
                    s.wait_ge(sv.h, sv0 + ti * 2 * npt)
                sh.inc(s.dma_start(out=tadd[:, :],
                                   in_=t_add[ti * 128:(ti + 1) * 128, :]), 16)
        ph.done_q(s, sh)

    @blk.tensor
    def _(t):
        ph.begin(t)
        t.wait_ge(sd.h, sd0 + 16)
        for tw in range(T // 512):
            b = tw % 2
            t.wait_ge(sll[b].h, sll0[b] + 16 * (tw // 2 + 1))
            for tc in range(4):
                for dh in range(2):
                    i = (tw * 4 + tc) * 2 + dh
                    pb = i % 2
                    if i >= 2:
                        t.wait_ge(sv.h, sv0 + (i - 1) * npt)
                    ins = None
                    for c in range(NF):
                        ins = t.matmul(
                            ps[:, pb * 512:(pb + 1) * 512],
                            afm[:, b * NF * 512 + c * 512 + tc * 128:
                                b * NF * 512 + c * 512 + (tc + 1) * 128],
                            wsb[:, c * D + dh * 512: c * D + (dh + 1) * 512],
                            start=(c == 0), stop=(c == NF - 1))
                    st.inc(ins, 1)
        ph.done(_last_inc_t(t, k))

    @blk.vector
    def _(v):
        ph.begin(v)
        for i in range(64):
            ti, dh = divmod(i, 2)
            pb = i % 2
            v.wait_ge(st.h, st0 + i + 1)
            if i >= 2:
                v.wait_ge(sgf.h, sgf0 + i - 1)
            base = sv0 + i * npt
            if h_out is not None:
                v.wait_ge(sh.h, sh0 + 16 * (ti + 1))
                v.tensor_add(hv[:, pb * 512:(pb + 1) * 512],
                             ps[:, pb * 512:(pb + 1) * 512],
                             tadd[:, dh * 512:(dh + 1) * 512]).then_inc(sv.h, 1)
                v.wait_ge(sv.h, base + 1)
                v.tensor_scalar_mul(ev[:, pb * 512:(pb + 1) * 512],
                                    hv[:, pb * 512:(pb + 1) * 512],
                                    scale).then_inc(sv.h, 1)
            else:
                v.tensor_scalar_mul(ev[:, pb * 512:(pb + 1) * 512],
                                    ps[:, pb * 512:(pb + 1) * 512],
                                    scale).then_inc(sv.h, 1)
        sv.n = sv0 + 64 * npt
        ph.done(_last_inc(v, k))

    @blk.gpsimd
    def _(g):
        ph.begin(g)
        for i in range(64):
            ti, dh = divmod(i, 2)
            pb = i % 2
            if i >= 1:
                g.wait_ge(sg.h, sg.n)
                g.sem_inc(sgf.h, 1)
                sgf.n += 1
            g.wait_ge(sv.h, sv0 + (i + 1) * npt)
            sg.inc(g.dma_start(
                out=xres[ti * 128:(ti + 1) * 128, dh * 512:(dh + 1) * 512],
                in_=ev[:, pb * 512:(pb + 1) * 512],
                accum_op=ALU.add), 16)
            if h_out is not None:
                sg.inc(g.dma_start(
                    out=h_out[ti * 128:(ti + 1) * 128, dh * 512:(dh + 1) * 512],
                    in_=hv[:, pb * 512:(pb + 1) * 512]), 16)
        ph.done_q(g, sg)
    blk.__exit__(None, None, None)
    loc.close()


def emit_attn(k, q_fm, k_fm, v_tm, out_fm, tag, causal, alibi, kv_group, nbh=None):
    nc = k.nc
    NH = 16
    VKH = NH // kv_group
    if alibi:
        NAUG_A, NAUG_B, SROW = 68, 69, 68
        x_ = (2.0 ** 8) ** (1.0 / NH)
        s8 = [8.0 / x_ ** (h + 1) for h in range(NH)]
        s8_hi = [float(np.float32(ml_dtypes.bfloat16(v))) for v in s8]
        s8_lo = [float(np.float32(ml_dtypes.bfloat16(v - h_)))
                 for v, h_ in zip(s8, s8_hi)]
    else:
        NAUG_A, NAUG_B, SROW = 64, 65, 64

    if causal:
        nktw = [4 * w + 4 for w in range(4)]
        kwin = [qt // 4 + 1 for qt in range(16)]
    else:
        nktw = [16] * 4
        kwin = [4] * 16
    NKT = sum(nktw)
    A_sv = (2 if causal else 1) * 16
    svw = (4 if causal else 0) + 2
    st_bh = 17 + 2 * NKT + 4
    sv_bh = A_sv + 1 + 4 * svw
    sa_bh = NKT + 4
    stoff = [17]
    for w in range(3):
        stoff.append(stoff[-1] + 2 * nktw[w] + 1)
    svoff = [A_sv + 1]
    for w in range(3):
        svoff.append(svoff[-1] + svw)
    saoff = [0]
    for w in range(3):
        saoff.append(saoff[-1] + nktw[w] + 1)
    sld_bh = 48
    sgp_bh = (4 * 16 if alibi else 0) + 16
    sst0_holder = []

    loc = Loc(k)
    q_aug = loc.sb("at_q" + tag, [NAUG_B, S], BF16)
    k_aug = loc.sb("at_k" + tag, [NAUG_B, S], BF16)
    v_sb = loc.sb("at_v" + tag, [128, 16 * 65], BF16)
    pT = loc.sb("at_p" + tag, [128, 2 * 512], BF16)
    negm_all = loc.sb("at_nm" + tag, [128, 16], F32)
    stage = loc.sb("at_st" + tag, [16, 128], BF16)
    rl_row = loc.sb("at_rl" + tag, [1, 512], F32)
    rl_bc = loc.sb("at_rb" + tag, [64, 512], F32)
    out_sb = loc.sb("at_o" + tag, [64, 2 * 512], BF16)
    jrow_hi = loc.sb("at_jh" + tag, [1, S], BF16)
    jrow_lo = loc.sb("at_jl" + tag, [1, S], BF16)
    sh_row = loc.sb("at_sh" + tag, [1, S], BF16)
    sl_row = loc.sb("at_sl" + tag, [1, S], BF16)
    onesr = loc.sb("at_1r" + tag, [1, S], BF16)
    maskA = loc.sb("at_mA" + tag, [128, 4 * 512], F32) if causal else None
    maskB = loc.sb("at_mB" + tag, [128, 4 * 512], F32) if causal else None
    ps_sA = loc.ps("at_pA" + tag, [128, 4 * 512], F32)
    ps_sB = loc.ps("at_pB" + tag, [128, 2 * 512], F32)
    ps_pv = loc.ps("at_pv" + tag, [65, 512], F32)
    ps_ms = loc.ps("at_pm" + tag, [128, 512], F32)

    sld = k.sem("at_d")
    sgp = k.sem("at_g")
    sgc = k.sem("at_gc")
    sst = k.sem("at_s")
    sstf = k.sem("at_sf")
    st = k.sem("at_t")
    sv = k.sem("at_v")
    sa = k.sem("at_a")
    ph = Phase(k)
    sld0, sgp0, st0, sv0, sa0, sst0 = sld.n, sgp.n, st.n, sv.n, sa.n, sst.n
    sstf0 = sstf.n
    BH = NBATCH * NH if nbh is None else nbh

    def stb(bh):
        return st0 + bh * st_bh

    def svb(bh):
        return sv0 + bh * sv_bh

    def sab(bh):
        return sa0 + bh * sa_bh

    def sgpb(bh):
        return sgp0 + SETUP_GP + bh * sgp_bh

    SETUP_GP = 0

    blk = nc.Block()
    blk.__enter__()

    @blk.gpsimd
    def _(g):
        nonlocal_setup = []
        ph.begin(g)
        n = 0
        if alibi:
            g.iota(jrow_hi[:, :], [[256, 8], [0, 256]], base=0,
                   channel_multiplier=0, allow_small_or_imprecise_dtypes=True)
            g.iota(jrow_lo[:, :], [[0, 8], [1, 256]], base=0,
                   channel_multiplier=0, allow_small_or_imprecise_dtypes=True)
            sgc.inc(g.memset(onesr[:, :], 1.0), 1)
            g.wait_ge(sgc.h, sgc.n)
            sgp.inc(g.dma_start(out=k_aug[64:65, :], in_=jrow_hi[:, :]), 16)
            sgp.inc(g.dma_start(out=k_aug[65:66, :], in_=jrow_lo[:, :]), 16)
            sgp.inc(g.dma_start(out=k_aug[66:67, :], in_=jrow_hi[:, :]), 16)
            sgp.inc(g.dma_start(out=k_aug[67:68, :], in_=jrow_lo[:, :]), 16)
            n = 64
        else:
            sgc.inc(g.memset(onesr[:, :], 1.0), 1)
            g.wait_ge(sgc.h, sgc.n)
            n = 0
        sgp.inc(g.dma_start(out=k_aug[SROW:SROW + 1, :], in_=onesr[:, :]), 16)
        n += 16
        for kt in range(15):
            g.memset(v_sb[:, kt * 65 + 64:kt * 65 + 65], 1.0)
        if causal:
            sgc.inc(g.memset(v_sb[:, 15 * 65 + 64:15 * 65 + 65], 1.0), 1)
            for r in range(4):
                sgc.inc(g.memset(maskA[:, r * 512:(r + 1) * 512], 0.0), 1)
                g.wait_ge(sgc.h, sgc.n)
                g.affine_select(out=maskA[:, r * 512:(r + 1) * 512],
                                in_=maskA[:, r * 512:(r + 1) * 512],
                                compare_op=ALU.is_ge, fill=NEGINF,
                                base=r * 128, channel_multiplier=1,
                                pattern=[[-1, 512]])
                sgc.inc(g.memset(maskB[:, r * 512:(r + 1) * 512], 0.0), 1)
                g.wait_ge(sgc.h, sgc.n)
                sgc.inc(g.affine_select(out=maskB[:, r * 512:(r + 1) * 512],
                                in_=maskB[:, r * 512:(r + 1) * 512],
                                compare_op=ALU.is_ge, fill=NEGINF,
                                base=-r * 128, channel_multiplier=-1,
                                pattern=[[1, 512]]), 1)
        else:
            sgc.inc(g.memset(v_sb[:, 15 * 65 + 64:15 * 65 + 65], 1.0), 1)
        assert sgp.n - sgp0 == n
        k.marks["at_setup_gc" + tag] = sgc.n
        for bh in range(BH):
            b, h = divmod(bh, NH)
            g.wait_ge(st.h, stb(bh))
            if alibi:
                sgc.inc(g.memset(sh_row[:, :], s8_hi[h]), 1)
                sgc.inc(g.memset(sl_row[:, :], s8_lo[h]), 1)
                g.wait_ge(sgc.h, sgc.n)
                sgp.inc(g.dma_start(out=q_aug[64:65, :], in_=sh_row[:, :]), 16)
                sgp.inc(g.dma_start(out=q_aug[65:66, :], in_=sh_row[:, :]), 16)
                sgp.inc(g.dma_start(out=q_aug[66:67, :], in_=sl_row[:, :]), 16)
                sgp.inc(g.dma_start(out=q_aug[67:68, :], in_=sl_row[:, :]), 16)
            g.wait_ge(sv.h, svb(bh) + A_sv + 1)
            sgp.inc(g.dma_start(
                out=q_aug[SROW:SROW + 1, :].rearrange("o (p t) -> o p t", p=16),
                in_=stage[:, :]), 16)
            for w in range(4):
                gw = bh * 4 + w
                if gw >= 1:
                    g.wait_ge(sst.h, sst.n)
                    g.sem_inc(sstf.h, 1)
                    sstf.n += 1
                g.wait_ge(sv.h, svb(bh) + svoff[w] + svw)
                sst.inc(g.dma_start(
                    out=out_fm[h * 64:(h + 1) * 64,
                               b * S + w * 512: b * S + (w + 1) * 512],
                    in_=out_sb[:, (w % 2) * 512:(w % 2 + 1) * 512]), 16)
        g.wait_ge(sgp.h, sgp.n)
        ph.done_q(g, sst)

    SETUP_GP = (80 if alibi else 16)

    @blk.sync
    def _(sync):
        ph.begin(sync)
        for bh in range(BH):
            b, h = divmod(bh, NH)
            kvh = h // kv_group
            sync.wait_ge(st.h, stb(bh))
            sld.inc(sync.dma_start(
                out=q_aug[0:64, :],
                in_=q_fm[h * 64:(h + 1) * 64, b * S:(b + 1) * S]), 16)
            sld.inc(sync.dma_start(
                out=k_aug[0:64, :],
                in_=k_fm[kvh * 64:(kvh + 1) * 64, b * S:(b + 1) * S]), 16)
            sld.inc(sync.dma_start(
                out=v_sb[:, :].rearrange("p (kt x) -> p kt x", x=65)[:, :, 0:64],
                in_=v_tm[b * S:(b + 1) * S, kvh * 64:(kvh + 1) * 64]
                    .rearrange("(kt p) d -> p kt d", p=128)), 16)
        ph.done_q(sync, sld)

    SETUP_GC = sgc.n

    @blk.tensor
    def _(t):
        ph.begin(t)
        t.wait_ge(sgc.h, k.marks["at_setup_gc" + tag])
        for bh in range(BH):
            b, h = divmod(bh, NH)
            t.wait_ge(sld.h, sld0 + sld_bh * (bh + 1))
            if alibi:
                t.wait_ge(sgp.h, sgpb(bh) + 64)
            t.wait_ge(sa.h, sab(bh))
            for qt in range(16):
                if qt >= 1:
                    t.wait_ge(sv.h, svb(bh) + (2 * qt if causal else qt))
                elif bh >= 1:
                    t.wait_ge(sv.h, svb(bh - 1) + A_sv)
                ins = None
                for w2 in range(kwin[qt]):
                    ins = t.matmul(
                        ps_sA[:, w2 * 512:(w2 + 1) * 512],
                        q_aug[0:NAUG_A, qt * 128:(qt + 1) * 128],
                        k_aug[0:NAUG_A, w2 * 512:(w2 + 1) * 512],
                        start=True, stop=True)
                st.inc(ins, 1)
            t.wait_ge(sv.h, svb(bh) + A_sv)
            st.inc(t.transpose(ps_ms[0:16, 0:128], negm_all[:, :],
                               k.identf[:, :]), 1)
            t.wait_ge(sgp.h, sgpb(bh) + sgp_bh)
            for w in range(4):
                for j in range(nktw[w]):
                    kt = j
                    if j >= 2:
                        t.wait_ge(sa.h, sab(bh) + saoff[w] + j - 1)
                    elif w >= 1:
                        t.wait_ge(sa.h, sab(bh) + saoff[w - 1]
                                  + nktw[w - 1] - 1 + j)
                    ins = t.matmul(
                        ps_sB[:, (j % 2) * 512:(j % 2 + 1) * 512],
                        k_aug[0:NAUG_B, kt * 128:(kt + 1) * 128],
                        q_aug[0:NAUG_B, w * 512:(w + 1) * 512],
                        start=True, stop=True)
                    st.inc(ins, 1)
                    t.wait_ge(sa.h, sab(bh) + saoff[w] + j + 1)
                    if j == 0:
                        if w >= 1:
                            t.wait_ge(sv.h, svb(bh) + svoff[w - 1] + svw)
                        elif bh >= 1:
                            t.wait_ge(sv.h, svb(bh - 1) + sv_bh)
                    ins = t.matmul(ps_pv[:, :], v_sb[:, kt * 65:(kt + 1) * 65],
                                   pT[:, (j % 2) * 512:(j % 2 + 1) * 512],
                                   start=(j == 0), stop=(j == nktw[w] - 1))
                    st.inc(ins, 1)
                t.wait_ge(sv.h, svb(bh) + svoff[w] + svw - 1)
                st.inc(t.matmul(ps_ms[0:64, 0:512], k.ones_rowf[0:1, 0:64],
                                rl_row[:, :], start=True, stop=True), 1)
        t.wait_ge(sa.h, sa0 + BH * sa_bh)
        k.bar.inc(t.matmul(ps_ms[0:1, 0:1], k.ones_rowf[0:1, 0:1],
                           k.ones_rowf[0:1, 0:1], start=True, stop=True), 1)

    @blk.vector
    def _(v):
        ph.begin(v)
        if causal:
            v.wait_ge(sgc.h, k.marks["at_setup_gc" + tag])
        for bh in range(BH):
            for qt in range(16):
                v.wait_ge(st.h, stb(bh) + qt + 1)
                base = svb(bh)
                if causal:
                    dw = qt // 4
                    r = qt % 4
                    v.tensor_tensor(ps_sA[:, dw * 512:(dw + 1) * 512],
                                    ps_sA[:, dw * 512:(dw + 1) * 512],
                                    maskA[:, r * 512:(r + 1) * 512],
                                    ALU.add).then_inc(sv.h, 1)
                    v.wait_ge(sv.h, base + 2 * qt + 1)
                    v.reduce_max(negm_all[:, qt:qt + 1],
                                 ps_sA[:, 0:kwin[qt] * 512], axis=AX.X,
                                 negate=True).then_inc(sv.h, 1)
                else:
                    v.reduce_max(negm_all[:, qt:qt + 1],
                                 ps_sA[:, 0:kwin[qt] * 512], axis=AX.X,
                                 negate=True).then_inc(sv.h, 1)
            v.wait_ge(st.h, stb(bh) + 17)
            v.wait_ge(sv.h, svb(bh) + A_sv)
            v.tensor_copy(stage[:, :], ps_ms[0:16, 0:128]).then_inc(sv.h, 1)
            for w in range(4):
                base = svb(bh) + svoff[w]
                if causal:
                    for r2 in range(4):
                        kt = w * 4 + r2
                        j = kt
                        v.wait_ge(st.h, stb(bh) + stoff[w] + 2 * j + 1)
                        v.tensor_tensor(
                            ps_sB[:, (j % 2) * 512:(j % 2 + 1) * 512],
                            ps_sB[:, (j % 2) * 512:(j % 2 + 1) * 512],
                            maskB[:, r2 * 512:(r2 + 1) * 512],
                            ALU.add).then_inc(sv.h, 1)
                v.wait_ge(st.h, stb(bh) + stoff[w] + 2 * nktw[w])
                v.reciprocal(rl_row[:, :], ps_pv[64:65, :]).then_inc(sv.h, 1)
                v.wait_ge(sa.h, sab(bh) + saoff[w] + nktw[w] + 1)
                gl = bh * 4 + w - 2
                if gl >= 0:
                    v.wait_ge(sstf.h, sstf0 + gl + 2)
                v.tensor_mul(out_sb[:, (w % 2) * 512:(w % 2 + 1) * 512],
                             ps_pv[0:64, :], rl_bc[:, :]).then_inc(sv.h, 1)
        sv.n = sv0 + BH * sv_bh
        ph.done(_last_inc(v, k))

    @blk.scalar
    def _(s):
        ph.begin(s)
        for bh in range(BH):
            for w in range(4):
                for j in range(nktw[w]):
                    s.wait_ge(st.h, stb(bh) + stoff[w] + 2 * j + 1)
                    if causal and j >= w * 4:
                        r2 = j - w * 4
                        if 0 <= r2 < 4:
                            s.wait_ge(sv.h, svb(bh) + svoff[w] + r2 + 1)
                    if j >= 2:
                        s.wait_ge(st.h, stb(bh) + stoff[w] + 2 * (j - 1))
                    s.activation(pT[:, (j % 2) * 512:(j % 2 + 1) * 512],
                                 ps_sB[:, (j % 2) * 512:(j % 2 + 1) * 512],
                                 AF.Exp, scale=0.125).then_inc(sa.h, 1)
                s.wait_ge(st.h, stb(bh) + stoff[w] + 2 * nktw[w] + 1)
                if bh * 4 + w >= 1:
                    pw = bh * 4 + w - 1
                    s.wait_ge(sv.h, sv0 + (pw // 4) * sv_bh
                              + svoff[pw % 4] + svw)
                s.activation(rl_bc[:, :], ps_ms[0:64, 0:512], AF.Copy,
                             scale=1.0).then_inc(sa.h, 1)
        sa.n = sa0 + BH * sa_bh
        ph.done(_last_inc_a(s, k))

    blk.__exit__(None, None, None)
    loc.close()




def emit_rmsnorm(k, src, w_vec, out_tm, tag, src_dt=F32):
    nc = k.nc
    ph = Phase(k)
    sdl = [k.sem("rm_d%d" % j) for j in range(4)]
    sw = k.sem("rm_w")
    sv = k.sem("rm_v")
    sa = k.sem("rm_a")
    so = k.sem("rm_o")
    sf = k.sem("rm_f")
    st = k.sem("rm_t")
    NBUF = 4
    loc = Loc(k)
    k.ps_anchor = loc.ps("rm_anc" + tag, [1, 8], F32)
    xt = loc.sb("rm_x" + tag, [128, NBUF * D], src_dt)
    sq = loc.sb("rm_q" + tag, [128, D], F32)
    tt = loc.sb("rm_t" + tag, [128, NBUF * D], BF16)
    cols = loc.sb("rm_s" + tag, [128, 8 * NBUF], F32)
    wrow = loc.sb("rm_wr" + tag, [1, D], F32)
    wbc = loc.sb("rm_wb" + tag, [128, D], F32)
    ps_w = loc.ps("rm_pw" + tag, [128, D], F32)
    sdl0 = [x.n for x in sdl]
    sw0, sv0, sa0, so0, sf0, st0 = sw.n, sv.n, sa.n, so.n, sf.n, st.n
    blk = nc.Block()
    blk.__enter__()

    @blk.sync
    def _(sync):
        ph.begin(sync)
        sw.inc(sync.dma_start(out=wrow[:, :], in_=w_vec.rearrange("(o d) -> o d", o=1)), 16)
        for i in range(NT):
            b = i % NBUF
            if i >= NBUF:
                sync.wait_ge(sf.h, sf0 + (i - NBUF + 1))
            sdl[b].inc(sync.dma_start(out=xt[:, b * D:(b + 1) * D],
                                      in_=src[i * 128:(i + 1) * 128, :]), 16)
        for j in range(4):
            sync.wait_ge(sdl[j].h, sdl[j].n)
        sync.sem_inc(k.bar.h, 1)
    k.bar.n += 1

    @blk.tensor
    def _(t):
        ph.begin(t)
        t.wait_ge(sw.h, sw0 + 16)
        st.inc(t.matmul(ps_w[:, 0:512], k.ones_rowf[0:1, 0:128],
                        wrow[:, 0:512], start=True, stop=True), 1)
        st.inc(t.matmul(ps_w[:, 512:1024], k.ones_rowf[0:1, 0:128],
                        wrow[:, 512:1024], start=True, stop=True), 1)
        ph.done(_last_inc_t(t, k))

    @blk.scalar
    def _(s):
        ph.begin(s)
        for i in range(NT):
            b = i % NBUF
            c = 8 * b
            s.wait_ge(sdl[b].h, sdl0[b] + 16 * (i // NBUF + 1))
            s.activation(sq[:, :], xt[:, b * D:(b + 1) * D], AF.Square,
                         accum_out=cols[:, c:c + 1]).then_inc(sa.h, 1)
            s.wait_ge(sa.h, sa0 + 2 * i + 1)
            s.activation(cols[:, c + 1:c + 2], cols[:, c:c + 1], AF.Sqrt,
                         bias=k.consts[:, 1:2], scale=1.0 / D).then_inc(sa.h, 1)
        sa.n = sa0 + 2 * NT
        ph.done(_last_inc_a(s, k))

    @blk.vector
    def _(v):
        ph.begin(v)
        v.wait_ge(st.h, st0 + 2)
        v.tensor_copy(wbc[:, :], ps_w[:, :]).then_inc(sv.h, 1)
        for i in range(NT):
            b = i % NBUF
            c = 8 * b
            v.wait_ge(sa.h, sa0 + 2 * (i + 1))
            v.reciprocal(cols[:, c + 2:c + 3], cols[:, c + 1:c + 2]).then_inc(sv.h, 1)
            v.wait_ge(sv.h, sv0 + 2 * i + 2)
            v.scalar_tensor_tensor(tt[:, b * D:(b + 1) * D],
                                   xt[:, b * D:(b + 1) * D],
                                   cols[:, c + 2:c + 3], wbc[:, :],
                                   ALU.mult, ALU.mult).then_inc(sv.h, 1)
        sv.n = sv0 + 1 + 2 * NT
        ph.done(_last_inc(v, k))

    @blk.gpsimd
    def _(g):
        ph.begin(g)
        for i in range(NT):
            b = i % NBUF
            if i >= 1:
                g.wait_ge(so.h, so0 + 16 * i)
                g.sem_inc(sf.h, 1)
                sf.n += 1
            g.wait_ge(sv.h, sv0 + 2 * (i + 1) + 1)
            so.inc(g.dma_start(out=out_tm[i * 128:(i + 1) * 128, :],
                               in_=tt[:, b * D:(b + 1) * D]), 16)
        ph.done_q(g, so)
    blk.__exit__(None, None, None)
    loc.close()


def emit_routing_l1(k, t_fm, gate_w_f32, gates, tag):
    nc = k.nc
    loc = Loc(k)
    k.ps_anchor = loc.ps("q_anc" + tag, [1, 8], F32)
    gw_sb = loc.sb("q_gw" + tag, [E, D], BF16)
    gwT = loc.sb("q_gwT" + tag, [128, NF * E], BF16)
    tfm = loc.sb("q_t" + tag, [128, NF * 512], BF16)
    max8 = loc.sb("q_m8" + tag, [128, 8], F32)
    lgt = loc.sb("q_lg" + tag, [128, 2 * E], F32)
    idxu = loc.sb("q_iu" + tag, [128, 8], U32)
    cols = loc.sb("q_c" + tag, [128, 16], F32)
    oh0 = loc.sb("q_o0" + tag, [128, 8], F32)
    ps_lg = loc.ps("q_plg" + tag, [128, 1024], F32)
    ps_t = loc.ps("q_pt" + tag, [128, 128], BF16)
    sd = k.sem("q_d")
    st = k.sem("q_t")
    sv = k.sem("q_v")
    sa = k.sem("q_a")
    ph = Phase(k)
    sd0, st0, sv0, sa0 = sd.n, st.n, sv.n, sa.n
    NV = 13

    blk = nc.Block()
    blk.__enter__()

    sdg = k.sem("q_dg")
    sdg0 = sdg.n

    @blk.gpsimd
    def _(g):
        ph.begin(g)
        sdg.inc(g.dma_start(out=gw_sb[:, :], in_=gate_w_f32[:, :]), 16)

    @blk.sync
    def _(sync):
        ph.begin(sync)
        for w in range(NT // 4):
            if w >= 1:
                sync.wait_ge(st.h, st0 + NF + 4 * w)
            sd.inc(sync.dma_start(
                out=tfm[:, :].rearrange("p (c t) -> p c t", c=NF),
                in_=t_fm[:, w * 512:(w + 1) * 512].rearrange("(c p) t -> p c t", p=128)), 16)
        ph.done_q(sync, sd)

    @blk.tensor
    def _(t):
        ph.begin(t)
        t.wait_ge(sdg.h, sdg0 + 16)
        for c in range(NF):
            if c >= 1:
                t.wait_ge(sv.h, sv0 + c)
            st.inc(t.transpose(ps_t[:, 0:E], gw_sb[:, c * 128:(c + 1) * 128],
                               k.identb[0:E, 0:E]), 1)
        for i in range(NT):
            w, j = divmod(i, 4)
            b = i % 2
            t.wait_ge(sd.h, sd0 + 16 * (w + 1))
            t.wait_ge(sv.h, sv0 + NF)
            if i >= 2:
                t.wait_ge(sv.h, sv0 + NF + NV * (i - 2) + 1)
            ins = None
            for c in range(NF):
                ins = t.matmul(ps_lg[:, b * 512:b * 512 + E],
                               tfm[:, c * 512 + j * 128: c * 512 + (j + 1) * 128],
                               gwT[:, c * E:(c + 1) * E],
                               start=(c == 0), stop=(c == NF - 1))
            st.inc(ins, 1)
        ph.done(_last_inc_t(t, k))

    @blk.vector
    def _(v):
        ph.begin(v)
        for c in range(NF):
            v.wait_ge(st.h, st0 + c + 1)
            v.tensor_copy(gwT[:, c * E:(c + 1) * E], ps_t[:, 0:E]).then_inc(sv.h, 1)
        sv.n = sv0 + NF
        base0 = sv.n
        for i in range(NT):
            b = i % 2
            base = base0 + NV * i
            v.wait_ge(st.h, st0 + NF + i + 1)
            v.tensor_copy(lgt[:, b * E:(b + 1) * E], ps_lg[:, b * 512:b * 512 + E]).then_inc(sv.h, 1)
            v.wait_ge(sv.h, base + 1)
            v.max(max8[:, :], lgt[:, b * E:(b + 1) * E]).then_inc(sv.h, 1)
            v.wait_ge(sv.h, base + 2)
            v.max_index(idxu[:, :], max8[:, :], lgt[:, b * E:(b + 1) * E]).then_inc(sv.h, 1)
            v.wait_ge(sv.h, base + 3)
            v.tensor_copy(cols[:, 0:2], idxu[:, 0:2]).then_inc(sv.h, 1)
            v.tensor_tensor(cols[:, 2:3], max8[:, 1:2], max8[:, 0:1],
                            ALU.subtract).then_inc(sv.h, 1)
            v.wait_ge(sa.h, sa0 + i + 1)
            v.tensor_scalar_add(cols[:, 4:5], cols[:, 3:4], 1.0).then_inc(sv.h, 1)
            v.wait_ge(sv.h, base + 6)
            v.reciprocal(cols[:, 5:6], cols[:, 4:5]).then_inc(sv.h, 1)
            v.wait_ge(sv.h, base + 7)
            v.tensor_scalar(cols[:, 6:7], cols[:, 5:6], -0.5, 0.5,
                            ALU.mult, ALU.add).then_inc(sv.h, 1)
            v.tensor_scalar_mul(cols[:, 5:6], cols[:, 5:6], 0.5).then_inc(sv.h, 1)
            v.wait_ge(sv.h, base + 4)
            v.tensor_single_scalar(oh0[:, :], k.egrid[:, :], cols[:, 0:1],
                                   ALU.is_equal).then_inc(sv.h, 1)
            v.tensor_single_scalar(gates[:, i * E:(i + 1) * E], k.egrid[:, :],
                                   cols[:, 1:2], ALU.is_equal).then_inc(sv.h, 1)
            v.wait_ge(sv.h, base + 11)
            v.tensor_scalar_mul(gates[:, i * E:(i + 1) * E],
                                gates[:, i * E:(i + 1) * E],
                                cols[:, 6:7]).then_inc(sv.h, 1)
            v.wait_ge(sv.h, base + 12)
            v.scalar_tensor_tensor(gates[:, i * E:(i + 1) * E],
                                   oh0[:, :], cols[:, 5:6],
                                   gates[:, i * E:(i + 1) * E],
                                   ALU.mult, ALU.add).then_inc(sv.h, 1)
        sv.n = base0 + NV * NT
        ph.done(_last_inc(v, k))

    @blk.scalar
    def _(s):
        ph.begin(s)
        for i in range(NT):
            s.wait_ge(sv.h, sv0 + NF + NV * i + 5)
            s.activation(cols[:, 3:4], cols[:, 2:3], AF.Exp).then_inc(sa.h, 1)
        sa.n = sa0 + NT
        ph.done(_last_inc_a(s, k))
    blk.__exit__(None, None, None)
    loc.close()


def emit_l0attn(k, q_fm, k_fm, v_tm, a_fm, tag):
    nc = k.nc
    G = 204
    L = S - G
    NLC = (L + 127) // 128
    lastc = L - 128 * (NLC - 1)
    NH = 16
    BH = NBATCH * NH
    QSZ = [128, G - 128]

    loc = Loc(k)
    qg = loc.sb("la_qg" + tag, [64, 256], BF16)
    kg = loc.sb("la_kg" + tag, [64, 256], BF16)
    ql = loc.sb("la_ql" + tag, [64, L], BF16)
    kl = loc.sb("la_kl" + tag, [64, L], BF16)
    klsm = loc.sb("la_ks" + tag, [64, L], BF16)
    klsmT = loc.sb("la_kT" + tag, [128, NLC * 64], BF16)
    vg = loc.sb("la_vg" + tag, [128, 2 * 64], BF16)
    vl = loc.sb("la_vl" + tag, [128, NLC * 64], BF16)
    pg = loc.sb("la_pg" + tag, [128, 2 * 256], BF16)
    pgT = loc.sb("la_pT" + tag, [128, 2 * 204], BF16)
    kv_sb = loc.sb("la_kv" + tag, [64, 64], BF16)
    og = loc.sb("la_og" + tag, [64, 256], BF16)
    lo = loc.sb("la_lo" + tag, [64, L], BF16)
    cols = loc.sb("la_c" + tag, [128, 8], F32)
    k.ps_anchor = loc.ps("la_anc" + tag, [1, 8], F32)
    ps_g = loc.ps("la_psg" + tag, [128, 256], F32)
    ps_t = loc.ps("la_pst" + tag, [128, 128], BF16)
    ps_kv = loc.ps("la_psk" + tag, [64, 64], F32)
    ps_o = loc.ps("la_pso" + tag, [64, 512], F32)

    sld = k.sem("la_d")
    st = k.sem("la_t")
    sv = k.sem("la_v")
    sa = k.sem("la_a")
    sst = k.sem("la_s")
    sstf = k.sem("la_sf")
    ph = Phase(k)
    sld0, st0, sv0, sa0, sst0, sstf0 = sld.n, st.n, sv.n, sa.n, sst.n, sstf.n

    ST_BH = 2 + 4 + 1 + NLC + 1 + 4
    SV_BH = 8 + 4 + 3 + NLC + 1
    SA_BH = 2 + 1 + 1 + 4
    SLD_BH = 7 * 16
    SLD_BH = 8 * 16
    SST_BH = 2 * 16

    def stb(bh):
        return st0 + bh * ST_BH

    def svb(bh):
        return sv0 + bh * SV_BH

    def sab(bh):
        return sa0 + bh * SA_BH

    blk = nc.Block()
    blk.__enter__()

    @blk.sync
    def _(sync):
        ph.begin(sync)
        for bh in range(BH):
            b, h = divmod(bh, NH)
            r0, c0 = h * 64, b * S
            sync.wait_ge(st.h, stb(bh))
            sld.inc(sync.dma_start(out=qg[:, 0:G], in_=q_fm[r0:r0 + 64, c0:c0 + G]), 16)
            sld.inc(sync.dma_start(out=kg[:, 0:G], in_=k_fm[r0:r0 + 64, c0:c0 + G]), 16)
            sld.inc(sync.dma_start(out=ql[:, :], in_=q_fm[r0:r0 + 64, c0 + G:c0 + S]), 16)
            sld.inc(sync.dma_start(out=kl[:, :], in_=k_fm[r0:r0 + 64, c0 + G:c0 + S]), 16)
            sld.inc(sync.dma_start(out=vg[:, 0:64], in_=v_tm[c0:c0 + 128, r0:r0 + 64]), 16)
            sld.inc(sync.dma_start(out=vg[0:G - 128, 64:128],
                                   in_=v_tm[c0 + 128:c0 + G, r0:r0 + 64]), 16)
            sld.inc(sync.dma_start(
                out=vl[:, 0:(NLC - 1) * 64].rearrange("p (j d) -> p j d", d=64),
                in_=v_tm[c0 + G:c0 + G + (NLC - 1) * 128, r0:r0 + 64]
                    .rearrange("(j p) d -> p j d", p=128)), 16)
            sld.inc(sync.dma_start(out=vl[0:lastc, (NLC - 1) * 64:NLC * 64],
                                   in_=v_tm[c0 + G + (NLC - 1) * 128:c0 + S,
                                            r0:r0 + 64]), 16)
        ph.done_q(sync, sld)

    @blk.tensor
    def _(t):
        ph.begin(t)
        for bh in range(BH):
            t.wait_ge(sld.h, sld0 + SLD_BH * (bh + 1))
            if bh >= 1:
                t.wait_ge(sv.h, svb(bh))
                t.wait_ge(sa.h, sab(bh))
            for q2 in range(2):
                sz = QSZ[q2]
                st.inc(t.matmul(ps_g[0:sz, 0:G], qg[:, q2 * 128:q2 * 128 + sz],
                                kg[:, 0:G], start=True, stop=True), 1)
                t.wait_ge(sv.h, svb(bh) + 4 * q2 + 4)
            for q2 in range(2):
                sz = QSZ[q2]
                for kc in range(2):
                    cnt = 128 if kc == 0 else G - 128
                    t.wait_ge(sv.h, svb(bh) + 8 + 2 * q2 + kc)
                    st.inc(t.transpose(ps_t[0:cnt, 0:sz],
                                       pg[0:sz, q2 * 256 + kc * 128:
                                          q2 * 256 + kc * 128 + cnt],
                                       k.identb[0:sz, 0:sz]), 1)
            t.wait_ge(sv.h, svb(bh) + 12)
            t.matmul(ps_o[:, 0:G], vg[:, 0:64], pgT[:, 0:G],
                     start=True, stop=False)
            st.inc(t.matmul(ps_o[:, 0:G], vg[0:G - 128, 64:128],
                            pgT[0:G - 128, 204:204 + G], start=False,
                            stop=True), 1)
            for j in range(NLC):
                cnt = 128 if j < NLC - 1 else lastc
                t.wait_ge(sv.h, svb(bh) + 15 + j)
                st.inc(t.transpose(ps_t[0:cnt, 0:64],
                                   klsm[:, j * 128:j * 128 + cnt],
                                   k.identb[0:64, 0:64]), 1)
            t.wait_ge(sv.h, svb(bh) + 15 + NLC)
            ins = None
            for j in range(NLC):
                cnt = 128 if j < NLC - 1 else lastc
                ins = t.matmul(ps_kv[:, :], vl[0:cnt, j * 64:(j + 1) * 64],
                               klsmT[0:cnt, j * 64:(j + 1) * 64],
                               start=(j == 0), stop=(j == NLC - 1))
            st.inc(ins, 1)
            t.wait_ge(sv.h, svb(bh) + SV_BH)
            off = 0
            for w in range(4):
                ww = 512 if w < 3 else L - 1536
                if w >= 1:
                    t.wait_ge(sa.h, sab(bh) + 4 + w)
                else:
                    t.wait_ge(sa.h, sab(bh) + 3)
                st.inc(t.matmul(ps_o[:, 0:ww], kv_sb[:, :],
                                ql[:, off:off + ww], start=True, stop=True), 1)
                off += ww
        ph.done(_last_inc_t(t, k))

    @blk.vector
    def _(v):
        ph.begin(v)
        for bh in range(BH):
            base = svb(bh)
            for q2 in range(2):
                sz = QSZ[q2]
                v.wait_ge(st.h, stb(bh) + q2 + 1)
                v.reduce_max(cols[0:sz, 0:1], ps_g[0:sz, 0:G], axis=AX.X,
                             negate=True).then_inc(sv.h, 1)
                v.wait_ge(sv.h, base + 4 * q2 + 1)
                v.tensor_scalar_mul(cols[0:sz, 1:2], cols[0:sz, 0:1], 0.125).then_inc(sv.h, 1)
                v.wait_ge(sa.h, sab(bh) + q2 + 1)
                v.reciprocal(cols[0:sz, 3:4], cols[0:sz, 2:3]).then_inc(sv.h, 1)
                v.wait_ge(sv.h, base + 4 * q2 + 3)
                v.tensor_scalar_mul(pg[0:sz, q2 * 256:q2 * 256 + G],
                                    pg[0:sz, q2 * 256:q2 * 256 + G],
                                    cols[0:sz, 3:4]).then_inc(sv.h, 1)
            for q2 in range(2):
                sz = QSZ[q2]
                for kc in range(2):
                    cnt = 128 if kc == 0 else G - 128
                    v.wait_ge(st.h, stb(bh) + 2 + 2 * q2 + kc + 1)
                    v.tensor_copy(pgT[0:cnt, kc * 204 + q2 * 128:
                                      kc * 204 + q2 * 128 + sz],
                                  ps_t[0:cnt, 0:sz]).then_inc(sv.h, 1)
            v.wait_ge(sld.h, sld0 + SLD_BH * (bh + 1))
            v.reduce_max(cols[0:64, 4:5], kl[:, :], axis=AX.X, negate=True).then_inc(sv.h, 1)
            v.wait_ge(sa.h, sab(bh) + 4)
            v.reciprocal(cols[0:64, 6:7], cols[0:64, 5:6]).then_inc(sv.h, 1)
            v.wait_ge(sv.h, base + 14)
            v.tensor_scalar_mul(klsm[:, :], klsm[:, :], cols[0:64, 6:7]).then_inc(sv.h, 1)
            for j in range(NLC):
                cnt = 128 if j < NLC - 1 else lastc
                v.wait_ge(st.h, stb(bh) + 7 + j + 1)
                v.tensor_copy(klsmT[0:cnt, j * 64:(j + 1) * 64],
                              ps_t[0:cnt, 0:64]).then_inc(sv.h, 1)
            v.wait_ge(st.h, stb(bh) + 7 + NLC + 1)
            v.tensor_copy(kv_sb[:, :], ps_kv[:, :]).then_inc(sv.h, 1)
        sv.n = sv0 + BH * SV_BH
        ph.done(_last_inc(v, k))

    @blk.scalar
    def _(s):
        ph.begin(s)
        for bh in range(BH):
            base = sab(bh)
            for q2 in range(2):
                sz = QSZ[q2]
                s.wait_ge(sv.h, svb(bh) + 4 * q2 + 2)
                s.activation(pg[0:sz, q2 * 256:q2 * 256 + G], ps_g[0:sz, 0:G],
                             AF.Exp, bias=cols[0:sz, 1:2], scale=0.125,
                             accum_out=cols[0:sz, 2:3]).then_inc(sa.h, 1)
            s.wait_ge(st.h, stb(bh) + 7)
            if bh >= 1:
                s.wait_ge(sstf.h, sstf0 + 2 * bh - 1)
            s.activation(og[:, 0:G], ps_o[:, 0:G], AF.Copy,
                         scale=1.0).then_inc(sa.h, 1)
            s.wait_ge(sv.h, svb(bh) + 13)
            s.activation(klsm[:, :], kl[:, :], AF.Exp, bias=cols[0:64, 4:5],
                         accum_out=cols[0:64, 5:6]).then_inc(sa.h, 1)
            if bh >= 1:
                s.wait_ge(sstf.h, sstf0 + 2 * bh)
            off = 0
            for w in range(4):
                ww = 512 if w < 3 else L - 1536
                s.wait_ge(st.h, stb(bh) + 7 + NLC + 1 + w + 1)
                s.activation(lo[:, off:off + ww], ps_o[:, 0:ww], AF.Copy,
                             scale=1.0).then_inc(sa.h, 1)
                off += ww
        sa.n = sa0 + BH * SA_BH
        ph.done(_last_inc_a(s, k))

    @blk.gpsimd
    def _(g):
        ph.begin(g)
        for bh in range(BH):
            b, h = divmod(bh, NH)
            r0, c0 = h * 64, b * S
            for snum in range(2):
                gw = bh * 2 + snum
                if gw >= 1:
                    g.wait_ge(sst.h, sst.n)
                    g.sem_inc(sstf.h, 1)
                    sstf.n += 1
                if snum == 0:
                    g.wait_ge(sa.h, sab(bh) + 3)
                    sst.inc(g.dma_start(out=a_fm[r0:r0 + 64, c0:c0 + G],
                                        in_=og[:, 0:G]), 16)
                else:
                    g.wait_ge(sa.h, sab(bh) + SA_BH)
                    sst.inc(g.dma_start(out=a_fm[r0:r0 + 64, c0 + G:c0 + S],
                                        in_=lo[:, :]), 16)
        ph.done_q(g, sst)
    blk.__exit__(None, None, None)
    loc.close()


def emit_cast(k, dst, src, rows, tag, nsplit=4):
    ph = Phase(k)
    sg = k.sem("wc_g")
    blk = k.nc.Block()
    blk.__enter__()

    @blk.gpsimd
    def _(g):
        ph.begin(g)
        step = rows // nsplit
        for i in range(nsplit):
            sg.inc(g.dma_start(out=dst[i * step:(i + 1) * step, :],
                               in_=src[i * step:(i + 1) * step, :]), 16)
        ph.done_q(g, sg)
    blk.__exit__(None, None, None)


def emit_transpose_mat(k, dst, src, R, C, tag):
    ph = Phase(k)
    sdl = [k.sem("wt_d%d" % j) for j in range(2)]
    so = k.sem("wt_o")
    sf = k.sem("wt_f")
    loc = Loc(k)
    buf = loc.sb("wt_b" + tag, [128, 2 * R], BF16)
    sdl0 = [x.n for x in sdl]
    so0, sf0 = so.n, sf.n
    nch = C // 128
    blk = k.nc.Block()
    blk.__enter__()

    @blk.sync
    def _(sync):
        ph.begin(sync)
        for c in range(nch):
            b = c % 2
            if c >= 2:
                sync.wait_ge(sf.h, sf0 + (c - 1))
            sdl[b].inc(sync.dma_start(out=buf[:, b * R:(b + 1) * R],
                                      in_=src[:, c * 128:(c + 1) * 128],
                                      transpose=True), 16)
        for j in range(2):
            sync.wait_ge(sdl[j].h, sdl[j].n)
        sync.sem_inc(k.bar.h, 1)
    k.bar.n += 1

    @blk.gpsimd
    def _(g):
        ph.begin(g)
        for c in range(nch):
            b = c % 2
            if c >= 1:
                g.wait_ge(so.h, so0 + 16 * c)
                g.sem_inc(sf.h, 1)
                sf.n += 1
            g.wait_ge(sdl[b].h, sdl0[b] + 16 * (c // 2 + 1))
            so.inc(g.dma_start(out=dst[c * 128:(c + 1) * 128, :],
                               in_=buf[:, b * R:(b + 1) * R]), 16)
        ph.done_q(g, so)
    blk.__exit__(None, None, None)
    loc.close()


def emit_rotate(k, dst, src, R, tag):
    ph = Phase(k)
    sdl = [k.sem("wr_d%d" % j) for j in range(2)]
    sv = k.sem("wr_v")
    so = k.sem("wr_o")
    sf = k.sem("wr_f")
    loc = Loc(k)
    bin_ = loc.sb("wr_i" + tag, [128, 2 * D], BF16)
    bout = loc.sb("wr_o" + tag, [128, 2 * D], BF16)
    sdl0 = [x.n for x in sdl]
    sv0, so0, sf0 = sv.n, so.n, sf.n
    nch = R // 128
    blk = k.nc.Block()
    blk.__enter__()

    @blk.sync
    def _(sync):
        ph.begin(sync)
        for c in range(nch):
            b = c % 2
            if c >= 2:
                sync.wait_ge(sv.h, sv0 + 2 * (c - 1))
            sdl[b].inc(sync.dma_start(out=bin_[:, b * D:(b + 1) * D],
                                      in_=src[c * 128:(c + 1) * 128, :]), 16)
        for j in range(2):
            sync.wait_ge(sdl[j].h, sdl[j].n)
        sync.sem_inc(k.bar.h, 1)
    k.bar.n += 1

    @blk.vector
    def _(v):
        ph.begin(v)
        for c in range(nch):
            b = c % 2
            v.wait_ge(sdl[b].h, sdl0[b] + 16 * (c // 2 + 1))
            if c >= 2:
                v.wait_ge(sf.h, sf0 + c - 1)
            iv = bin_[:, b * D:(b + 1) * D].rearrange("p (x two c) -> p x two c", two=2, c=32)
            ov = bout[:, b * D:(b + 1) * D].rearrange("p (x two c) -> p x two c", two=2, c=32)
            v.tensor_scalar_mul(ov[:, :, 0, :], iv[:, :, 1, :], -1.0).then_inc(sv.h, 1)
            v.wait_ge(sv.h, sv0 + 2 * c + 1)
            v.tensor_scalar_mul(ov[:, :, 1, :], iv[:, :, 0, :], 1.0).then_inc(sv.h, 1)
        sv.n = sv0 + 2 * nch
        ph.done(_last_inc(v, k))

    @blk.gpsimd
    def _(g):
        ph.begin(g)
        for c in range(nch):
            b = c % 2
            if c >= 1:
                g.wait_ge(so.h, so0 + 16 * c)
                g.sem_inc(sf.h, 1)
                sf.n += 1
            g.wait_ge(sv.h, sv0 + 2 * (c + 1))
            so.inc(g.dma_start(out=dst[c * 128:(c + 1) * 128, :],
                               in_=bout[:, b * D:(b + 1) * D]), 16)
        ph.done_q(g, so)
    blk.__exit__(None, None, None)
    loc.close()


def build_prep():
    k = K("prep")
    nc = k.nc
    f = lambda n, s: k.dram(n, s, F32, kind="ExternalInput")
    o = lambda n, s: k.dram(n, s, BF16, kind="ExternalOutput")
    i_ = {}
    for n, shape in [("L0q", [D, D]), ("L0k", [D, D]), ("L0v", [D, D]),
                     ("L0o", [D, D]), ("A1q", [D, D]), ("A1kv", [D, 512]),
                     ("A1o", [D, D]),
                     ("F0w1", [E, D, H_FFN]), ("F0w2", [E, H_FFN, D]),
                     ("L1q", [D, D]), ("L1k", [D, D]), ("L1v", [D, D]),
                     ("L1o", [D, D]),
                     ("F1w1", [E, H_FFN, D]), ("F1w2", [E, D, H_FFN])]:
        i_[n] = f(n, shape)
    outs = {}
    for n, shape in [("L0q_b", [D, D]), ("L0qr_b", [D, D]), ("L0k_b", [D, D]),
                     ("L0kr_b", [D, D]), ("L0v_b", [D, D]), ("L0o_b", [D, D]),
                     ("A1q_b", [D, D]), ("A1kv_b", [D, 512]), ("A1o_b", [D, D]),
                     ("F0w1_b", [E, D, H_FFN]), ("F0w2_b", [E, H_FFN, D]),
                     ("L1qT_b", [D, D]), ("L1qTr_b", [D, D]),
                     ("L1kT_b", [D, D]), ("L1kTr_b", [D, D]),
                     ("L1vT_b", [D, D]), ("L1oT_b", [D, D]),
                     ("F1w1T_b", [E, D, H_FFN]), ("F1w2T_b", [E, H_FFN, D])]:
        outs[n] = o(n, shape)
    tmp = {n: k.dram("tmp_" + n, shape, BF16) for n, shape in
           [("L1q", [D, D]), ("L1k", [D, D]), ("L1v", [D, D]), ("L1o", [D, D]),
            ("F1w1", [E, H_FFN, D]), ("F1w2", [E, D, H_FFN])]}
    common_init(k)
    for n in ["L0q", "L0k", "L0v", "L0o", "A1q", "A1kv", "A1o"]:
        emit_cast(k, outs[n + "_b"], i_[n], D, "c" + n)
    for e in range(E):
        emit_cast(k, outs["F0w1_b"][e], i_["F0w1"][e], D, "cF0w1%d" % e, nsplit=2)
        emit_cast(k, outs["F0w2_b"][e], i_["F0w2"][e], H_FFN, "cF0w2%d" % e, nsplit=2)
    for n in ["L1q", "L1k", "L1v", "L1o"]:
        emit_cast(k, tmp[n], i_[n], D, "c" + n)
        emit_transpose_mat(k, outs[n + "T_b"], tmp[n], D, D, "t" + n)
    for e in range(E):
        emit_cast(k, tmp["F1w1"][e], i_["F1w1"][e], H_FFN, "cF1w1%d" % e, nsplit=2)
        emit_transpose_mat(k, outs["F1w1T_b"][e], tmp["F1w1"][e], H_FFN, D,
                           "tF1w1%d" % e)
        emit_cast(k, tmp["F1w2"][e], i_["F1w2"][e], D, "cF1w2%d" % e, nsplit=2)
        emit_transpose_mat(k, outs["F1w2T_b"][e], tmp["F1w2"][e], D, H_FFN,
                           "tF1w2%d" % e)
    emit_rotate(k, outs["L0qr_b"], outs["L0q_b"], D, "rL0q")
    emit_rotate(k, outs["L0kr_b"], outs["L0k_b"], D, "rL0k")
    emit_rotate(k, outs["L1qTr_b"], outs["L1qT_b"], D, "rL1q")
    emit_rotate(k, outs["L1kTr_b"], outs["L1kT_b"], D, "rL1k")
    return k


def build_s_l0ffn():
    k = K("s_l0ffn")
    xin = k.dram("xin", [T, D], F32, kind="ExternalInput")
    wg = k.dram("wg", [D, E], F32, kind="ExternalInput")
    w1b = k.dram("w1b", [E, D, H_FFN], BF16, kind="ExternalInput")
    w2b = k.dram("w2b", [E, H_FFN, D], BF16, kind="ExternalInput")
    b1 = k.dram("b1", [E, H_FFN], F32, kind="ExternalInput")
    b2 = k.dram("b2", [E, D], F32, kind="ExternalInput")
    xout = k.dram("xout", [T, D], F32, kind="ExternalOutput")
    t_tm = k.dram("t_tm", [T, D], BF16)
    t_fm = k.dram("t_fm", [D, T], BF16)
    common_init(k)
    gates = k.sb("gates", [128, NT * E], F32)
    emit_copy(k, xout, xin, T, D, "x")
    emit_ln(k, xin, t_tm, "0")
    emit_tf(k, t_tm, t_fm, "0")
    emit_routing_l0(k, t_fm, wg, gates, "0")
    emit_ffn(k, t_fm, w1b, w2b, b1, b2, gates, xout, "0")
    return k


def build_s_l0attn():
    k = K("s_l0attn")
    xin = k.dram("xin", [T, D], F32, kind="ExternalInput")
    wq = k.dram("wq", [D, D], BF16, kind="ExternalInput")
    wqr = k.dram("wqr", [D, D], BF16, kind="ExternalInput")
    wk = k.dram("wk", [D, D], BF16, kind="ExternalInput")
    wkr = k.dram("wkr", [D, D], BF16, kind="ExternalInput")
    wv = k.dram("wv", [D, D], BF16, kind="ExternalInput")
    wo = k.dram("wo", [D, D], BF16, kind="ExternalInput")
    xout = k.dram("xout", [T, D], F32, kind="ExternalOutput")
    t_tm = k.dram("t_tm", [T, D], BF16)
    t_fm = k.dram("t_fm", [D, T], BF16)
    q_fm = k.dram("q_fm", [D, T], BF16)
    k_fm = k.dram("k_fm", [D, T], BF16)
    v_tm = k.dram("v_tm", [T, D], BF16)
    a_fm = k.dram("a_fm", [D, T], BF16)
    common_init(k)
    rl = Loc(k)
    emit_rope_tables(k, rl)
    emit_copy(k, xout, xin, T, D, "x")
    emit_ln(k, xin, t_tm, "0")
    emit_tf(k, t_tm, t_fm, "0")
    emit_proj_fm(k, t_fm, wq.ap(), q_fm, D, "q", rope=True, w_rot=wqr.ap())
    emit_proj_fm(k, t_fm, wk.ap(), k_fm, D, "k", rope=True, w_rot=wkr.ap())
    emit_proj_tm(k, t_fm, wv.ap(), v_tm, D, "v")
    rl.close()
    emit_l0attn(k, q_fm, k_fm, v_tm, a_fm, "a")
    emit_proj_res(k, a_fm, wo.ap(), xout, "o", scale=1.0)
    return k


def build_s_l1ffn():
    k = K("s_l1ffn")
    xin = k.dram("xin", [T, D], F32, kind="ExternalInput")
    anw = k.dram("anw", [D], F32, kind="ExternalInput")
    wqT = k.dram("wqT", [D, D], BF16, kind="ExternalInput")
    wqTr = k.dram("wqTr", [D, D], BF16, kind="ExternalInput")
    wkT = k.dram("wkT", [D, D], BF16, kind="ExternalInput")
    wkTr = k.dram("wkTr", [D, D], BF16, kind="ExternalInput")
    wvT = k.dram("wvT", [D, D], BF16, kind="ExternalInput")
    woT = k.dram("woT", [D, D], BF16, kind="ExternalInput")
    mnw = k.dram("mnw", [D], F32, kind="ExternalInput")
    gw = k.dram("gw", [E, D], F32, kind="ExternalInput")
    w1T = k.dram("w1T", [E, D, H_FFN], BF16, kind="ExternalInput")
    w2T = k.dram("w2T", [E, H_FFN, D], BF16, kind="ExternalInput")
    b1 = k.dram("b1", [E, H_FFN], F32, kind="ExternalInput")
    b2 = k.dram("b2", [E, D], F32, kind="ExternalInput")
    xout = k.dram("xout", [T, D], F32, kind="ExternalOutput")
    t_tm = k.dram("t_tm", [T, D], BF16)
    t1_tm = k.dram("t1_tm", [T, D], BF16)
    t1_fm = k.dram("t1_fm", [D, T], BF16)
    q_fm = k.dram("q_fm", [D, T], BF16)
    k_fm = k.dram("k_fm", [D, T], BF16)
    v_tm = k.dram("v_tm", [T, D], BF16)
    a_fm = k.dram("a_fm", [D, T], BF16)
    h_int = k.dram("h_int", [T, D], F32)
    t2_tm = k.dram("t2_tm", [T, D], BF16)
    t2_fm = k.dram("t2_fm", [D, T], BF16)
    common_init(k)
    gates = k.sb("gates", [128, NT * E], F32)
    rl = Loc(k)
    emit_rope_tables(k, rl)
    emit_copy(k, xout, xin, T, D, "x")
    emit_ln(k, xin, t_tm, "0")
    emit_rmsnorm(k, t_tm, anw.ap(), t1_tm, "r1", src_dt=BF16)
    emit_tf(k, t1_tm, t1_fm, "1")
    emit_proj_fm(k, t1_fm, wqT.ap(), q_fm, D, "q", rope=True, w_rot=wqTr.ap())
    emit_proj_fm(k, t1_fm, wkT.ap(), k_fm, D, "k", rope=True, w_rot=wkTr.ap())
    emit_proj_tm(k, t1_fm, wvT.ap(), v_tm, D, "v")
    rl.close()
    emit_attn(k, q_fm, k_fm, v_tm, a_fm, "a", causal=True, alibi=False,
              kv_group=1)
    emit_proj_res(k, a_fm, woT.ap(), xout, "o", scale=0.5, t_add=t_tm,
                  h_out=h_int)
    emit_rmsnorm(k, h_int, mnw.ap(), t2_tm, "r2", src_dt=F32)
    emit_tf(k, t2_tm, t2_fm, "2")
    emit_routing_l1(k, t2_fm, gw, gates, "g")
    emit_ffn(k, t2_fm, w1T, w2T, b1, b2, gates, xout, "f")
    return k


def build_s_l1attn():
    k = K("s_l1attn")
    xin = k.dram("xin", [T, D], F32, kind="ExternalInput")
    wq = k.dram("wq", [D, D], BF16, kind="ExternalInput")
    wkv = k.dram("wkv", [D, 512], BF16, kind="ExternalInput")
    wo = k.dram("wo", [D, D], BF16, kind="ExternalInput")
    xout = k.dram("xout", [T, D], F32, kind="ExternalOutput")
    t_tm = k.dram("t_tm", [T, D], BF16)
    t_fm = k.dram("t_fm", [D, T], BF16)
    q_fm = k.dram("q_fm", [D, T], BF16)
    k_fm = k.dram("k_fm", [256, T], BF16)
    v_tm = k.dram("v_tm", [T, 256], BF16)
    a_fm = k.dram("a_fm", [D, T], BF16)
    common_init(k)
    emit_copy(k, xout, xin, T, D, "x")
    emit_ln(k, xin, t_tm, "0")
    emit_tf(k, t_tm, t_fm, "0")
    emit_proj_fm(k, t_fm, wq.ap(), q_fm, D, "q")
    emit_proj_fm(k, t_fm, wkv.ap()[:, 0:256], k_fm, 256, "k")
    emit_proj_tm(k, t_fm, wkv.ap()[:, 256:512], v_tm, 256, "v")
    emit_attn(k, q_fm, k_fm, v_tm, a_fm, "a", causal=False, alibi=True,
              kv_group=4)
    emit_proj_res(k, a_fm, wo.ap(), xout, "o", scale=1.0)
    return k


_PROGS = {}


def _prog(name, builder):
    if name not in _PROGS:
        _PROGS[name] = builder()
    return _PROGS[name]


def kernel(**inputs):
    return kernel_fused(**inputs)


def kernel_multilaunch(**inputs):
    x = np.ascontiguousarray(inputs["L0_attn_x"], dtype=np.float32)
    shards = [np.ascontiguousarray(s) for s in x.reshape(8, T, D)]
    cores = list(range(8))

    def run(k, per_core):
        res = run_bass_kernel_spmd(k.nc, per_core, core_ids=cores)
        return res.results

    kp = _prog("prep", build_prep)
    pin = dict(
        L0q=inputs["L0_attn_Wq"], L0k=inputs["L0_attn_Wk"],
        L0v=inputs["L0_attn_Wv"], L0o=inputs["L0_attn_Wout"],
        A1q=inputs["L1_attn_Wq"], A1kv=inputs["L1_attn_Wkv"],
        A1o=inputs["L1_attn_Wo"],
        F0w1=inputs["L0_ffn_W1"], F0w2=inputs["L0_ffn_W2"],
        L1q=inputs["L1_ffn_wq"], L1k=inputs["L1_ffn_wk"],
        L1v=inputs["L1_ffn_wv"], L1o=inputs["L1_ffn_wo"],
        F1w1=inputs["L1_ffn_w1"], F1w2=inputs["L1_ffn_w2"])
    pin = {k2: np.ascontiguousarray(v, dtype=np.float32) for k2, v in pin.items()}
    pres = run(kp, [pin] * 8)
    W = pres[0]

    kf0 = _prog("l0ffn", build_s_l0ffn)
    ka0 = _prog("l0attn", build_s_l0attn)
    kf1 = _prog("l1ffn", build_s_l1ffn)
    ka1 = _prog("l1attn", build_s_l1attn)

    f0_in = dict(wg=np.ascontiguousarray(inputs["L0_ffn_wg"], np.float32),
                 w1b=W["F0w1_b"], w2b=W["F0w2_b"],
                 b1=np.ascontiguousarray(inputs["L0_ffn_b1"], np.float32),
                 b2=np.ascontiguousarray(inputs["L0_ffn_b2"], np.float32))
    a0_in = dict(wq=W["L0q_b"], wqr=W["L0qr_b"], wk=W["L0k_b"],
                 wkr=W["L0kr_b"], wv=W["L0v_b"], wo=W["L0o_b"])
    f1_in = dict(anw=np.ascontiguousarray(inputs["L1_ffn_attn_norm_w"], np.float32),
                 wqT=W["L1qT_b"], wqTr=W["L1qTr_b"], wkT=W["L1kT_b"],
                 wkTr=W["L1kTr_b"], wvT=W["L1vT_b"], woT=W["L1oT_b"],
                 mnw=np.ascontiguousarray(inputs["L1_ffn_moe_norm_w"], np.float32),
                 gw=np.ascontiguousarray(inputs["L1_ffn_gate_w"], np.float32),
                 w1T=W["F1w1T_b"], w2T=W["F1w2T_b"],
                 b1=np.ascontiguousarray(inputs["L1_ffn_b1"], np.float32),
                 b2=np.ascontiguousarray(inputs["L1_ffn_b2"], np.float32))
    a1_in = dict(wq=W["A1q_b"], wkv=W["A1kv_b"], wo=W["A1o_b"])

    cur = shards
    for k2, extra in [(kf0, f0_in), (ka0, a0_in), (kf0, f0_in),
                      (kf1, f1_in), (ka1, a1_in), (kf1, f1_in)]:
        res = run(k2, [dict(xin=cur[c], **extra) for c in range(8)])
        cur = [res[c]["xout"] for c in range(8)]

    return np.concatenate([c[None] for c in cur], 0).reshape(16, S, D)


CAPC_L0 = 1024
CAPC_L1 = 1536


def emit_sub_l0ffn(k, xres, wg, w1b, w2b, b1, b2, gates, tag):
    t_tm = k.dram("t_tm" + tag, [T + 128, D], BF16)
    t_fm = k.dram("t_fm" + tag, [D, T], BF16)
    ig_gate = k.dram("ig_gate" + tag, [E * CAPC_L0], F32)
    ig_tok32 = k.dram("ig_tok32" + tag, [E * CAPC_L0], U32)
    emit_ln(k, xres, t_tm, tag)
    emit_tf(k, t_tm, t_fm, tag)
    emit_routing_l0(k, t_fm, wg, gates, tag)
    loc_w = Loc(k)
    w1sb = loc_w.sb("f_w1" + tag, [128, NF * H_FFN], BF16)
    w2sb = loc_w.sb("f_w2" + tag, [128, (H_FFN // 128) * D], BF16)
    emit_listbuild(k, gates, ig_gate, ig_tok32, t_tm, tag, CAPC_L0, 1,
                   prefetch=(w1sb, w2sb, w1b, w2b))
    emit_ffn_routed(k, t_tm, w1b, w2b, b1, b2, ig_gate, ig_tok32,
                    xres, tag, CAPC_L0, w1sb, w2sb)
    loc_w.close()


def emit_sub_l0attn(k, xres, wq, wqr, wk, wkr, wv, wo, tag):
    t_tm = k.dram("t_tm" + tag, [T, D], BF16)
    t_fm = k.dram("t_fm" + tag, [D, T], BF16)
    q_fm = k.dram("q_fm" + tag, [D, T], BF16)
    k_fm = k.dram("k_fm" + tag, [D, T], BF16)
    v_tm = k.dram("v_tm" + tag, [T, D], BF16)
    a_fm = k.dram("a_fm" + tag, [D, T], BF16)
    rl = Loc(k)
    emit_rope_tables(k, rl)
    emit_ln(k, xres, t_tm, tag)
    emit_tf(k, t_tm, t_fm, tag)
    emit_proj_fm(k, t_fm, wq.ap(), q_fm, D, tag + "q", rope=True, w_rot=wqr.ap())
    emit_proj_fm(k, t_fm, wk.ap(), k_fm, D, tag + "k", rope=True, w_rot=wkr.ap())
    emit_proj_tm(k, t_fm, wv.ap(), v_tm, D, tag + "v")
    rl.close()
    emit_l0attn(k, q_fm, k_fm, v_tm, a_fm, tag)
    emit_proj_res(k, a_fm, wo.ap(), xres, tag + "o", scale=1.0)


def emit_sub_l1ffn(k, xres, anw, wqT, wqTr, wkT, wkTr, wvT, woT, mnw, gw,
                   w1T, w2T, b1, b2, gates, tag):
    t_tm = k.dram("t_tm" + tag, [T, D], BF16)
    t1_tm = k.dram("t1_tm" + tag, [T, D], BF16)
    t1_fm = k.dram("t1_fm" + tag, [D, T], BF16)
    q_fm = k.dram("q_fm" + tag, [D, T], BF16)
    k_fm = k.dram("k_fm" + tag, [D, T], BF16)
    v_tm = k.dram("v_tm" + tag, [T, D], BF16)
    a_fm = k.dram("a_fm" + tag, [D, T], BF16)
    h_int = k.dram("h_int" + tag, [T, D], F32)
    t2_tm = k.dram("t2_tm" + tag, [T + 128, D], BF16)
    t2_fm = k.dram("t2_fm" + tag, [D, T], BF16)
    rl = Loc(k)
    emit_rope_tables(k, rl)
    emit_ln(k, xres, t_tm, tag)
    emit_rmsnorm(k, t_tm, anw.ap(), t1_tm, tag + "r1", src_dt=BF16)
    emit_tf(k, t1_tm, t1_fm, tag + "1")
    emit_proj_fm(k, t1_fm, wqT.ap(), q_fm, D, tag + "q", rope=True,
                 w_rot=wqTr.ap())
    emit_proj_fm(k, t1_fm, wkT.ap(), k_fm, D, tag + "k", rope=True,
                 w_rot=wkTr.ap())
    emit_proj_tm(k, t1_fm, wvT.ap(), v_tm, D, tag + "v")
    rl.close()
    emit_attn(k, q_fm, k_fm, v_tm, a_fm, tag, causal=True, alibi=False,
              kv_group=1)
    emit_proj_res(k, a_fm, woT.ap(), xres, tag + "o", scale=0.5, t_add=t_tm,
                  h_out=h_int)
    emit_rmsnorm(k, h_int, mnw.ap(), t2_tm, tag + "r2", src_dt=F32)
    emit_tf(k, t2_tm, t2_fm, tag + "2")
    emit_routing_l1(k, t2_fm, gw, gates, tag)
    ig_gate = k.dram("ig_gate" + tag, [E * CAPC_L1], F32)
    ig_tok32 = k.dram("ig_tok32" + tag, [E * CAPC_L1], U32)
    loc_w = Loc(k)
    w1sb = loc_w.sb("f_w1" + tag, [128, NF * H_FFN], BF16)
    w2sb = loc_w.sb("f_w2" + tag, [128, (H_FFN // 128) * D], BF16)
    emit_listbuild(k, gates, ig_gate, ig_tok32, t2_tm, tag, CAPC_L1, 2,
                   prefetch=(w1sb, w2sb, w1T, w2T))
    emit_ffn_routed(k, t2_tm, w1T, w2T, b1, b2, ig_gate, ig_tok32,
                    xres, tag + "f", CAPC_L1, w1sb, w2sb)
    loc_w.close()


def emit_sub_l1attn(k, xres, wq, wkv, wo, tag):
    t_tm = k.dram("t_tm" + tag, [T, D], BF16)
    t_fm = k.dram("t_fm" + tag, [D, T], BF16)
    q_fm = k.dram("q_fm" + tag, [D, T], BF16)
    k_fm = k.dram("k_fm" + tag, [256, T], BF16)
    v_tm = k.dram("v_tm" + tag, [T, 256], BF16)
    a_fm = k.dram("a_fm" + tag, [D, T], BF16)
    emit_ln(k, xres, t_tm, tag)
    emit_tf(k, t_tm, t_fm, tag)
    emit_proj_fm(k, t_fm, wq.ap(), q_fm, D, tag + "q")
    emit_proj_fm(k, t_fm, wkv.ap()[:, 0:256], k_fm, 256, tag + "k")
    emit_proj_tm(k, t_fm, wkv.ap()[:, 256:512], v_tm, 256, tag + "v")
    emit_attn(k, q_fm, k_fm, v_tm, a_fm, tag, causal=False, alibi=True,
              kv_group=4)
    emit_proj_res(k, a_fm, wo.ap(), xres, tag + "o", scale=1.0)


def build_fused():
    k = K("fused")
    f = lambda n, shp: k.dram(n, shp, F32, kind="ExternalInput")
    xin = f("L0_attn_x", [T, D])
    i_ = {n: f(n, shp) for n, shp in [
        ("L0_attn_Wq", [D, D]), ("L0_attn_Wk", [D, D]), ("L0_attn_Wv", [D, D]),
        ("L0_attn_Wout", [D, D]), ("L0_ffn_wg", [D, E]),
        ("L0_ffn_W1", [E, D, H_FFN]), ("L0_ffn_b1", [E, H_FFN]),
        ("L0_ffn_W2", [E, H_FFN, D]), ("L0_ffn_b2", [E, D]),
        ("L1_attn_Wq", [D, D]), ("L1_attn_Wkv", [D, 512]),
        ("L1_attn_Wo", [D, D]), ("L1_ffn_attn_norm_w", [D]),
        ("L1_ffn_wq", [D, D]), ("L1_ffn_wk", [D, D]), ("L1_ffn_wv", [D, D]),
        ("L1_ffn_wo", [D, D]), ("L1_ffn_moe_norm_w", [D]),
        ("L1_ffn_gate_w", [E, D]), ("L1_ffn_w1", [E, H_FFN, D]),
        ("L1_ffn_b1", [E, H_FFN]), ("L1_ffn_w2", [E, D, H_FFN]),
        ("L1_ffn_b2", [E, D])]}
    xout = k.dram("xout", [T, D], F32, kind="ExternalOutput")
    b = lambda n, shp: k.dram(n, shp, BF16, kind="Internal")
    W = {n: b(n, shp) for n, shp in [
        ("L0q_b", [D, D]), ("L0qr_b", [D, D]), ("L0k_b", [D, D]),
        ("L0kr_b", [D, D]), ("L0v_b", [D, D]), ("L0o_b", [D, D]),
        ("A1q_b", [D, D]), ("A1kv_b", [D, 512]), ("A1o_b", [D, D]),
        ("F0w1_b", [E, D, H_FFN]), ("F0w2_b", [E, H_FFN, D]),
        ("L1qT_b", [D, D]), ("L1qTr_b", [D, D]), ("L1kT_b", [D, D]),
        ("L1kTr_b", [D, D]), ("L1vT_b", [D, D]), ("L1oT_b", [D, D]),
        ("F1w1T_b", [E, D, H_FFN]), ("F1w2T_b", [E, H_FFN, D]),
        ("tmp_L1q", [D, D]), ("tmp_L1k", [D, D]), ("tmp_L1v", [D, D]),
        ("tmp_L1o", [D, D]), ("tmp_F1w1", [E, H_FFN, D]),
        ("tmp_F1w2", [E, D, H_FFN])]}
    common_init(k)
    gates = k.sb("gates", [128, NT * E], F32)
    for src, dst in [("L0_attn_Wq", "L0q_b"), ("L0_attn_Wk", "L0k_b"),
                     ("L0_attn_Wv", "L0v_b"), ("L0_attn_Wout", "L0o_b"),
                     ("L1_attn_Wq", "A1q_b"), ("L1_attn_Wkv", "A1kv_b"),
                     ("L1_attn_Wo", "A1o_b")]:
        emit_cast(k, W[dst], i_[src], D, "c" + dst)
    for e in range(E):
        emit_cast(k, W["F0w1_b"][e], i_["L0_ffn_W1"][e], D, "cf1%d" % e, nsplit=2)
        emit_cast(k, W["F0w2_b"][e], i_["L0_ffn_W2"][e], H_FFN, "cf2%d" % e, nsplit=2)
    for src, tm, dst in [("L1_ffn_wq", "tmp_L1q", "L1qT_b"),
                         ("L1_ffn_wk", "tmp_L1k", "L1kT_b"),
                         ("L1_ffn_wv", "tmp_L1v", "L1vT_b"),
                         ("L1_ffn_wo", "tmp_L1o", "L1oT_b")]:
        emit_cast(k, W[tm], i_[src], D, "c" + tm)
        emit_transpose_mat(k, W[dst], W[tm], D, D, "t" + dst)
    for e in range(E):
        emit_cast(k, W["tmp_F1w1"][e], i_["L1_ffn_w1"][e], H_FFN, "cg1%d" % e, nsplit=2)
        emit_transpose_mat(k, W["F1w1T_b"][e], W["tmp_F1w1"][e], H_FFN, D, "tg1%d" % e)
        emit_cast(k, W["tmp_F1w2"][e], i_["L1_ffn_w2"][e], D, "cg2%d" % e, nsplit=2)
        emit_transpose_mat(k, W["F1w2T_b"][e], W["tmp_F1w2"][e], D, H_FFN, "tg2%d" % e)
    emit_rotate(k, W["L0qr_b"], W["L0q_b"], D, "r0q")
    emit_rotate(k, W["L0kr_b"], W["L0k_b"], D, "r0k")
    emit_rotate(k, W["L1qTr_b"], W["L1qT_b"], D, "r1q")
    emit_rotate(k, W["L1kTr_b"], W["L1kT_b"], D, "r1k")
    xres = k.dram("xres_big", [T + 128, D], F32)
    emit_copy(k, xres, xin, T, D, "x")
    emit_sub_l0ffn(k, xres, i_["L0_ffn_wg"], W["F0w1_b"], W["F0w2_b"],
                   i_["L0_ffn_b1"], i_["L0_ffn_b2"], gates, "A")
    emit_sub_l0attn(k, xres, W["L0q_b"], W["L0qr_b"], W["L0k_b"], W["L0kr_b"],
                    W["L0v_b"], W["L0o_b"], "B")
    emit_sub_l0ffn(k, xres, i_["L0_ffn_wg"], W["F0w1_b"], W["F0w2_b"],
                   i_["L0_ffn_b1"], i_["L0_ffn_b2"], gates, "C")
    emit_sub_l1ffn(k, xres, i_["L1_ffn_attn_norm_w"], W["L1qT_b"],
                   W["L1qTr_b"], W["L1kT_b"], W["L1kTr_b"], W["L1vT_b"],
                   W["L1oT_b"], i_["L1_ffn_moe_norm_w"], i_["L1_ffn_gate_w"],
                   W["F1w1T_b"], W["F1w2T_b"], i_["L1_ffn_b1"],
                   i_["L1_ffn_b2"], gates, "D")
    emit_sub_l1attn(k, xres, W["A1q_b"], W["A1kv_b"], W["A1o_b"], "E")
    emit_sub_l1ffn(k, xres, i_["L1_ffn_attn_norm_w"], W["L1qT_b"],
                   W["L1qTr_b"], W["L1kT_b"], W["L1kTr_b"], W["L1vT_b"],
                   W["L1oT_b"], i_["L1_ffn_moe_norm_w"], i_["L1_ffn_gate_w"],
                   W["F1w1T_b"], W["F1w2T_b"], i_["L1_ffn_b1"],
                   i_["L1_ffn_b2"], gates, "F")
    emit_copy(k, xout, xres, T, D, "fin")
    return k


def kernel_fused(**inputs):
    x = np.ascontiguousarray(inputs["L0_attn_x"], dtype=np.float32)
    shards = [np.ascontiguousarray(s) for s in x.reshape(8, T, D)]
    k = _prog("fused", build_fused)
    base = {n: np.ascontiguousarray(inputs[n], dtype=np.float32)
            for n in inputs if n != "L0_attn_x"}
    in_maps = [dict(L0_attn_x=shards[c], **base) for c in range(8)]
    res = run_bass_kernel_spmd(k.nc, in_maps, core_ids=list(range(8)))
    cur = [res.results[c]["xout"] for c in range(8)]
    return np.concatenate([c[None] for c in cur], 0).reshape(16, S, D)
